# version 09
# baseline (speedup 1.0000x reference)
import math
import numpy as np
import ml_dtypes
import concourse.bass as bass
import concourse.mybir as mybir
from concourse.bass_utils import run_bass_kernel_spmd
from concourse.alu_op_type import AluOpType as OP
from contextlib import ExitStack

F32 = mybir.dt.float32
BF16 = mybir.dt.bfloat16
AF = mybir.ActivationFunctionType
AX = mybir.AxisListType

D = 1024
NH = 16
DH = 64
FH = 8
FW = 512
INNER = 512
HID = 2730
HIDP = 2816
MOB = HIDP // 128
TT = 512
EPS_Q = 1e-5
NEG_BIG = -3.4028234663852886e38



def _rotary_tables(S):
    d = np.arange(0, DH, 2, dtype=np.float64)
    inv_freq = 1.0 / 10000.0 ** (d / DH)
    t = np.arange(S, dtype=np.float64)
    freqs = t[:, None] * inv_freq[None, :]
    freqs = np.concatenate([freqs, freqs], axis=1)
    base = (d + 0.4 * DH) / (1.4 * DH)
    power = (t - S // 2) / (FW // 2)
    scale = base[None, :] ** power[:, None]
    scale = np.concatenate([scale, scale], axis=1)
    cos, sin = np.cos(freqs), np.sin(freqs)
    q1 = (cos * scale).T
    q2 = (sin * scale).T
    k1 = (cos / scale).T
    k2 = (sin / scale).T
    sgn = np.ones((DH, 1)); sgn[:32] = -1.0
    q2 = q2 * sgn
    k2 = k2 * sgn
    out = []
    for a in (q1, q2, k1, k2):
        out.append(np.tile(a, (2, 1)).astype(np.float32))
    return out


def _local_mask():
    kk = np.arange(1024)[:, None]
    qq = np.arange(512)[None, :]
    m = ((kk >= qq) & (kk <= qq + 512)).astype(np.float32)
    m = m.reshape(8, 128, 512).transpose(1, 0, 2).reshape(128, 8 * 512)
    return m.astype(ml_dtypes.bfloat16)



ENGS = ("te", "ve", "se", "ge", "sq")


class B:
    __slots__ = ("w", "r")

    def __init__(self):
        self.w = None
        self.r = []


class Em:

    def __init__(self):
        self.plan = {e: [] for e in ENGS}
        self.cnt = {e: 0 for e in ENGS}
        self.dcnt = {}
        self.dqueue = {}
        self.waited = {e: {} for e in ENGS}
        self.pend_r = {e: [] for e in ENGS}
        self.pend_w = {e: [] for e in ENGS}

    def _wait(self, eng, tok):
        s, v = tok
        if s == eng == "te":
            return
        if self.waited[eng].get(s, 0) >= v:
            return
        self.waited[eng][s] = v
        self.plan[eng].append(("w", s, v))

    def _deps(self, eng, reads, writes, extra):
        toks = []
        for b in reads:
            if b.w is not None:
                toks.append(b.w)
        for b in writes:
            toks.extend(b.r)
            if b.w is not None:
                toks.append(b.w)
        toks.extend(extra)
        for t in toks:
            self._wait(eng, t)

    def _post(self, tok, reads, writes):
        if tok is None:
            return
        for b in reads:
            b.r.append(tok)
        for b in writes:
            b.w = tok
            b.r = []

    def op(self, eng, fn, reads=(), writes=(), extra=(), notify=True):
        self._deps(eng, reads, writes, extra)
        tok = None
        if notify:
            self.cnt[eng] += 1
            tok = (eng, self.cnt[eng])
        self.plan[eng].append(("o", fn, tok))
        if notify:
            self._post(tok, list(reads) + self.pend_r[eng],
                       list(writes) + self.pend_w[eng])
            self.pend_r[eng] = []
            self.pend_w[eng] = []
        else:
            self.pend_r[eng].extend(reads)
            self.pend_w[eng].extend(writes)
        return tok

    def dma(self, stream, out, in_, reads=(), writes=(), extra=(), eng=None,
            transpose=False, **kw):
        if stream not in self.dqueue:
            self.dqueue[stream] = eng or {"sq": "sq", "se": "se", "ge": "ge",
                                          "ve": "ve"}[stream.split(":")[0]]
            self.dcnt[stream] = 0
        eng = self.dqueue[stream]
        if self.dcnt[stream] > 0:
            self._wait(eng, (stream, self.dcnt[stream]))
        self._deps(eng, reads, writes, extra)
        self.dcnt[stream] += 16
        tok = (stream, self.dcnt[stream])
        self.plan[eng].append(("d", out, in_, stream, transpose, kw, tok))
        self._post(tok, reads, writes)
        return tok

    def barrier(self):
        toks = [(e, self.cnt[e]) for e in ENGS if self.cnt[e] > 0]
        toks += [(s, c) for s, c in self.dcnt.items() if c > 0]
        for e in ENGS:
            for t in toks:
                if t[0] == e == "te":
                    continue
                s, v = t
                if self.waited[e].get(s, 0) >= v:
                    continue
                self.waited[e][s] = v
                self.plan[e].append(("w", s, v))

    def emit(self, nc, es):
        sems = {}
        for e in ENGS:
            sems[e] = es.enter_context(nc.semaphore(f"sem_{e}"))
        for s in self.dcnt:
            nm = "sem_" + s.replace(":", "_")
            sems[s] = es.enter_context(nc.semaphore(nm))
        n_sems = len(sems)
        if n_sems > 70:
            raise RuntimeError(f"too many semaphores: {n_sems}")
        block = es.enter_context(nc.Block())
        eng_attr = {"te": "tensor", "ve": "vector", "se": "scalar",
                    "ge": "gpsimd", "sq": "sync"}
        for e in ENGS:
            items = self.plan[e]

            def section(h, items=items, e=e):
                for it in items:
                    if it[0] == "w":
                        h.wait_ge(sems[it[1]], it[2])
                    elif it[0] == "o":
                        inst = it[1](h)
                        if it[2] is not None:
                            if inst is None:
                                raise RuntimeError("notify op returned no inst")
                            inst.then_inc(sems[e], 1)
                    else:
                        _, out, in_, s, tr, kw, tok = it
                        h.dma_start(out=out, in_=in_, transpose=tr,
                                    **kw).then_inc(sems[s], 16)
            getattr(block, eng_attr[e])(section)



def ceil_div(a, b):
    return (a + b - 1) // b


class Builder:

    def __init__(self, S, debug_taps=()):
        self.S = S
        self.NTT = S // TT
        self.NTC = S // 128
        self.NW = S // FW
        self.nc = bass.Bass("TRN2", target_bir_lowering=False)
        self.em = Em()
        self.es = ExitStack()
        self.debug_taps = set(debug_taps)
        self.taps = {}
        self._sbuf_bytes = 0
        self._declare_io()

    def _declare_io(self):
        nc, S = self.nc, self.S
        dt = nc.dram_tensor
        self.i_x = dt("x_fm", [D, S], F32, kind="ExternalInput").ap()
        for nm in ("wq_t", "wk_t", "wv_t", "wo_t"):
            setattr(self, "i_" + nm, dt(nm, [D, D], F32, kind="ExternalInput").ap())
        self.i_f_wqkv = dt("f_wqkv", [2, 128, 8, 3 * INNER], F32,
                           kind="ExternalInput").ap()
        self.i_f_wo = dt("f_wo", [2, 128, 4, D], F32, kind="ExternalInput").ap()
        self.i_f_w1a = dt("f_w1a", [2, 128, MOB, D], F32, kind="ExternalInput").ap()
        self.i_f_w1g = dt("f_w1g", [2, 128, MOB, D], F32, kind="ExternalInput").ap()
        self.i_f_w2 = dt("f_w2", [2, 128, 8, HIDP], F32, kind="ExternalInput").ap()
        self.i_b1a = dt("b1a_col", [2, 128, MOB], F32, kind="ExternalInput").ap()
        self.i_b1g = dt("b1g_col", [2, 128, MOB], F32, kind="ExternalInput").ap()
        self.i_b2 = dt("b2_col", [2, 128, 8], F32, kind="ExternalInput").ap()
        self.i_lng = dt("lng_col", [2, 128, 8], F32, kind="ExternalInput").ap()
        self.i_lnb = dt("lnb_col", [2, 128, 8], F32, kind="ExternalInput").ap()
        self.i_qs = dt("qs_col", [2, 128, 1], F32, kind="ExternalInput").ap()
        self.i_ks = dt("ks_col", [2, 128, 1], F32, kind="ExternalInput").ap()
        self.i_tbl = dt("tbl", [4, 128, S], F32, kind="ExternalInput").ap()
        self.i_mask = dt("mask_loc", [128, 8 * 512], BF16, kind="ExternalInput").ap()
        self.i_maskb = dt("maskb_loc", [128, 8 * 512], BF16, kind="ExternalInput").ap()
        self.i_ident = dt("ident", [128, 128], F32, kind="ExternalInput").ap()
        self.i_ones1 = dt("ones1", [128, 1], F32, kind="ExternalInput").ap()
        self.i_ones2 = dt("ones2", [128, 2], F32, kind="ExternalInput").ap()
        self.o_y = dt("y_fm", [D, S], F32, kind="ExternalOutput").ap()

        self.h_xq = dt("h_xq", [D, S], BF16, kind="Internal").ap()
        self.h_wt = dt("h_wt", [4, D, D], BF16, kind="Internal").ap()
        self.h_ctx = dt("h_ctx", [D, S], F32, kind="Internal").ap()
        self.h_ctxq = dt("h_ctxq", [D, S], BF16, kind="Internal").ap()
        self.h_x2 = dt("h_x2", [D, S], F32, kind="Internal").ap()
        self.h_x3 = dt("h_x3", [D, S], F32, kind="Internal").ap()
        self.h_x4 = dt("h_x4", [D, S], F32, kind="Internal").ap()
        self.h_x5 = dt("h_x5", [D, S], F32, kind="Internal").ap()
        self.h_x6 = dt("h_x6", [D, S], F32, kind="Internal").ap()
        self.h_qkv = dt("h_qkv", [3 * INNER, S], F32, kind="Internal").ap()
        self.h_ohi = dt("h_ohi", [INNER, S], BF16, kind="Internal").ap()
        self.h_olo = dt("h_olo", [INNER, S], BF16, kind="Internal").ap()
        self.h_w1hi = dt("h_w1hi", [2, 128, MOB, D], BF16, kind="Internal").ap()
        self.h_w1lo = dt("h_w1lo", [2, 128, MOB, D], BF16, kind="Internal").ap()
        self.h_w2hi = dt("h_w2hi", [128, 8, HIDP], BF16, kind="Internal").ap()
        self.h_w2lo = dt("h_w2lo", [128, 8, HIDP], BF16, kind="Internal").ap()
        self.h_rows = dt("h_rows", [64, S], F32, kind="Internal").ap()

    R_SS = 0
    R_MX = 1
    R_RHO = 2
    R_G0 = 3
    R_SWQ = 4
    R_L = 12
    R_MU = 6
    R_SIG = 7
    R_C = 8
    R_TMP = 9
    R_RHO2 = 10
    R_G02 = 11

    def tap(self, name, shape, dtype=F32):
        if name in self.debug_taps:
            ap = self.nc.dram_tensor("tap_" + name, shape, dtype,
                                     kind="ExternalOutput").ap()
            self.taps[name] = ap
            return ap
        return None

    def sb(self, name, shape, dtype=F32, scope=None):
        st = scope or getattr(self, "pes", None) or self.es
        return st.enter_context(self.nc.sbuf_tensor(name, shape, dtype))

    def ps(self, name, shape, dtype=F32):
        st = getattr(self, "pes", None) or self.es
        return st.enter_context(self.nc.psum_tensor(name, shape, dtype))


    def hb(self, name):
        if not hasattr(self, "_hb"):
            self._hb = {}
        if name not in self._hb:
            self._hb[name] = B()
        return self._hb[name]

    def row_ap(self, slot, lo, n):
        return self.h_rows[slot, lo:lo + n]

    def bcast_in(self, slot, lo, n, p):
        return self.h_rows[slot, lo:lo + n][None, :].to_broadcast([p, n])

    def nr_rsqrt(self, dst, src, sA, sB, b_src, b_all):
        em = self.em
        em.op("se", lambda h: h.activation(out=sA, in_=src, func=AF.Sqrt),
              reads=[b_all] + b_src, writes=[b_all])
        em.op("ve", lambda h: h.reciprocal(out=dst, in_=sA),
              reads=[b_all], writes=[b_all])
        em.op("ve", lambda h: h.tensor_tensor(out=sB, in0=src, in1=dst,
                                              op=OP.mult),
              reads=[b_all], writes=[b_all])
        em.op("ve", lambda h: h.tensor_tensor(out=sB, in0=sB, in1=dst,
                                              op=OP.mult),
              reads=[b_all], writes=[b_all])
        em.op("ve", lambda h: h.tensor_scalar(out=sB, in0=sB, scalar1=-0.5,
                                              scalar2=1.5, op0=OP.mult,
                                              op1=OP.add),
              reads=[b_all], writes=[b_all])
        em.op("ve", lambda h: h.tensor_tensor(out=dst, in0=dst, in1=sB,
                                              op=OP.mult),
              reads=[b_all], writes=[b_all])

    def p0_quant_x(self):
        self.quant_pipeline(self.i_x, self.h_xq, self.R_RHO,
                            self.R_G0, 'p0', None)

    def quant_pipeline(self, src, dst, r_rho, r_g0, tag, src_hb):
        em, S = self.em, self.S
        xt = [self.sb(f"{tag}_x{c}", [128, TT]) for c in range(8)]
        sqt = [self.sb(f"{tag}_sq{i}", [128, TT]) for i in range(2)]
        acc = self.sb("{}_acc".format(tag), [128, TT])
        tmp = self.sb("{}_tmp".format(tag), [64, TT])
        tmp2 = self.sb("{}_tmp2".format(tag), [128, TT])
        rows = self.sb("{}_rows".format(tag), [1, 8 * TT])
        RS, RM, RR, RD_, RC, RP, RG = (slice(0, TT), slice(TT, 2 * TT),
                                       slice(2 * TT, 3 * TT), slice(3 * TT, 4 * TT),
                                       slice(4 * TT, 5 * TT), slice(5 * TT, 6 * TT),
                                       slice(6 * TT, 7 * TT))
        RSC = slice(7 * TT, 8 * TT)
        brho = self.sb("{}_brho".format(tag), [128, TT])
        t0 = [self.sb(f"{tag}_t0{i}", [128, TT]) for i in range(2)]
        t1 = [self.sb(f"{tag}_t1{i}", [128, TT]) for i in range(2)]
        ti = [self.sb(f"{tag}_ti{i}", [128, TT], mybir.dt.int32)
              for i in range(2)]
        xqt = [self.sb(f"{tag}_xq{i}", [128, TT], BF16) for i in range(2)]
        pss = self.ps("{}_pss".format(tag), [1, TT])
        bx = [B() for _ in range(8)]
        bsq = [B(), B()]
        bacc, btmp, brows, bbrho = B(), B(), B(), B()
        btmp2 = B()
        bps = B()
        bt = [[B(), B()] for _ in range(2)]
        bti = [B(), B()]
        bxq = [B(), B()]
        tap_rho = self.tap(tag + "_rho", [self.NTT, TT])
        tap_mx = self.tap(tag + "_mx", [self.NTT, TT])

        for tt in range(self.NTT):
            sl = slice(tt * TT, (tt + 1) * TT)
            for c in range(8):
                self.em.dma(f"sq:xld{c % 4}", xt[c][:],
                            src[c * 128:(c + 1) * 128, sl],
                            writes=[bx[c]], reads=[src_hb] if src_hb else [])
            for c in range(8):
                s = sqt[c % 2]
                em.op("ve", (lambda h, c=c, s=s: h.tensor_tensor(
                    out=s[:], in0=xt[c][:], in1=xt[c][:], op=OP.mult)),
                    reads=[bx[c]], writes=[bsq[c % 2]])
                em.op("te", (lambda h, c=c, s=s: h.matmul(
                    out=pss[:], lhsT=self.c_ones1[:], rhs=s[:],
                    start=(c == 0), stop=(c == 7))),
                    reads=[bsq[c % 2], self.b_ones1], writes=[bps],
                    notify=True)
            em.op("ve", lambda h: h.tensor_tensor(
                out=acc[:], in0=xt[0][:], in1=xt[0][:], op=OP.mult),
                reads=[bx[0]], writes=[bacc])
            for c in range(1, 8):
                em.op("ve", (lambda h, c=c: h.tensor_tensor(
                    out=tmp2[:], in0=xt[c][:], in1=xt[c][:], op=OP.mult)),
                    reads=[bx[c]], writes=[btmp2])
                em.op("ve", lambda h: h.tensor_tensor(
                    out=acc[:], in0=acc[:], in1=tmp2[:], op=OP.max),
                    reads=[btmp2, bacc], writes=[bacc])
            n = 64
            while n >= 1:
                self.em.dma("ge:tree", tmp[0:n, :], acc[n:2 * n, :],
                            reads=[bacc], writes=[btmp])
                em.op("ve", (lambda h, n=n: h.tensor_tensor(
                    out=acc[0:n, :], in0=acc[0:n, :], in1=tmp[0:n, :],
                    op=OP.max)), reads=[btmp, bacc], writes=[bacc])
                n //= 2
            em.op("ve", lambda h: h.tensor_copy(out=rows[:, RS], in_=pss[:]),
                  reads=[bps], writes=[brows])
            em.op("ve", lambda h: h.tensor_scalar(
                out=rows[:, RM], in0=acc[0:1, :], scalar1=1e-30, scalar2=None,
                op0=OP.max), reads=[bacc], writes=[brows])
            self.nr_rsqrt(rows[:, RC], rows[:, RM], rows[:, RSC], rows[:, RD_],
                          [], brows)
            em.op("ve", lambda h: h.tensor_tensor(
                out=rows[:, RM], in0=rows[:, RM], in1=rows[:, RC], op=OP.mult),
                reads=[brows], writes=[brows])
            em.op("ve", lambda h: h.tensor_scalar(
                out=rows[:, RSC], in0=rows[:, RS], scalar1=1.0 / D, scalar2=1e-6,
                op0=OP.mult, op1=OP.add), reads=[brows], writes=[brows])
            self.nr_rsqrt(rows[:, RR], rows[:, RSC], rows[:, RD_], rows[:, RC],
                          [], brows)
            em.op("ve", lambda h: h.tensor_tensor(
                out=rows[:, RD_], in0=rows[:, RR], in1=rows[:, RM], op=OP.mult),
                reads=[brows], writes=[brows])
            em.op("ve", lambda h: h.tensor_scalar(
                out=rows[:, RD_], in0=rows[:, RD_], scalar1=EPS_Q, scalar2=None,
                op0=OP.max), reads=[brows], writes=[brows])
            em.op("ve", lambda h: h.reciprocal(out=rows[:, RC], in_=rows[:, RD_]),
                  reads=[brows], writes=[brows])
            em.op("ve", lambda h: h.scalar_tensor_tensor(
                out=rows[:, RP], in0=rows[:, RR], scalar=127.0, in1=rows[:, RC],
                op0=OP.mult, op1=OP.mult), reads=[brows], writes=[brows])
            em.op("ve", lambda h: h.tensor_scalar(
                out=rows[:, RG], in0=rows[:, RD_], scalar1=1.0 / 127.0,
                scalar2=None, op0=OP.mult), reads=[brows], writes=[brows])
            hrho, hg0 = self.hb(tag + "rho"), self.hb(tag + "g0")
            self.em.dma("ge:rows", self.row_ap(r_rho, tt * TT, TT),
                        rows[:, RP], reads=[brows], writes=[hrho])
            self.em.dma("ge:rows", self.row_ap(r_g0, tt * TT, TT),
                        rows[:, RG], reads=[brows], writes=[hg0])
            if tap_rho is not None:
                self.em.dma("ge:rows", tap_rho[tt, :][None, :],
                            rows[:, RP], reads=[brows])
                self.em.dma("ge:rows", tap_mx[tt, :][None, :],
                            rows[:, RM], reads=[brows])
            self.em.dma("se:bc", brho[:], self.bcast_in(r_rho, tt * TT, TT, 128),
                        reads=[hrho], writes=[bbrho])
            for c in range(8):
                ln = c % 2
                eng = "ve" if ln == 0 else "ge"
                em.op(eng, (lambda h, c=c, ln=ln: h.tensor_tensor(
                    out=t0[ln][:], in0=xt[c][:], in1=brho[:], op=OP.mult)),
                    reads=[bx[c], bbrho], writes=[bt[ln][0]])
                em.op("ve", (lambda h, ln=ln: h.tensor_scalar(
                    out=ti[ln][:], in0=t0[ln][:], scalar1=0.5, scalar2=None,
                    op0=OP.add)), reads=[bt[ln][0]], writes=[bti[ln]])
                em.op("ve", (lambda h, ln=ln: h.scalar_tensor_tensor(
                    out=t1[ln][:], in0=t0[ln][:], scalar=0.5, in1=ti[ln][:],
                    op0=OP.add, op1=OP.subtract)),
                    reads=[bti[ln], bt[ln][0]], writes=[bt[ln][1]])
                em.op(eng, (lambda h, ln=ln: h.tensor_scalar(
                    out=t1[ln][:], in0=t1[ln][:], scalar1=0.0, scalar2=None,
                    op0=OP.is_lt)), reads=[bt[ln][1]], writes=[bt[ln][1]])
                em.op("ve", (lambda h, ln=ln: h.tensor_tensor(
                    out=t0[ln][:], in0=ti[ln][:], in1=t1[ln][:],
                    op=OP.subtract)), reads=[bti[ln], bt[ln][1]],
                    writes=[bt[ln][0]])
                em.op(eng, (lambda h, ln=ln: h.tensor_scalar(
                    out=xqt[ln][:], in0=t0[ln][:], scalar1=-128.0, scalar2=127.0,
                    op0=OP.max, op1=OP.min)), reads=[bt[ln][0]], writes=[bxq[ln]])
                self.em.dma("ge:st1" if ln else "sq:st0",
                            dst[c * 128:(c + 1) * 128, sl], xqt[ln][:],
                            reads=[bxq[ln]], writes=[self.hb(tag + "dst")])
        self.em.barrier()

    def p1_ternarize(self):
        em = self.em
        wch = [self.sb(f"p1_w{i}", [128, D]) for i in range(2)]
        wsum = self.sb("p1_wsum", [128, 8])
        wtot = self.sb("p1_wtot", [128, 1])
        srow = self.sb("p1_srow", [1, 8])
        sbc = self.sb("p1_sbc", [128, 1])
        ya = self.sb("p1_ya", [128, D])
        ab = self.sb("p1_ab", [128, D])
        mk = self.sb("p1_mk", [128, D])
        sg = self.sb("p1_sg", [128, D])
        tq = self.sb("p1_tq", [128, D], BF16)
        pmm = self.ps("p1_pmm", [1, 1])
        mats = [self.i_wq_t, self.i_wk_t, self.i_wv_t, self.i_wo_t]
        bw = [B(), B()]
        bws, bwt, bsr, bsb = B(), B(), B(), B()
        bya, bab, bmk, bsg, btq = B(), B(), B(), B(), B()
        bp = B()
        for M in range(4):
            for c in range(8):
                ln = c % 2
                self.em.dma(f"sq:xld{ln}", wch[ln][:],
                            mats[M][c * 128:(c + 1) * 128, :], writes=[bw[ln]])
                em.op("ve", (lambda h, c=c, ln=ln: h.reduce_sum(
                    out=wsum[:, c:c + 1], in_=wch[ln][:], axis=AX.X,
                    apply_absolute_value=True)), reads=[bw[ln]], writes=[bws])
            em.op("ve", lambda h: h.reduce_sum(out=wtot[:], in_=wsum[:], axis=AX.X),
                  reads=[bws], writes=[bwt])
            em.op("te", lambda h: h.matmul(out=pmm[:], lhsT=self.c_ones1[:],
                                           rhs=wtot[:], start=True, stop=True),
                  reads=[bwt, self.b_ones1], writes=[bp])
            em.op("ve", lambda h: h.tensor_scalar(
                out=srow[:, 0:1], in0=pmm[:], scalar1=1.0 / (D * D), scalar2=EPS_Q,
                op0=OP.mult, op1=OP.max), reads=[bp], writes=[bsr])
            em.op("ve", lambda h: h.reciprocal(out=srow[:, 1:2], in_=srow[:, 0:1]),
                  reads=[bsr], writes=[bsr])
            hsw = self.hb("sw")
            self.em.dma("ge:rows", self.row_ap(self.R_SWQ, M, 1),
                        srow[:, 0:1], reads=[bsr], writes=[hsw])
            self.em.dma("ge:rows", self.row_ap(self.R_SWQ, 4 + M, 1),
                        srow[:, 1:2], reads=[bsr], writes=[hsw])
            self.em.dma("se:bc", sbc[:], self.bcast_in(self.R_SWQ, 4 + M, 1, 128),
                        reads=[hsw], writes=[bsb])
            for c in range(8):
                ln = c % 2
                self.em.dma(f"sq:xld{ln}", wch[ln][:],
                            mats[M][c * 128:(c + 1) * 128, :], writes=[bw[ln]])
                em.op("ve", (lambda h, ln=ln: h.tensor_scalar(
                    out=ya[:], in0=wch[ln][:], scalar1=sbc[:, 0:1], scalar2=None,
                    op0=OP.mult)), reads=[bw[ln], bsb], writes=[bya])
                em.op("ge", lambda h: h.tensor_tensor(
                    out=ab[:], in0=ya[:], in1=ya[:], op=OP.mult),
                    reads=[bya], writes=[bab])
                em.op("ge", lambda h: h.tensor_scalar(
                    out=mk[:], in0=ab[:], scalar1=0.25, scalar2=None,
                    op0=OP.is_gt), reads=[bab], writes=[bmk])
                em.op("se", lambda h: h.activation(out=sg[:], in_=ya[:],
                                                   func=AF.Sign),
                      reads=[bya], writes=[bsg])
                em.op("ve", lambda h: h.tensor_tensor(
                    out=tq[:], in0=sg[:], in1=mk[:], op=OP.mult),
                    reads=[bsg, bmk], writes=[btq])
                self.em.dma("ge:st1", self.h_wt[M, c * 128:(c + 1) * 128, :],
                            tq[:], reads=[btq], writes=[self.hb("wt")])
        self.em.barrier()

    def p2_attention(self):
        em, S, NTC = self.em, self.S, self.NTC
        xq = self.sb("p2_xq", [128, 8 * S], BF16)
        wt3 = [self.sb(f"p2_wt{m}", [128, 8 * D], BF16) for m in range(3)]
        bg0 = self.sb("p2_bg0", [128, S])
        g0col = self.sb("p2_g0col", [128, NTC])
        swc = self.sb("p2_swc", [128, 4])
        qt = self.sb("p2_qt", [128, D])
        kt = self.sb("p2_kt", [128, D])
        vt = self.sb("p2_vt", [128, D])
        prod = self.sb("p2_prod", [128, D])
        prod2 = self.sb("p2_prod2", [128, D])
        sc = self.sb("p2_sc", [128, 256])
        ee = self.sb("p2_ee", [128, 256])
        linv = self.sb("p2_linv", [128, 16])
        redq = self.sb("p2_redq", [128, 16])
        negc = self.sb("p2_negc", [128, 1])
        ctx = self.sb("p2_ctx", [128, D])
        ctile = self.sb("p2_ctile", [128, 128])
        psq = [self.ps(f"p2_psq{i}", [128, TT]) for i in range(2)]
        pst = self.ps("p2_pst", [128, 128])
        bxq, bwt = B(), [B(), B(), B()]
        bbg0, bg0c, bswc = B(), B(), B()
        bqt, bkt, bvt = B(), B(), B()
        bprod, bprod2, bsc, bee, blinv = B(), B(), B(), B(), B()
        bredq, bnegc, bctx, bctile = B(), B(), B(), B()
        bpsq = [B(), B()]
        bpst = B()
        hctx = self.hb("ctx")

        for c in range(8):
            self.em.dma(f"sq:xld{c % 4}", xq[:, c * S:(c + 1) * S],
                        self.h_xq[c * 128:(c + 1) * 128, :],
                        reads=[self.hb("p0dst")], writes=[bxq])
            for m in range(3):
                self.em.dma("sq:wsl", wt3[m][:, c * D:(c + 1) * D],
                            self.h_wt[m, c * 128:(c + 1) * 128, :],
                            reads=[self.hb("wt")], writes=[bwt[m]])
        self.em.dma("se:bc", bg0[:], self.bcast_in(self.R_G0, 0, S, 128),
                    reads=[self.hb("p0g0")], writes=[bbg0])
        for M in range(4):
            self.em.dma("se:bc", swc[:, M:M + 1],
                        self.bcast_in(self.R_SWQ, M, 1, 128),
                        reads=[self.hb("sw")], writes=[bswc])
        for tc in range(NTC):
            em.op("te", (lambda h, tc=tc: h.transpose(
                out=pst[:], in_=bg0[:, tc * 128:(tc + 1) * 128],
                identity=self.c_ident[:])),
                reads=[bbg0, self.b_ident], writes=[bpst])
            em.op("ve", (lambda h, tc=tc: h.tensor_copy(
                out=g0col[:, tc:tc + 1], in_=pst[:, 0:1])),
                reads=[bpst], writes=[bg0c])

        r3 = lambda ap, d: ap.rearrange("p (a b) -> p a b", b=d)
        for scnk in range(NTC):
            ssl = slice(scnk * 128, (scnk + 1) * 128)
            for m, (dst, bdst) in enumerate(((qt, bqt), (kt, bkt), (vt, bvt))):
                for et in range(2):
                    pp = et
                    for c in range(8):
                        em.op("te", (lambda h, m=m, et=et, c=c, pp=pp,
                                     ssl=ssl: h.matmul(
                            out=psq[pp][:],
                            lhsT=xq[:, c * S + ssl.start:c * S + ssl.stop],
                            rhs=wt3[m][:, c * D + et * TT:c * D + (et + 1) * TT],
                            start=(c == 0), stop=(c == 7))),
                            reads=[bxq, bwt[m]], writes=[bpsq[pp]],
                            notify=(c == 7))
                    em.op("ve", (lambda h, m=m, et=et, pp=pp, scnk=scnk,
                                 dst=dst: h.tensor_scalar(
                        out=dst[:, et * TT:(et + 1) * TT], in0=psq[pp][:],
                        scalar1=g0col[:, scnk:scnk + 1],
                        scalar2=swc[:, m:m + 1], op0=OP.mult, op1=OP.mult)),
                        reads=[bpsq[pp], bg0c, bswc], writes=[bdst])
            for t in range(16):
                em.op("ge", (lambda h, t=t: h.tensor_tensor(
                    out=r3(prod[:], 64), in0=r3(qt[:], 64),
                    in1=kt[:, t * 64:(t + 1) * 64].rearrange(
                        "p (a d) -> p a d", a=1).to_broadcast([128, 16, 64]),
                    op=OP.mult)), reads=[bqt, bkt], writes=[bprod])
                em.op("ve", (lambda h, t=t: h.reduce_sum(
                    out=sc[:, t::16].rearrange("p (a b) -> p a b", b=1),
                    in_=r3(prod[:], 64), axis=AX.X)),
                    reads=[bprod], writes=[bsc])
            em.op("ve", lambda h: h.reduce_max(
                out=redq[:, 0:16].rearrange("p (a b) -> p a b", b=1),
                in_=r3(sc[:], 16), axis=AX.X), reads=[bsc], writes=[bredq])
            em.op("ve", lambda h: h.tensor_tensor(
                out=r3(sc[:], 16), in0=r3(sc[:], 16),
                in1=redq[:, 0:16].rearrange("p (h a) -> p h a",
                                            a=1).to_broadcast([128, 16, 16]),
                op=OP.subtract), reads=[bredq, bsc], writes=[bsc])
            em.op("se", lambda h: h.activation(
                out=ee[:], in_=sc[:], func=AF.Exp,
                scale=0.125), reads=[bsc], writes=[bee])
            em.op("ve", lambda h: h.reduce_sum(
                out=linv[:].rearrange("p (a b) -> p a b", b=1),
                in_=r3(ee[:], 16), axis=AX.X), reads=[bee], writes=[blinv])
            em.op("ve", lambda h: h.reciprocal(out=linv[:], in_=linv[:]),
                  reads=[blinv], writes=[blinv])
            for hh in range(16):
                em.op("ge", (lambda h, hh=hh: h.tensor_tensor(
                    out=r3(prod2[:], 64),
                    in0=ee[:, hh * 16:(hh + 1) * 16].rearrange(
                        "p (t a) -> p t a", a=1).to_broadcast([128, 16, 64]),
                    in1=r3(vt[:], 64), op=OP.mult)),
                    reads=[bee, bvt], writes=[bprod2])
                em.op("ve", (lambda h, hh=hh: h.reduce_sum(
                    out=ctx[:, hh * 64:(hh + 1) * 64].rearrange(
                        "p (d a) -> p d a", a=1),
                    in_=prod2[:].rearrange("p (t d) -> p d t", d=64),
                    axis=AX.X)), reads=[bprod2], writes=[bctx])
            em.op("ge", lambda h: h.tensor_tensor(
                out=r3(ctx[:], 64), in0=r3(ctx[:], 64),
                in1=linv[:].rearrange("p (h a) -> p h a", a=1).to_broadcast(
                    [128, 16, 64]), op=OP.mult),
                reads=[blinv, bctx], writes=[bctx])
            for ec in range(8):
                em.op("te", (lambda h, ec=ec: h.transpose(
                    out=pst[:], in_=ctx[:, ec * 128:(ec + 1) * 128],
                    identity=self.c_ident[:])),
                    reads=[bctx, self.b_ident], writes=[bpst])
                em.op("ve", (lambda h, ec=ec: h.tensor_copy(
                    out=ctile[:], in_=pst[:])),
                    reads=[bpst], writes=[bctile])
                self.em.dma("ge:st1", self.h_ctx[ec * 128:(ec + 1) * 128, ssl],
                            ctile[:], reads=[bctile], writes=[hctx])
        self.em.barrier()

    def p3_scramble_quant(self):
        em, S = self.em, self.S
        NB = S // 16
        cnk = self.sb("p3_cnk", [128, S])
        cnk2 = self.sb("p3_cnk2", [128, S])
        yb = [self.sb(f"p3_yb{a}", [128, NB]) for a in range(8)]
        sqt = self.sb("p3_sqt", [128, NB])
        sq2 = self.sb("p3_sq2", [128, NB])
        acc = self.sb("p3_acc", [128, NB])
        tmp = self.sb("p3_tmp", [64, NB])
        rows = self.sb("p3_rows", [1, 8 * NB])
        RS, RM, RR, RD_, RP, RG = (slice(0, NB), slice(NB, 2 * NB),
                                   slice(2 * NB, 3 * NB), slice(3 * NB, 4 * NB),
                                   slice(4 * NB, 5 * NB), slice(5 * NB, 6 * NB))
        RC = slice(6 * NB, 7 * NB)
        RSC = slice(7 * NB, 8 * NB)
        brho = self.sb("p3_brho", [128, NB])
        t0 = self.sb("p3_t0", [128, NB])
        t1 = self.sb("p3_t1", [128, NB])
        ti = self.sb("p3_ti", [128, NB], mybir.dt.int32)
        yq = self.sb("p3_yq", [128, NB], BF16)
        pss = self.ps("p3_pss", [1, NB])
        bcnk, bcnk2 = B(), B()
        byb = [B() for _ in range(8)]
        bsqt, bsq2, bacc, btmp, brows, bbrho = B(), B(), B(), B(), B(), B()
        bt0, bt1, bti, byq, bps = B(), B(), B(), B(), B()
        hctx = self.hb("ctx")
        hrho, hg0 = self.hb("p3rho"), self.hb("p3g0")
        hdst = self.hb("p3dst")

        for hh in range(16):
            base = hh * NB
            if hh % 2 == 0:
                cc = hh // 2
                self.em.dma("sq:xld0", cnk[:],
                            self.h_ctx[cc * 128:(cc + 1) * 128, :],
                            reads=[hctx], writes=[bcnk])
                self.em.dma("ge:tree", cnk2[0:64, :], cnk[64:128, :],
                            reads=[bcnk], writes=[bcnk2])
                self.em.dma("ge:tree", cnk2[64:128, :], cnk[0:64, :],
                            reads=[bcnk], writes=[bcnk2])
            for a in range(8):
                for j in (2 * a, 2 * a + 1):
                    same = (j % 2) == (hh % 2)
                    src = cnk if same else cnk2
                    bsrc = bcnk if same else bcnk2
                    ob = (j % 2) * 64
                    em.op("ve" if a % 2 == 0 else "ge",
                          (lambda h, a=a, j=j, src=src, ob=ob: h.tensor_copy(
                              out=yb[a][ob:ob + 64, :],
                              in_=src[ob:ob + 64, j::16])),
                          reads=[bsrc], writes=[byb[a]])
            for a in range(8):
                em.op("ve", (lambda h, a=a: h.tensor_tensor(
                    out=sqt[:], in0=yb[a][:], in1=yb[a][:], op=OP.mult)),
                    reads=[byb[a]], writes=[bsqt])
                em.op("te", (lambda h, a=a: h.matmul(
                    out=pss[:], lhsT=self.c_ones1[:], rhs=sqt[:],
                    start=(a == 0), stop=(a == 7))),
                    reads=[bsqt, self.b_ones1], writes=[bps], notify=True)
                if a == 0:
                    em.op("ge", lambda h: h.tensor_tensor(
                        out=acc[:], in0=yb[0][:], in1=yb[0][:], op=OP.mult),
                        reads=[byb[0]], writes=[bacc])
                else:
                    em.op("ge", (lambda h, a=a: h.tensor_tensor(
                        out=sq2[:], in0=yb[a][:], in1=yb[a][:], op=OP.mult)),
                        reads=[byb[a]], writes=[bsq2])
                    em.op("ve", lambda h: h.tensor_tensor(
                        out=acc[:], in0=acc[:], in1=sq2[:], op=OP.max),
                        reads=[bsq2, bacc], writes=[bacc])
            n = 64
            while n >= 1:
                self.em.dma("ge:tree", tmp[0:n, :], acc[n:2 * n, :],
                            reads=[bacc], writes=[btmp])
                em.op("ve", (lambda h, n=n: h.tensor_tensor(
                    out=acc[0:n, :], in0=acc[0:n, :], in1=tmp[0:n, :],
                    op=OP.max)), reads=[btmp, bacc], writes=[bacc])
                n //= 2
            em.op("ve", lambda h: h.tensor_copy(out=rows[:, RS], in_=pss[:]),
                  reads=[bps], writes=[brows])
            em.op("ve", lambda h: h.tensor_scalar(
                out=rows[:, RM], in0=acc[0:1, :], scalar1=1e-30, scalar2=None,
                op0=OP.max), reads=[bacc], writes=[brows])
            self.nr_rsqrt(rows[:, RC], rows[:, RM], rows[:, RSC], rows[:, RD_],
                          [], brows)
            em.op("ve", lambda h: h.tensor_tensor(
                out=rows[:, RM], in0=rows[:, RM], in1=rows[:, RC], op=OP.mult),
                reads=[brows], writes=[brows])
            em.op("ve", lambda h: h.tensor_scalar(
                out=rows[:, RSC], in0=rows[:, RS], scalar1=1.0 / D, scalar2=1e-6,
                op0=OP.mult, op1=OP.add), reads=[brows], writes=[brows])
            self.nr_rsqrt(rows[:, RR], rows[:, RSC], rows[:, RD_], rows[:, RC],
                          [], brows)
            em.op("ve", lambda h: h.tensor_tensor(
                out=rows[:, RD_], in0=rows[:, RR], in1=rows[:, RM], op=OP.mult),
                reads=[brows], writes=[brows])
            em.op("ve", lambda h: h.tensor_scalar(
                out=rows[:, RD_], in0=rows[:, RD_], scalar1=EPS_Q, scalar2=None,
                op0=OP.max), reads=[brows], writes=[brows])
            em.op("ve", lambda h: h.reciprocal(out=rows[:, RP], in_=rows[:, RD_]),
                  reads=[brows], writes=[brows])
            em.op("ve", lambda h: h.scalar_tensor_tensor(
                out=rows[:, RP], in0=rows[:, RR], scalar=127.0, in1=rows[:, RP],
                op0=OP.mult, op1=OP.mult), reads=[brows], writes=[brows])
            em.op("ve", lambda h: h.tensor_scalar(
                out=rows[:, RG], in0=rows[:, RD_], scalar1=1.0 / 127.0,
                scalar2=None, op0=OP.mult), reads=[brows], writes=[brows])
            self.em.dma("ge:rows", self.row_ap(self.R_RHO2, base, NB),
                        rows[:, RP], reads=[brows], writes=[hrho])
            self.em.dma("ge:rows", self.row_ap(self.R_G02, base, NB),
                        rows[:, RG], reads=[brows], writes=[hg0])
            self.em.dma("se:bc", brho[:], self.bcast_in(self.R_RHO2, base, NB, 128),
                        reads=[hrho], writes=[bbrho])
            for a in range(8):
                em.op("ve", (lambda h, a=a: h.tensor_tensor(
                    out=t0[:], in0=yb[a][:], in1=brho[:], op=OP.mult)),
                    reads=[byb[a], bbrho], writes=[bt0])
                em.op("ve", lambda h: h.tensor_scalar(
                    out=ti[:], in0=t0[:], scalar1=0.5, scalar2=None,
                    op0=OP.add), reads=[bt0], writes=[bti])
                em.op("ve", lambda h: h.scalar_tensor_tensor(
                    out=t1[:], in0=t0[:], scalar=0.5, in1=ti[:],
                    op0=OP.add, op1=OP.subtract),
                    reads=[bti, bt0], writes=[bt1])
                em.op("ge", lambda h: h.tensor_scalar(
                    out=t1[:], in0=t1[:], scalar1=0.0, scalar2=None,
                    op0=OP.is_lt), reads=[bt1], writes=[bt1])
                em.op("ve", lambda h: h.tensor_tensor(
                    out=t0[:], in0=ti[:], in1=t1[:], op=OP.subtract),
                    reads=[bti, bt1], writes=[bt0])
                em.op("ve", lambda h: h.tensor_scalar(
                    out=yq[:], in0=t0[:], scalar1=-128.0, scalar2=127.0,
                    op0=OP.max, op1=OP.min), reads=[bt0], writes=[byq])
                self.em.dma("ge:st1",
                            self.h_ctxq[a * 128:(a + 1) * 128, base:base + NB],
                            yq[:], reads=[byq], writes=[hdst])
        self.em.barrier()

    def p4_oproj_ln(self):
        em, S, NTT = self.em, self.S, self.NTT
        cq = self.sb("p4_cq", [128, 8 * S], BF16)
        wot = self.sb("p4_wot", [128, 8 * D], BF16)
        bgo = self.sb("p4_bgo", [128, S])
        swo = self.sb("p4_swo", [128, 1])
        x2t = [self.sb(f"p4_x2t{c}", [128, TT]) for c in range(8)]
        yt = self.sb("p4_yt", [128, TT])
        sqb = self.sb("p4_sqb", [128, TT])
        rows = self.sb("p4_rows", [1, 7 * TT])
        RSU, RSS, RMU, RSG = (slice(0, TT), slice(TT, 2 * TT),
                              slice(2 * TT, 3 * TT), slice(3 * TT, 4 * TT))
        RS1, RS2, RS3 = (slice(4 * TT, 5 * TT), slice(5 * TT, 6 * TT),
                         slice(6 * TT, 7 * TT))
        bmu = self.sb("p4_bmu", [128, TT])
        bsg = self.sb("p4_bsg", [128, TT])
        x3t = self.sb("p4_x3t", [128, TT])
        psy = self.ps("p4_psy", [128, TT])
        psr = self.ps("p4_psr", [1, TT])
        psr2 = self.ps("p4_psr2", [1, TT])
        bcq, bwot, bbgo, bswo = B(), B(), B(), B()
        bx2 = [B() for _ in range(8)]
        byt, bsqb, brows, bbmu, bbsg, bx3 = B(), B(), B(), B(), B(), B()
        bpsy, bpsr, bpsr2 = B(), B(), B()
        hx3 = self.hb("x3")

        for c in range(8):
            self.em.dma(f"sq:xld{c % 4}", cq[:, c * S:(c + 1) * S],
                        self.h_ctxq[c * 128:(c + 1) * 128, :],
                        reads=[self.hb("p3dst")], writes=[bcq])
            self.em.dma("sq:wsl", wot[:, c * D:(c + 1) * D],
                        self.h_wt[3, c * 128:(c + 1) * 128, :],
                        reads=[self.hb("wt")], writes=[bwot])
        self.em.dma("se:bc", bgo[:], self.bcast_in(self.R_G02, 0, S, 128),
                    reads=[self.hb("p3g0")], writes=[bbgo])
        self.em.dma("se:bc", swo[:], self.bcast_in(self.R_SWQ, 3, 1, 128),
                    reads=[self.hb("sw")], writes=[bswo])

        for tt in range(NTT):
            sl = slice(tt * TT, (tt + 1) * TT)
            for oc in range(8):
                for c in range(8):
                    em.op("te", (lambda h, oc=oc, c=c, sl=sl: h.matmul(
                        out=psy[:],
                        lhsT=wot[:, c * D + oc * 128:c * D + (oc + 1) * 128],
                        rhs=cq[:, c * S + sl.start:c * S + sl.stop],
                        start=(c == 0), stop=(c == 7))),
                        reads=[bwot, bcq], writes=[bpsy], notify=(c == 7))
                em.op("ve", (lambda h, sl=sl: h.scalar_tensor_tensor(
                    out=yt[:], in0=psy[:], scalar=swo[:, 0:1], in1=bgo[:, sl],
                    op0=OP.mult, op1=OP.mult)),
                    reads=[bpsy, bswo, bbgo], writes=[byt])
                self.em.dma(f"sq:xld{oc % 4}", x2t[oc][:],
                            self.i_x[oc * 128:(oc + 1) * 128, sl],
                            writes=[bx2[oc]])
                em.op("ve", (lambda h, oc=oc: h.tensor_tensor(
                    out=x2t[oc][:], in0=x2t[oc][:], in1=yt[:], op=OP.add)),
                    reads=[byt, bx2[oc]], writes=[bx2[oc]])
                em.op("te", (lambda h, oc=oc: h.matmul(
                    out=psr[:], lhsT=self.c_ones1[:], rhs=x2t[oc][:],
                    start=(oc == 0), stop=(oc == 7))),
                    reads=[bx2[oc], self.b_ones1], writes=[bpsr],
                    notify=(oc == 7))
                em.op("ge", (lambda h, oc=oc: h.tensor_tensor(
                    out=sqb[:], in0=x2t[oc][:], in1=x2t[oc][:], op=OP.mult)),
                    reads=[bx2[oc]], writes=[bsqb])
                em.op("te", (lambda h, oc=oc: h.matmul(
                    out=psr2[:], lhsT=self.c_ones1[:], rhs=sqb[:],
                    start=(oc == 0), stop=(oc == 7))),
                    reads=[bsqb, self.b_ones1], writes=[bpsr2],
                    notify=True)
            em.op("ve", lambda h: h.tensor_scalar(
                out=rows[:, RMU], in0=psr[:], scalar1=1.0 / D, scalar2=None,
                op0=OP.mult), reads=[bpsr], writes=[brows])
            em.op("ve", lambda h: h.tensor_tensor(
                out=rows[:, RSU], in0=rows[:, RMU], in1=rows[:, RMU],
                op=OP.mult), reads=[brows], writes=[brows])
            em.op("ve", lambda h: h.scalar_tensor_tensor(
                out=rows[:, RSG], in0=psr2[:], scalar=1.0 / D, in1=rows[:, RSU],
                op0=OP.mult, op1=OP.subtract),
                reads=[bpsr2, brows], writes=[brows])
            em.op("ve", lambda h: h.tensor_scalar(
                out=rows[:, RS1], in0=rows[:, RSG], scalar1=1e-5, scalar2=None,
                op0=OP.add), reads=[brows], writes=[brows])
            self.nr_rsqrt(rows[:, RSG], rows[:, RS1], rows[:, RS2],
                          rows[:, RS3], [], brows)
            hmu, hsg = self.hb("mu"), self.hb("sig")
            self.em.dma("ge:rows", self.row_ap(self.R_MU, tt * TT, TT),
                        rows[:, RMU], reads=[brows], writes=[hmu])
            self.em.dma("ge:rows", self.row_ap(self.R_SIG, tt * TT, TT),
                        rows[:, RSG], reads=[brows], writes=[hsg])
            self.em.dma("se:bc", bmu[:], self.bcast_in(self.R_MU, tt * TT, TT, 128),
                        reads=[hmu], writes=[bbmu])
            self.em.dma("se:bc", bsg[:], self.bcast_in(self.R_SIG, tt * TT, TT, 128),
                        reads=[hsg], writes=[bbsg])
            for c in range(8):
                em.op("ve" if c % 2 == 0 else "ge",
                      (lambda h, c=c: h.tensor_tensor(
                          out=x3t[:], in0=x2t[c][:], in1=bmu[:],
                          op=OP.subtract)),
                      reads=[bx2[c], bbmu], writes=[bx3])
                em.op("ve" if c % 2 == 0 else "ge",
                      (lambda h: h.tensor_tensor(
                          out=x3t[:], in0=x3t[:], in1=bsg[:], op=OP.mult)),
                      reads=[bbsg, bx3], writes=[bx3])
                self.em.dma("ge:st1" if c % 2 else "sq:st0",
                            self.h_x3[c * 128:(c + 1) * 128, sl], x3t[:],
                            reads=[bx3], writes=[hx3])
        self.em.barrier()

    def load_shifted(self, stream, dst, dst_b, src, src_b, c, tt, shifted):
        sl = slice(tt * TT, (tt + 1) * TT)
        if not shifted or c < 4:
            self.em.dma(stream, dst[:], src[c * 128:(c + 1) * 128, sl],
                        reads=[src_b], writes=[dst_b])
        elif tt > 0:
            self.em.dma(stream, dst[:],
                        src[c * 128:(c + 1) * 128, sl.start - 1:sl.stop - 1],
                        reads=[src_b], writes=[dst_b])
        else:
            self.em.op("ve", (lambda h: h.memset(dst[:, 0:1], 0.0)),
                       writes=[dst_b])
            self.em.dma(stream, dst[:, 1:TT],
                        src[c * 128:(c + 1) * 128, 0:TT - 1],
                        reads=[src_b], writes=[dst_b])

    def p6_layer(self, l):
        em, S, NTT, NTC, NW = self.em, self.S, self.NTT, self.NTC, self.NW
        xin = self.h_x3 if l == 0 else self.h_x5
        bxin = self.hb("x3" if l == 0 else "x5")
        xout = self.h_x5 if l == 0 else self.h_x6
        bxout = self.hb("x5" if l == 0 else "x6")
        xmid, bxmid = self.h_x4, self.hb("x4")
        hqkv, bhqkv = self.h_qkv, self.hb("qkv")
        bho, bho2 = self.hb("ohi"), self.hb("olo")
        cols = self.sb(f"l{l}_cols", [128, 2 * MOB + 3 * 8 + 2])
        C_B1A, C_B1G = 0, MOB
        C_B2, C_LNG, C_LNB = 2 * MOB, 2 * MOB + 8, 2 * MOB + 16
        C_QS, C_KS = 2 * MOB + 24, 2 * MOB + 25
        bcols = B()
        for ap_, off, n in ((self.i_b1a[l], C_B1A, MOB),
                            (self.i_b1g[l], C_B1G, MOB),
                            (self.i_b2[l], C_B2, 8),
                            (self.i_lng[l], C_LNG, 8),
                            (self.i_lnb[l], C_LNB, 8),
                            (self.i_qs[l], C_QS, 1),
                            (self.i_ks[l], C_KS, 1)):
            self.em.dma("sq:wsl", cols[:, off:off + n], ap_, writes=[bcols])
        neglc = self.sb(f"l{l}_neglc", [128, 1])
        srw = self.sb(f"l{l}_srw", [1, 260])
        bneglc, bsrw = B(), B()
        _pst_stack = ExitStack()
        pst = _pst_stack.enter_context(
            self.nc.psum_tensor(f"l{l}_pst", [128, 128], F32))
        bpst = B()
        red2 = self.sb(f"l{l}_red2", [2, 1])
        bred2 = B()
        em.op("te", lambda h: h.transpose(
            out=pst[0:2, :], in_=cols[:, C_QS:C_KS + 1],
            identity=self.c_ident[:]),
            reads=[bcols, self.b_ident], writes=[bpst])
        em.op("ve", lambda h: h.reduce_max(
            out=red2[:], in_=pst[0:2, :], axis=AX.X,
            apply_absolute_value=True), reads=[bpst], writes=[bred2])
        em.op("te", lambda h: h.transpose(
            out=pst[0:1, 0:2], in_=red2[:], identity=self.c_ident[0:2, 0:2]),
            reads=[bred2, self.b_ident], writes=[bpst])
        em.op("ve", lambda h: h.tensor_copy(out=srw[:, 0:2],
                                            in_=pst[0:1, 0:2]),
              reads=[bpst], writes=[bsrw])
        em.op("ve", lambda h: h.tensor_tensor(
            out=srw[:, 258:259], in0=srw[:, 0:1], in1=srw[:, 1:2],
            op=OP.mult), reads=[bsrw], writes=[bsrw])
        em.op("ve", lambda h: h.tensor_scalar(
            out=srw[:, 258:259], in0=srw[:, 258:259], scalar1=-8.0,
            scalar2=None, op0=OP.mult), reads=[bsrw], writes=[bsrw])
        hcl = self.hb(f"l{l}c")
        self.em.dma("ge:rows", self.row_ap(self.R_C, 16 + l, 1),
                    srw[:, 258:259], reads=[bsrw], writes=[hcl])
        self.em.dma("se:bc", neglc[:], self.bcast_in(self.R_C, 16 + l, 1, 128),
                    reads=[hcl], writes=[bneglc])
        self.em.barrier()
        _pst_stack.close()

        with ExitStack() as st:
            old = self.pes; self.pes = st
            wtmp = [self.sb(f"l{l}_wt{i}", [128, HIDP]) for i in range(2)]
            whi = self.sb(f"l{l}_whi", [128, HIDP], BF16)
            wlo = self.sb(f"l{l}_wlo", [128, HIDP], BF16)
            bwt = [B(), B()]
            bwhi, bwlo = B(), B()
            hw1, hw2 = self.hb("w1s"), self.hb("w2s")
            for half, isrc in ((0, self.i_f_w1a), (1, self.i_f_w1g)):
                for mob in range(MOB):
                    ln = mob % 2
                    self.em.dma(f"sq:xld{ln}", wtmp[ln][:, 0:D],
                                isrc[l, :, mob, :], writes=[bwt[ln]])
                    em.op("ve", (lambda h, ln=ln: h.tensor_copy(
                        out=whi[:, 0:D], in_=wtmp[ln][:, 0:D])),
                        reads=[bwt[ln]], writes=[bwhi])
                    em.op("ge", (lambda h, ln=ln: h.tensor_tensor(
                        out=wlo[:, 0:D], in0=wtmp[ln][:, 0:D],
                        in1=whi[:, 0:D], op=OP.subtract)),
                        reads=[bwt[ln], bwhi], writes=[bwlo])
                    self.em.dma("ge:st1", self.h_w1hi[half, :, mob, :],
                                whi[:, 0:D], reads=[bwhi], writes=[hw1])
                    self.em.dma("ge:st2", self.h_w1lo[half, :, mob, :],
                                wlo[:, 0:D], reads=[bwlo], writes=[hw1])
            for oc in range(8):
                ln = oc % 2
                self.em.dma(f"sq:xld{ln}", wtmp[ln][:], self.i_f_w2[l, :, oc, :],
                            writes=[bwt[ln]])
                em.op("ve", (lambda h, ln=ln: h.tensor_copy(
                    out=whi[:], in_=wtmp[ln][:])),
                    reads=[bwt[ln]], writes=[bwhi])
                em.op("ge", (lambda h, ln=ln: h.tensor_tensor(
                    out=wlo[:], in0=wtmp[ln][:], in1=whi[:], op=OP.subtract)),
                    reads=[bwt[ln], bwhi], writes=[bwlo])
                self.em.dma("ge:st1", self.h_w2hi[:, oc, :], whi[:],
                            reads=[bwhi], writes=[hw2])
                self.em.dma("ge:st2", self.h_w2lo[:, oc, :], wlo[:],
                            reads=[bwlo], writes=[hw2])
            self.em.barrier()
            self.pes = old

        with ExitStack() as st:
            old = self.pes; self.pes = st
            wqh = self.sb(f"l{l}_wqh", [128, 8 * 1536], BF16)
            wql = self.sb(f"l{l}_wql", [128, 8 * 1536], BF16)
            wt1 = self.sb(f"l{l}_f1wt", [128, 1536])
            bwqh, bwql, bwt1 = B(), B(), B()
            for c in range(8):
                self.em.dma(f"sq:xld{c % 4}", wt1[:],
                            self.i_f_wqkv[l, :, c, :], writes=[bwt1])
                em.op("ve", (lambda h, c=c: h.tensor_copy(
                    out=wqh[:, c * 1536:(c + 1) * 1536], in_=wt1[:])),
                    reads=[bwt1], writes=[bwqh])
                em.op("ge", (lambda h, c=c: h.tensor_tensor(
                    out=wql[:, c * 1536:(c + 1) * 1536], in0=wt1[:],
                    in1=wqh[:, c * 1536:(c + 1) * 1536], op=OP.subtract)),
                    reads=[bwt1, bwqh], writes=[bwql])
            hft = [self.sb(f"l{l}_hf{c}", [128, TT]) for c in range(8)]
            hht = [self.sb(f"l{l}_hh{c}", [128, TT], BF16) for c in range(8)]
            hlt = [self.sb(f"l{l}_hl{c}", [128, TT], BF16) for c in range(8)]
            bhf = [B() for _ in range(8)]
            bhh = [B() for _ in range(8)]
            bhl = [B() for _ in range(8)]
            qevac = self.sb(f"l{l}_qev", [128, TT])
            bqev = B()
            psq = [self.ps(f"l{l}_psq{i}", [128, TT]) for i in range(2)]
            bpsq = [B(), B()]
            for tt in range(NTT):
                sl = slice(tt * TT, (tt + 1) * TT)
                for c in range(8):
                    self.load_shifted(f"sq:xld{c % 4}", hft[c], bhf[c],
                                      xin, bxin, c, tt, True)
                    em.op("ve", (lambda h, c=c: h.tensor_copy(
                        out=hht[c][:], in_=hft[c][:])),
                        reads=[bhf[c]], writes=[bhh[c]])
                    em.op("ge", (lambda h, c=c: h.tensor_tensor(
                        out=hlt[c][:], in0=hft[c][:], in1=hht[c][:],
                        op=OP.subtract)), reads=[bhf[c], bhh[c]],
                        writes=[bhl[c]])
                for ob in range(12):
                    pp = ob % 2
                    passes = ((wqh, bwqh, hht, bhh), (wqh, bwqh, hlt, bhl),
                              (wql, bwql, hht, bhh))
                    n = 0
                    for i, (ww, bww, hh_, bhh_) in enumerate(passes):
                        for c in range(8):
                            em.op("te", (lambda h, ww=ww, hh_=hh_, c=c, ob=ob,
                                         i=i, pp=pp, n=n: h.matmul(
                                out=psq[pp][:],
                                lhsT=ww[:, c * 1536 + ob * 128:
                                        c * 1536 + (ob + 1) * 128],
                                rhs=hh_[c][:],
                                start=(n == 0), stop=(n == 23))),
                                reads=[bww, bhh_[c]], writes=[bpsq[pp]],
                                notify=(n == 23))
                            n += 1
                    em.op("ve", (lambda h, pp=pp: h.tensor_copy(
                        out=qevac[:], in_=psq[pp][:])),
                        reads=[bpsq[pp]], writes=[bqev])
                    self.em.dma("ge:st1", hqkv[ob * 128:(ob + 1) * 128, sl],
                                qevac[:], reads=[bqev], writes=[bhqkv])
            self.em.barrier()
            self.pes = old

        with ExitStack() as st:
            old = self.pes; self.pes = st
            qraw = self.sb(f"l{l}_qraw", [128, S])
            rot = self.sb(f"l{l}_rot", [128, S])
            binv = self.sb(f"l{l}_binv", [128, S])
            tblb = self.sb(f"l{l}_tblb", [128, S])
            qkhi = [self.sb(f"l{l}_qkhi{i}", [128, S], BF16) for i in range(2)]
            qklo = [self.sb(f"l{l}_qklo{i}", [128, S], BF16) for i in range(2)]
            vth = [self.sb(f"l{l}_vth{i}", [128, NTC * 65], BF16)
                   for i in range(2)]
            vtl = [self.sb(f"l{l}_vtl{i}", [128, NTC * 65], BF16)
                   for i in range(2)]
            maskb = self.sb(f"l{l}_maskb", [128, 8 * 512], BF16)
            ssr = self.sb(f"l{l}_ssr", [2, 4 * TT])
            smb = [self.sb(f"l{l}_smb{i}", [128, TT]) for i in range(2)]
            ef32 = [self.sb(f"l{l}_ef{i}", [128, TT]) for i in range(2)]
            ehi = [[self.sb(f"l{l}_ehi{i}{j}", [128, TT], BF16)
                    for j in range(2)] for i in range(2)]
            elo = [[self.sb(f"l{l}_elo{i}{j}", [128, TT], BF16)
                    for j in range(2)] for i in range(2)]
            lrow = self.sb(f"l{l}_lrow", [1, TT])
            blv = self.sb(f"l{l}_blv", [64, TT])
            oslc = self.sb(f"l{l}_oslc", [64, TT])
            ohh = self.sb(f"l{l}_ohh", [64, TT], BF16)
            oll = self.sb(f"l{l}_oll", [64, TT], BF16)
            psS = [[self.ps(f"l{l}_psS{i}{j}", [128, TT]) for j in range(2)]
                   for i in range(2)]
            psC = [self.ps(f"l{l}_psC{i}", [65, TT]) for i in range(2)]
            psv = self.ps(f"l{l}_psv", [128, 128])
            psr = self.ps(f"l{l}_psr", [2, TT])
            bqraw, brot, bbinv, btbl = B(), B(), B(), B()
            bqkhi = [B(), B()]
            bqklo = [B(), B()]
            bvth = [B(), B()]
            bvtl = [B(), B()]
            bmaskb, bssr = B(), B()
            bsmb = [B(), B()]
            bef = [B(), B()]
            behi = [[B(), B()], [B(), B()]]
            belo = [[B(), B()], [B(), B()]]
            blrow, bblv, boslc, bohh, boll = B(), B(), B(), B(), B()
            bpsS = [[B(), B()], [B(), B()]]
            bpsC = [B(), B()]
            bpsv, bpsr = B(), B()
            self.em.dma("sq:wsl", maskb[:], self.i_maskb, writes=[bmaskb])
            for p in range(FH // 2):
                for qk in range(2):
                    base = qk * INNER + p * 128
                    self.em.dma("sq:xld0", qraw[:],
                                hqkv[base:base + 128, :],
                                reads=[bhqkv], writes=[bqraw])
                    em.op("ve", lambda h: h.tensor_tensor(
                        out=rot[:], in0=qraw[:], in1=qraw[:], op=OP.mult),
                        reads=[bqraw], writes=[brot])
                    for tt in range(NTT):
                        sl = slice(tt * TT, (tt + 1) * TT)
                        em.op("te", (lambda h, sl=sl: h.matmul(
                            out=psr[:], lhsT=self.c_ones2[:], rhs=rot[:, sl],
                            start=True, stop=True)),
                            reads=[brot, self.b_ones2], writes=[bpsr])
                        em.op("ve", lambda h: h.tensor_scalar(
                            out=ssr[:, TT:2 * TT], in0=psr[:], scalar1=1e-24,
                            scalar2=None, op0=OP.max),
                            reads=[bpsr], writes=[bssr])
                        self.nr_rsqrt(ssr[:, 0:TT], ssr[:, TT:2 * TT],
                                      ssr[:, 2 * TT:3 * TT],
                                      ssr[:, 3 * TT:4 * TT], [], bssr)
                        hl2 = self.hb(f"l{l}l2")
                        self.em.dma("ge:rows",
                                    self.row_ap(self.R_L, tt * TT, TT),
                                    ssr[0:1, 0:TT], reads=[bssr], writes=[hl2])
                        self.em.dma("ge:rows",
                                    self.row_ap(self.R_L + 1, tt * TT, TT),
                                    ssr[1:2, 0:TT], reads=[bssr], writes=[hl2])
                    for hd in range(2):
                        self.em.dma("se:bc", binv[hd * 64:(hd + 1) * 64, :],
                                    self.bcast_in(self.R_L + hd, 0, S, 64),
                                    reads=[self.hb(f"l{l}l2")], writes=[bbinv])
                    em.op("ve", (lambda h, qk=qk: h.scalar_tensor_tensor(
                        out=qraw[:], in0=qraw[:],
                        scalar=cols[:, C_QS + qk:C_QS + qk + 1], in1=binv[:],
                        op0=OP.mult, op1=OP.mult)),
                        reads=[bbinv, bcols, bqraw], writes=[bqraw])
                    for a, bq in ((0, 32), (32, 0), (64, 96), (96, 64)):
                        self.em.dma("ge:tree", rot[a:a + 32, :],
                                    qraw[bq:bq + 32, :],
                                    reads=[bqraw], writes=[brot])
                    self.em.dma("sq:xld1", tblb[:], self.i_tbl[2 * qk],
                                writes=[btbl])
                    em.op("ve", lambda h: h.tensor_tensor(
                        out=qraw[:], in0=qraw[:], in1=tblb[:], op=OP.mult),
                        reads=[btbl, bqraw], writes=[bqraw])
                    self.em.dma("sq:xld1", tblb[:], self.i_tbl[2 * qk + 1],
                                writes=[btbl])
                    em.op("ve", lambda h: h.tensor_tensor(
                        out=rot[:], in0=rot[:], in1=tblb[:], op=OP.mult),
                        reads=[btbl, brot], writes=[brot])
                    em.op("ve", lambda h: h.tensor_tensor(
                        out=qraw[:], in0=qraw[:], in1=rot[:], op=OP.add),
                        reads=[brot, bqraw], writes=[bqraw])
                    em.op("ve", (lambda h, qk=qk: h.tensor_copy(
                        out=qkhi[qk][:], in_=qraw[:])),
                        reads=[bqraw], writes=[bqkhi[qk]])
                    em.op("ge", (lambda h, qk=qk: h.tensor_tensor(
                        out=qklo[qk][:], in0=qraw[:], in1=qkhi[qk][:],
                        op=OP.subtract)),
                        reads=[bqraw, bqkhi[qk]], writes=[bqklo[qk]])
                self.em.dma("sq:xld0", rot[:],
                            hqkv[2 * INNER + p * 128:2 * INNER + (p + 1) * 128, :],
                            reads=[bhqkv], writes=[brot])
                for tc in range(NTC):
                    em.op("te", (lambda h, tc=tc: h.transpose(
                        out=psv[:], in_=rot[:, tc * 128:(tc + 1) * 128],
                        identity=self.c_ident[:])),
                        reads=[brot, self.b_ident], writes=[bpsv])
                    for hd in range(2):
                        em.op("ve", (lambda h, tc=tc, hd=hd: h.tensor_copy(
                            out=vth[hd][:, tc * 65:tc * 65 + 64],
                            in_=psv[:, hd * 64:hd * 64 + 64])),
                            reads=[bpsv], writes=[bvth[hd]])
                        em.op("ve", (lambda h, tc=tc, hd=hd: h.tensor_tensor(
                            out=vtl[hd][:, tc * 65:tc * 65 + 64],
                            in0=psv[:, hd * 64:hd * 64 + 64],
                            in1=vth[hd][:, tc * 65:tc * 65 + 64],
                            op=OP.subtract)),
                            reads=[bpsv, bvth[hd]], writes=[bvtl[hd]])
                for hd in range(2):
                    em.op("ve", (lambda h, hd=hd: h.memset(
                        vth[hd][:].rearrange("p (t j) -> p t j",
                                             j=65)[:, :, 64:65], 1.0)),
                        writes=[bvth[hd]])
                    em.op("ve", (lambda h, hd=hd: h.memset(
                        vtl[hd][:].rearrange("p (t j) -> p t j",
                                             j=65)[:, :, 64:65], 0.0)),
                        writes=[bvtl[hd]])
                for w in range(NW):
                    qsl = slice(w * FW, (w + 1) * FW)
                    kcs = [kc for kc in range(8) if w > 0 or kc >= 4]
                    for ik, kc in enumerate(kcs):
                        par = ik % 2
                        kt = (w - 1) * 4 + kc
                        ksl = slice(kt * 128, (kt + 1) * 128)
                        for hd in range(2):
                            hsl = slice(hd * 64, hd * 64 + 64)
                            tp = (hd * 64, 0)
                            passes = ((qkhi[1], bqkhi[1], qkhi[0], bqkhi[0]),
                                      (qkhi[1], bqkhi[1], qklo[0], bqklo[0]),
                                      (qklo[1], bqklo[1], qkhi[0], bqkhi[0]))
                            for i, (kk, bkk, qq, bqq) in enumerate(passes):
                                em.op("te", (lambda h, kk=kk, qq=qq, i=i,
                                             hd=hd, par=par, hsl=hsl,
                                             ksl=ksl, qsl=qsl, tp=tp:
                                             h.matmul(
                                    out=psS[hd][par][:], lhsT=kk[hsl, ksl],
                                    rhs=qq[hsl, qsl], start=(i == 0),
                                    stop=(i == 2), tile_position=tp)),
                                    reads=[bkk, bqq],
                                    writes=[bpsS[hd][par]], notify=(i == 2))
                            em.op("ve", (lambda h, hd=hd, par=par, kc=kc:
                                         h.scalar_tensor_tensor(
                                out=smb[hd][:], in0=psS[hd][par][:],
                                scalar=8.0,
                                in1=maskb[:, kc * 512:(kc + 1) * 512],
                                op0=OP.mult, op1=OP.add)),
                                reads=[bpsS[hd][par], bmaskb],
                                writes=[bsmb[hd]])
                            em.op("se", (lambda h, hd=hd: h.activation(
                                out=ef32[hd][:], in_=smb[hd][:], func=AF.Exp,
                                bias=neglc[:, 0:1], scale=1.0)),
                                reads=[bsmb[hd], bneglc], writes=[bef[hd]])
                            em.op("ve", (lambda h, hd=hd, par=par:
                                         h.tensor_copy(
                                out=ehi[hd][par][:], in_=ef32[hd][:])),
                                reads=[bef[hd]], writes=[behi[hd][par]])
                            em.op("ge", (lambda h, hd=hd, par=par:
                                         h.tensor_tensor(
                                out=elo[hd][par][:], in0=ef32[hd][:],
                                in1=ehi[hd][par][:], op=OP.subtract)),
                                reads=[bef[hd], behi[hd][par]],
                                writes=[belo[hd][par]])
                            vsl = slice(kt * 65, (kt + 1) * 65)
                            cpasses = ((vth[hd], bvth[hd], ehi[hd][par],
                                        behi[hd][par]),
                                       (vtl[hd], bvtl[hd], ehi[hd][par],
                                        behi[hd][par]),
                                       (vth[hd], bvth[hd], elo[hd][par],
                                        belo[hd][par]))
                            for i, (vv, bvv, ee, bee) in enumerate(cpasses):
                                em.op("te", (lambda h, vv=vv, ee=ee, i=i,
                                             hd=hd, vsl=vsl, ik=ik,
                                             last=len(kcs) - 1: h.matmul(
                                    out=psC[hd][:], lhsT=vv[:, vsl],
                                    rhs=ee[:],
                                    start=(ik == 0 and i == 0),
                                    stop=(ik == last and i == 2))),
                                    reads=[bvv, bee], writes=[bpsC[hd]],
                                    notify=(i == 2))
                    for hd in range(2):
                        orow = p * 128 + hd * 64
                        em.op("ve", (lambda h, hd=hd: h.reciprocal(
                            out=lrow[:], in_=psC[hd][64:65, :])),
                            reads=[bpsC[hd]], writes=[blrow])
                        hl3 = self.hb(f"l{l}lv")
                        self.em.dma("ge:rows",
                                    self.row_ap(self.R_L + 2, w * FW, FW),
                                    lrow[:], reads=[blrow], writes=[hl3])
                        self.em.dma("se:bc", blv[:],
                                    self.bcast_in(self.R_L + 2, w * FW, FW, 64),
                                    reads=[hl3], writes=[bblv])
                        em.op("ve", (lambda h, hd=hd: h.tensor_tensor(
                            out=oslc[:], in0=psC[hd][0:64, :], in1=blv[:],
                            op=OP.mult)),
                            reads=[bpsC[hd], bblv], writes=[boslc])
                        em.op("ve", lambda h: h.tensor_copy(
                            out=ohh[:], in_=oslc[:]),
                            reads=[boslc], writes=[bohh])
                        em.op("ge", lambda h: h.tensor_tensor(
                            out=oll[:], in0=oslc[:], in1=ohh[:],
                            op=OP.subtract),
                            reads=[boslc, bohh], writes=[boll])
                        self.em.dma("ge:st1",
                                    self.h_ohi[orow:orow + 64, qsl], ohh[:],
                                    reads=[bohh], writes=[bho])
                        self.em.dma("ge:st2",
                                    self.h_olo[orow:orow + 64, qsl], oll[:],
                                    reads=[boll], writes=[bho2])
            self.em.barrier()
            self.pes = old

        with ExitStack() as st:
            old = self.pes; self.pes = st
            woh = self.sb(f"l{l}_woh", [128, 4 * D], BF16)
            wol = self.sb(f"l{l}_wol", [128, 4 * D], BF16)
            wt3 = self.sb(f"l{l}_f3wt", [128, D])
            bwoh, bwol, bwt3 = B(), B(), B()
            for c in range(4):
                self.em.dma(f"sq:xld{c % 4}", wt3[:], self.i_f_wo[l, :, c, :],
                            writes=[bwt3])
                em.op("ve", (lambda h, c=c: h.tensor_copy(
                    out=woh[:, c * D:(c + 1) * D], in_=wt3[:])),
                    reads=[bwt3], writes=[bwoh])
                em.op("ge", (lambda h, c=c: h.tensor_tensor(
                    out=wol[:, c * D:(c + 1) * D], in0=wt3[:],
                    in1=woh[:, c * D:(c + 1) * D], op=OP.subtract)),
                    reads=[bwt3, bwoh], writes=[bwol])
            oht = [self.sb(f"l{l}_oht{c}", [128, TT], BF16) for c in range(4)]
            olt = [self.sb(f"l{l}_olt{c}", [128, TT], BF16) for c in range(4)]
            boht = [B() for _ in range(4)]
            bolt = [B() for _ in range(4)]
            xt = self.sb(f"l{l}_f3x", [128, TT])
            bxt = B()
            psm = [self.ps(f"l{l}_psm{i}", [128, TT]) for i in range(2)]
            bpsm = [B(), B()]
            for tt in range(NTT):
                sl = slice(tt * TT, (tt + 1) * TT)
                for c in range(4):
                    self.em.dma(f"sq:xld{c % 4}", oht[c][:],
                                self.h_ohi[c * 128:(c + 1) * 128, sl],
                                reads=[bho], writes=[boht[c]])
                    self.em.dma(f"sq:xld{c % 4}", olt[c][:],
                                self.h_olo[c * 128:(c + 1) * 128, sl],
                                reads=[bho2], writes=[bolt[c]])
                for oc in range(8):
                    pp = oc % 2
                    n = 0
                    passes = ((woh, bwoh, oht, boht), (woh, bwoh, olt, bolt),
                              (wol, bwol, oht, boht))
                    for i, (ww, bww, tt_, btt_) in enumerate(passes):
                        for c in range(4):
                            em.op("te", (lambda h, ww=ww, tt_=tt_, c=c,
                                         oc=oc, n=n, pp=pp: h.matmul(
                                out=psm[pp][:],
                                lhsT=ww[:, c * D + oc * 128:
                                        c * D + (oc + 1) * 128],
                                rhs=tt_[c][:],
                                start=(n == 0), stop=(n == 11))),
                                reads=[bww, btt_[c]], writes=[bpsm[pp]],
                                notify=(n == 11))
                            n += 1
                    self.em.dma(f"sq:xld{oc % 4}", xt[:],
                                xin[oc * 128:(oc + 1) * 128, sl],
                                reads=[bxin], writes=[bxt])
                    em.op("ve", (lambda h, pp=pp: h.tensor_tensor(
                        out=xt[:], in0=xt[:], in1=psm[pp][:], op=OP.add)),
                        reads=[bpsm[pp], bxt], writes=[bxt])
                    self.em.dma("ge:st1", xmid[oc * 128:(oc + 1) * 128, sl],
                                xt[:], reads=[bxt], writes=[bxmid])
            self.em.barrier()
            self.pes = old

        with ExitStack() as st:
            old = self.pes; self.pes = st
            ht = [self.sb(f"l{l}_s4h{i}", [128, TT]) for i in range(2)]
            sqt = [self.sb(f"l{l}_s4q{i}", [128, TT]) for i in range(2)]
            rows = self.sb(f"l{l}_s4r", [1, 5 * TT])
            RMU, RSG = slice(0, TT), slice(TT, 2 * TT)
            RS1, RS2, RS3 = (slice(2 * TT, 3 * TT), slice(3 * TT, 4 * TT),
                             slice(4 * TT, 5 * TT))
            psr1 = self.ps(f"l{l}_psr1", [1, TT])
            psr2 = self.ps(f"l{l}_psr2", [1, TT])
            bht = [B(), B()]
            bsqt = [B(), B()]
            brows = B()
            bpsr1, bpsr2 = B(), B()
            hmu, hsg = self.hb(f"l{l}mu"), self.hb(f"l{l}sg")
            for tt in range(NTT):
                for c in range(8):
                    ln = c % 2
                    self.load_shifted(f"sq:xld{ln}", ht[ln], bht[ln],
                                      xmid, bxmid, c, tt, True)
                    em.op("te", (lambda h, ln=ln, c=c: h.matmul(
                        out=psr1[:], lhsT=self.c_ones1[:], rhs=ht[ln][:],
                        start=(c == 0), stop=(c == 7))),
                        reads=[bht[ln], self.b_ones1], writes=[bpsr1],
                        notify=True)
                    em.op("ge", (lambda h, ln=ln: h.tensor_tensor(
                        out=sqt[ln][:], in0=ht[ln][:], in1=ht[ln][:],
                        op=OP.mult)), reads=[bht[ln]], writes=[bsqt[ln]])
                    em.op("te", (lambda h, ln=ln, c=c: h.matmul(
                        out=psr2[:], lhsT=self.c_ones1[:], rhs=sqt[ln][:],
                        start=(c == 0), stop=(c == 7))),
                        reads=[bsqt[ln], self.b_ones1], writes=[bpsr2],
                        notify=True)
                em.op("ve", lambda h: h.tensor_scalar(
                    out=rows[:, RMU], in0=psr1[:], scalar1=1.0 / D,
                    scalar2=None, op0=OP.mult), reads=[bpsr1], writes=[brows])
                em.op("ve", lambda h: h.tensor_tensor(
                    out=rows[:, RSG], in0=rows[:, RMU], in1=rows[:, RMU],
                    op=OP.mult), reads=[brows], writes=[brows])
                em.op("ve", lambda h: h.scalar_tensor_tensor(
                    out=rows[:, RSG], in0=psr2[:], scalar=1.0 / D,
                    in1=rows[:, RSG], op0=OP.mult, op1=OP.subtract),
                    reads=[bpsr2, brows], writes=[brows])
                em.op("ve", lambda h: h.tensor_scalar(
                    out=rows[:, RS1], in0=rows[:, RSG], scalar1=1e-5,
                    scalar2=None, op0=OP.add), reads=[brows], writes=[brows])
                self.nr_rsqrt(rows[:, RSG], rows[:, RS1], rows[:, RS2],
                              rows[:, RS3], [], brows)
                self.em.dma("ge:rows", self.row_ap(self.R_MU, tt * TT, TT),
                            rows[:, RMU], reads=[brows], writes=[hmu])
                self.em.dma("ge:rows", self.row_ap(self.R_SIG, tt * TT, TT),
                            rows[:, RSG], reads=[brows], writes=[hsg])
            self.em.barrier()
            self.pes = old

        with ExitStack() as st:
            old = self.pes; self.pes = st
            h2h = [self.sb(f"l{l}_h2h{c}", [128, TT], BF16) for c in range(8)]
            h2l = [self.sb(f"l{l}_h2l{c}", [128, TT], BF16) for c in range(8)]
            bh2h = [B() for _ in range(8)]
            bh2l = [B() for _ in range(8)]
            hf = self.sb(f"l{l}_h2f", [128, TT])
            bhf = B()
            bmu = self.sb(f"l{l}_bmu", [128, TT])
            bsg = self.sb(f"l{l}_bsg", [128, TT])
            bbmu, bbsg = B(), B()
            mhi = self.sb(f"l{l}_mhi", [128, MOB * TT], BF16)
            mlo = self.sb(f"l{l}_mlo", [128, MOB * TT], BF16)
            bmhi, bmlo = B(), B()
            w1b = [self.sb(f"l{l}_w1b{i}", [128, D], BF16) for i in range(4)]
            bw1b = [B() for _ in range(4)]
            w2b = [self.sb(f"l{l}_w2b{i}", [128, HIDP], BF16) for i in range(2)]
            bw2b = [B(), B()]
            gel = self.sb(f"l{l}_gel", [128, TT])
            mf = self.sb(f"l{l}_mf", [128, TT])
            bgel, bmf = B(), B()
            x4t = self.sb(f"l{l}_x4t", [128, TT])
            bx4t = B()
            psA = self.ps(f"l{l}_psA", [128, TT])
            psG = self.ps(f"l{l}_psG", [128, TT])
            psY = [self.ps(f"l{l}_psY{i}", [128, TT]) for i in range(2)]
            bpsA, bpsG = B(), B()
            bpsY = [B(), B()]
            hmu, hsg = self.hb(f"l{l}mu"), self.hb(f"l{l}sg")
            for tt in range(NTT):
                sl = slice(tt * TT, (tt + 1) * TT)
                self.em.dma("se:bc", bmu[:],
                            self.bcast_in(self.R_MU, tt * TT, TT, 128),
                            reads=[hmu], writes=[bbmu])
                self.em.dma("se:bc", bsg[:],
                            self.bcast_in(self.R_SIG, tt * TT, TT, 128),
                            reads=[hsg], writes=[bbsg])
                for c in range(8):
                    self.load_shifted(f"sq:xld{c % 4}", hf, bhf,
                                      xmid, bxmid, c, tt, True)
                    em.op("ve", lambda h: h.tensor_tensor(
                        out=hf[:], in0=hf[:], in1=bmu[:], op=OP.subtract),
                        reads=[bbmu, bhf], writes=[bhf])
                    em.op("ve", lambda h: h.tensor_tensor(
                        out=hf[:], in0=hf[:], in1=bsg[:], op=OP.mult),
                        reads=[bbsg, bhf], writes=[bhf])
                    em.op("ve", (lambda h, c=c: h.tensor_scalar(
                        out=hf[:], in0=hf[:],
                        scalar1=cols[:, C_LNG + c:C_LNG + c + 1],
                        scalar2=cols[:, C_LNB + c:C_LNB + c + 1],
                        op0=OP.mult, op1=OP.add)),
                        reads=[bcols, bhf], writes=[bhf])
                    em.op("ve", (lambda h, c=c: h.tensor_copy(
                        out=h2h[c][:], in_=hf[:])), reads=[bhf],
                        writes=[bh2h[c]])
                    em.op("ge", (lambda h, c=c: h.tensor_tensor(
                        out=h2l[c][:], in0=hf[:], in1=h2h[c][:],
                        op=OP.subtract)), reads=[bhf, bh2h[c]],
                        writes=[bh2l[c]])
                for mob in range(MOB):
                    self.em.dma("sq:xld0", w1b[0][:],
                                self.h_w1hi[0, :, mob, :],
                                reads=[self.hb("w1s")], writes=[bw1b[0]])
                    self.em.dma("sq:xld1", w1b[1][:],
                                self.h_w1lo[0, :, mob, :],
                                reads=[self.hb("w1s")], writes=[bw1b[1]])
                    self.em.dma("sq:xld2", w1b[2][:],
                                self.h_w1hi[1, :, mob, :],
                                reads=[self.hb("w1s")], writes=[bw1b[2]])
                    self.em.dma("sq:xld3", w1b[3][:],
                                self.h_w1lo[1, :, mob, :],
                                reads=[self.hb("w1s")], writes=[bw1b[3]])
                    for half, ps_, bps_ in ((0, psA, bpsA), (1, psG, bpsG)):
                        n = 0
                        passes = ((w1b[2 * half], bw1b[2 * half], h2h, bh2h),
                                  (w1b[2 * half], bw1b[2 * half], h2l, bh2l),
                                  (w1b[2 * half + 1], bw1b[2 * half + 1],
                                   h2h, bh2h))
                        for i, (ww, bww, hh_, bhh_) in enumerate(passes):
                            for c in range(8):
                                em.op("te", (lambda h, ww=ww, hh_=hh_, c=c,
                                             ps_=ps_, n=n: h.matmul(
                                    out=ps_[:],
                                    lhsT=ww[:, c * 128:(c + 1) * 128],
                                    rhs=hh_[c][:],
                                    start=(n == 0), stop=(n == 23))),
                                    reads=[bww, bhh_[c]], writes=[bps_],
                                    notify=(n == 23))
                                n += 1
                    em.op("se", (lambda h, mob=mob: h.activation(
                        out=gel[:], in_=psG[:], func=AF.Gelu,
                        bias=cols[:, C_B1G + mob:C_B1G + mob + 1],
                        scale=1.0)), reads=[bpsG, bcols], writes=[bgel])
                    em.op("ve", (lambda h, mob=mob: h.scalar_tensor_tensor(
                        out=mf[:], in0=psA[:],
                        scalar=cols[:, C_B1A + mob:C_B1A + mob + 1],
                        in1=gel[:], op0=OP.add, op1=OP.mult)),
                        reads=[bpsA, bgel, bcols], writes=[bmf])
                    em.op("ve", (lambda h, mob=mob: h.tensor_copy(
                        out=mhi[:, mob * TT:(mob + 1) * TT], in_=mf[:])),
                        reads=[bmf], writes=[bmhi])
                    em.op("ge", (lambda h, mob=mob: h.tensor_tensor(
                        out=mlo[:, mob * TT:(mob + 1) * TT], in0=mf[:],
                        in1=mhi[:, mob * TT:(mob + 1) * TT],
                        op=OP.subtract)), reads=[bmf, bmhi], writes=[bmlo])
                for oc in range(8):
                    pp = oc % 2
                    self.em.dma("sq:xld0", w2b[0][:], self.h_w2hi[:, oc, :],
                                reads=[self.hb("w2s")], writes=[bw2b[0]])
                    self.em.dma("sq:xld1", w2b[1][:], self.h_w2lo[:, oc, :],
                                reads=[self.hb("w2s")], writes=[bw2b[1]])
                    n = 0
                    passes = ((w2b[0], bw2b[0], mhi, bmhi),
                              (w2b[0], bw2b[0], mlo, bmlo),
                              (w2b[1], bw2b[1], mhi, bmhi))
                    for i, (ww, bww, mm_, bmm_) in enumerate(passes):
                        for mob in range(MOB):
                            em.op("te", (lambda h, ww=ww, mm_=mm_, mob=mob,
                                         n=n, pp=pp: h.matmul(
                                out=psY[pp][:],
                                lhsT=ww[:, mob * 128:(mob + 1) * 128],
                                rhs=mm_[:, mob * TT:(mob + 1) * TT],
                                start=(n == 0), stop=(n == 3 * MOB - 1))),
                                reads=[bww, bmm_], writes=[bpsY[pp]],
                                notify=(n == 3 * MOB - 1))
                            n += 1
                    self.em.dma(f"sq:xld{oc % 4}", x4t[:],
                                xmid[oc * 128:(oc + 1) * 128, sl],
                                reads=[bxmid], writes=[bx4t])
                    em.op("ve", (lambda h, oc=oc, pp=pp:
                                 h.scalar_tensor_tensor(
                        out=x4t[:], in0=psY[pp][:],
                        scalar=cols[:, C_B2 + oc:C_B2 + oc + 1], in1=x4t[:],
                        op0=OP.add, op1=OP.add)),
                        reads=[bpsY[pp], bcols, bx4t], writes=[bx4t])
                    self.em.dma("ge:st1", xout[oc * 128:(oc + 1) * 128, sl],
                                x4t[:], reads=[bx4t], writes=[bxout])
            self.em.barrier()
            self.pes = old

    def p7_final(self):
        em, S, NTT = self.em, self.S, self.NTT
        x6t = [self.sb(f"p7_x{c}", [128, TT]) for c in range(8)]
        sqb = self.sb("p7_sq", [128, TT])
        rows = self.sb("p7_rows", [1, 5 * TT])
        RMU, RSG = slice(0, TT), slice(TT, 2 * TT)
        RS1, RS2, RS3 = (slice(2 * TT, 3 * TT), slice(3 * TT, 4 * TT),
                         slice(4 * TT, 5 * TT))
        bmu = self.sb("p7_bmu", [128, TT])
        bsg = self.sb("p7_bsg", [128, TT])
        yt = self.sb("p7_yt", [128, TT])
        psr1 = self.ps("p7_psr1", [1, TT])
        psr2 = self.ps("p7_psr2", [1, TT])
        bx6 = [B() for _ in range(8)]
        bsqb, brows, bbmu, bbsg, byt = B(), B(), B(), B(), B()
        bpsr1, bpsr2 = B(), B()
        hy = self.hb("y")
        bx3h, bx6h = self.hb("x3"), self.hb("x6")
        for tt in range(NTT):
            sl = slice(tt * TT, (tt + 1) * TT)
            for c in range(8):
                self.em.dma(f"sq:xld{c % 4}", x6t[c][:],
                            self.h_x6[c * 128:(c + 1) * 128, sl],
                            reads=[bx6h], writes=[bx6[c]])
                self.em.dma(f"sq:xld{c % 4}", yt[:],
                            self.h_x3[c * 128:(c + 1) * 128, sl],
                            reads=[bx3h], writes=[byt])
                em.op("ve", (lambda h, c=c: h.tensor_tensor(
                    out=x6t[c][:], in0=x6t[c][:], in1=yt[:], op=OP.add)),
                    reads=[byt, bx6[c]], writes=[bx6[c]])
                em.op("te", (lambda h, c=c: h.matmul(
                    out=psr1[:], lhsT=self.c_ones1[:], rhs=x6t[c][:],
                    start=(c == 0), stop=(c == 7))),
                    reads=[bx6[c], self.b_ones1], writes=[bpsr1],
                    notify=(c == 7))
                em.op("ge", (lambda h, c=c: h.tensor_tensor(
                    out=sqb[:], in0=x6t[c][:], in1=x6t[c][:], op=OP.mult)),
                    reads=[bx6[c]], writes=[bsqb])
                em.op("te", (lambda h, c=c: h.matmul(
                    out=psr2[:], lhsT=self.c_ones1[:], rhs=sqb[:],
                    start=(c == 0), stop=(c == 7))),
                    reads=[bsqb, self.b_ones1], writes=[bpsr2],
                    notify=True)
            em.op("ve", lambda h: h.tensor_scalar(
                out=rows[:, RMU], in0=psr1[:], scalar1=1.0 / D, scalar2=None,
                op0=OP.mult), reads=[bpsr1], writes=[brows])
            em.op("ve", lambda h: h.tensor_tensor(
                out=rows[:, RSG], in0=rows[:, RMU], in1=rows[:, RMU],
                op=OP.mult), reads=[brows], writes=[brows])
            em.op("ve", lambda h: h.scalar_tensor_tensor(
                out=rows[:, RSG], in0=psr2[:], scalar=1.0 / D,
                in1=rows[:, RSG], op0=OP.mult, op1=OP.subtract),
                reads=[bpsr2, brows], writes=[brows])
            em.op("ve", lambda h: h.tensor_scalar(
                out=rows[:, RS1], in0=rows[:, RSG], scalar1=1e-5,
                scalar2=None, op0=OP.add), reads=[brows], writes=[brows])
            self.nr_rsqrt(rows[:, RSG], rows[:, RS1], rows[:, RS2],
                          rows[:, RS3], [], brows)
            hmu, hsg = self.hb("p7mu"), self.hb("p7sg")
            self.em.dma("ge:rows", self.row_ap(self.R_MU, tt * TT, TT),
                        rows[:, RMU], reads=[brows], writes=[hmu])
            self.em.dma("ge:rows", self.row_ap(self.R_SIG, tt * TT, TT),
                        rows[:, RSG], reads=[brows], writes=[hsg])
            self.em.dma("se:bc", bmu[:],
                        self.bcast_in(self.R_MU, tt * TT, TT, 128),
                        reads=[hmu], writes=[bbmu])
            self.em.dma("se:bc", bsg[:],
                        self.bcast_in(self.R_SIG, tt * TT, TT, 128),
                        reads=[hsg], writes=[bbsg])
            for c in range(8):
                em.op("ve" if c % 2 == 0 else "ge",
                      (lambda h, c=c: h.tensor_tensor(
                          out=x6t[c][:], in0=x6t[c][:], in1=bmu[:],
                          op=OP.subtract)),
                      reads=[bbmu, bx6[c]], writes=[bx6[c]])
                em.op("ve" if c % 2 == 0 else "ge",
                      (lambda h, c=c: h.tensor_tensor(
                          out=x6t[c][:], in0=x6t[c][:], in1=bsg[:],
                          op=OP.mult)),
                      reads=[bbsg, bx6[c]], writes=[bx6[c]])
                self.em.dma("ge:st1" if c % 2 else "sq:st0",
                            self.o_y[c * 128:(c + 1) * 128, sl], x6t[c][:],
                            reads=[bx6[c]], writes=[hy])
        self.em.barrier()

    def build(self):
        self.c_ones1 = self.sb("c_ones1", [128, 1])
        self.c_ones2 = self.sb("c_ones2", [128, 2])
        self.c_ident = self.sb("c_ident", [128, 128])
        self.b_ones1, self.b_ones2, self.b_ident = B(), B(), B()
        self.em.dma("sq:xld0", self.c_ones1[:], self.i_ones1, writes=[self.b_ones1])
        self.em.dma("sq:xld1", self.c_ones2[:], self.i_ones2, writes=[self.b_ones2])
        self.em.dma("sq:xld2", self.c_ident[:], self.i_ident, writes=[self.b_ident])
        def tapdump(name, region, rows):
            if name in self.debug_taps:
                ap = self.nc.dram_tensor("tap_" + name, [rows, self.S], F32,
                                         kind="ExternalOutput").ap()
                self.em.dma("ge:tap", ap, region[0:rows, :])
                self.em.barrier()
        phases = [self.p0_quant_x, self.p1_ternarize, ("stop_p1",),
                  self.p2_attention,
                  (lambda: tapdump("s_ctx", self.h_ctx, D)), ("stop_p2",),
                  self.p3_scramble_quant, self.p4_oproj_ln,
                  (lambda: tapdump("s_x3", self.h_x3, D)), ("stop_p4",),
                  (lambda: self.p6_layer(0)),
                  (lambda: tapdump("s_qkv0", self.h_qkv, 3 * INNER)),
                  (lambda: tapdump("s_x40", self.h_x4, D)),
                  (lambda: tapdump("s_x50", self.h_x5, D)), ("stop_f0",),
                  (lambda: self.p6_layer(1)),
                  (lambda: tapdump("s_x6", self.h_x6, D)),
                  self.p7_final]
        for ph in phases:
            if isinstance(ph, tuple):
                if ph[0] in self.debug_taps:
                    return
                continue
            self.pes = ExitStack()
            with self.pes:
                ph()
            self.pes = None


def build_program(S, debug_taps=()):
    b = Builder(S, debug_taps)
    with b.es:
        b.build()
        b.em.emit(b.nc, b.es)
    return b


def kernel(**inputs):
    x = np.asarray(inputs["L0_attn_x"])
    S = x.shape[1]
    prog = build_program(S)
    in_maps = _host_in_maps(inputs, prog)
    res = run_bass_kernel_spmd(prog.nc, in_maps, core_ids=list(range(8)))
    out = np.stack([np.ascontiguousarray(r["y_fm"]).T for r in res.results], axis=0)
    return out.astype(np.float32)


def _host_in_maps(inputs, prog):
    S = prog.S
    x = np.asarray(inputs["L0_attn_x"], dtype=np.float32)
    wq = np.asarray(inputs["L0_attn_wq"], dtype=np.float32)
    wk = np.asarray(inputs["L0_attn_wk"], dtype=np.float32)
    wv = np.asarray(inputs["L0_attn_wv"], dtype=np.float32)
    wo = np.asarray(inputs["L0_attn_wo"], dtype=np.float32)
    wqkv = np.asarray(inputs["L0_ffn_wqkv"], dtype=np.float32)
    qs = np.asarray(inputs["L0_ffn_q_scale"], dtype=np.float32)
    ks = np.asarray(inputs["L0_ffn_k_scale"], dtype=np.float32)
    fwo = np.asarray(inputs["L0_ffn_wo"], dtype=np.float32)
    lng = np.asarray(inputs["L0_ffn_ln_g"], dtype=np.float32)
    lnb = np.asarray(inputs["L0_ffn_ln_b"], dtype=np.float32)
    w1 = np.asarray(inputs["L0_ffn_w1"], dtype=np.float32)
    b1 = np.asarray(inputs["L0_ffn_b1"], dtype=np.float32)
    w2 = np.asarray(inputs["L0_ffn_w2"], dtype=np.float32)
    b2 = np.asarray(inputs["L0_ffn_b2"], dtype=np.float32)

    wq_t = np.ascontiguousarray(wq.T)
    wk_t = np.ascontiguousarray(wk.T)
    wv_t = np.ascontiguousarray(wv.T)
    wo_t = np.ascontiguousarray(wo.T)
    w1a = np.zeros((2, D, HIDP), np.float32)
    w1g = np.zeros((2, D, HIDP), np.float32)
    w1a[:, :, :HID] = w1[:, :, :HID]
    w1g[:, :, :HID] = w1[:, :, HID:]
    w2p = np.zeros((2, HIDP, D), np.float32)
    w2p[:, :HID, :] = w2
    wqkv_r = np.ascontiguousarray(
        wqkv.reshape(2, 8, 128, 3 * INNER).transpose(0, 2, 1, 3))
    fwo_r = np.ascontiguousarray(fwo.reshape(2, 4, 128, D).transpose(0, 2, 1, 3))
    w1a_r = np.ascontiguousarray(
        w1a.reshape(2, 8, 128, MOB, 128).transpose(0, 2, 3, 1, 4)
        .reshape(2, 128, MOB, D))
    w1g_r = np.ascontiguousarray(
        w1g.reshape(2, 8, 128, MOB, 128).transpose(0, 2, 3, 1, 4)
        .reshape(2, 128, MOB, D))
    w2_r = np.ascontiguousarray(
        w2p.reshape(2, MOB, 128, 8, 128).transpose(0, 2, 3, 1, 4)
        .reshape(2, 128, 8, HIDP))
    b1a = np.zeros((2, HIDP), np.float32)
    b1g = np.zeros((2, HIDP), np.float32)
    b1a[:, :HID] = b1[:, :HID]
    b1g[:, :HID] = b1[:, HID:]
    b1a_col = np.ascontiguousarray(b1a.reshape(2, MOB, 128).transpose(0, 2, 1))
    b1g_col = np.ascontiguousarray(b1g.reshape(2, MOB, 128).transpose(0, 2, 1))
    b2_col = np.ascontiguousarray(b2.reshape(2, 8, 128).transpose(0, 2, 1))
    lng_col = np.ascontiguousarray(lng.reshape(2, 8, 128).transpose(0, 2, 1))
    lnb_col = np.ascontiguousarray(lnb.reshape(2, 8, 128).transpose(0, 2, 1))
    qs_col = np.ascontiguousarray(np.tile(qs, (1, 2))[:, :, None])
    ks_col = np.ascontiguousarray(np.tile(ks, (1, 2))[:, :, None])
    tbl = np.stack(_rotary_tables(S), axis=0)
    mask = _local_mask()
    maskb = ((1.0 - np.asarray(mask, np.float32)) * -3.0e38).astype(
        ml_dtypes.bfloat16)
    ident = np.eye(128, dtype=np.float32)
    ones1 = np.ones((128, 1), np.float32)
    ones2 = np.zeros((128, 2), np.float32)
    ones2[:64, 0] = 1.0
    ones2[64:, 1] = 1.0

    shared = dict(
        wq_t=wq_t, wk_t=wk_t, wv_t=wv_t, wo_t=wo_t,
        f_wqkv=wqkv_r, f_wo=fwo_r, f_w1a=w1a_r, f_w1g=w1g_r, f_w2=w2_r,
        b1a_col=b1a_col, b1g_col=b1g_col, b2_col=b2_col,
        lng_col=lng_col, lnb_col=lnb_col, qs_col=qs_col, ks_col=ks_col,
        tbl=tbl, mask_loc=mask, maskb_loc=maskb, ident=ident, ones1=ones1,
        ones2=ones2,
    )
    maps = []
    for b_ in range(x.shape[0]):
        m = dict(shared)
        m["x_fm"] = np.ascontiguousarray(x[b_].T)
        maps.append(m)
    return maps


# version 10
# speedup vs baseline: 1.0846x; 1.0846x over previous
import math
import numpy as np
import ml_dtypes
import concourse.bass as bass
import concourse.mybir as mybir
from concourse.bass_utils import run_bass_kernel_spmd
from concourse.alu_op_type import AluOpType as OP
from contextlib import ExitStack

F32 = mybir.dt.float32
BF16 = mybir.dt.bfloat16
AF = mybir.ActivationFunctionType
AX = mybir.AxisListType

D = 1024
NH = 16
DH = 64
FH = 8
FW = 512
INNER = 512
HID = 2730
HIDP = 2816
MOB = HIDP // 128
TT = 512
EPS_Q = 1e-5
NEG_BIG = -3.4028234663852886e38



def _rotary_tables(S):
    d = np.arange(0, DH, 2, dtype=np.float64)
    inv_freq = 1.0 / 10000.0 ** (d / DH)
    t = np.arange(S, dtype=np.float64)
    freqs = t[:, None] * inv_freq[None, :]
    freqs = np.concatenate([freqs, freqs], axis=1)
    base = (d + 0.4 * DH) / (1.4 * DH)
    power = (t - S // 2) / (FW // 2)
    scale = base[None, :] ** power[:, None]
    scale = np.concatenate([scale, scale], axis=1)
    cos, sin = np.cos(freqs), np.sin(freqs)
    q1 = (cos * scale).T
    q2 = (sin * scale).T
    k1 = (cos / scale).T
    k2 = (sin / scale).T
    sgn = np.ones((DH, 1)); sgn[:32] = -1.0
    q2 = q2 * sgn
    k2 = k2 * sgn
    out = []
    for a in (q1, q2, k1, k2):
        out.append(np.tile(a, (2, 1)).astype(np.float32))
    return out


def _local_mask():
    kk = np.arange(1024)[:, None]
    qq = np.arange(512)[None, :]
    m = ((kk >= qq) & (kk <= qq + 512)).astype(np.float32)
    m = m.reshape(8, 128, 512).transpose(1, 0, 2).reshape(128, 8 * 512)
    return m.astype(ml_dtypes.bfloat16)



ENGS = ("te", "ve", "se", "ge", "sq")


class B:
    __slots__ = ("w", "r")

    def __init__(self):
        self.w = None
        self.r = []


class Em:

    def __init__(self):
        self.plan = {e: [] for e in ENGS}
        self.cnt = {e: 0 for e in ENGS}
        self.dcnt = {}
        self.dqueue = {}
        self.waited = {e: {} for e in ENGS}
        self.pend_r = {e: [] for e in ENGS}
        self.pend_w = {e: [] for e in ENGS}

    def _wait(self, eng, tok):
        s, v = tok
        if s == eng == "te":
            return
        if self.waited[eng].get(s, 0) >= v:
            return
        self.waited[eng][s] = v
        self.plan[eng].append(("w", s, v))

    def _deps(self, eng, reads, writes, extra):
        toks = []
        for b in reads:
            if b.w is not None:
                toks.append(b.w)
        for b in writes:
            toks.extend(b.r)
            if b.w is not None:
                toks.append(b.w)
        toks.extend(extra)
        for t in toks:
            self._wait(eng, t)

    def _post(self, tok, reads, writes):
        if tok is None:
            return
        for b in reads:
            b.r.append(tok)
        for b in writes:
            b.w = tok
            b.r = []

    def op(self, eng, fn, reads=(), writes=(), extra=(), notify=True):
        self._deps(eng, reads, writes, extra)
        tok = None
        if notify:
            self.cnt[eng] += 1
            tok = (eng, self.cnt[eng])
        self.plan[eng].append(("o", fn, tok))
        if notify:
            self._post(tok, list(reads) + self.pend_r[eng],
                       list(writes) + self.pend_w[eng])
            self.pend_r[eng] = []
            self.pend_w[eng] = []
        else:
            self.pend_r[eng].extend(reads)
            self.pend_w[eng].extend(writes)
        return tok

    def dma(self, stream, out, in_, reads=(), writes=(), extra=(), eng=None,
            transpose=False, **kw):
        if stream not in self.dqueue:
            self.dqueue[stream] = eng or {"sq": "sq", "se": "se", "ge": "ge",
                                          "ve": "ve"}[stream.split(":")[0]]
            self.dcnt[stream] = 0
        eng = self.dqueue[stream]
        if self.dcnt[stream] > 0:
            self._wait(eng, (stream, self.dcnt[stream]))
        self._deps(eng, reads, writes, extra)
        self.dcnt[stream] += 16
        tok = (stream, self.dcnt[stream])
        self.plan[eng].append(("d", out, in_, stream, transpose, kw, tok))
        self._post(tok, reads, writes)
        return tok

    def barrier(self):
        toks = [(e, self.cnt[e]) for e in ENGS if self.cnt[e] > 0]
        toks += [(s, c) for s, c in self.dcnt.items() if c > 0]
        for e in ENGS:
            for t in toks:
                if t[0] == e == "te":
                    continue
                s, v = t
                if self.waited[e].get(s, 0) >= v:
                    continue
                self.waited[e][s] = v
                self.plan[e].append(("w", s, v))

    def emit(self, nc, es):
        sems = {}
        for e in ENGS:
            sems[e] = es.enter_context(nc.semaphore(f"sem_{e}"))
        for s in self.dcnt:
            nm = "sem_" + s.replace(":", "_")
            sems[s] = es.enter_context(nc.semaphore(nm))
        n_sems = len(sems)
        if n_sems > 70:
            raise RuntimeError(f"too many semaphores: {n_sems}")
        block = es.enter_context(nc.Block())
        eng_attr = {"te": "tensor", "ve": "vector", "se": "scalar",
                    "ge": "gpsimd", "sq": "sync"}
        for e in ENGS:
            items = self.plan[e]

            def section(h, items=items, e=e):
                for it in items:
                    if it[0] == "w":
                        h.wait_ge(sems[it[1]], it[2])
                    elif it[0] == "o":
                        inst = it[1](h)
                        if it[2] is not None:
                            if inst is None:
                                raise RuntimeError("notify op returned no inst")
                            inst.then_inc(sems[e], 1)
                    else:
                        _, out, in_, s, tr, kw, tok = it
                        h.dma_start(out=out, in_=in_, transpose=tr,
                                    **kw).then_inc(sems[s], 16)
            getattr(block, eng_attr[e])(section)



def ceil_div(a, b):
    return (a + b - 1) // b


class Builder:

    def __init__(self, S, debug_taps=()):
        self.S = S
        self.NTT = S // TT
        self.NTC = S // 128
        self.NW = S // FW
        self.nc = bass.Bass("TRN2", target_bir_lowering=False)
        self.em = Em()
        self.es = ExitStack()
        self.debug_taps = set(debug_taps)
        self.taps = {}
        self._sbuf_bytes = 0
        self._declare_io()

    def _declare_io(self):
        nc, S = self.nc, self.S
        dt = nc.dram_tensor
        self.i_x = dt("x_fm", [D, S], F32, kind="ExternalInput").ap()
        for nm in ("wq_t", "wk_t", "wv_t", "wo_t"):
            setattr(self, "i_" + nm, dt(nm, [D, D], F32, kind="ExternalInput").ap())
        self.i_f_wqkv = dt("f_wqkv", [2, 128, 8, 3 * INNER], F32,
                           kind="ExternalInput").ap()
        self.i_f_wo = dt("f_wo", [2, 128, 4, D], F32, kind="ExternalInput").ap()
        self.i_f_w1a = dt("f_w1a", [2, 128, MOB, D], F32, kind="ExternalInput").ap()
        self.i_f_w1g = dt("f_w1g", [2, 128, MOB, D], F32, kind="ExternalInput").ap()
        self.i_f_w2 = dt("f_w2", [2, 128, 8, HIDP], F32, kind="ExternalInput").ap()
        self.i_b1a = dt("b1a_col", [2, 128, MOB], F32, kind="ExternalInput").ap()
        self.i_b1g = dt("b1g_col", [2, 128, MOB], F32, kind="ExternalInput").ap()
        self.i_b2 = dt("b2_col", [2, 128, 8], F32, kind="ExternalInput").ap()
        self.i_lng = dt("lng_col", [2, 128, 8], F32, kind="ExternalInput").ap()
        self.i_lnb = dt("lnb_col", [2, 128, 8], F32, kind="ExternalInput").ap()
        self.i_qs = dt("qs_col", [2, 128, 1], F32, kind="ExternalInput").ap()
        self.i_ks = dt("ks_col", [2, 128, 1], F32, kind="ExternalInput").ap()
        self.i_tbl = dt("tbl", [4, 128, S], F32, kind="ExternalInput").ap()
        self.i_mask = dt("mask_loc", [128, 8 * 512], BF16, kind="ExternalInput").ap()
        self.i_maskb = dt("maskb_loc", [128, 8 * 512], BF16, kind="ExternalInput").ap()
        self.i_ident = dt("ident", [128, 128], F32, kind="ExternalInput").ap()
        self.i_ones1 = dt("ones1", [128, 1], F32, kind="ExternalInput").ap()
        self.i_ones2 = dt("ones2", [128, 2], F32, kind="ExternalInput").ap()
        self.o_y = dt("y_fm", [D, S], F32, kind="ExternalOutput").ap()

        self.h_xq = dt("h_xq", [D, S], BF16, kind="Internal").ap()
        self.h_wt = dt("h_wt", [4, D, D], BF16, kind="Internal").ap()
        self.h_ctx = dt("h_ctx", [D, S], F32, kind="Internal").ap()
        self.h_ctxq = dt("h_ctxq", [D, S], BF16, kind="Internal").ap()
        self.h_x2 = dt("h_x2", [D, S], F32, kind="Internal").ap()
        self.h_x3 = dt("h_x3", [D, S], F32, kind="Internal").ap()
        self.h_x4 = dt("h_x4", [D, S], F32, kind="Internal").ap()
        self.h_x5 = dt("h_x5", [D, S], F32, kind="Internal").ap()
        self.h_x6 = dt("h_x6", [D, S], F32, kind="Internal").ap()
        self.h_qkv = dt("h_qkv", [3 * INNER, S], F32, kind="Internal").ap()
        self.h_ohi = dt("h_ohi", [INNER, S], BF16, kind="Internal").ap()
        self.h_olo = dt("h_olo", [INNER, S], BF16, kind="Internal").ap()
        self.h_w1hi = dt("h_w1hi", [2, 128, MOB, D], BF16, kind="Internal").ap()
        self.h_w1lo = dt("h_w1lo", [2, 128, MOB, D], BF16, kind="Internal").ap()
        self.h_w2hi = dt("h_w2hi", [128, 8, HIDP], BF16, kind="Internal").ap()
        self.h_w2lo = dt("h_w2lo", [128, 8, HIDP], BF16, kind="Internal").ap()
        self.h_rows = dt("h_rows", [64, S], F32, kind="Internal").ap()

    R_SS = 0
    R_MX = 1
    R_RHO = 2
    R_G0 = 3
    R_SWQ = 4
    R_L = 12
    R_MU = 6
    R_SIG = 7
    R_C = 8
    R_TMP = 9
    R_RHO2 = 10
    R_G02 = 11

    def tap(self, name, shape, dtype=F32):
        if name in self.debug_taps:
            ap = self.nc.dram_tensor("tap_" + name, shape, dtype,
                                     kind="ExternalOutput").ap()
            self.taps[name] = ap
            return ap
        return None

    def sb(self, name, shape, dtype=F32, scope=None):
        st = scope or getattr(self, "pes", None) or self.es
        return st.enter_context(self.nc.sbuf_tensor(name, shape, dtype))

    def ps(self, name, shape, dtype=F32):
        st = getattr(self, "pes", None) or self.es
        return st.enter_context(self.nc.psum_tensor(name, shape, dtype))


    def hb(self, name):
        if not hasattr(self, "_hb"):
            self._hb = {}
        if name not in self._hb:
            self._hb[name] = B()
        return self._hb[name]

    def row_ap(self, slot, lo, n):
        return self.h_rows[slot, lo:lo + n]

    def bcast_in(self, slot, lo, n, p):
        return self.h_rows[slot, lo:lo + n][None, :].to_broadcast([p, n])

    def nr_rsqrt(self, dst, src, sA, sB, b_src, b_all):
        em = self.em
        em.op("se", lambda h: h.activation(out=sA, in_=src, func=AF.Sqrt),
              reads=[b_all] + b_src, writes=[b_all])
        em.op("ve", lambda h: h.reciprocal(out=dst, in_=sA),
              reads=[b_all], writes=[b_all])
        em.op("ve", lambda h: h.tensor_tensor(out=sB, in0=src, in1=dst,
                                              op=OP.mult),
              reads=[b_all], writes=[b_all])
        em.op("ve", lambda h: h.tensor_tensor(out=sB, in0=sB, in1=dst,
                                              op=OP.mult),
              reads=[b_all], writes=[b_all])
        em.op("ve", lambda h: h.tensor_scalar(out=sB, in0=sB, scalar1=-0.5,
                                              scalar2=1.5, op0=OP.mult,
                                              op1=OP.add),
              reads=[b_all], writes=[b_all])
        em.op("ve", lambda h: h.tensor_tensor(out=dst, in0=dst, in1=sB,
                                              op=OP.mult),
              reads=[b_all], writes=[b_all])

    def p0_quant_x(self):
        self.quant_pipeline(self.i_x, self.h_xq, self.R_RHO,
                            self.R_G0, 'p0', None)

    def quant_pipeline(self, src, dst, r_rho, r_g0, tag, src_hb):
        em, S = self.em, self.S
        xt = [self.sb(f"{tag}_x{c}", [128, TT]) for c in range(8)]
        sqt = [self.sb(f"{tag}_sq{i}", [128, TT]) for i in range(2)]
        acc = self.sb("{}_acc".format(tag), [128, TT])
        tmp = self.sb("{}_tmp".format(tag), [64, TT])
        tmp2 = self.sb("{}_tmp2".format(tag), [128, TT])
        rows = self.sb("{}_rows".format(tag), [1, 8 * TT])
        RS, RM, RR, RD_, RC, RP, RG = (slice(0, TT), slice(TT, 2 * TT),
                                       slice(2 * TT, 3 * TT), slice(3 * TT, 4 * TT),
                                       slice(4 * TT, 5 * TT), slice(5 * TT, 6 * TT),
                                       slice(6 * TT, 7 * TT))
        RSC = slice(7 * TT, 8 * TT)
        brho = self.sb("{}_brho".format(tag), [128, TT])
        t0 = [self.sb(f"{tag}_t0{i}", [128, TT]) for i in range(2)]
        t1 = [self.sb(f"{tag}_t1{i}", [128, TT]) for i in range(2)]
        ti = [self.sb(f"{tag}_ti{i}", [128, TT], mybir.dt.int32)
              for i in range(2)]
        xqt = [self.sb(f"{tag}_xq{i}", [128, TT], BF16) for i in range(2)]
        pss = self.ps("{}_pss".format(tag), [1, TT])
        bx = [B() for _ in range(8)]
        bsq = [B(), B()]
        bacc, btmp, brows, bbrho = B(), B(), B(), B()
        btmp2 = B()
        bps = B()
        bt = [[B(), B()] for _ in range(2)]
        bti = [B(), B()]
        bxq = [B(), B()]
        tap_rho = self.tap(tag + "_rho", [self.NTT, TT])
        tap_mx = self.tap(tag + "_mx", [self.NTT, TT])

        for tt in range(self.NTT):
            sl = slice(tt * TT, (tt + 1) * TT)
            for c in range(8):
                self.em.dma(f"sq:xld{c % 4}", xt[c][:],
                            src[c * 128:(c + 1) * 128, sl],
                            writes=[bx[c]], reads=[src_hb] if src_hb else [])
            for c in range(8):
                s = sqt[c % 2]
                em.op("ve", (lambda h, c=c, s=s: h.tensor_tensor(
                    out=s[:], in0=xt[c][:], in1=xt[c][:], op=OP.mult)),
                    reads=[bx[c]], writes=[bsq[c % 2]])
                em.op("te", (lambda h, c=c, s=s: h.matmul(
                    out=pss[:], lhsT=self.c_ones1[:], rhs=s[:],
                    start=(c == 0), stop=(c == 7))),
                    reads=[bsq[c % 2], self.b_ones1], writes=[bps],
                    notify=True)
            em.op("ve", lambda h: h.tensor_tensor(
                out=acc[:], in0=xt[0][:], in1=xt[0][:], op=OP.mult),
                reads=[bx[0]], writes=[bacc])
            for c in range(1, 8):
                em.op("ve", (lambda h, c=c: h.tensor_tensor(
                    out=tmp2[:], in0=xt[c][:], in1=xt[c][:], op=OP.mult)),
                    reads=[bx[c]], writes=[btmp2])
                em.op("ve", lambda h: h.tensor_tensor(
                    out=acc[:], in0=acc[:], in1=tmp2[:], op=OP.max),
                    reads=[btmp2, bacc], writes=[bacc])
            n = 64
            while n >= 1:
                self.em.dma("ge:tree", tmp[0:n, :], acc[n:2 * n, :],
                            reads=[bacc], writes=[btmp])
                em.op("ve", (lambda h, n=n: h.tensor_tensor(
                    out=acc[0:n, :], in0=acc[0:n, :], in1=tmp[0:n, :],
                    op=OP.max)), reads=[btmp, bacc], writes=[bacc])
                n //= 2
            em.op("ve", lambda h: h.tensor_copy(out=rows[:, RS], in_=pss[:]),
                  reads=[bps], writes=[brows])
            em.op("ve", lambda h: h.tensor_scalar(
                out=rows[:, RM], in0=acc[0:1, :], scalar1=1e-30, scalar2=None,
                op0=OP.max), reads=[bacc], writes=[brows])
            self.nr_rsqrt(rows[:, RC], rows[:, RM], rows[:, RSC], rows[:, RD_],
                          [], brows)
            em.op("ve", lambda h: h.tensor_tensor(
                out=rows[:, RM], in0=rows[:, RM], in1=rows[:, RC], op=OP.mult),
                reads=[brows], writes=[brows])
            em.op("ve", lambda h: h.tensor_scalar(
                out=rows[:, RSC], in0=rows[:, RS], scalar1=1.0 / D, scalar2=1e-6,
                op0=OP.mult, op1=OP.add), reads=[brows], writes=[brows])
            self.nr_rsqrt(rows[:, RR], rows[:, RSC], rows[:, RD_], rows[:, RC],
                          [], brows)
            em.op("ve", lambda h: h.tensor_tensor(
                out=rows[:, RD_], in0=rows[:, RR], in1=rows[:, RM], op=OP.mult),
                reads=[brows], writes=[brows])
            em.op("ve", lambda h: h.tensor_scalar(
                out=rows[:, RD_], in0=rows[:, RD_], scalar1=EPS_Q, scalar2=None,
                op0=OP.max), reads=[brows], writes=[brows])
            em.op("ve", lambda h: h.reciprocal(out=rows[:, RC], in_=rows[:, RD_]),
                  reads=[brows], writes=[brows])
            em.op("ve", lambda h: h.scalar_tensor_tensor(
                out=rows[:, RP], in0=rows[:, RR], scalar=127.0, in1=rows[:, RC],
                op0=OP.mult, op1=OP.mult), reads=[brows], writes=[brows])
            em.op("ve", lambda h: h.tensor_scalar(
                out=rows[:, RG], in0=rows[:, RD_], scalar1=1.0 / 127.0,
                scalar2=None, op0=OP.mult), reads=[brows], writes=[brows])
            hrho, hg0 = self.hb(tag + "rho"), self.hb(tag + "g0")
            self.em.dma("ge:rows", self.row_ap(r_rho, tt * TT, TT),
                        rows[:, RP], reads=[brows], writes=[hrho])
            self.em.dma("ge:rows", self.row_ap(r_g0, tt * TT, TT),
                        rows[:, RG], reads=[brows], writes=[hg0])
            if tap_rho is not None:
                self.em.dma("ge:rows", tap_rho[tt, :][None, :],
                            rows[:, RP], reads=[brows])
                self.em.dma("ge:rows", tap_mx[tt, :][None, :],
                            rows[:, RM], reads=[brows])
            self.em.dma("se:bc", brho[:], self.bcast_in(r_rho, tt * TT, TT, 128),
                        reads=[hrho], writes=[bbrho])
            for c in range(8):
                ln = c % 2
                eng = "ve" if ln == 0 else "ge"
                em.op(eng, (lambda h, c=c, ln=ln: h.tensor_tensor(
                    out=t0[ln][:], in0=xt[c][:], in1=brho[:], op=OP.mult)),
                    reads=[bx[c], bbrho], writes=[bt[ln][0]])
                em.op("ve", (lambda h, ln=ln: h.tensor_scalar(
                    out=ti[ln][:], in0=t0[ln][:], scalar1=0.5, scalar2=None,
                    op0=OP.add)), reads=[bt[ln][0]], writes=[bti[ln]])
                em.op("ve", (lambda h, ln=ln: h.scalar_tensor_tensor(
                    out=t1[ln][:], in0=t0[ln][:], scalar=0.5, in1=ti[ln][:],
                    op0=OP.add, op1=OP.subtract)),
                    reads=[bti[ln], bt[ln][0]], writes=[bt[ln][1]])
                em.op(eng, (lambda h, ln=ln: h.tensor_scalar(
                    out=t1[ln][:], in0=t1[ln][:], scalar1=0.0, scalar2=None,
                    op0=OP.is_lt)), reads=[bt[ln][1]], writes=[bt[ln][1]])
                em.op("ve", (lambda h, ln=ln: h.tensor_tensor(
                    out=t0[ln][:], in0=ti[ln][:], in1=t1[ln][:],
                    op=OP.subtract)), reads=[bti[ln], bt[ln][1]],
                    writes=[bt[ln][0]])
                em.op(eng, (lambda h, ln=ln: h.tensor_scalar(
                    out=xqt[ln][:], in0=t0[ln][:], scalar1=-128.0, scalar2=127.0,
                    op0=OP.max, op1=OP.min)), reads=[bt[ln][0]], writes=[bxq[ln]])
                self.em.dma("ge:st1" if ln else "sq:st0",
                            dst[c * 128:(c + 1) * 128, sl], xqt[ln][:],
                            reads=[bxq[ln]], writes=[self.hb(tag + "dst")])
        self.em.barrier()

    def p1_ternarize(self):
        em = self.em
        wch = [self.sb(f"p1_w{i}", [128, D]) for i in range(2)]
        wsum = self.sb("p1_wsum", [128, 8])
        wtot = self.sb("p1_wtot", [128, 1])
        srow = self.sb("p1_srow", [1, 8])
        sbc = self.sb("p1_sbc", [128, 1])
        ya = self.sb("p1_ya", [128, D])
        ab = self.sb("p1_ab", [128, D])
        mk = self.sb("p1_mk", [128, D])
        sg = self.sb("p1_sg", [128, D])
        tq = self.sb("p1_tq", [128, D], BF16)
        pmm = self.ps("p1_pmm", [1, 1])
        mats = [self.i_wq_t, self.i_wk_t, self.i_wv_t, self.i_wo_t]
        bw = [B(), B()]
        bws, bwt, bsr, bsb = B(), B(), B(), B()
        bya, bab, bmk, bsg, btq = B(), B(), B(), B(), B()
        bp = B()
        for M in range(4):
            for c in range(8):
                ln = c % 2
                self.em.dma(f"sq:xld{ln}", wch[ln][:],
                            mats[M][c * 128:(c + 1) * 128, :], writes=[bw[ln]])
                em.op("ve", (lambda h, c=c, ln=ln: h.reduce_sum(
                    out=wsum[:, c:c + 1], in_=wch[ln][:], axis=AX.X,
                    apply_absolute_value=True)), reads=[bw[ln]], writes=[bws])
            em.op("ve", lambda h: h.reduce_sum(out=wtot[:], in_=wsum[:], axis=AX.X),
                  reads=[bws], writes=[bwt])
            em.op("te", lambda h: h.matmul(out=pmm[:], lhsT=self.c_ones1[:],
                                           rhs=wtot[:], start=True, stop=True),
                  reads=[bwt, self.b_ones1], writes=[bp])
            em.op("ve", lambda h: h.tensor_scalar(
                out=srow[:, 0:1], in0=pmm[:], scalar1=1.0 / (D * D), scalar2=EPS_Q,
                op0=OP.mult, op1=OP.max), reads=[bp], writes=[bsr])
            em.op("ve", lambda h: h.reciprocal(out=srow[:, 1:2], in_=srow[:, 0:1]),
                  reads=[bsr], writes=[bsr])
            hsw = self.hb("sw")
            self.em.dma("ge:rows", self.row_ap(self.R_SWQ, M, 1),
                        srow[:, 0:1], reads=[bsr], writes=[hsw])
            self.em.dma("ge:rows", self.row_ap(self.R_SWQ, 4 + M, 1),
                        srow[:, 1:2], reads=[bsr], writes=[hsw])
            self.em.dma("se:bc", sbc[:], self.bcast_in(self.R_SWQ, 4 + M, 1, 128),
                        reads=[hsw], writes=[bsb])
            for c in range(8):
                ln = c % 2
                self.em.dma(f"sq:xld{ln}", wch[ln][:],
                            mats[M][c * 128:(c + 1) * 128, :], writes=[bw[ln]])
                em.op("ve", (lambda h, ln=ln: h.tensor_scalar(
                    out=ya[:], in0=wch[ln][:], scalar1=sbc[:, 0:1], scalar2=None,
                    op0=OP.mult)), reads=[bw[ln], bsb], writes=[bya])
                em.op("ge", lambda h: h.tensor_tensor(
                    out=ab[:], in0=ya[:], in1=ya[:], op=OP.mult),
                    reads=[bya], writes=[bab])
                em.op("ge", lambda h: h.tensor_scalar(
                    out=mk[:], in0=ab[:], scalar1=0.25, scalar2=None,
                    op0=OP.is_gt), reads=[bab], writes=[bmk])
                em.op("se", lambda h: h.activation(out=sg[:], in_=ya[:],
                                                   func=AF.Sign),
                      reads=[bya], writes=[bsg])
                em.op("ve", lambda h: h.tensor_tensor(
                    out=tq[:], in0=sg[:], in1=mk[:], op=OP.mult),
                    reads=[bsg, bmk], writes=[btq])
                self.em.dma("ge:st1", self.h_wt[M, c * 128:(c + 1) * 128, :],
                            tq[:], reads=[btq], writes=[self.hb("wt")])
        self.em.barrier()

    def p2_attention(self):
        em, S, NTC = self.em, self.S, self.NTC
        xq = self.sb("p2_xq", [128, 8 * S], BF16)
        wt3 = [self.sb(f"p2_wt{m}", [128, 8 * D], BF16) for m in range(3)]
        bg0 = self.sb("p2_bg0", [128, S])
        g0col = self.sb("p2_g0col", [128, NTC])
        swc = self.sb("p2_swc", [128, 4])
        qt = self.sb("p2_qt", [128, D])
        kt = self.sb("p2_kt", [128, D])
        vt = self.sb("p2_vt", [128, D])
        prod = [self.sb(f"p2_prod{i}", [128, D]) for i in range(2)]
        prod2 = [self.sb(f"p2_prod2{i}", [128, D]) for i in range(2)]
        sc = self.sb("p2_sc", [128, 256])
        ee = self.sb("p2_ee", [128, 256])
        linv = self.sb("p2_linv", [128, 16])
        redq = self.sb("p2_redq", [128, 16])
        negc = self.sb("p2_negc", [128, 1])
        ctx = self.sb("p2_ctx", [128, D])
        ctile = self.sb("p2_ctile", [128, 128])
        psq = [self.ps(f"p2_psq{i}", [128, TT]) for i in range(2)]
        pst = self.ps("p2_pst", [128, 128])
        bxq, bwt = B(), [B(), B(), B()]
        bbg0, bg0c, bswc = B(), B(), B()
        bqt, bkt, bvt = B(), B(), B()
        bprod = [B(), B()]
        bprod2 = [B(), B()]
        bsc, bee, blinv = B(), B(), B()
        bredq, bnegc, bctx, bctile = B(), B(), B(), B()
        bpsq = [B(), B()]
        bpst = B()
        hctx = self.hb("ctx")

        for c in range(8):
            self.em.dma(f"sq:xld{c % 4}", xq[:, c * S:(c + 1) * S],
                        self.h_xq[c * 128:(c + 1) * 128, :],
                        reads=[self.hb("p0dst")], writes=[bxq])
            for m in range(3):
                self.em.dma("sq:wsl", wt3[m][:, c * D:(c + 1) * D],
                            self.h_wt[m, c * 128:(c + 1) * 128, :],
                            reads=[self.hb("wt")], writes=[bwt[m]])
        self.em.dma("se:bc", bg0[:], self.bcast_in(self.R_G0, 0, S, 128),
                    reads=[self.hb("p0g0")], writes=[bbg0])
        for M in range(4):
            self.em.dma("se:bc", swc[:, M:M + 1],
                        self.bcast_in(self.R_SWQ, M, 1, 128),
                        reads=[self.hb("sw")], writes=[bswc])
        for tc in range(NTC):
            em.op("te", (lambda h, tc=tc: h.transpose(
                out=pst[:], in_=bg0[:, tc * 128:(tc + 1) * 128],
                identity=self.c_ident[:])),
                reads=[bbg0, self.b_ident], writes=[bpst])
            em.op("ve", (lambda h, tc=tc: h.tensor_copy(
                out=g0col[:, tc:tc + 1], in_=pst[:, 0:1])),
                reads=[bpst], writes=[bg0c])

        r3 = lambda ap, d: ap.rearrange("p (a b) -> p a b", b=d)
        for scnk in range(NTC):
            ssl = slice(scnk * 128, (scnk + 1) * 128)
            for m, (dst, bdst) in enumerate(((qt, bqt), (kt, bkt), (vt, bvt))):
                for et in range(2):
                    pp = et
                    for c in range(8):
                        em.op("te", (lambda h, m=m, et=et, c=c, pp=pp,
                                     ssl=ssl: h.matmul(
                            out=psq[pp][:],
                            lhsT=xq[:, c * S + ssl.start:c * S + ssl.stop],
                            rhs=wt3[m][:, c * D + et * TT:c * D + (et + 1) * TT],
                            start=(c == 0), stop=(c == 7))),
                            reads=[bxq, bwt[m]], writes=[bpsq[pp]],
                            notify=(c == 7))
                    em.op("ve", (lambda h, m=m, et=et, pp=pp, scnk=scnk,
                                 dst=dst: h.tensor_scalar(
                        out=dst[:, et * TT:(et + 1) * TT], in0=psq[pp][:],
                        scalar1=g0col[:, scnk:scnk + 1],
                        scalar2=swc[:, m:m + 1], op0=OP.mult, op1=OP.mult)),
                        reads=[bpsq[pp], bg0c, bswc], writes=[bdst])
            for t in range(16):
                pi = t % 2
                em.op("ge", (lambda h, t=t, pi=pi: h.tensor_tensor(
                    out=r3(prod[pi][:], 64), in0=r3(qt[:], 64),
                    in1=kt[:, t * 64:(t + 1) * 64].rearrange(
                        "p (a d) -> p a d", a=1).to_broadcast([128, 16, 64]),
                    op=OP.mult)), reads=[bqt, bkt], writes=[bprod[pi]])
                em.op("ve", (lambda h, t=t, pi=pi: h.reduce_sum(
                    out=sc[:, t::16].rearrange("p (a b) -> p a b", b=1),
                    in_=r3(prod[pi][:], 64), axis=AX.X)),
                    reads=[bprod[pi]], writes=[bsc])
            em.op("ve", lambda h: h.reduce_max(
                out=redq[:, 0:16].rearrange("p (a b) -> p a b", b=1),
                in_=r3(sc[:], 16), axis=AX.X), reads=[bsc], writes=[bredq])
            em.op("ve", lambda h: h.tensor_tensor(
                out=r3(sc[:], 16), in0=r3(sc[:], 16),
                in1=redq[:, 0:16].rearrange("p (h a) -> p h a",
                                            a=1).to_broadcast([128, 16, 16]),
                op=OP.subtract), reads=[bredq, bsc], writes=[bsc])
            em.op("se", lambda h: h.activation(
                out=ee[:], in_=sc[:], func=AF.Exp,
                scale=0.125), reads=[bsc], writes=[bee])
            em.op("ve", lambda h: h.reduce_sum(
                out=linv[:].rearrange("p (a b) -> p a b", b=1),
                in_=r3(ee[:], 16), axis=AX.X), reads=[bee], writes=[blinv])
            em.op("ve", lambda h: h.reciprocal(out=linv[:], in_=linv[:]),
                  reads=[blinv], writes=[blinv])
            for hh in range(16):
                pi = hh % 2
                em.op("ge", (lambda h, hh=hh, pi=pi: h.tensor_tensor(
                    out=r3(prod2[pi][:], 64),
                    in0=ee[:, hh * 16:(hh + 1) * 16].rearrange(
                        "p (t a) -> p t a", a=1).to_broadcast([128, 16, 64]),
                    in1=r3(vt[:], 64), op=OP.mult)),
                    reads=[bee, bvt], writes=[bprod2[pi]])
                em.op("ve", (lambda h, hh=hh, pi=pi: h.reduce_sum(
                    out=ctx[:, hh * 64:(hh + 1) * 64].rearrange(
                        "p (d a) -> p d a", a=1),
                    in_=prod2[pi][:].rearrange("p (t d) -> p d t", d=64),
                    axis=AX.X)), reads=[bprod2[pi]], writes=[bctx])
            em.op("ge", lambda h: h.tensor_tensor(
                out=r3(ctx[:], 64), in0=r3(ctx[:], 64),
                in1=linv[:].rearrange("p (h a) -> p h a", a=1).to_broadcast(
                    [128, 16, 64]), op=OP.mult),
                reads=[blinv, bctx], writes=[bctx])
            for ec in range(8):
                em.op("te", (lambda h, ec=ec: h.transpose(
                    out=pst[:], in_=ctx[:, ec * 128:(ec + 1) * 128],
                    identity=self.c_ident[:])),
                    reads=[bctx, self.b_ident], writes=[bpst])
                em.op("ve", (lambda h, ec=ec: h.tensor_copy(
                    out=ctile[:], in_=pst[:])),
                    reads=[bpst], writes=[bctile])
                self.em.dma("ge:st1", self.h_ctx[ec * 128:(ec + 1) * 128, ssl],
                            ctile[:], reads=[bctile], writes=[hctx])
        self.em.barrier()

    def p3_scramble_quant(self):
        em, S = self.em, self.S
        NB = S // 16
        cnk = self.sb("p3_cnk", [128, S])
        cnk2 = self.sb("p3_cnk2", [128, S])
        yb = [self.sb(f"p3_yb{a}", [128, NB]) for a in range(8)]
        sqt = self.sb("p3_sqt", [128, NB])
        sq2 = self.sb("p3_sq2", [128, NB])
        acc = self.sb("p3_acc", [128, NB])
        tmp = self.sb("p3_tmp", [64, NB])
        rows = self.sb("p3_rows", [1, 8 * NB])
        RS, RM, RR, RD_, RP, RG = (slice(0, NB), slice(NB, 2 * NB),
                                   slice(2 * NB, 3 * NB), slice(3 * NB, 4 * NB),
                                   slice(4 * NB, 5 * NB), slice(5 * NB, 6 * NB))
        RC = slice(6 * NB, 7 * NB)
        RSC = slice(7 * NB, 8 * NB)
        brho = self.sb("p3_brho", [128, NB])
        t0 = self.sb("p3_t0", [128, NB])
        t1 = self.sb("p3_t1", [128, NB])
        ti = self.sb("p3_ti", [128, NB], mybir.dt.int32)
        yq = self.sb("p3_yq", [128, NB], BF16)
        pss = self.ps("p3_pss", [1, NB])
        bcnk, bcnk2 = B(), B()
        byb = [B() for _ in range(8)]
        bsqt, bsq2, bacc, btmp, brows, bbrho = B(), B(), B(), B(), B(), B()
        bt0, bt1, bti, byq, bps = B(), B(), B(), B(), B()
        hctx = self.hb("ctx")
        hrho, hg0 = self.hb("p3rho"), self.hb("p3g0")
        hdst = self.hb("p3dst")

        for hh in range(16):
            base = hh * NB
            if hh % 2 == 0:
                cc = hh // 2
                self.em.dma("sq:xld0", cnk[:],
                            self.h_ctx[cc * 128:(cc + 1) * 128, :],
                            reads=[hctx], writes=[bcnk])
                self.em.dma("ge:tree", cnk2[0:64, :], cnk[64:128, :],
                            reads=[bcnk], writes=[bcnk2])
                self.em.dma("ge:tree", cnk2[64:128, :], cnk[0:64, :],
                            reads=[bcnk], writes=[bcnk2])
            for a in range(8):
                for j in (2 * a, 2 * a + 1):
                    same = (j % 2) == (hh % 2)
                    src = cnk if same else cnk2
                    bsrc = bcnk if same else bcnk2
                    ob = (j % 2) * 64
                    em.op("ve" if a % 2 == 0 else "ge",
                          (lambda h, a=a, j=j, src=src, ob=ob: h.tensor_copy(
                              out=yb[a][ob:ob + 64, :],
                              in_=src[ob:ob + 64, j::16])),
                          reads=[bsrc], writes=[byb[a]])
            for a in range(8):
                em.op("ve", (lambda h, a=a: h.tensor_tensor(
                    out=sqt[:], in0=yb[a][:], in1=yb[a][:], op=OP.mult)),
                    reads=[byb[a]], writes=[bsqt])
                em.op("te", (lambda h, a=a: h.matmul(
                    out=pss[:], lhsT=self.c_ones1[:], rhs=sqt[:],
                    start=(a == 0), stop=(a == 7))),
                    reads=[bsqt, self.b_ones1], writes=[bps], notify=True)
                if a == 0:
                    em.op("ge", lambda h: h.tensor_tensor(
                        out=acc[:], in0=yb[0][:], in1=yb[0][:], op=OP.mult),
                        reads=[byb[0]], writes=[bacc])
                else:
                    em.op("ge", (lambda h, a=a: h.tensor_tensor(
                        out=sq2[:], in0=yb[a][:], in1=yb[a][:], op=OP.mult)),
                        reads=[byb[a]], writes=[bsq2])
                    em.op("ve", lambda h: h.tensor_tensor(
                        out=acc[:], in0=acc[:], in1=sq2[:], op=OP.max),
                        reads=[bsq2, bacc], writes=[bacc])
            n = 64
            while n >= 1:
                self.em.dma("ge:tree", tmp[0:n, :], acc[n:2 * n, :],
                            reads=[bacc], writes=[btmp])
                em.op("ve", (lambda h, n=n: h.tensor_tensor(
                    out=acc[0:n, :], in0=acc[0:n, :], in1=tmp[0:n, :],
                    op=OP.max)), reads=[btmp, bacc], writes=[bacc])
                n //= 2
            em.op("ve", lambda h: h.tensor_copy(out=rows[:, RS], in_=pss[:]),
                  reads=[bps], writes=[brows])
            em.op("ve", lambda h: h.tensor_scalar(
                out=rows[:, RM], in0=acc[0:1, :], scalar1=1e-30, scalar2=None,
                op0=OP.max), reads=[bacc], writes=[brows])
            self.nr_rsqrt(rows[:, RC], rows[:, RM], rows[:, RSC], rows[:, RD_],
                          [], brows)
            em.op("ve", lambda h: h.tensor_tensor(
                out=rows[:, RM], in0=rows[:, RM], in1=rows[:, RC], op=OP.mult),
                reads=[brows], writes=[brows])
            em.op("ve", lambda h: h.tensor_scalar(
                out=rows[:, RSC], in0=rows[:, RS], scalar1=1.0 / D, scalar2=1e-6,
                op0=OP.mult, op1=OP.add), reads=[brows], writes=[brows])
            self.nr_rsqrt(rows[:, RR], rows[:, RSC], rows[:, RD_], rows[:, RC],
                          [], brows)
            em.op("ve", lambda h: h.tensor_tensor(
                out=rows[:, RD_], in0=rows[:, RR], in1=rows[:, RM], op=OP.mult),
                reads=[brows], writes=[brows])
            em.op("ve", lambda h: h.tensor_scalar(
                out=rows[:, RD_], in0=rows[:, RD_], scalar1=EPS_Q, scalar2=None,
                op0=OP.max), reads=[brows], writes=[brows])
            em.op("ve", lambda h: h.reciprocal(out=rows[:, RP], in_=rows[:, RD_]),
                  reads=[brows], writes=[brows])
            em.op("ve", lambda h: h.scalar_tensor_tensor(
                out=rows[:, RP], in0=rows[:, RR], scalar=127.0, in1=rows[:, RP],
                op0=OP.mult, op1=OP.mult), reads=[brows], writes=[brows])
            em.op("ve", lambda h: h.tensor_scalar(
                out=rows[:, RG], in0=rows[:, RD_], scalar1=1.0 / 127.0,
                scalar2=None, op0=OP.mult), reads=[brows], writes=[brows])
            self.em.dma("ge:rows", self.row_ap(self.R_RHO2, base, NB),
                        rows[:, RP], reads=[brows], writes=[hrho])
            self.em.dma("ge:rows", self.row_ap(self.R_G02, base, NB),
                        rows[:, RG], reads=[brows], writes=[hg0])
            self.em.dma("se:bc", brho[:], self.bcast_in(self.R_RHO2, base, NB, 128),
                        reads=[hrho], writes=[bbrho])
            for a in range(8):
                em.op("ve", (lambda h, a=a: h.tensor_tensor(
                    out=t0[:], in0=yb[a][:], in1=brho[:], op=OP.mult)),
                    reads=[byb[a], bbrho], writes=[bt0])
                em.op("ve", lambda h: h.tensor_scalar(
                    out=ti[:], in0=t0[:], scalar1=0.5, scalar2=None,
                    op0=OP.add), reads=[bt0], writes=[bti])
                em.op("ve", lambda h: h.scalar_tensor_tensor(
                    out=t1[:], in0=t0[:], scalar=0.5, in1=ti[:],
                    op0=OP.add, op1=OP.subtract),
                    reads=[bti, bt0], writes=[bt1])
                em.op("ge", lambda h: h.tensor_scalar(
                    out=t1[:], in0=t1[:], scalar1=0.0, scalar2=None,
                    op0=OP.is_lt), reads=[bt1], writes=[bt1])
                em.op("ve", lambda h: h.tensor_tensor(
                    out=t0[:], in0=ti[:], in1=t1[:], op=OP.subtract),
                    reads=[bti, bt1], writes=[bt0])
                em.op("ve", lambda h: h.tensor_scalar(
                    out=yq[:], in0=t0[:], scalar1=-128.0, scalar2=127.0,
                    op0=OP.max, op1=OP.min), reads=[bt0], writes=[byq])
                self.em.dma("ge:st1",
                            self.h_ctxq[a * 128:(a + 1) * 128, base:base + NB],
                            yq[:], reads=[byq], writes=[hdst])
        self.em.barrier()

    def p4_oproj_ln(self):
        em, S, NTT = self.em, self.S, self.NTT
        cq = self.sb("p4_cq", [128, 8 * S], BF16)
        wot = self.sb("p4_wot", [128, 8 * D], BF16)
        bgo = self.sb("p4_bgo", [128, S])
        swo = self.sb("p4_swo", [128, 1])
        x2t = [self.sb(f"p4_x2t{c}", [128, TT]) for c in range(8)]
        yt = self.sb("p4_yt", [128, TT])
        sqb = self.sb("p4_sqb", [128, TT])
        rows = self.sb("p4_rows", [1, 7 * TT])
        RSU, RSS, RMU, RSG = (slice(0, TT), slice(TT, 2 * TT),
                              slice(2 * TT, 3 * TT), slice(3 * TT, 4 * TT))
        RS1, RS2, RS3 = (slice(4 * TT, 5 * TT), slice(5 * TT, 6 * TT),
                         slice(6 * TT, 7 * TT))
        bmu = self.sb("p4_bmu", [128, TT])
        bsg = self.sb("p4_bsg", [128, TT])
        x3t = self.sb("p4_x3t", [128, TT])
        psy = self.ps("p4_psy", [128, TT])
        psr = self.ps("p4_psr", [1, TT])
        psr2 = self.ps("p4_psr2", [1, TT])
        bcq, bwot, bbgo, bswo = B(), B(), B(), B()
        bx2 = [B() for _ in range(8)]
        byt, bsqb, brows, bbmu, bbsg, bx3 = B(), B(), B(), B(), B(), B()
        bpsy, bpsr, bpsr2 = B(), B(), B()
        hx3 = self.hb("x3")

        for c in range(8):
            self.em.dma(f"sq:xld{c % 4}", cq[:, c * S:(c + 1) * S],
                        self.h_ctxq[c * 128:(c + 1) * 128, :],
                        reads=[self.hb("p3dst")], writes=[bcq])
            self.em.dma("sq:wsl", wot[:, c * D:(c + 1) * D],
                        self.h_wt[3, c * 128:(c + 1) * 128, :],
                        reads=[self.hb("wt")], writes=[bwot])
        self.em.dma("se:bc", bgo[:], self.bcast_in(self.R_G02, 0, S, 128),
                    reads=[self.hb("p3g0")], writes=[bbgo])
        self.em.dma("se:bc", swo[:], self.bcast_in(self.R_SWQ, 3, 1, 128),
                    reads=[self.hb("sw")], writes=[bswo])

        for tt in range(NTT):
            sl = slice(tt * TT, (tt + 1) * TT)
            for oc in range(8):
                for c in range(8):
                    em.op("te", (lambda h, oc=oc, c=c, sl=sl: h.matmul(
                        out=psy[:],
                        lhsT=wot[:, c * D + oc * 128:c * D + (oc + 1) * 128],
                        rhs=cq[:, c * S + sl.start:c * S + sl.stop],
                        start=(c == 0), stop=(c == 7))),
                        reads=[bwot, bcq], writes=[bpsy], notify=(c == 7))
                em.op("ve", (lambda h, sl=sl: h.scalar_tensor_tensor(
                    out=yt[:], in0=psy[:], scalar=swo[:, 0:1], in1=bgo[:, sl],
                    op0=OP.mult, op1=OP.mult)),
                    reads=[bpsy, bswo, bbgo], writes=[byt])
                self.em.dma(f"sq:xld{oc % 4}", x2t[oc][:],
                            self.i_x[oc * 128:(oc + 1) * 128, sl],
                            writes=[bx2[oc]])
                em.op("ve", (lambda h, oc=oc: h.tensor_tensor(
                    out=x2t[oc][:], in0=x2t[oc][:], in1=yt[:], op=OP.add)),
                    reads=[byt, bx2[oc]], writes=[bx2[oc]])
                em.op("te", (lambda h, oc=oc: h.matmul(
                    out=psr[:], lhsT=self.c_ones1[:], rhs=x2t[oc][:],
                    start=(oc == 0), stop=(oc == 7))),
                    reads=[bx2[oc], self.b_ones1], writes=[bpsr],
                    notify=(oc == 7))
                em.op("ge", (lambda h, oc=oc: h.tensor_tensor(
                    out=sqb[:], in0=x2t[oc][:], in1=x2t[oc][:], op=OP.mult)),
                    reads=[bx2[oc]], writes=[bsqb])
                em.op("te", (lambda h, oc=oc: h.matmul(
                    out=psr2[:], lhsT=self.c_ones1[:], rhs=sqb[:],
                    start=(oc == 0), stop=(oc == 7))),
                    reads=[bsqb, self.b_ones1], writes=[bpsr2],
                    notify=True)
            em.op("ve", lambda h: h.tensor_scalar(
                out=rows[:, RMU], in0=psr[:], scalar1=1.0 / D, scalar2=None,
                op0=OP.mult), reads=[bpsr], writes=[brows])
            em.op("ve", lambda h: h.tensor_tensor(
                out=rows[:, RSU], in0=rows[:, RMU], in1=rows[:, RMU],
                op=OP.mult), reads=[brows], writes=[brows])
            em.op("ve", lambda h: h.scalar_tensor_tensor(
                out=rows[:, RSG], in0=psr2[:], scalar=1.0 / D, in1=rows[:, RSU],
                op0=OP.mult, op1=OP.subtract),
                reads=[bpsr2, brows], writes=[brows])
            em.op("ve", lambda h: h.tensor_scalar(
                out=rows[:, RS1], in0=rows[:, RSG], scalar1=1e-5, scalar2=None,
                op0=OP.add), reads=[brows], writes=[brows])
            self.nr_rsqrt(rows[:, RSG], rows[:, RS1], rows[:, RS2],
                          rows[:, RS3], [], brows)
            hmu, hsg = self.hb("mu"), self.hb("sig")
            self.em.dma("ge:rows", self.row_ap(self.R_MU, tt * TT, TT),
                        rows[:, RMU], reads=[brows], writes=[hmu])
            self.em.dma("ge:rows", self.row_ap(self.R_SIG, tt * TT, TT),
                        rows[:, RSG], reads=[brows], writes=[hsg])
            self.em.dma("se:bc", bmu[:], self.bcast_in(self.R_MU, tt * TT, TT, 128),
                        reads=[hmu], writes=[bbmu])
            self.em.dma("se:bc", bsg[:], self.bcast_in(self.R_SIG, tt * TT, TT, 128),
                        reads=[hsg], writes=[bbsg])
            for c in range(8):
                em.op("ve" if c % 2 == 0 else "ge",
                      (lambda h, c=c: h.tensor_tensor(
                          out=x3t[:], in0=x2t[c][:], in1=bmu[:],
                          op=OP.subtract)),
                      reads=[bx2[c], bbmu], writes=[bx3])
                em.op("ve" if c % 2 == 0 else "ge",
                      (lambda h: h.tensor_tensor(
                          out=x3t[:], in0=x3t[:], in1=bsg[:], op=OP.mult)),
                      reads=[bbsg, bx3], writes=[bx3])
                self.em.dma("ge:st1" if c % 2 else "sq:st0",
                            self.h_x3[c * 128:(c + 1) * 128, sl], x3t[:],
                            reads=[bx3], writes=[hx3])
        self.em.barrier()

    def load_shifted(self, stream, dst, dst_b, src, src_b, c, tt, shifted):
        sl = slice(tt * TT, (tt + 1) * TT)
        if not shifted or c < 4:
            self.em.dma(stream, dst[:], src[c * 128:(c + 1) * 128, sl],
                        reads=[src_b], writes=[dst_b])
        elif tt > 0:
            self.em.dma(stream, dst[:],
                        src[c * 128:(c + 1) * 128, sl.start - 1:sl.stop - 1],
                        reads=[src_b], writes=[dst_b])
        else:
            self.em.op("ve", (lambda h: h.memset(dst[:, 0:1], 0.0)),
                       writes=[dst_b])
            self.em.dma(stream, dst[:, 1:TT],
                        src[c * 128:(c + 1) * 128, 0:TT - 1],
                        reads=[src_b], writes=[dst_b])

    def p6_layer(self, l):
        em, S, NTT, NTC, NW = self.em, self.S, self.NTT, self.NTC, self.NW
        xin = self.h_x3 if l == 0 else self.h_x5
        bxin = self.hb("x3" if l == 0 else "x5")
        xout = self.h_x5 if l == 0 else self.h_x6
        bxout = self.hb("x5" if l == 0 else "x6")
        xmid, bxmid = self.h_x4, self.hb("x4")
        hqkv, bhqkv = self.h_qkv, self.hb("qkv")
        bho, bho2 = self.hb("ohi"), self.hb("olo")
        cols = self.sb(f"l{l}_cols", [128, 2 * MOB + 3 * 8 + 2])
        C_B1A, C_B1G = 0, MOB
        C_B2, C_LNG, C_LNB = 2 * MOB, 2 * MOB + 8, 2 * MOB + 16
        C_QS, C_KS = 2 * MOB + 24, 2 * MOB + 25
        bcols = B()
        for ap_, off, n in ((self.i_b1a[l], C_B1A, MOB),
                            (self.i_b1g[l], C_B1G, MOB),
                            (self.i_b2[l], C_B2, 8),
                            (self.i_lng[l], C_LNG, 8),
                            (self.i_lnb[l], C_LNB, 8),
                            (self.i_qs[l], C_QS, 1),
                            (self.i_ks[l], C_KS, 1)):
            self.em.dma("sq:wsl", cols[:, off:off + n], ap_, writes=[bcols])
        neglc = self.sb(f"l{l}_neglc", [128, 1])
        srw = self.sb(f"l{l}_srw", [1, 260])
        bneglc, bsrw = B(), B()
        _pst_stack = ExitStack()
        pst = _pst_stack.enter_context(
            self.nc.psum_tensor(f"l{l}_pst", [128, 128], F32))
        bpst = B()
        red2 = self.sb(f"l{l}_red2", [2, 1])
        bred2 = B()
        em.op("te", lambda h: h.transpose(
            out=pst[0:2, :], in_=cols[:, C_QS:C_KS + 1],
            identity=self.c_ident[:]),
            reads=[bcols, self.b_ident], writes=[bpst])
        em.op("ve", lambda h: h.reduce_max(
            out=red2[:], in_=pst[0:2, :], axis=AX.X,
            apply_absolute_value=True), reads=[bpst], writes=[bred2])
        em.op("te", lambda h: h.transpose(
            out=pst[0:1, 0:2], in_=red2[:], identity=self.c_ident[0:2, 0:2]),
            reads=[bred2, self.b_ident], writes=[bpst])
        em.op("ve", lambda h: h.tensor_copy(out=srw[:, 0:2],
                                            in_=pst[0:1, 0:2]),
              reads=[bpst], writes=[bsrw])
        em.op("ve", lambda h: h.tensor_tensor(
            out=srw[:, 258:259], in0=srw[:, 0:1], in1=srw[:, 1:2],
            op=OP.mult), reads=[bsrw], writes=[bsrw])
        em.op("ve", lambda h: h.tensor_scalar(
            out=srw[:, 258:259], in0=srw[:, 258:259], scalar1=-8.0,
            scalar2=None, op0=OP.mult), reads=[bsrw], writes=[bsrw])
        hcl = self.hb(f"l{l}c")
        self.em.dma("ge:rows", self.row_ap(self.R_C, 16 + l, 1),
                    srw[:, 258:259], reads=[bsrw], writes=[hcl])
        self.em.dma("se:bc", neglc[:], self.bcast_in(self.R_C, 16 + l, 1, 128),
                    reads=[hcl], writes=[bneglc])
        self.em.barrier()
        _pst_stack.close()

        with ExitStack() as st:
            old = self.pes; self.pes = st
            wtmp = [self.sb(f"l{l}_wt{i}", [128, HIDP]) for i in range(2)]
            whi = self.sb(f"l{l}_whi", [128, HIDP], BF16)
            wlo = self.sb(f"l{l}_wlo", [128, HIDP], BF16)
            bwt = [B(), B()]
            bwhi, bwlo = B(), B()
            hw1, hw2 = self.hb("w1s"), self.hb("w2s")
            for half, isrc in ((0, self.i_f_w1a), (1, self.i_f_w1g)):
                for mob in range(MOB):
                    ln = mob % 2
                    self.em.dma(f"sq:xld{ln}", wtmp[ln][:, 0:D],
                                isrc[l, :, mob, :], writes=[bwt[ln]])
                    em.op("ve", (lambda h, ln=ln: h.tensor_copy(
                        out=whi[:, 0:D], in_=wtmp[ln][:, 0:D])),
                        reads=[bwt[ln]], writes=[bwhi])
                    em.op("ge", (lambda h, ln=ln: h.tensor_tensor(
                        out=wlo[:, 0:D], in0=wtmp[ln][:, 0:D],
                        in1=whi[:, 0:D], op=OP.subtract)),
                        reads=[bwt[ln], bwhi], writes=[bwlo])
                    self.em.dma("ge:st1", self.h_w1hi[half, :, mob, :],
                                whi[:, 0:D], reads=[bwhi], writes=[hw1])
                    self.em.dma("ge:st2", self.h_w1lo[half, :, mob, :],
                                wlo[:, 0:D], reads=[bwlo], writes=[hw1])
            for oc in range(8):
                ln = oc % 2
                self.em.dma(f"sq:xld{ln}", wtmp[ln][:], self.i_f_w2[l, :, oc, :],
                            writes=[bwt[ln]])
                em.op("ve", (lambda h, ln=ln: h.tensor_copy(
                    out=whi[:], in_=wtmp[ln][:])),
                    reads=[bwt[ln]], writes=[bwhi])
                em.op("ge", (lambda h, ln=ln: h.tensor_tensor(
                    out=wlo[:], in0=wtmp[ln][:], in1=whi[:], op=OP.subtract)),
                    reads=[bwt[ln], bwhi], writes=[bwlo])
                self.em.dma("ge:st1", self.h_w2hi[:, oc, :], whi[:],
                            reads=[bwhi], writes=[hw2])
                self.em.dma("ge:st2", self.h_w2lo[:, oc, :], wlo[:],
                            reads=[bwlo], writes=[hw2])
            self.em.barrier()
            self.pes = old

        with ExitStack() as st:
            old = self.pes; self.pes = st
            wqh = self.sb(f"l{l}_wqh", [128, 8 * 1536], BF16)
            wql = self.sb(f"l{l}_wql", [128, 8 * 1536], BF16)
            wt1 = self.sb(f"l{l}_f1wt", [128, 1536])
            bwqh, bwql, bwt1 = B(), B(), B()
            for c in range(8):
                self.em.dma(f"sq:xld{c % 4}", wt1[:],
                            self.i_f_wqkv[l, :, c, :], writes=[bwt1])
                em.op("ve", (lambda h, c=c: h.tensor_copy(
                    out=wqh[:, c * 1536:(c + 1) * 1536], in_=wt1[:])),
                    reads=[bwt1], writes=[bwqh])
                em.op("ge", (lambda h, c=c: h.tensor_tensor(
                    out=wql[:, c * 1536:(c + 1) * 1536], in0=wt1[:],
                    in1=wqh[:, c * 1536:(c + 1) * 1536], op=OP.subtract)),
                    reads=[bwt1, bwqh], writes=[bwql])
            hft = [self.sb(f"l{l}_hf{c}", [128, TT]) for c in range(8)]
            hht = [self.sb(f"l{l}_hh{c}", [128, TT], BF16) for c in range(8)]
            hlt = [self.sb(f"l{l}_hl{c}", [128, TT], BF16) for c in range(8)]
            bhf = [B() for _ in range(8)]
            bhh = [B() for _ in range(8)]
            bhl = [B() for _ in range(8)]
            qevac = self.sb(f"l{l}_qev", [128, TT])
            bqev = B()
            psq = [self.ps(f"l{l}_psq{i}", [128, TT]) for i in range(2)]
            bpsq = [B(), B()]
            for tt in range(NTT):
                sl = slice(tt * TT, (tt + 1) * TT)
                for c in range(8):
                    self.load_shifted(f"sq:xld{c % 4}", hft[c], bhf[c],
                                      xin, bxin, c, tt, True)
                    em.op("ve", (lambda h, c=c: h.tensor_copy(
                        out=hht[c][:], in_=hft[c][:])),
                        reads=[bhf[c]], writes=[bhh[c]])
                    em.op("ge", (lambda h, c=c: h.tensor_tensor(
                        out=hlt[c][:], in0=hft[c][:], in1=hht[c][:],
                        op=OP.subtract)), reads=[bhf[c], bhh[c]],
                        writes=[bhl[c]])
                for ob in range(12):
                    pp = ob % 2
                    passes = ((wqh, bwqh, hht, bhh), (wqh, bwqh, hlt, bhl),
                              (wql, bwql, hht, bhh))
                    n = 0
                    for i, (ww, bww, hh_, bhh_) in enumerate(passes):
                        for c in range(8):
                            em.op("te", (lambda h, ww=ww, hh_=hh_, c=c, ob=ob,
                                         i=i, pp=pp, n=n: h.matmul(
                                out=psq[pp][:],
                                lhsT=ww[:, c * 1536 + ob * 128:
                                        c * 1536 + (ob + 1) * 128],
                                rhs=hh_[c][:],
                                start=(n == 0), stop=(n == 23))),
                                reads=[bww, bhh_[c]], writes=[bpsq[pp]],
                                notify=(n == 23))
                            n += 1
                    em.op("ve", (lambda h, pp=pp: h.tensor_copy(
                        out=qevac[:], in_=psq[pp][:])),
                        reads=[bpsq[pp]], writes=[bqev])
                    self.em.dma("ge:st1", hqkv[ob * 128:(ob + 1) * 128, sl],
                                qevac[:], reads=[bqev], writes=[bhqkv])
            self.em.barrier()
            self.pes = old

        with ExitStack() as st:
            old = self.pes; self.pes = st
            qraw = self.sb(f"l{l}_qraw", [128, S])
            rot = self.sb(f"l{l}_rot", [128, S])
            binv = self.sb(f"l{l}_binv", [128, S])
            tblb = self.sb(f"l{l}_tblb", [128, S])
            qkhi = [self.sb(f"l{l}_qkhi{i}", [128, S], BF16) for i in range(2)]
            qklo = [self.sb(f"l{l}_qklo{i}", [128, S], BF16) for i in range(2)]
            vth = [self.sb(f"l{l}_vth{i}", [128, NTC * 65], BF16)
                   for i in range(2)]
            vtl = [self.sb(f"l{l}_vtl{i}", [128, NTC * 65], BF16)
                   for i in range(2)]
            maskb = self.sb(f"l{l}_maskb", [128, 8 * 512], BF16)
            ssr = self.sb(f"l{l}_ssr", [2, 4 * TT])
            smb = [[self.sb(f"l{l}_smb{i}{j}", [128, TT]) for j in range(2)]
                   for i in range(2)]
            ef32 = [[self.sb(f"l{l}_ef{i}{j}", [128, TT]) for j in range(2)]
                    for i in range(2)]
            ehi = [[self.sb(f"l{l}_ehi{i}{j}", [128, TT], BF16)
                    for j in range(2)] for i in range(2)]
            elo = [[self.sb(f"l{l}_elo{i}{j}", [128, TT], BF16)
                    for j in range(2)] for i in range(2)]
            lrow = self.sb(f"l{l}_lrow", [1, TT])
            blv = self.sb(f"l{l}_blv", [64, TT])
            oslc = self.sb(f"l{l}_oslc", [64, TT])
            ohh = self.sb(f"l{l}_ohh", [64, TT], BF16)
            oll = self.sb(f"l{l}_oll", [64, TT], BF16)
            psS = [[self.ps(f"l{l}_psS{i}{j}", [128, TT]) for j in range(2)]
                   for i in range(2)]
            psC = [self.ps(f"l{l}_psC{i}", [65, TT]) for i in range(2)]
            psv = self.ps(f"l{l}_psv", [128, 128])
            psr = self.ps(f"l{l}_psr", [2, TT])
            bqraw, brot, bbinv, btbl = B(), B(), B(), B()
            bqkhi = [B(), B()]
            bqklo = [B(), B()]
            bvth = [B(), B()]
            bvtl = [B(), B()]
            bmaskb, bssr = B(), B()
            bsmb = [[B(), B()], [B(), B()]]
            bef = [[B(), B()], [B(), B()]]
            behi = [[B(), B()], [B(), B()]]
            belo = [[B(), B()], [B(), B()]]
            blrow, bblv, boslc, bohh, boll = B(), B(), B(), B(), B()
            bpsS = [[B(), B()], [B(), B()]]
            bpsC = [B(), B()]
            bpsv, bpsr = B(), B()
            self.em.dma("sq:wsl", maskb[:], self.i_maskb, writes=[bmaskb])
            for p in range(FH // 2):
                for qk in range(2):
                    base = qk * INNER + p * 128
                    self.em.dma("sq:xld0", qraw[:],
                                hqkv[base:base + 128, :],
                                reads=[bhqkv], writes=[bqraw])
                    em.op("ve", lambda h: h.tensor_tensor(
                        out=rot[:], in0=qraw[:], in1=qraw[:], op=OP.mult),
                        reads=[bqraw], writes=[brot])
                    for tt in range(NTT):
                        sl = slice(tt * TT, (tt + 1) * TT)
                        em.op("te", (lambda h, sl=sl: h.matmul(
                            out=psr[:], lhsT=self.c_ones2[:], rhs=rot[:, sl],
                            start=True, stop=True)),
                            reads=[brot, self.b_ones2], writes=[bpsr])
                        em.op("ve", lambda h: h.tensor_scalar(
                            out=ssr[:, TT:2 * TT], in0=psr[:], scalar1=1e-24,
                            scalar2=None, op0=OP.max),
                            reads=[bpsr], writes=[bssr])
                        self.nr_rsqrt(ssr[:, 0:TT], ssr[:, TT:2 * TT],
                                      ssr[:, 2 * TT:3 * TT],
                                      ssr[:, 3 * TT:4 * TT], [], bssr)
                        hl2 = self.hb(f"l{l}l2")
                        self.em.dma("ge:rows",
                                    self.row_ap(self.R_L, tt * TT, TT),
                                    ssr[0:1, 0:TT], reads=[bssr], writes=[hl2])
                        self.em.dma("ge:rows",
                                    self.row_ap(self.R_L + 1, tt * TT, TT),
                                    ssr[1:2, 0:TT], reads=[bssr], writes=[hl2])
                    for hd in range(2):
                        self.em.dma("se:bc", binv[hd * 64:(hd + 1) * 64, :],
                                    self.bcast_in(self.R_L + hd, 0, S, 64),
                                    reads=[self.hb(f"l{l}l2")], writes=[bbinv])
                    em.op("ve", (lambda h, qk=qk: h.scalar_tensor_tensor(
                        out=qraw[:], in0=qraw[:],
                        scalar=cols[:, C_QS + qk:C_QS + qk + 1], in1=binv[:],
                        op0=OP.mult, op1=OP.mult)),
                        reads=[bbinv, bcols, bqraw], writes=[bqraw])
                    for a, bq in ((0, 32), (32, 0), (64, 96), (96, 64)):
                        self.em.dma("ge:tree", rot[a:a + 32, :],
                                    qraw[bq:bq + 32, :],
                                    reads=[bqraw], writes=[brot])
                    self.em.dma("sq:xld1", tblb[:], self.i_tbl[2 * qk],
                                writes=[btbl])
                    em.op("ve", lambda h: h.tensor_tensor(
                        out=qraw[:], in0=qraw[:], in1=tblb[:], op=OP.mult),
                        reads=[btbl, bqraw], writes=[bqraw])
                    self.em.dma("sq:xld1", tblb[:], self.i_tbl[2 * qk + 1],
                                writes=[btbl])
                    em.op("ve", lambda h: h.tensor_tensor(
                        out=rot[:], in0=rot[:], in1=tblb[:], op=OP.mult),
                        reads=[btbl, brot], writes=[brot])
                    em.op("ve", lambda h: h.tensor_tensor(
                        out=qraw[:], in0=qraw[:], in1=rot[:], op=OP.add),
                        reads=[brot, bqraw], writes=[bqraw])
                    em.op("ve", (lambda h, qk=qk: h.tensor_copy(
                        out=qkhi[qk][:], in_=qraw[:])),
                        reads=[bqraw], writes=[bqkhi[qk]])
                    em.op("ge", (lambda h, qk=qk: h.tensor_tensor(
                        out=qklo[qk][:], in0=qraw[:], in1=qkhi[qk][:],
                        op=OP.subtract)),
                        reads=[bqraw, bqkhi[qk]], writes=[bqklo[qk]])
                self.em.dma("sq:xld0", rot[:],
                            hqkv[2 * INNER + p * 128:2 * INNER + (p + 1) * 128, :],
                            reads=[bhqkv], writes=[brot])
                for tc in range(NTC):
                    em.op("te", (lambda h, tc=tc: h.transpose(
                        out=psv[:], in_=rot[:, tc * 128:(tc + 1) * 128],
                        identity=self.c_ident[:])),
                        reads=[brot, self.b_ident], writes=[bpsv])
                    for hd in range(2):
                        em.op("ve", (lambda h, tc=tc, hd=hd: h.tensor_copy(
                            out=vth[hd][:, tc * 65:tc * 65 + 64],
                            in_=psv[:, hd * 64:hd * 64 + 64])),
                            reads=[bpsv], writes=[bvth[hd]])
                        em.op("ve", (lambda h, tc=tc, hd=hd: h.tensor_tensor(
                            out=vtl[hd][:, tc * 65:tc * 65 + 64],
                            in0=psv[:, hd * 64:hd * 64 + 64],
                            in1=vth[hd][:, tc * 65:tc * 65 + 64],
                            op=OP.subtract)),
                            reads=[bpsv, bvth[hd]], writes=[bvtl[hd]])
                for hd in range(2):
                    em.op("ve", (lambda h, hd=hd: h.memset(
                        vth[hd][:].rearrange("p (t j) -> p t j",
                                             j=65)[:, :, 64:65], 1.0)),
                        writes=[bvth[hd]])
                    em.op("ve", (lambda h, hd=hd: h.memset(
                        vtl[hd][:].rearrange("p (t j) -> p t j",
                                             j=65)[:, :, 64:65], 0.0)),
                        writes=[bvtl[hd]])
                for w in range(NW):
                    qsl = slice(w * FW, (w + 1) * FW)
                    kcs = [kc for kc in range(8) if w > 0 or kc >= 4]
                    for ik, kc in enumerate(kcs):
                        par = ik % 2
                        kt = (w - 1) * 4 + kc
                        ksl = slice(kt * 128, (kt + 1) * 128)
                        for hd in range(2):
                            hsl = slice(hd * 64, hd * 64 + 64)
                            tp = (hd * 64, 0)
                            passes = ((qkhi[1], bqkhi[1], qkhi[0], bqkhi[0]),
                                      (qkhi[1], bqkhi[1], qklo[0], bqklo[0]),
                                      (qklo[1], bqklo[1], qkhi[0], bqkhi[0]))
                            for i, (kk, bkk, qq, bqq) in enumerate(passes):
                                em.op("te", (lambda h, kk=kk, qq=qq, i=i,
                                             hd=hd, par=par, hsl=hsl,
                                             ksl=ksl, qsl=qsl, tp=tp:
                                             h.matmul(
                                    out=psS[hd][par][:], lhsT=kk[hsl, ksl],
                                    rhs=qq[hsl, qsl], start=(i == 0),
                                    stop=(i == 2), tile_position=tp)),
                                    reads=[bkk, bqq],
                                    writes=[bpsS[hd][par]], notify=(i == 2))
                            em.op("ve", (lambda h, hd=hd, par=par, kc=kc:
                                         h.scalar_tensor_tensor(
                                out=smb[hd][par][:], in0=psS[hd][par][:],
                                scalar=8.0,
                                in1=maskb[:, kc * 512:(kc + 1) * 512],
                                op0=OP.mult, op1=OP.add)),
                                reads=[bpsS[hd][par], bmaskb],
                                writes=[bsmb[hd][par]])
                            em.op("se", (lambda h, hd=hd, par=par: h.activation(
                                out=ef32[hd][par][:], in_=smb[hd][par][:],
                                func=AF.Exp,
                                bias=neglc[:, 0:1], scale=1.0)),
                                reads=[bsmb[hd][par], bneglc],
                                writes=[bef[hd][par]])
                            em.op("ve", (lambda h, hd=hd, par=par:
                                         h.tensor_copy(
                                out=ehi[hd][par][:], in_=ef32[hd][par][:])),
                                reads=[bef[hd][par]], writes=[behi[hd][par]])
                            em.op("ge", (lambda h, hd=hd, par=par:
                                         h.tensor_tensor(
                                out=elo[hd][par][:], in0=ef32[hd][par][:],
                                in1=ehi[hd][par][:], op=OP.subtract)),
                                reads=[bef[hd][par], behi[hd][par]],
                                writes=[belo[hd][par]])
                            vsl = slice(kt * 65, (kt + 1) * 65)
                            cpasses = ((vth[hd], bvth[hd], ehi[hd][par],
                                        behi[hd][par]),
                                       (vtl[hd], bvtl[hd], ehi[hd][par],
                                        behi[hd][par]),
                                       (vth[hd], bvth[hd], elo[hd][par],
                                        belo[hd][par]))
                            for i, (vv, bvv, ee, bee) in enumerate(cpasses):
                                em.op("te", (lambda h, vv=vv, ee=ee, i=i,
                                             hd=hd, vsl=vsl, ik=ik,
                                             last=len(kcs) - 1: h.matmul(
                                    out=psC[hd][:], lhsT=vv[:, vsl],
                                    rhs=ee[:],
                                    start=(ik == 0 and i == 0),
                                    stop=(ik == last and i == 2))),
                                    reads=[bvv, bee], writes=[bpsC[hd]],
                                    notify=(i == 2))
                    for hd in range(2):
                        orow = p * 128 + hd * 64
                        em.op("ve", (lambda h, hd=hd: h.reciprocal(
                            out=lrow[:], in_=psC[hd][64:65, :])),
                            reads=[bpsC[hd]], writes=[blrow])
                        hl3 = self.hb(f"l{l}lv")
                        self.em.dma("ge:rows",
                                    self.row_ap(self.R_L + 2, w * FW, FW),
                                    lrow[:], reads=[blrow], writes=[hl3])
                        self.em.dma("se:bc", blv[:],
                                    self.bcast_in(self.R_L + 2, w * FW, FW, 64),
                                    reads=[hl3], writes=[bblv])
                        em.op("ve", (lambda h, hd=hd: h.tensor_tensor(
                            out=oslc[:], in0=psC[hd][0:64, :], in1=blv[:],
                            op=OP.mult)),
                            reads=[bpsC[hd], bblv], writes=[boslc])
                        em.op("ve", lambda h: h.tensor_copy(
                            out=ohh[:], in_=oslc[:]),
                            reads=[boslc], writes=[bohh])
                        em.op("ge", lambda h: h.tensor_tensor(
                            out=oll[:], in0=oslc[:], in1=ohh[:],
                            op=OP.subtract),
                            reads=[boslc, bohh], writes=[boll])
                        self.em.dma("ge:st1",
                                    self.h_ohi[orow:orow + 64, qsl], ohh[:],
                                    reads=[bohh], writes=[bho])
                        self.em.dma("ge:st2",
                                    self.h_olo[orow:orow + 64, qsl], oll[:],
                                    reads=[boll], writes=[bho2])
            self.em.barrier()
            self.pes = old

        with ExitStack() as st:
            old = self.pes; self.pes = st
            woh = self.sb(f"l{l}_woh", [128, 4 * D], BF16)
            wol = self.sb(f"l{l}_wol", [128, 4 * D], BF16)
            wt3 = self.sb(f"l{l}_f3wt", [128, D])
            bwoh, bwol, bwt3 = B(), B(), B()
            for c in range(4):
                self.em.dma(f"sq:xld{c % 4}", wt3[:], self.i_f_wo[l, :, c, :],
                            writes=[bwt3])
                em.op("ve", (lambda h, c=c: h.tensor_copy(
                    out=woh[:, c * D:(c + 1) * D], in_=wt3[:])),
                    reads=[bwt3], writes=[bwoh])
                em.op("ge", (lambda h, c=c: h.tensor_tensor(
                    out=wol[:, c * D:(c + 1) * D], in0=wt3[:],
                    in1=woh[:, c * D:(c + 1) * D], op=OP.subtract)),
                    reads=[bwt3, bwoh], writes=[bwol])
            oht = [self.sb(f"l{l}_oht{c}", [128, TT], BF16) for c in range(4)]
            olt = [self.sb(f"l{l}_olt{c}", [128, TT], BF16) for c in range(4)]
            boht = [B() for _ in range(4)]
            bolt = [B() for _ in range(4)]
            xt = self.sb(f"l{l}_f3x", [128, TT])
            bxt = B()
            psm = [self.ps(f"l{l}_psm{i}", [128, TT]) for i in range(2)]
            bpsm = [B(), B()]
            for tt in range(NTT):
                sl = slice(tt * TT, (tt + 1) * TT)
                for c in range(4):
                    self.em.dma(f"sq:xld{c % 4}", oht[c][:],
                                self.h_ohi[c * 128:(c + 1) * 128, sl],
                                reads=[bho], writes=[boht[c]])
                    self.em.dma(f"sq:xld{c % 4}", olt[c][:],
                                self.h_olo[c * 128:(c + 1) * 128, sl],
                                reads=[bho2], writes=[bolt[c]])
                for oc in range(8):
                    pp = oc % 2
                    n = 0
                    passes = ((woh, bwoh, oht, boht), (woh, bwoh, olt, bolt),
                              (wol, bwol, oht, boht))
                    for i, (ww, bww, tt_, btt_) in enumerate(passes):
                        for c in range(4):
                            em.op("te", (lambda h, ww=ww, tt_=tt_, c=c,
                                         oc=oc, n=n, pp=pp: h.matmul(
                                out=psm[pp][:],
                                lhsT=ww[:, c * D + oc * 128:
                                        c * D + (oc + 1) * 128],
                                rhs=tt_[c][:],
                                start=(n == 0), stop=(n == 11))),
                                reads=[bww, btt_[c]], writes=[bpsm[pp]],
                                notify=(n == 11))
                            n += 1
                    self.em.dma(f"sq:xld{oc % 4}", xt[:],
                                xin[oc * 128:(oc + 1) * 128, sl],
                                reads=[bxin], writes=[bxt])
                    em.op("ve", (lambda h, pp=pp: h.tensor_tensor(
                        out=xt[:], in0=xt[:], in1=psm[pp][:], op=OP.add)),
                        reads=[bpsm[pp], bxt], writes=[bxt])
                    self.em.dma("ge:st1", xmid[oc * 128:(oc + 1) * 128, sl],
                                xt[:], reads=[bxt], writes=[bxmid])
            self.em.barrier()
            self.pes = old

        with ExitStack() as st:
            old = self.pes; self.pes = st
            ht = [self.sb(f"l{l}_s4h{i}", [128, TT]) for i in range(2)]
            sqt = [self.sb(f"l{l}_s4q{i}", [128, TT]) for i in range(2)]
            rows = self.sb(f"l{l}_s4r", [1, 5 * TT])
            RMU, RSG = slice(0, TT), slice(TT, 2 * TT)
            RS1, RS2, RS3 = (slice(2 * TT, 3 * TT), slice(3 * TT, 4 * TT),
                             slice(4 * TT, 5 * TT))
            psr1 = self.ps(f"l{l}_psr1", [1, TT])
            psr2 = self.ps(f"l{l}_psr2", [1, TT])
            bht = [B(), B()]
            bsqt = [B(), B()]
            brows = B()
            bpsr1, bpsr2 = B(), B()
            hmu, hsg = self.hb(f"l{l}mu"), self.hb(f"l{l}sg")
            for tt in range(NTT):
                for c in range(8):
                    ln = c % 2
                    self.load_shifted(f"sq:xld{ln}", ht[ln], bht[ln],
                                      xmid, bxmid, c, tt, True)
                    em.op("te", (lambda h, ln=ln, c=c: h.matmul(
                        out=psr1[:], lhsT=self.c_ones1[:], rhs=ht[ln][:],
                        start=(c == 0), stop=(c == 7))),
                        reads=[bht[ln], self.b_ones1], writes=[bpsr1],
                        notify=True)
                    em.op("ge", (lambda h, ln=ln: h.tensor_tensor(
                        out=sqt[ln][:], in0=ht[ln][:], in1=ht[ln][:],
                        op=OP.mult)), reads=[bht[ln]], writes=[bsqt[ln]])
                    em.op("te", (lambda h, ln=ln, c=c: h.matmul(
                        out=psr2[:], lhsT=self.c_ones1[:], rhs=sqt[ln][:],
                        start=(c == 0), stop=(c == 7))),
                        reads=[bsqt[ln], self.b_ones1], writes=[bpsr2],
                        notify=True)
                em.op("ve", lambda h: h.tensor_scalar(
                    out=rows[:, RMU], in0=psr1[:], scalar1=1.0 / D,
                    scalar2=None, op0=OP.mult), reads=[bpsr1], writes=[brows])
                em.op("ve", lambda h: h.tensor_tensor(
                    out=rows[:, RSG], in0=rows[:, RMU], in1=rows[:, RMU],
                    op=OP.mult), reads=[brows], writes=[brows])
                em.op("ve", lambda h: h.scalar_tensor_tensor(
                    out=rows[:, RSG], in0=psr2[:], scalar=1.0 / D,
                    in1=rows[:, RSG], op0=OP.mult, op1=OP.subtract),
                    reads=[bpsr2, brows], writes=[brows])
                em.op("ve", lambda h: h.tensor_scalar(
                    out=rows[:, RS1], in0=rows[:, RSG], scalar1=1e-5,
                    scalar2=None, op0=OP.add), reads=[brows], writes=[brows])
                self.nr_rsqrt(rows[:, RSG], rows[:, RS1], rows[:, RS2],
                              rows[:, RS3], [], brows)
                self.em.dma("ge:rows", self.row_ap(self.R_MU, tt * TT, TT),
                            rows[:, RMU], reads=[brows], writes=[hmu])
                self.em.dma("ge:rows", self.row_ap(self.R_SIG, tt * TT, TT),
                            rows[:, RSG], reads=[brows], writes=[hsg])
            self.em.barrier()
            self.pes = old

        with ExitStack() as st:
            old = self.pes; self.pes = st
            h2h = [self.sb(f"l{l}_h2h{c}", [128, TT], BF16) for c in range(8)]
            h2l = [self.sb(f"l{l}_h2l{c}", [128, TT], BF16) for c in range(8)]
            bh2h = [B() for _ in range(8)]
            bh2l = [B() for _ in range(8)]
            hf = self.sb(f"l{l}_h2f", [128, TT])
            bhf = B()
            bmu = self.sb(f"l{l}_bmu", [128, TT])
            bsg = self.sb(f"l{l}_bsg", [128, TT])
            bbmu, bbsg = B(), B()
            mhi = self.sb(f"l{l}_mhi", [128, MOB * TT], BF16)
            mlo = self.sb(f"l{l}_mlo", [128, MOB * TT], BF16)
            bmhi, bmlo = B(), B()
            w1b = [self.sb(f"l{l}_w1b{i}", [128, D], BF16) for i in range(8)]
            bw1b = [B() for _ in range(8)]
            w2b = [self.sb(f"l{l}_w2b{i}", [128, HIDP], BF16) for i in range(4)]
            bw2b = [B() for _ in range(4)]
            gel = self.sb(f"l{l}_gel", [128, TT])
            mf = self.sb(f"l{l}_mf", [128, TT])
            bgel, bmf = B(), B()
            x4t = self.sb(f"l{l}_x4t", [128, TT])
            bx4t = B()
            psA = self.ps(f"l{l}_psA", [128, TT])
            psG = self.ps(f"l{l}_psG", [128, TT])
            psY = [self.ps(f"l{l}_psY{i}", [128, TT]) for i in range(2)]
            bpsA, bpsG = B(), B()
            bpsY = [B(), B()]
            hmu, hsg = self.hb(f"l{l}mu"), self.hb(f"l{l}sg")
            for tt in range(NTT):
                sl = slice(tt * TT, (tt + 1) * TT)
                self.em.dma("se:bc", bmu[:],
                            self.bcast_in(self.R_MU, tt * TT, TT, 128),
                            reads=[hmu], writes=[bbmu])
                self.em.dma("se:bc", bsg[:],
                            self.bcast_in(self.R_SIG, tt * TT, TT, 128),
                            reads=[hsg], writes=[bbsg])
                for c in range(8):
                    self.load_shifted(f"sq:xld{c % 4}", hf, bhf,
                                      xmid, bxmid, c, tt, True)
                    em.op("ve", lambda h: h.tensor_tensor(
                        out=hf[:], in0=hf[:], in1=bmu[:], op=OP.subtract),
                        reads=[bbmu, bhf], writes=[bhf])
                    em.op("ve", lambda h: h.tensor_tensor(
                        out=hf[:], in0=hf[:], in1=bsg[:], op=OP.mult),
                        reads=[bbsg, bhf], writes=[bhf])
                    em.op("ve", (lambda h, c=c: h.tensor_scalar(
                        out=hf[:], in0=hf[:],
                        scalar1=cols[:, C_LNG + c:C_LNG + c + 1],
                        scalar2=cols[:, C_LNB + c:C_LNB + c + 1],
                        op0=OP.mult, op1=OP.add)),
                        reads=[bcols, bhf], writes=[bhf])
                    em.op("ve", (lambda h, c=c: h.tensor_copy(
                        out=h2h[c][:], in_=hf[:])), reads=[bhf],
                        writes=[bh2h[c]])
                    em.op("ge", (lambda h, c=c: h.tensor_tensor(
                        out=h2l[c][:], in0=hf[:], in1=h2h[c][:],
                        op=OP.subtract)), reads=[bhf, bh2h[c]],
                        writes=[bh2l[c]])
                for mob in range(MOB):
                    o4 = (mob % 2) * 4
                    self.em.dma("sq:xld0", w1b[o4 + 0][:],
                                self.h_w1hi[0, :, mob, :],
                                reads=[self.hb("w1s")], writes=[bw1b[o4 + 0]])
                    self.em.dma("sq:xld1", w1b[o4 + 1][:],
                                self.h_w1lo[0, :, mob, :],
                                reads=[self.hb("w1s")], writes=[bw1b[o4 + 1]])
                    self.em.dma("sq:xld2", w1b[o4 + 2][:],
                                self.h_w1hi[1, :, mob, :],
                                reads=[self.hb("w1s")], writes=[bw1b[o4 + 2]])
                    self.em.dma("sq:xld3", w1b[o4 + 3][:],
                                self.h_w1lo[1, :, mob, :],
                                reads=[self.hb("w1s")], writes=[bw1b[o4 + 3]])
                    for half, ps_, bps_ in ((0, psA, bpsA), (1, psG, bpsG)):
                        n = 0
                        passes = ((w1b[o4 + 2 * half], bw1b[o4 + 2 * half],
                                   h2h, bh2h),
                                  (w1b[o4 + 2 * half], bw1b[o4 + 2 * half],
                                   h2l, bh2l),
                                  (w1b[o4 + 2 * half + 1],
                                   bw1b[o4 + 2 * half + 1], h2h, bh2h))
                        for i, (ww, bww, hh_, bhh_) in enumerate(passes):
                            for c in range(8):
                                em.op("te", (lambda h, ww=ww, hh_=hh_, c=c,
                                             ps_=ps_, n=n: h.matmul(
                                    out=ps_[:],
                                    lhsT=ww[:, c * 128:(c + 1) * 128],
                                    rhs=hh_[c][:],
                                    start=(n == 0), stop=(n == 23))),
                                    reads=[bww, bhh_[c]], writes=[bps_],
                                    notify=(n == 23))
                                n += 1
                    em.op("se", (lambda h, mob=mob: h.activation(
                        out=gel[:], in_=psG[:], func=AF.Gelu,
                        bias=cols[:, C_B1G + mob:C_B1G + mob + 1],
                        scale=1.0)), reads=[bpsG, bcols], writes=[bgel])
                    em.op("ve", (lambda h, mob=mob: h.scalar_tensor_tensor(
                        out=mf[:], in0=psA[:],
                        scalar=cols[:, C_B1A + mob:C_B1A + mob + 1],
                        in1=gel[:], op0=OP.add, op1=OP.mult)),
                        reads=[bpsA, bgel, bcols], writes=[bmf])
                    em.op("ve", (lambda h, mob=mob: h.tensor_copy(
                        out=mhi[:, mob * TT:(mob + 1) * TT], in_=mf[:])),
                        reads=[bmf], writes=[bmhi])
                    em.op("ge", (lambda h, mob=mob: h.tensor_tensor(
                        out=mlo[:, mob * TT:(mob + 1) * TT], in0=mf[:],
                        in1=mhi[:, mob * TT:(mob + 1) * TT],
                        op=OP.subtract)), reads=[bmf, bmhi], writes=[bmlo])
                for oc in range(8):
                    pp = oc % 2
                    o2 = pp * 2
                    self.em.dma("sq:xld0", w2b[o2][:], self.h_w2hi[:, oc, :],
                                reads=[self.hb("w2s")], writes=[bw2b[o2]])
                    self.em.dma("sq:xld1", w2b[o2 + 1][:], self.h_w2lo[:, oc, :],
                                reads=[self.hb("w2s")], writes=[bw2b[o2 + 1]])
                    n = 0
                    passes = ((w2b[o2], bw2b[o2], mhi, bmhi),
                              (w2b[o2], bw2b[o2], mlo, bmlo),
                              (w2b[o2 + 1], bw2b[o2 + 1], mhi, bmhi))
                    for i, (ww, bww, mm_, bmm_) in enumerate(passes):
                        for mob in range(MOB):
                            em.op("te", (lambda h, ww=ww, mm_=mm_, mob=mob,
                                         n=n, pp=pp: h.matmul(
                                out=psY[pp][:],
                                lhsT=ww[:, mob * 128:(mob + 1) * 128],
                                rhs=mm_[:, mob * TT:(mob + 1) * TT],
                                start=(n == 0), stop=(n == 3 * MOB - 1))),
                                reads=[bww, bmm_], writes=[bpsY[pp]],
                                notify=(n == 3 * MOB - 1))
                            n += 1
                    self.em.dma(f"sq:xld{oc % 4}", x4t[:],
                                xmid[oc * 128:(oc + 1) * 128, sl],
                                reads=[bxmid], writes=[bx4t])
                    em.op("ve", (lambda h, oc=oc, pp=pp:
                                 h.scalar_tensor_tensor(
                        out=x4t[:], in0=psY[pp][:],
                        scalar=cols[:, C_B2 + oc:C_B2 + oc + 1], in1=x4t[:],
                        op0=OP.add, op1=OP.add)),
                        reads=[bpsY[pp], bcols, bx4t], writes=[bx4t])
                    self.em.dma("ge:st1", xout[oc * 128:(oc + 1) * 128, sl],
                                x4t[:], reads=[bx4t], writes=[bxout])
            self.em.barrier()
            self.pes = old

    def p7_final(self):
        em, S, NTT = self.em, self.S, self.NTT
        x6t = [self.sb(f"p7_x{c}", [128, TT]) for c in range(8)]
        sqb = self.sb("p7_sq", [128, TT])
        rows = self.sb("p7_rows", [1, 5 * TT])
        RMU, RSG = slice(0, TT), slice(TT, 2 * TT)
        RS1, RS2, RS3 = (slice(2 * TT, 3 * TT), slice(3 * TT, 4 * TT),
                         slice(4 * TT, 5 * TT))
        bmu = self.sb("p7_bmu", [128, TT])
        bsg = self.sb("p7_bsg", [128, TT])
        yt = self.sb("p7_yt", [128, TT])
        psr1 = self.ps("p7_psr1", [1, TT])
        psr2 = self.ps("p7_psr2", [1, TT])
        bx6 = [B() for _ in range(8)]
        bsqb, brows, bbmu, bbsg, byt = B(), B(), B(), B(), B()
        bpsr1, bpsr2 = B(), B()
        hy = self.hb("y")
        bx3h, bx6h = self.hb("x3"), self.hb("x6")
        for tt in range(NTT):
            sl = slice(tt * TT, (tt + 1) * TT)
            for c in range(8):
                self.em.dma(f"sq:xld{c % 4}", x6t[c][:],
                            self.h_x6[c * 128:(c + 1) * 128, sl],
                            reads=[bx6h], writes=[bx6[c]])
                self.em.dma(f"sq:xld{c % 4}", yt[:],
                            self.h_x3[c * 128:(c + 1) * 128, sl],
                            reads=[bx3h], writes=[byt])
                em.op("ve", (lambda h, c=c: h.tensor_tensor(
                    out=x6t[c][:], in0=x6t[c][:], in1=yt[:], op=OP.add)),
                    reads=[byt, bx6[c]], writes=[bx6[c]])
                em.op("te", (lambda h, c=c: h.matmul(
                    out=psr1[:], lhsT=self.c_ones1[:], rhs=x6t[c][:],
                    start=(c == 0), stop=(c == 7))),
                    reads=[bx6[c], self.b_ones1], writes=[bpsr1],
                    notify=(c == 7))
                em.op("ge", (lambda h, c=c: h.tensor_tensor(
                    out=sqb[:], in0=x6t[c][:], in1=x6t[c][:], op=OP.mult)),
                    reads=[bx6[c]], writes=[bsqb])
                em.op("te", (lambda h, c=c: h.matmul(
                    out=psr2[:], lhsT=self.c_ones1[:], rhs=sqb[:],
                    start=(c == 0), stop=(c == 7))),
                    reads=[bsqb, self.b_ones1], writes=[bpsr2],
                    notify=True)
            em.op("ve", lambda h: h.tensor_scalar(
                out=rows[:, RMU], in0=psr1[:], scalar1=1.0 / D, scalar2=None,
                op0=OP.mult), reads=[bpsr1], writes=[brows])
            em.op("ve", lambda h: h.tensor_tensor(
                out=rows[:, RSG], in0=rows[:, RMU], in1=rows[:, RMU],
                op=OP.mult), reads=[brows], writes=[brows])
            em.op("ve", lambda h: h.scalar_tensor_tensor(
                out=rows[:, RSG], in0=psr2[:], scalar=1.0 / D,
                in1=rows[:, RSG], op0=OP.mult, op1=OP.subtract),
                reads=[bpsr2, brows], writes=[brows])
            em.op("ve", lambda h: h.tensor_scalar(
                out=rows[:, RS1], in0=rows[:, RSG], scalar1=1e-5,
                scalar2=None, op0=OP.add), reads=[brows], writes=[brows])
            self.nr_rsqrt(rows[:, RSG], rows[:, RS1], rows[:, RS2],
                          rows[:, RS3], [], brows)
            hmu, hsg = self.hb("p7mu"), self.hb("p7sg")
            self.em.dma("ge:rows", self.row_ap(self.R_MU, tt * TT, TT),
                        rows[:, RMU], reads=[brows], writes=[hmu])
            self.em.dma("ge:rows", self.row_ap(self.R_SIG, tt * TT, TT),
                        rows[:, RSG], reads=[brows], writes=[hsg])
            self.em.dma("se:bc", bmu[:],
                        self.bcast_in(self.R_MU, tt * TT, TT, 128),
                        reads=[hmu], writes=[bbmu])
            self.em.dma("se:bc", bsg[:],
                        self.bcast_in(self.R_SIG, tt * TT, TT, 128),
                        reads=[hsg], writes=[bbsg])
            for c in range(8):
                em.op("ve" if c % 2 == 0 else "ge",
                      (lambda h, c=c: h.tensor_tensor(
                          out=x6t[c][:], in0=x6t[c][:], in1=bmu[:],
                          op=OP.subtract)),
                      reads=[bbmu, bx6[c]], writes=[bx6[c]])
                em.op("ve" if c % 2 == 0 else "ge",
                      (lambda h, c=c: h.tensor_tensor(
                          out=x6t[c][:], in0=x6t[c][:], in1=bsg[:],
                          op=OP.mult)),
                      reads=[bbsg, bx6[c]], writes=[bx6[c]])
                self.em.dma("ge:st1" if c % 2 else "sq:st0",
                            self.o_y[c * 128:(c + 1) * 128, sl], x6t[c][:],
                            reads=[bx6[c]], writes=[hy])
        self.em.barrier()

    def build(self):
        self.c_ones1 = self.sb("c_ones1", [128, 1])
        self.c_ones2 = self.sb("c_ones2", [128, 2])
        self.c_ident = self.sb("c_ident", [128, 128])
        self.b_ones1, self.b_ones2, self.b_ident = B(), B(), B()
        self.em.dma("sq:xld0", self.c_ones1[:], self.i_ones1, writes=[self.b_ones1])
        self.em.dma("sq:xld1", self.c_ones2[:], self.i_ones2, writes=[self.b_ones2])
        self.em.dma("sq:xld2", self.c_ident[:], self.i_ident, writes=[self.b_ident])
        def tapdump(name, region, rows):
            if name in self.debug_taps:
                ap = self.nc.dram_tensor("tap_" + name, [rows, self.S], F32,
                                         kind="ExternalOutput").ap()
                self.em.dma("ge:tap", ap, region[0:rows, :])
                self.em.barrier()
        phases = [self.p0_quant_x, self.p1_ternarize, ("stop_p1",),
                  self.p2_attention,
                  (lambda: tapdump("s_ctx", self.h_ctx, D)), ("stop_p2",),
                  self.p3_scramble_quant, self.p4_oproj_ln,
                  (lambda: tapdump("s_x3", self.h_x3, D)), ("stop_p4",),
                  (lambda: self.p6_layer(0)),
                  (lambda: tapdump("s_qkv0", self.h_qkv, 3 * INNER)),
                  (lambda: tapdump("s_x40", self.h_x4, D)),
                  (lambda: tapdump("s_x50", self.h_x5, D)), ("stop_f0",),
                  (lambda: self.p6_layer(1)),
                  (lambda: tapdump("s_x6", self.h_x6, D)),
                  self.p7_final]
        for ph in phases:
            if isinstance(ph, tuple):
                if ph[0] in self.debug_taps:
                    return
                continue
            self.pes = ExitStack()
            with self.pes:
                ph()
            self.pes = None


def build_program(S, debug_taps=()):
    b = Builder(S, debug_taps)
    with b.es:
        b.build()
        b.em.emit(b.nc, b.es)
    return b


def kernel(**inputs):
    x = np.asarray(inputs["L0_attn_x"])
    S = x.shape[1]
    prog = build_program(S)
    in_maps = _host_in_maps(inputs, prog)
    res = run_bass_kernel_spmd(prog.nc, in_maps, core_ids=list(range(8)))
    out = np.stack([np.ascontiguousarray(r["y_fm"]).T for r in res.results], axis=0)
    return out.astype(np.float32)


def _host_in_maps(inputs, prog):
    S = prog.S
    x = np.asarray(inputs["L0_attn_x"], dtype=np.float32)
    wq = np.asarray(inputs["L0_attn_wq"], dtype=np.float32)
    wk = np.asarray(inputs["L0_attn_wk"], dtype=np.float32)
    wv = np.asarray(inputs["L0_attn_wv"], dtype=np.float32)
    wo = np.asarray(inputs["L0_attn_wo"], dtype=np.float32)
    wqkv = np.asarray(inputs["L0_ffn_wqkv"], dtype=np.float32)
    qs = np.asarray(inputs["L0_ffn_q_scale"], dtype=np.float32)
    ks = np.asarray(inputs["L0_ffn_k_scale"], dtype=np.float32)
    fwo = np.asarray(inputs["L0_ffn_wo"], dtype=np.float32)
    lng = np.asarray(inputs["L0_ffn_ln_g"], dtype=np.float32)
    lnb = np.asarray(inputs["L0_ffn_ln_b"], dtype=np.float32)
    w1 = np.asarray(inputs["L0_ffn_w1"], dtype=np.float32)
    b1 = np.asarray(inputs["L0_ffn_b1"], dtype=np.float32)
    w2 = np.asarray(inputs["L0_ffn_w2"], dtype=np.float32)
    b2 = np.asarray(inputs["L0_ffn_b2"], dtype=np.float32)

    wq_t = np.ascontiguousarray(wq.T)
    wk_t = np.ascontiguousarray(wk.T)
    wv_t = np.ascontiguousarray(wv.T)
    wo_t = np.ascontiguousarray(wo.T)
    w1a = np.zeros((2, D, HIDP), np.float32)
    w1g = np.zeros((2, D, HIDP), np.float32)
    w1a[:, :, :HID] = w1[:, :, :HID]
    w1g[:, :, :HID] = w1[:, :, HID:]
    w2p = np.zeros((2, HIDP, D), np.float32)
    w2p[:, :HID, :] = w2
    wqkv_r = np.ascontiguousarray(
        wqkv.reshape(2, 8, 128, 3 * INNER).transpose(0, 2, 1, 3))
    fwo_r = np.ascontiguousarray(fwo.reshape(2, 4, 128, D).transpose(0, 2, 1, 3))
    w1a_r = np.ascontiguousarray(
        w1a.reshape(2, 8, 128, MOB, 128).transpose(0, 2, 3, 1, 4)
        .reshape(2, 128, MOB, D))
    w1g_r = np.ascontiguousarray(
        w1g.reshape(2, 8, 128, MOB, 128).transpose(0, 2, 3, 1, 4)
        .reshape(2, 128, MOB, D))
    w2_r = np.ascontiguousarray(
        w2p.reshape(2, MOB, 128, 8, 128).transpose(0, 2, 3, 1, 4)
        .reshape(2, 128, 8, HIDP))
    b1a = np.zeros((2, HIDP), np.float32)
    b1g = np.zeros((2, HIDP), np.float32)
    b1a[:, :HID] = b1[:, :HID]
    b1g[:, :HID] = b1[:, HID:]
    b1a_col = np.ascontiguousarray(b1a.reshape(2, MOB, 128).transpose(0, 2, 1))
    b1g_col = np.ascontiguousarray(b1g.reshape(2, MOB, 128).transpose(0, 2, 1))
    b2_col = np.ascontiguousarray(b2.reshape(2, 8, 128).transpose(0, 2, 1))
    lng_col = np.ascontiguousarray(lng.reshape(2, 8, 128).transpose(0, 2, 1))
    lnb_col = np.ascontiguousarray(lnb.reshape(2, 8, 128).transpose(0, 2, 1))
    qs_col = np.ascontiguousarray(np.tile(qs, (1, 2))[:, :, None])
    ks_col = np.ascontiguousarray(np.tile(ks, (1, 2))[:, :, None])
    tbl = np.stack(_rotary_tables(S), axis=0)
    mask = _local_mask()
    maskb = ((1.0 - np.asarray(mask, np.float32)) * -3.0e38).astype(
        ml_dtypes.bfloat16)
    ident = np.eye(128, dtype=np.float32)
    ones1 = np.ones((128, 1), np.float32)
    ones2 = np.zeros((128, 2), np.float32)
    ones2[:64, 0] = 1.0
    ones2[64:, 1] = 1.0

    shared = dict(
        wq_t=wq_t, wk_t=wk_t, wv_t=wv_t, wo_t=wo_t,
        f_wqkv=wqkv_r, f_wo=fwo_r, f_w1a=w1a_r, f_w1g=w1g_r, f_w2=w2_r,
        b1a_col=b1a_col, b1g_col=b1g_col, b2_col=b2_col,
        lng_col=lng_col, lnb_col=lnb_col, qs_col=qs_col, ks_col=ks_col,
        tbl=tbl, mask_loc=mask, maskb_loc=maskb, ident=ident, ones1=ones1,
        ones2=ones2,
    )
    maps = []
    for b_ in range(x.shape[0]):
        m = dict(shared)
        m["x_fm"] = np.ascontiguousarray(x[b_].T)
        maps.append(m)
    return maps


# version 11
# speedup vs baseline: 1.0852x; 1.0005x over previous
import math
import numpy as np
import ml_dtypes
import concourse.bass as bass
import concourse.mybir as mybir
from concourse.bass_utils import run_bass_kernel_spmd
from concourse.alu_op_type import AluOpType as OP
from contextlib import ExitStack

F32 = mybir.dt.float32
BF16 = mybir.dt.bfloat16
AF = mybir.ActivationFunctionType
AX = mybir.AxisListType

D = 1024
NH = 16
DH = 64
FH = 8
FW = 512
INNER = 512
HID = 2730
HIDP = 2816
MOB = HIDP // 128
TT = 512
EPS_Q = 1e-5
NEG_BIG = -3.4028234663852886e38



def _rotary_tables(S):
    d = np.arange(0, DH, 2, dtype=np.float64)
    inv_freq = 1.0 / 10000.0 ** (d / DH)
    t = np.arange(S, dtype=np.float64)
    freqs = t[:, None] * inv_freq[None, :]
    freqs = np.concatenate([freqs, freqs], axis=1)
    base = (d + 0.4 * DH) / (1.4 * DH)
    power = (t - S // 2) / (FW // 2)
    scale = base[None, :] ** power[:, None]
    scale = np.concatenate([scale, scale], axis=1)
    cos, sin = np.cos(freqs), np.sin(freqs)
    q1 = (cos * scale).T
    q2 = (sin * scale).T
    k1 = (cos / scale).T
    k2 = (sin / scale).T
    sgn = np.ones((DH, 1)); sgn[:32] = -1.0
    q2 = q2 * sgn
    k2 = k2 * sgn
    out = []
    for a in (q1, q2, k1, k2):
        out.append(np.tile(a, (2, 1)).astype(np.float32))
    return out


def _local_mask():
    kk = np.arange(1024)[:, None]
    qq = np.arange(512)[None, :]
    m = ((kk >= qq) & (kk <= qq + 512)).astype(np.float32)
    m = m.reshape(8, 128, 512).transpose(1, 0, 2).reshape(128, 8 * 512)
    return m.astype(ml_dtypes.bfloat16)



ENGS = ("te", "ve", "se", "ge", "sq")


class B:
    __slots__ = ("w", "r")

    def __init__(self):
        self.w = None
        self.r = []


class Em:

    def __init__(self):
        self.plan = {e: [] for e in ENGS}
        self.cnt = {e: 0 for e in ENGS}
        self.dcnt = {}
        self.dqueue = {}
        self.waited = {e: {} for e in ENGS}
        self.pend_r = {e: [] for e in ENGS}
        self.pend_w = {e: [] for e in ENGS}

    def _wait(self, eng, tok):
        s, v = tok
        if s == eng == "te":
            return
        if self.waited[eng].get(s, 0) >= v:
            return
        self.waited[eng][s] = v
        self.plan[eng].append(("w", s, v))

    def _deps(self, eng, reads, writes, extra):
        toks = []
        for b in reads:
            if b.w is not None:
                toks.append(b.w)
        for b in writes:
            toks.extend(b.r)
            if b.w is not None:
                toks.append(b.w)
        toks.extend(extra)
        for t in toks:
            self._wait(eng, t)

    def _post(self, tok, reads, writes):
        if tok is None:
            return
        for b in reads:
            b.r.append(tok)
        for b in writes:
            b.w = tok
            b.r = []

    def op(self, eng, fn, reads=(), writes=(), extra=(), notify=True):
        self._deps(eng, reads, writes, extra)
        tok = None
        if notify:
            self.cnt[eng] += 1
            tok = (eng, self.cnt[eng])
        self.plan[eng].append(("o", fn, tok))
        if notify:
            self._post(tok, list(reads) + self.pend_r[eng],
                       list(writes) + self.pend_w[eng])
            self.pend_r[eng] = []
            self.pend_w[eng] = []
        else:
            self.pend_r[eng].extend(reads)
            self.pend_w[eng].extend(writes)
        return tok

    def dma(self, stream, out, in_, reads=(), writes=(), extra=(), eng=None,
            transpose=False, **kw):
        if stream not in self.dqueue:
            self.dqueue[stream] = eng or {"sq": "sq", "se": "se", "ge": "ge",
                                          "ve": "ve"}[stream.split(":")[0]]
            self.dcnt[stream] = 0
        eng = self.dqueue[stream]
        if self.dcnt[stream] > 0:
            self._wait(eng, (stream, self.dcnt[stream]))
        self._deps(eng, reads, writes, extra)
        self.dcnt[stream] += 16
        tok = (stream, self.dcnt[stream])
        self.plan[eng].append(("d", out, in_, stream, transpose, kw, tok))
        self._post(tok, reads, writes)
        return tok

    def barrier(self):
        toks = [(e, self.cnt[e]) for e in ENGS if self.cnt[e] > 0]
        toks += [(s, c) for s, c in self.dcnt.items() if c > 0]
        for e in ENGS:
            for t in toks:
                if t[0] == e == "te":
                    continue
                s, v = t
                if self.waited[e].get(s, 0) >= v:
                    continue
                self.waited[e][s] = v
                self.plan[e].append(("w", s, v))

    def emit(self, nc, es):
        sems = {}
        for e in ENGS:
            sems[e] = es.enter_context(nc.semaphore(f"sem_{e}"))
        for s in self.dcnt:
            nm = "sem_" + s.replace(":", "_")
            sems[s] = es.enter_context(nc.semaphore(nm))
        n_sems = len(sems)
        if n_sems > 70:
            raise RuntimeError(f"too many semaphores: {n_sems}")
        block = es.enter_context(nc.Block())
        eng_attr = {"te": "tensor", "ve": "vector", "se": "scalar",
                    "ge": "gpsimd", "sq": "sync"}
        for e in ENGS:
            items = self.plan[e]

            def section(h, items=items, e=e):
                for it in items:
                    if it[0] == "w":
                        h.wait_ge(sems[it[1]], it[2])
                    elif it[0] == "o":
                        inst = it[1](h)
                        if it[2] is not None:
                            if inst is None:
                                raise RuntimeError("notify op returned no inst")
                            inst.then_inc(sems[e], 1)
                    else:
                        _, out, in_, s, tr, kw, tok = it
                        h.dma_start(out=out, in_=in_, transpose=tr,
                                    **kw).then_inc(sems[s], 16)
            getattr(block, eng_attr[e])(section)



def ceil_div(a, b):
    return (a + b - 1) // b


class Builder:

    def __init__(self, S, debug_taps=()):
        self.S = S
        self.NTT = S // TT
        self.NTC = S // 128
        self.NW = S // FW
        self.nc = bass.Bass("TRN2", target_bir_lowering=False)
        self.em = Em()
        self.es = ExitStack()
        self.debug_taps = set(debug_taps)
        self.taps = {}
        self._sbuf_bytes = 0
        self._declare_io()

    def _declare_io(self):
        nc, S = self.nc, self.S
        dt = nc.dram_tensor
        self.i_x = dt("x_fm", [D, S], F32, kind="ExternalInput").ap()
        for nm in ("wq_t", "wk_t", "wv_t", "wo_t"):
            setattr(self, "i_" + nm, dt(nm, [D, D], F32, kind="ExternalInput").ap())
        self.i_f_wqkv = dt("f_wqkv", [2, 128, 8, 3 * INNER], F32,
                           kind="ExternalInput").ap()
        self.i_f_wo = dt("f_wo", [2, 128, 4, D], F32, kind="ExternalInput").ap()
        self.i_f_w1a = dt("f_w1a", [2, 128, MOB, D], F32, kind="ExternalInput").ap()
        self.i_f_w1g = dt("f_w1g", [2, 128, MOB, D], F32, kind="ExternalInput").ap()
        self.i_f_w2 = dt("f_w2", [2, 128, 8, HIDP], F32, kind="ExternalInput").ap()
        self.i_b1a = dt("b1a_col", [2, 128, MOB], F32, kind="ExternalInput").ap()
        self.i_b1g = dt("b1g_col", [2, 128, MOB], F32, kind="ExternalInput").ap()
        self.i_b2 = dt("b2_col", [2, 128, 8], F32, kind="ExternalInput").ap()
        self.i_lng = dt("lng_col", [2, 128, 8], F32, kind="ExternalInput").ap()
        self.i_lnb = dt("lnb_col", [2, 128, 8], F32, kind="ExternalInput").ap()
        self.i_qs = dt("qs_col", [2, 128, 1], F32, kind="ExternalInput").ap()
        self.i_ks = dt("ks_col", [2, 128, 1], F32, kind="ExternalInput").ap()
        self.i_tbl = dt("tbl", [4, 128, S], F32, kind="ExternalInput").ap()
        self.i_mask = dt("mask_loc", [128, 8 * 512], BF16, kind="ExternalInput").ap()
        self.i_maskb = dt("maskb_loc", [128, 8 * 512], BF16, kind="ExternalInput").ap()
        self.i_ident = dt("ident", [128, 128], F32, kind="ExternalInput").ap()
        self.i_ones1 = dt("ones1", [128, 1], F32, kind="ExternalInput").ap()
        self.i_ones2 = dt("ones2", [128, 2], F32, kind="ExternalInput").ap()
        self.o_y = dt("y_fm", [D, S], F32, kind="ExternalOutput").ap()

        self.h_xq = dt("h_xq", [D, S], BF16, kind="Internal").ap()
        self.h_wt = dt("h_wt", [4, D, D], BF16, kind="Internal").ap()
        self.h_ctx = dt("h_ctx", [D, S], F32, kind="Internal").ap()
        self.h_ctxq = dt("h_ctxq", [D, S], BF16, kind="Internal").ap()
        self.h_x2 = dt("h_x2", [D, S], F32, kind="Internal").ap()
        self.h_x3 = dt("h_x3", [D, S], F32, kind="Internal").ap()
        self.h_x4 = dt("h_x4", [D, S], F32, kind="Internal").ap()
        self.h_x5 = dt("h_x5", [D, S], F32, kind="Internal").ap()
        self.h_x6 = dt("h_x6", [D, S], F32, kind="Internal").ap()
        self.h_qkv = dt("h_qkv", [3 * INNER, S], F32, kind="Internal").ap()
        self.h_ohi = dt("h_ohi", [INNER, S], BF16, kind="Internal").ap()
        self.h_olo = dt("h_olo", [INNER, S], BF16, kind="Internal").ap()
        self.h_w1hi = dt("h_w1hi", [2, 128, MOB, D], BF16, kind="Internal").ap()
        self.h_w1lo = dt("h_w1lo", [2, 128, MOB, D], BF16, kind="Internal").ap()
        self.h_w2hi = dt("h_w2hi", [128, 8, HIDP], BF16, kind="Internal").ap()
        self.h_w2lo = dt("h_w2lo", [128, 8, HIDP], BF16, kind="Internal").ap()
        self.h_rows = dt("h_rows", [64, S], F32, kind="Internal").ap()

    R_SS = 0
    R_MX = 1
    R_RHO = 2
    R_G0 = 3
    R_SWQ = 4
    R_L = 12
    R_MU = 6
    R_SIG = 7
    R_C = 8
    R_TMP = 9
    R_RHO2 = 10
    R_G02 = 11

    def tap(self, name, shape, dtype=F32):
        if name in self.debug_taps:
            ap = self.nc.dram_tensor("tap_" + name, shape, dtype,
                                     kind="ExternalOutput").ap()
            self.taps[name] = ap
            return ap
        return None

    def sb(self, name, shape, dtype=F32, scope=None):
        st = scope or getattr(self, "pes", None) or self.es
        return st.enter_context(self.nc.sbuf_tensor(name, shape, dtype))

    def ps(self, name, shape, dtype=F32):
        st = getattr(self, "pes", None) or self.es
        return st.enter_context(self.nc.psum_tensor(name, shape, dtype))


    def hb(self, name):
        if not hasattr(self, "_hb"):
            self._hb = {}
        if name not in self._hb:
            self._hb[name] = B()
        return self._hb[name]

    def row_ap(self, slot, lo, n):
        return self.h_rows[slot, lo:lo + n]

    def bcast_in(self, slot, lo, n, p):
        return self.h_rows[slot, lo:lo + n][None, :].to_broadcast([p, n])

    def nr_rsqrt(self, dst, src, sA, sB, b_src, b_all):
        em = self.em
        em.op("se", lambda h: h.activation(out=sA, in_=src, func=AF.Sqrt),
              reads=[b_all] + b_src, writes=[b_all])
        em.op("ve", lambda h: h.reciprocal(out=dst, in_=sA),
              reads=[b_all], writes=[b_all])
        em.op("ve", lambda h: h.tensor_tensor(out=sB, in0=src, in1=dst,
                                              op=OP.mult),
              reads=[b_all], writes=[b_all])
        em.op("ve", lambda h: h.tensor_tensor(out=sB, in0=sB, in1=dst,
                                              op=OP.mult),
              reads=[b_all], writes=[b_all])
        em.op("ve", lambda h: h.tensor_scalar(out=sB, in0=sB, scalar1=-0.5,
                                              scalar2=1.5, op0=OP.mult,
                                              op1=OP.add),
              reads=[b_all], writes=[b_all])
        em.op("ve", lambda h: h.tensor_tensor(out=dst, in0=dst, in1=sB,
                                              op=OP.mult),
              reads=[b_all], writes=[b_all])

    def p0_quant_x(self):
        self.quant_pipeline(self.i_x, self.h_xq, self.R_RHO,
                            self.R_G0, 'p0', None)

    def quant_pipeline(self, src, dst, r_rho, r_g0, tag, src_hb):
        em, S = self.em, self.S
        xt = [self.sb(f"{tag}_x{c}", [128, TT]) for c in range(8)]
        sqt = [self.sb(f"{tag}_sq{i}", [128, TT]) for i in range(2)]
        acc = self.sb("{}_acc".format(tag), [128, TT])
        tmp = self.sb("{}_tmp".format(tag), [64, TT])
        tmp2 = self.sb("{}_tmp2".format(tag), [128, TT])
        rows = self.sb("{}_rows".format(tag), [1, 8 * TT])
        RS, RM, RR, RD_, RC, RP, RG = (slice(0, TT), slice(TT, 2 * TT),
                                       slice(2 * TT, 3 * TT), slice(3 * TT, 4 * TT),
                                       slice(4 * TT, 5 * TT), slice(5 * TT, 6 * TT),
                                       slice(6 * TT, 7 * TT))
        RSC = slice(7 * TT, 8 * TT)
        brho = self.sb("{}_brho".format(tag), [128, TT])
        t0 = [self.sb(f"{tag}_t0{i}", [128, TT]) for i in range(2)]
        t1 = [self.sb(f"{tag}_t1{i}", [128, TT]) for i in range(2)]
        ti = [self.sb(f"{tag}_ti{i}", [128, TT], mybir.dt.int32)
              for i in range(2)]
        xqt = [self.sb(f"{tag}_xq{i}", [128, TT], BF16) for i in range(2)]
        pss = self.ps("{}_pss".format(tag), [1, TT])
        bx = [B() for _ in range(8)]
        bsq = [B(), B()]
        bacc, btmp, brows, bbrho = B(), B(), B(), B()
        btmp2 = B()
        bps = B()
        bt = [[B(), B()] for _ in range(2)]
        bti = [B(), B()]
        bxq = [B(), B()]
        tap_rho = self.tap(tag + "_rho", [self.NTT, TT])
        tap_mx = self.tap(tag + "_mx", [self.NTT, TT])

        for tt in range(self.NTT):
            sl = slice(tt * TT, (tt + 1) * TT)
            for c in range(8):
                self.em.dma(f"sq:xld{c % 4}", xt[c][:],
                            src[c * 128:(c + 1) * 128, sl],
                            writes=[bx[c]], reads=[src_hb] if src_hb else [])
            for c in range(8):
                s = sqt[c % 2]
                em.op("ve", (lambda h, c=c, s=s: h.tensor_tensor(
                    out=s[:], in0=xt[c][:], in1=xt[c][:], op=OP.mult)),
                    reads=[bx[c]], writes=[bsq[c % 2]])
                em.op("te", (lambda h, c=c, s=s: h.matmul(
                    out=pss[:], lhsT=self.c_ones1[:], rhs=s[:],
                    start=(c == 0), stop=(c == 7))),
                    reads=[bsq[c % 2], self.b_ones1], writes=[bps],
                    notify=True)
            em.op("ve", lambda h: h.tensor_tensor(
                out=acc[:], in0=xt[0][:], in1=xt[0][:], op=OP.mult),
                reads=[bx[0]], writes=[bacc])
            for c in range(1, 8):
                em.op("ve", (lambda h, c=c: h.tensor_tensor(
                    out=tmp2[:], in0=xt[c][:], in1=xt[c][:], op=OP.mult)),
                    reads=[bx[c]], writes=[btmp2])
                em.op("ve", lambda h: h.tensor_tensor(
                    out=acc[:], in0=acc[:], in1=tmp2[:], op=OP.max),
                    reads=[btmp2, bacc], writes=[bacc])
            n = 64
            while n >= 1:
                self.em.dma("ge:tree", tmp[0:n, :], acc[n:2 * n, :],
                            reads=[bacc], writes=[btmp])
                em.op("ve", (lambda h, n=n: h.tensor_tensor(
                    out=acc[0:n, :], in0=acc[0:n, :], in1=tmp[0:n, :],
                    op=OP.max)), reads=[btmp, bacc], writes=[bacc])
                n //= 2
            em.op("ve", lambda h: h.tensor_copy(out=rows[:, RS], in_=pss[:]),
                  reads=[bps], writes=[brows])
            em.op("ve", lambda h: h.tensor_scalar(
                out=rows[:, RM], in0=acc[0:1, :], scalar1=1e-30, scalar2=None,
                op0=OP.max), reads=[bacc], writes=[brows])
            self.nr_rsqrt(rows[:, RC], rows[:, RM], rows[:, RSC], rows[:, RD_],
                          [], brows)
            em.op("ve", lambda h: h.tensor_tensor(
                out=rows[:, RM], in0=rows[:, RM], in1=rows[:, RC], op=OP.mult),
                reads=[brows], writes=[brows])
            em.op("ve", lambda h: h.tensor_scalar(
                out=rows[:, RSC], in0=rows[:, RS], scalar1=1.0 / D, scalar2=1e-6,
                op0=OP.mult, op1=OP.add), reads=[brows], writes=[brows])
            self.nr_rsqrt(rows[:, RR], rows[:, RSC], rows[:, RD_], rows[:, RC],
                          [], brows)
            em.op("ve", lambda h: h.tensor_tensor(
                out=rows[:, RD_], in0=rows[:, RR], in1=rows[:, RM], op=OP.mult),
                reads=[brows], writes=[brows])
            em.op("ve", lambda h: h.tensor_scalar(
                out=rows[:, RD_], in0=rows[:, RD_], scalar1=EPS_Q, scalar2=None,
                op0=OP.max), reads=[brows], writes=[brows])
            em.op("ve", lambda h: h.reciprocal(out=rows[:, RC], in_=rows[:, RD_]),
                  reads=[brows], writes=[brows])
            em.op("ve", lambda h: h.scalar_tensor_tensor(
                out=rows[:, RP], in0=rows[:, RR], scalar=127.0, in1=rows[:, RC],
                op0=OP.mult, op1=OP.mult), reads=[brows], writes=[brows])
            em.op("ve", lambda h: h.tensor_scalar(
                out=rows[:, RG], in0=rows[:, RD_], scalar1=1.0 / 127.0,
                scalar2=None, op0=OP.mult), reads=[brows], writes=[brows])
            hrho, hg0 = self.hb(tag + "rho"), self.hb(tag + "g0")
            self.em.dma("ge:rows", self.row_ap(r_rho, tt * TT, TT),
                        rows[:, RP], reads=[brows], writes=[hrho])
            self.em.dma("ge:rows", self.row_ap(r_g0, tt * TT, TT),
                        rows[:, RG], reads=[brows], writes=[hg0])
            if tap_rho is not None:
                self.em.dma("ge:rows", tap_rho[tt, :][None, :],
                            rows[:, RP], reads=[brows])
                self.em.dma("ge:rows", tap_mx[tt, :][None, :],
                            rows[:, RM], reads=[brows])
            self.em.dma("se:bc", brho[:], self.bcast_in(r_rho, tt * TT, TT, 128),
                        reads=[hrho], writes=[bbrho])
            for c in range(8):
                ln = c % 2
                eng = "ve" if ln == 0 else "ge"
                em.op(eng, (lambda h, c=c, ln=ln: h.tensor_tensor(
                    out=t0[ln][:], in0=xt[c][:], in1=brho[:], op=OP.mult)),
                    reads=[bx[c], bbrho], writes=[bt[ln][0]])
                em.op("ve", (lambda h, ln=ln: h.tensor_scalar(
                    out=ti[ln][:], in0=t0[ln][:], scalar1=0.5, scalar2=None,
                    op0=OP.add)), reads=[bt[ln][0]], writes=[bti[ln]])
                em.op("ve", (lambda h, ln=ln: h.scalar_tensor_tensor(
                    out=t1[ln][:], in0=t0[ln][:], scalar=0.5, in1=ti[ln][:],
                    op0=OP.add, op1=OP.subtract)),
                    reads=[bti[ln], bt[ln][0]], writes=[bt[ln][1]])
                em.op(eng, (lambda h, ln=ln: h.tensor_scalar(
                    out=t1[ln][:], in0=t1[ln][:], scalar1=0.0, scalar2=None,
                    op0=OP.is_lt)), reads=[bt[ln][1]], writes=[bt[ln][1]])
                em.op("ve", (lambda h, ln=ln: h.tensor_tensor(
                    out=t0[ln][:], in0=ti[ln][:], in1=t1[ln][:],
                    op=OP.subtract)), reads=[bti[ln], bt[ln][1]],
                    writes=[bt[ln][0]])
                em.op(eng, (lambda h, ln=ln: h.tensor_scalar(
                    out=xqt[ln][:], in0=t0[ln][:], scalar1=-128.0, scalar2=127.0,
                    op0=OP.max, op1=OP.min)), reads=[bt[ln][0]], writes=[bxq[ln]])
                self.em.dma("ge:st1" if ln else "sq:st0",
                            dst[c * 128:(c + 1) * 128, sl], xqt[ln][:],
                            reads=[bxq[ln]], writes=[self.hb(tag + "dst")])
        self.em.barrier()

    def p1_ternarize(self):
        em = self.em
        wch = [self.sb(f"p1_w{i}", [128, D]) for i in range(2)]
        wsum = self.sb("p1_wsum", [128, 8])
        wtot = self.sb("p1_wtot", [128, 1])
        srow = self.sb("p1_srow", [1, 8])
        sbc = self.sb("p1_sbc", [128, 1])
        ya = self.sb("p1_ya", [128, D])
        ab = self.sb("p1_ab", [128, D])
        mk = self.sb("p1_mk", [128, D])
        sg = self.sb("p1_sg", [128, D])
        tq = self.sb("p1_tq", [128, D], BF16)
        pmm = self.ps("p1_pmm", [1, 1])
        mats = [self.i_wq_t, self.i_wk_t, self.i_wv_t, self.i_wo_t]
        bw = [B(), B()]
        bws, bwt, bsr, bsb = B(), B(), B(), B()
        bya, bab, bmk, bsg, btq = B(), B(), B(), B(), B()
        bp = B()
        for M in range(4):
            for c in range(8):
                ln = c % 2
                self.em.dma(f"sq:xld{ln}", wch[ln][:],
                            mats[M][c * 128:(c + 1) * 128, :], writes=[bw[ln]])
                em.op("ve", (lambda h, c=c, ln=ln: h.reduce_sum(
                    out=wsum[:, c:c + 1], in_=wch[ln][:], axis=AX.X,
                    apply_absolute_value=True)), reads=[bw[ln]], writes=[bws])
            em.op("ve", lambda h: h.reduce_sum(out=wtot[:], in_=wsum[:], axis=AX.X),
                  reads=[bws], writes=[bwt])
            em.op("te", lambda h: h.matmul(out=pmm[:], lhsT=self.c_ones1[:],
                                           rhs=wtot[:], start=True, stop=True),
                  reads=[bwt, self.b_ones1], writes=[bp])
            em.op("ve", lambda h: h.tensor_scalar(
                out=srow[:, 0:1], in0=pmm[:], scalar1=1.0 / (D * D), scalar2=EPS_Q,
                op0=OP.mult, op1=OP.max), reads=[bp], writes=[bsr])
            em.op("ve", lambda h: h.reciprocal(out=srow[:, 1:2], in_=srow[:, 0:1]),
                  reads=[bsr], writes=[bsr])
            hsw = self.hb("sw")
            self.em.dma("ge:rows", self.row_ap(self.R_SWQ, M, 1),
                        srow[:, 0:1], reads=[bsr], writes=[hsw])
            self.em.dma("ge:rows", self.row_ap(self.R_SWQ, 4 + M, 1),
                        srow[:, 1:2], reads=[bsr], writes=[hsw])
            self.em.dma("se:bc", sbc[:], self.bcast_in(self.R_SWQ, 4 + M, 1, 128),
                        reads=[hsw], writes=[bsb])
            for c in range(8):
                ln = c % 2
                self.em.dma(f"sq:xld{ln}", wch[ln][:],
                            mats[M][c * 128:(c + 1) * 128, :], writes=[bw[ln]])
                em.op("ve", (lambda h, ln=ln: h.tensor_scalar(
                    out=ya[:], in0=wch[ln][:], scalar1=sbc[:, 0:1], scalar2=None,
                    op0=OP.mult)), reads=[bw[ln], bsb], writes=[bya])
                em.op("ge", lambda h: h.tensor_tensor(
                    out=ab[:], in0=ya[:], in1=ya[:], op=OP.mult),
                    reads=[bya], writes=[bab])
                em.op("ge", lambda h: h.tensor_scalar(
                    out=mk[:], in0=ab[:], scalar1=0.25, scalar2=None,
                    op0=OP.is_gt), reads=[bab], writes=[bmk])
                em.op("se", lambda h: h.activation(out=sg[:], in_=ya[:],
                                                   func=AF.Sign),
                      reads=[bya], writes=[bsg])
                em.op("ve", lambda h: h.tensor_tensor(
                    out=tq[:], in0=sg[:], in1=mk[:], op=OP.mult),
                    reads=[bsg, bmk], writes=[btq])
                self.em.dma("ge:st1", self.h_wt[M, c * 128:(c + 1) * 128, :],
                            tq[:], reads=[btq], writes=[self.hb("wt")])
        self.em.barrier()

    def p2_attention(self):
        em, S, NTC = self.em, self.S, self.NTC
        xq = self.sb("p2_xq", [128, 8 * S], BF16)
        wt3 = [self.sb(f"p2_wt{m}", [128, 8 * D], BF16) for m in range(3)]
        bg0 = self.sb("p2_bg0", [128, S])
        g0col = self.sb("p2_g0col", [128, NTC])
        swc = self.sb("p2_swc", [128, 4])
        qt2 = [self.sb(f"p2_qt{i}", [128, D]) for i in range(2)]
        kt2 = [self.sb(f"p2_kt{i}", [128, D]) for i in range(2)]
        vt2 = [self.sb(f"p2_vt{i}", [128, D]) for i in range(2)]
        prod = [self.sb(f"p2_prod{i}", [128, D]) for i in range(4)]
        prod2 = [self.sb(f"p2_prod2{i}", [128, D]) for i in range(4)]
        sc2 = [self.sb(f"p2_sc{i}", [128, 256]) for i in range(2)]
        ee2 = [self.sb(f"p2_ee{i}", [128, 256]) for i in range(2)]
        linv2 = [self.sb(f"p2_linv{i}", [128, 16]) for i in range(2)]
        redq2 = [self.sb(f"p2_redq{i}", [128, 16]) for i in range(2)]
        ctx2 = [self.sb(f"p2_ctx{i}", [128, D]) for i in range(2)]
        ctile2 = [self.sb(f"p2_ctile{i}", [128, 128]) for i in range(2)]
        psq = [self.ps(f"p2_psq{i}", [128, TT]) for i in range(4)]
        pst = self.ps("p2_pst", [128, 128])
        bxq, bwt = B(), [B(), B(), B()]
        bbg0, bg0c, bswc = B(), B(), B()
        bqt2, bkt2, bvt2 = [B(), B()], [B(), B()], [B(), B()]
        bprod = [B(), B(), B(), B()]
        bprod2 = [B(), B(), B(), B()]
        bsc2, bee2, blinv2 = [B(), B()], [B(), B()], [B(), B()]
        bredq2 = [B(), B()]
        bctx2, bctile2 = [B(), B()], [B(), B()]
        bpsq = [B(), B(), B(), B()]
        bpst = B()
        hctx = self.hb("ctx")

        for c in range(8):
            self.em.dma(f"sq:xld{c % 4}", xq[:, c * S:(c + 1) * S],
                        self.h_xq[c * 128:(c + 1) * 128, :],
                        reads=[self.hb("p0dst")], writes=[bxq])
            for m in range(3):
                self.em.dma("sq:wsl", wt3[m][:, c * D:(c + 1) * D],
                            self.h_wt[m, c * 128:(c + 1) * 128, :],
                            reads=[self.hb("wt")], writes=[bwt[m]])
        self.em.dma("se:bc", bg0[:], self.bcast_in(self.R_G0, 0, S, 128),
                    reads=[self.hb("p0g0")], writes=[bbg0])
        for M in range(4):
            self.em.dma("se:bc", swc[:, M:M + 1],
                        self.bcast_in(self.R_SWQ, M, 1, 128),
                        reads=[self.hb("sw")], writes=[bswc])
        for tc in range(NTC):
            em.op("te", (lambda h, tc=tc: h.transpose(
                out=pst[:], in_=bg0[:, tc * 128:(tc + 1) * 128],
                identity=self.c_ident[:])),
                reads=[bbg0, self.b_ident], writes=[bpst])
            em.op("ve", (lambda h, tc=tc: h.tensor_copy(
                out=g0col[:, tc:tc + 1], in_=pst[:, 0:1])),
                reads=[bpst], writes=[bg0c])

        r3 = lambda ap, d: ap.rearrange("p (a b) -> p a b", b=d)

        def chunk_body(scnk, pc2, qt, kt, vt, bqt, bkt, bvt, sc, ee, linv,
                       bsc, bee, blinv, redq, bredq, ctx, bctx, ctile,
                       bctile):
            ssl = slice(scnk * 128, (scnk + 1) * 128)
            for m, (dst, bdst) in enumerate(((qt, bqt), (kt, bkt), (vt, bvt))):
                for et in range(2):
                    pp = pc2 * 2 + et
                    for c in range(8):
                        em.op("te", (lambda h, m=m, et=et, c=c, pp=pp,
                                     ssl=ssl: h.matmul(
                            out=psq[pp][:],
                            lhsT=xq[:, c * S + ssl.start:c * S + ssl.stop],
                            rhs=wt3[m][:, c * D + et * TT:c * D + (et + 1) * TT],
                            start=(c == 0), stop=(c == 7))),
                            reads=[bxq, bwt[m]], writes=[bpsq[pp]],
                            notify=(c == 7))
                    em.op("ve", (lambda h, m=m, et=et, pp=pp, scnk=scnk,
                                 dst=dst: h.tensor_scalar(
                        out=dst[:, et * TT:(et + 1) * TT], in0=psq[pp][:],
                        scalar1=g0col[:, scnk:scnk + 1],
                        scalar2=swc[:, m:m + 1], op0=OP.mult, op1=OP.mult)),
                        reads=[bpsq[pp], bg0c, bswc], writes=[bdst])
            for t in range(16):
                pi = pc2 * 2 + t % 2
                em.op("ge", (lambda h, t=t, pi=pi: h.tensor_tensor(
                    out=r3(prod[pi][:], 64), in0=r3(qt[:], 64),
                    in1=kt[:, t * 64:(t + 1) * 64].rearrange(
                        "p (a d) -> p a d", a=1).to_broadcast([128, 16, 64]),
                    op=OP.mult)), reads=[bqt, bkt], writes=[bprod[pi]])
                em.op("ve", (lambda h, t=t, pi=pi: h.reduce_sum(
                    out=sc[:, t::16].rearrange("p (a b) -> p a b", b=1),
                    in_=r3(prod[pi][:], 64), axis=AX.X)),
                    reads=[bprod[pi]], writes=[bsc])
            em.op("ve", lambda h: h.reduce_max(
                out=redq[:, 0:16].rearrange("p (a b) -> p a b", b=1),
                in_=r3(sc[:], 16), axis=AX.X), reads=[bsc], writes=[bredq])
            em.op("ve", lambda h: h.tensor_tensor(
                out=r3(sc[:], 16), in0=r3(sc[:], 16),
                in1=redq[:, 0:16].rearrange("p (h a) -> p h a",
                                            a=1).to_broadcast([128, 16, 16]),
                op=OP.subtract), reads=[bredq, bsc], writes=[bsc])
            em.op("se", lambda h: h.activation(
                out=ee[:], in_=sc[:], func=AF.Exp,
                scale=0.125), reads=[bsc], writes=[bee])
            em.op("ve", lambda h: h.reduce_sum(
                out=linv[:].rearrange("p (a b) -> p a b", b=1),
                in_=r3(ee[:], 16), axis=AX.X), reads=[bee], writes=[blinv])
            em.op("ve", lambda h: h.reciprocal(out=linv[:], in_=linv[:]),
                  reads=[blinv], writes=[blinv])
            for hh in range(16):
                pi = pc2 * 2 + hh % 2
                em.op("ge", (lambda h, hh=hh, pi=pi: h.tensor_tensor(
                    out=r3(prod2[pi][:], 64),
                    in0=ee[:, hh * 16:(hh + 1) * 16].rearrange(
                        "p (t a) -> p t a", a=1).to_broadcast([128, 16, 64]),
                    in1=r3(vt[:], 64), op=OP.mult)),
                    reads=[bee, bvt], writes=[bprod2[pi]])
                em.op("ve", (lambda h, hh=hh, pi=pi: h.reduce_sum(
                    out=ctx[:, hh * 64:(hh + 1) * 64].rearrange(
                        "p (d a) -> p d a", a=1),
                    in_=prod2[pi][:].rearrange("p (t d) -> p d t", d=64),
                    axis=AX.X)), reads=[bprod2[pi]], writes=[bctx])
            em.op("ge", lambda h: h.tensor_tensor(
                out=r3(ctx[:], 64), in0=r3(ctx[:], 64),
                in1=linv[:].rearrange("p (h a) -> p h a", a=1).to_broadcast(
                    [128, 16, 64]), op=OP.mult),
                reads=[blinv, bctx], writes=[bctx])
            for ec in range(8):
                em.op("te", (lambda h, ec=ec: h.transpose(
                    out=pst[:], in_=ctx[:, ec * 128:(ec + 1) * 128],
                    identity=self.c_ident[:])),
                    reads=[bctx, self.b_ident], writes=[bpst])
                em.op("ve", (lambda h, ec=ec: h.tensor_copy(
                    out=ctile[:], in_=pst[:])),
                    reads=[bpst], writes=[bctile])
                self.em.dma("ge:st1", self.h_ctx[ec * 128:(ec + 1) * 128, ssl],
                            ctile[:], reads=[bctile], writes=[hctx])

        for scnk in range(NTC):
            pc2 = scnk % 2
            chunk_body(scnk, pc2, qt2[pc2], kt2[pc2], vt2[pc2], bqt2[pc2],
                       bkt2[pc2], bvt2[pc2], sc2[pc2], ee2[pc2], linv2[pc2],
                       bsc2[pc2], bee2[pc2], blinv2[pc2], redq2[pc2],
                       bredq2[pc2], ctx2[pc2], bctx2[pc2], ctile2[pc2],
                       bctile2[pc2])
        self.em.barrier()

    def p3_scramble_quant(self):
        em, S = self.em, self.S
        NB = S // 16
        cnk = self.sb("p3_cnk", [128, S])
        cnk2 = self.sb("p3_cnk2", [128, S])
        yb = [self.sb(f"p3_yb{a}", [128, NB]) for a in range(8)]
        sqt = self.sb("p3_sqt", [128, NB])
        sq2 = self.sb("p3_sq2", [128, NB])
        acc = self.sb("p3_acc", [128, NB])
        tmp = self.sb("p3_tmp", [64, NB])
        rows = self.sb("p3_rows", [1, 8 * NB])
        RS, RM, RR, RD_, RP, RG = (slice(0, NB), slice(NB, 2 * NB),
                                   slice(2 * NB, 3 * NB), slice(3 * NB, 4 * NB),
                                   slice(4 * NB, 5 * NB), slice(5 * NB, 6 * NB))
        RC = slice(6 * NB, 7 * NB)
        RSC = slice(7 * NB, 8 * NB)
        brho = self.sb("p3_brho", [128, NB])
        t0 = self.sb("p3_t0", [128, NB])
        t1 = self.sb("p3_t1", [128, NB])
        ti = self.sb("p3_ti", [128, NB], mybir.dt.int32)
        yq = self.sb("p3_yq", [128, NB], BF16)
        pss = self.ps("p3_pss", [1, NB])
        bcnk, bcnk2 = B(), B()
        byb = [B() for _ in range(8)]
        bsqt, bsq2, bacc, btmp, brows, bbrho = B(), B(), B(), B(), B(), B()
        bt0, bt1, bti, byq, bps = B(), B(), B(), B(), B()
        hctx = self.hb("ctx")
        hrho, hg0 = self.hb("p3rho"), self.hb("p3g0")
        hdst = self.hb("p3dst")

        for hh in range(16):
            base = hh * NB
            if hh % 2 == 0:
                cc = hh // 2
                self.em.dma("sq:xld0", cnk[:],
                            self.h_ctx[cc * 128:(cc + 1) * 128, :],
                            reads=[hctx], writes=[bcnk])
                self.em.dma("ge:tree", cnk2[0:64, :], cnk[64:128, :],
                            reads=[bcnk], writes=[bcnk2])
                self.em.dma("ge:tree", cnk2[64:128, :], cnk[0:64, :],
                            reads=[bcnk], writes=[bcnk2])
            for a in range(8):
                for j in (2 * a, 2 * a + 1):
                    same = (j % 2) == (hh % 2)
                    src = cnk if same else cnk2
                    bsrc = bcnk if same else bcnk2
                    ob = (j % 2) * 64
                    em.op("ve" if a % 2 == 0 else "ge",
                          (lambda h, a=a, j=j, src=src, ob=ob: h.tensor_copy(
                              out=yb[a][ob:ob + 64, :],
                              in_=src[ob:ob + 64, j::16])),
                          reads=[bsrc], writes=[byb[a]])
            for a in range(8):
                em.op("ve", (lambda h, a=a: h.tensor_tensor(
                    out=sqt[:], in0=yb[a][:], in1=yb[a][:], op=OP.mult)),
                    reads=[byb[a]], writes=[bsqt])
                em.op("te", (lambda h, a=a: h.matmul(
                    out=pss[:], lhsT=self.c_ones1[:], rhs=sqt[:],
                    start=(a == 0), stop=(a == 7))),
                    reads=[bsqt, self.b_ones1], writes=[bps], notify=True)
                if a == 0:
                    em.op("ge", lambda h: h.tensor_tensor(
                        out=acc[:], in0=yb[0][:], in1=yb[0][:], op=OP.mult),
                        reads=[byb[0]], writes=[bacc])
                else:
                    em.op("ge", (lambda h, a=a: h.tensor_tensor(
                        out=sq2[:], in0=yb[a][:], in1=yb[a][:], op=OP.mult)),
                        reads=[byb[a]], writes=[bsq2])
                    em.op("ve", lambda h: h.tensor_tensor(
                        out=acc[:], in0=acc[:], in1=sq2[:], op=OP.max),
                        reads=[bsq2, bacc], writes=[bacc])
            n = 64
            while n >= 1:
                self.em.dma("ge:tree", tmp[0:n, :], acc[n:2 * n, :],
                            reads=[bacc], writes=[btmp])
                em.op("ve", (lambda h, n=n: h.tensor_tensor(
                    out=acc[0:n, :], in0=acc[0:n, :], in1=tmp[0:n, :],
                    op=OP.max)), reads=[btmp, bacc], writes=[bacc])
                n //= 2
            em.op("ve", lambda h: h.tensor_copy(out=rows[:, RS], in_=pss[:]),
                  reads=[bps], writes=[brows])
            em.op("ve", lambda h: h.tensor_scalar(
                out=rows[:, RM], in0=acc[0:1, :], scalar1=1e-30, scalar2=None,
                op0=OP.max), reads=[bacc], writes=[brows])
            self.nr_rsqrt(rows[:, RC], rows[:, RM], rows[:, RSC], rows[:, RD_],
                          [], brows)
            em.op("ve", lambda h: h.tensor_tensor(
                out=rows[:, RM], in0=rows[:, RM], in1=rows[:, RC], op=OP.mult),
                reads=[brows], writes=[brows])
            em.op("ve", lambda h: h.tensor_scalar(
                out=rows[:, RSC], in0=rows[:, RS], scalar1=1.0 / D, scalar2=1e-6,
                op0=OP.mult, op1=OP.add), reads=[brows], writes=[brows])
            self.nr_rsqrt(rows[:, RR], rows[:, RSC], rows[:, RD_], rows[:, RC],
                          [], brows)
            em.op("ve", lambda h: h.tensor_tensor(
                out=rows[:, RD_], in0=rows[:, RR], in1=rows[:, RM], op=OP.mult),
                reads=[brows], writes=[brows])
            em.op("ve", lambda h: h.tensor_scalar(
                out=rows[:, RD_], in0=rows[:, RD_], scalar1=EPS_Q, scalar2=None,
                op0=OP.max), reads=[brows], writes=[brows])
            em.op("ve", lambda h: h.reciprocal(out=rows[:, RP], in_=rows[:, RD_]),
                  reads=[brows], writes=[brows])
            em.op("ve", lambda h: h.scalar_tensor_tensor(
                out=rows[:, RP], in0=rows[:, RR], scalar=127.0, in1=rows[:, RP],
                op0=OP.mult, op1=OP.mult), reads=[brows], writes=[brows])
            em.op("ve", lambda h: h.tensor_scalar(
                out=rows[:, RG], in0=rows[:, RD_], scalar1=1.0 / 127.0,
                scalar2=None, op0=OP.mult), reads=[brows], writes=[brows])
            self.em.dma("ge:rows", self.row_ap(self.R_RHO2, base, NB),
                        rows[:, RP], reads=[brows], writes=[hrho])
            self.em.dma("ge:rows", self.row_ap(self.R_G02, base, NB),
                        rows[:, RG], reads=[brows], writes=[hg0])
            self.em.dma("se:bc", brho[:], self.bcast_in(self.R_RHO2, base, NB, 128),
                        reads=[hrho], writes=[bbrho])
            for a in range(8):
                em.op("ve", (lambda h, a=a: h.tensor_tensor(
                    out=t0[:], in0=yb[a][:], in1=brho[:], op=OP.mult)),
                    reads=[byb[a], bbrho], writes=[bt0])
                em.op("ve", lambda h: h.tensor_scalar(
                    out=ti[:], in0=t0[:], scalar1=0.5, scalar2=None,
                    op0=OP.add), reads=[bt0], writes=[bti])
                em.op("ve", lambda h: h.scalar_tensor_tensor(
                    out=t1[:], in0=t0[:], scalar=0.5, in1=ti[:],
                    op0=OP.add, op1=OP.subtract),
                    reads=[bti, bt0], writes=[bt1])
                em.op("ge", lambda h: h.tensor_scalar(
                    out=t1[:], in0=t1[:], scalar1=0.0, scalar2=None,
                    op0=OP.is_lt), reads=[bt1], writes=[bt1])
                em.op("ve", lambda h: h.tensor_tensor(
                    out=t0[:], in0=ti[:], in1=t1[:], op=OP.subtract),
                    reads=[bti, bt1], writes=[bt0])
                em.op("ve", lambda h: h.tensor_scalar(
                    out=yq[:], in0=t0[:], scalar1=-128.0, scalar2=127.0,
                    op0=OP.max, op1=OP.min), reads=[bt0], writes=[byq])
                self.em.dma("ge:st1",
                            self.h_ctxq[a * 128:(a + 1) * 128, base:base + NB],
                            yq[:], reads=[byq], writes=[hdst])
        self.em.barrier()

    def p4_oproj_ln(self):
        em, S, NTT = self.em, self.S, self.NTT
        cq = self.sb("p4_cq", [128, 8 * S], BF16)
        wot = self.sb("p4_wot", [128, 8 * D], BF16)
        bgo = self.sb("p4_bgo", [128, S])
        swo = self.sb("p4_swo", [128, 1])
        x2t = [self.sb(f"p4_x2t{c}", [128, TT]) for c in range(8)]
        yt = self.sb("p4_yt", [128, TT])
        sqb = self.sb("p4_sqb", [128, TT])
        rows = self.sb("p4_rows", [1, 7 * TT])
        RSU, RSS, RMU, RSG = (slice(0, TT), slice(TT, 2 * TT),
                              slice(2 * TT, 3 * TT), slice(3 * TT, 4 * TT))
        RS1, RS2, RS3 = (slice(4 * TT, 5 * TT), slice(5 * TT, 6 * TT),
                         slice(6 * TT, 7 * TT))
        bmu = self.sb("p4_bmu", [128, TT])
        bsg = self.sb("p4_bsg", [128, TT])
        x3t = self.sb("p4_x3t", [128, TT])
        psy = self.ps("p4_psy", [128, TT])
        psr = self.ps("p4_psr", [1, TT])
        psr2 = self.ps("p4_psr2", [1, TT])
        bcq, bwot, bbgo, bswo = B(), B(), B(), B()
        bx2 = [B() for _ in range(8)]
        byt, bsqb, brows, bbmu, bbsg, bx3 = B(), B(), B(), B(), B(), B()
        bpsy, bpsr, bpsr2 = B(), B(), B()
        hx3 = self.hb("x3")

        for c in range(8):
            self.em.dma(f"sq:xld{c % 4}", cq[:, c * S:(c + 1) * S],
                        self.h_ctxq[c * 128:(c + 1) * 128, :],
                        reads=[self.hb("p3dst")], writes=[bcq])
            self.em.dma("sq:wsl", wot[:, c * D:(c + 1) * D],
                        self.h_wt[3, c * 128:(c + 1) * 128, :],
                        reads=[self.hb("wt")], writes=[bwot])
        self.em.dma("se:bc", bgo[:], self.bcast_in(self.R_G02, 0, S, 128),
                    reads=[self.hb("p3g0")], writes=[bbgo])
        self.em.dma("se:bc", swo[:], self.bcast_in(self.R_SWQ, 3, 1, 128),
                    reads=[self.hb("sw")], writes=[bswo])

        for tt in range(NTT):
            sl = slice(tt * TT, (tt + 1) * TT)
            for oc in range(8):
                for c in range(8):
                    em.op("te", (lambda h, oc=oc, c=c, sl=sl: h.matmul(
                        out=psy[:],
                        lhsT=wot[:, c * D + oc * 128:c * D + (oc + 1) * 128],
                        rhs=cq[:, c * S + sl.start:c * S + sl.stop],
                        start=(c == 0), stop=(c == 7))),
                        reads=[bwot, bcq], writes=[bpsy], notify=(c == 7))
                em.op("ve", (lambda h, sl=sl: h.scalar_tensor_tensor(
                    out=yt[:], in0=psy[:], scalar=swo[:, 0:1], in1=bgo[:, sl],
                    op0=OP.mult, op1=OP.mult)),
                    reads=[bpsy, bswo, bbgo], writes=[byt])
                self.em.dma(f"sq:xld{oc % 4}", x2t[oc][:],
                            self.i_x[oc * 128:(oc + 1) * 128, sl],
                            writes=[bx2[oc]])
                em.op("ve", (lambda h, oc=oc: h.tensor_tensor(
                    out=x2t[oc][:], in0=x2t[oc][:], in1=yt[:], op=OP.add)),
                    reads=[byt, bx2[oc]], writes=[bx2[oc]])
                em.op("te", (lambda h, oc=oc: h.matmul(
                    out=psr[:], lhsT=self.c_ones1[:], rhs=x2t[oc][:],
                    start=(oc == 0), stop=(oc == 7))),
                    reads=[bx2[oc], self.b_ones1], writes=[bpsr],
                    notify=(oc == 7))
                em.op("ge", (lambda h, oc=oc: h.tensor_tensor(
                    out=sqb[:], in0=x2t[oc][:], in1=x2t[oc][:], op=OP.mult)),
                    reads=[bx2[oc]], writes=[bsqb])
                em.op("te", (lambda h, oc=oc: h.matmul(
                    out=psr2[:], lhsT=self.c_ones1[:], rhs=sqb[:],
                    start=(oc == 0), stop=(oc == 7))),
                    reads=[bsqb, self.b_ones1], writes=[bpsr2],
                    notify=True)
            em.op("ve", lambda h: h.tensor_scalar(
                out=rows[:, RMU], in0=psr[:], scalar1=1.0 / D, scalar2=None,
                op0=OP.mult), reads=[bpsr], writes=[brows])
            em.op("ve", lambda h: h.tensor_tensor(
                out=rows[:, RSU], in0=rows[:, RMU], in1=rows[:, RMU],
                op=OP.mult), reads=[brows], writes=[brows])
            em.op("ve", lambda h: h.scalar_tensor_tensor(
                out=rows[:, RSG], in0=psr2[:], scalar=1.0 / D, in1=rows[:, RSU],
                op0=OP.mult, op1=OP.subtract),
                reads=[bpsr2, brows], writes=[brows])
            em.op("ve", lambda h: h.tensor_scalar(
                out=rows[:, RS1], in0=rows[:, RSG], scalar1=1e-5, scalar2=None,
                op0=OP.add), reads=[brows], writes=[brows])
            self.nr_rsqrt(rows[:, RSG], rows[:, RS1], rows[:, RS2],
                          rows[:, RS3], [], brows)
            hmu, hsg = self.hb("mu"), self.hb("sig")
            self.em.dma("ge:rows", self.row_ap(self.R_MU, tt * TT, TT),
                        rows[:, RMU], reads=[brows], writes=[hmu])
            self.em.dma("ge:rows", self.row_ap(self.R_SIG, tt * TT, TT),
                        rows[:, RSG], reads=[brows], writes=[hsg])
            self.em.dma("se:bc", bmu[:], self.bcast_in(self.R_MU, tt * TT, TT, 128),
                        reads=[hmu], writes=[bbmu])
            self.em.dma("se:bc", bsg[:], self.bcast_in(self.R_SIG, tt * TT, TT, 128),
                        reads=[hsg], writes=[bbsg])
            for c in range(8):
                em.op("ve" if c % 2 == 0 else "ge",
                      (lambda h, c=c: h.tensor_tensor(
                          out=x3t[:], in0=x2t[c][:], in1=bmu[:],
                          op=OP.subtract)),
                      reads=[bx2[c], bbmu], writes=[bx3])
                em.op("ve" if c % 2 == 0 else "ge",
                      (lambda h: h.tensor_tensor(
                          out=x3t[:], in0=x3t[:], in1=bsg[:], op=OP.mult)),
                      reads=[bbsg, bx3], writes=[bx3])
                self.em.dma("ge:st1" if c % 2 else "sq:st0",
                            self.h_x3[c * 128:(c + 1) * 128, sl], x3t[:],
                            reads=[bx3], writes=[hx3])
        self.em.barrier()

    def load_shifted(self, stream, dst, dst_b, src, src_b, c, tt, shifted):
        sl = slice(tt * TT, (tt + 1) * TT)
        if not shifted or c < 4:
            self.em.dma(stream, dst[:], src[c * 128:(c + 1) * 128, sl],
                        reads=[src_b], writes=[dst_b])
        elif tt > 0:
            self.em.dma(stream, dst[:],
                        src[c * 128:(c + 1) * 128, sl.start - 1:sl.stop - 1],
                        reads=[src_b], writes=[dst_b])
        else:
            self.em.op("ve", (lambda h: h.memset(dst[:, 0:1], 0.0)),
                       writes=[dst_b])
            self.em.dma(stream, dst[:, 1:TT],
                        src[c * 128:(c + 1) * 128, 0:TT - 1],
                        reads=[src_b], writes=[dst_b])

    def p6_layer(self, l):
        em, S, NTT, NTC, NW = self.em, self.S, self.NTT, self.NTC, self.NW
        xin = self.h_x3 if l == 0 else self.h_x5
        bxin = self.hb("x3" if l == 0 else "x5")
        xout = self.h_x5 if l == 0 else self.h_x6
        bxout = self.hb("x5" if l == 0 else "x6")
        xmid, bxmid = self.h_x4, self.hb("x4")
        hqkv, bhqkv = self.h_qkv, self.hb("qkv")
        bho, bho2 = self.hb("ohi"), self.hb("olo")
        cols = self.sb(f"l{l}_cols", [128, 2 * MOB + 3 * 8 + 2])
        C_B1A, C_B1G = 0, MOB
        C_B2, C_LNG, C_LNB = 2 * MOB, 2 * MOB + 8, 2 * MOB + 16
        C_QS, C_KS = 2 * MOB + 24, 2 * MOB + 25
        bcols = B()
        for ap_, off, n in ((self.i_b1a[l], C_B1A, MOB),
                            (self.i_b1g[l], C_B1G, MOB),
                            (self.i_b2[l], C_B2, 8),
                            (self.i_lng[l], C_LNG, 8),
                            (self.i_lnb[l], C_LNB, 8),
                            (self.i_qs[l], C_QS, 1),
                            (self.i_ks[l], C_KS, 1)):
            self.em.dma("sq:wsl", cols[:, off:off + n], ap_, writes=[bcols])
        neglc = self.sb(f"l{l}_neglc", [128, 1])
        srw = self.sb(f"l{l}_srw", [1, 260])
        bneglc, bsrw = B(), B()
        _pst_stack = ExitStack()
        pst = _pst_stack.enter_context(
            self.nc.psum_tensor(f"l{l}_pst", [128, 128], F32))
        bpst = B()
        red2 = self.sb(f"l{l}_red2", [2, 1])
        bred2 = B()
        em.op("te", lambda h: h.transpose(
            out=pst[0:2, :], in_=cols[:, C_QS:C_KS + 1],
            identity=self.c_ident[:]),
            reads=[bcols, self.b_ident], writes=[bpst])
        em.op("ve", lambda h: h.reduce_max(
            out=red2[:], in_=pst[0:2, :], axis=AX.X,
            apply_absolute_value=True), reads=[bpst], writes=[bred2])
        em.op("te", lambda h: h.transpose(
            out=pst[0:1, 0:2], in_=red2[:], identity=self.c_ident[0:2, 0:2]),
            reads=[bred2, self.b_ident], writes=[bpst])
        em.op("ve", lambda h: h.tensor_copy(out=srw[:, 0:2],
                                            in_=pst[0:1, 0:2]),
              reads=[bpst], writes=[bsrw])
        em.op("ve", lambda h: h.tensor_tensor(
            out=srw[:, 258:259], in0=srw[:, 0:1], in1=srw[:, 1:2],
            op=OP.mult), reads=[bsrw], writes=[bsrw])
        em.op("ve", lambda h: h.tensor_scalar(
            out=srw[:, 258:259], in0=srw[:, 258:259], scalar1=-8.0,
            scalar2=None, op0=OP.mult), reads=[bsrw], writes=[bsrw])
        hcl = self.hb(f"l{l}c")
        self.em.dma("ge:rows", self.row_ap(self.R_C, 16 + l, 1),
                    srw[:, 258:259], reads=[bsrw], writes=[hcl])
        self.em.dma("se:bc", neglc[:], self.bcast_in(self.R_C, 16 + l, 1, 128),
                    reads=[hcl], writes=[bneglc])
        self.em.barrier()
        _pst_stack.close()

        with ExitStack() as st:
            old = self.pes; self.pes = st
            wtmp = [self.sb(f"l{l}_wt{i}", [128, HIDP]) for i in range(2)]
            whi = self.sb(f"l{l}_whi", [128, HIDP], BF16)
            wlo = self.sb(f"l{l}_wlo", [128, HIDP], BF16)
            bwt = [B(), B()]
            bwhi, bwlo = B(), B()
            hw1, hw2 = self.hb("w1s"), self.hb("w2s")
            for half, isrc in ((0, self.i_f_w1a), (1, self.i_f_w1g)):
                for mob in range(MOB):
                    ln = mob % 2
                    self.em.dma(f"sq:xld{ln}", wtmp[ln][:, 0:D],
                                isrc[l, :, mob, :], writes=[bwt[ln]])
                    em.op("ve", (lambda h, ln=ln: h.tensor_copy(
                        out=whi[:, 0:D], in_=wtmp[ln][:, 0:D])),
                        reads=[bwt[ln]], writes=[bwhi])
                    em.op("ge", (lambda h, ln=ln: h.tensor_tensor(
                        out=wlo[:, 0:D], in0=wtmp[ln][:, 0:D],
                        in1=whi[:, 0:D], op=OP.subtract)),
                        reads=[bwt[ln], bwhi], writes=[bwlo])
                    self.em.dma("ge:st1", self.h_w1hi[half, :, mob, :],
                                whi[:, 0:D], reads=[bwhi], writes=[hw1])
                    self.em.dma("ge:st2", self.h_w1lo[half, :, mob, :],
                                wlo[:, 0:D], reads=[bwlo], writes=[hw1])
            for oc in range(8):
                ln = oc % 2
                self.em.dma(f"sq:xld{ln}", wtmp[ln][:], self.i_f_w2[l, :, oc, :],
                            writes=[bwt[ln]])
                em.op("ve", (lambda h, ln=ln: h.tensor_copy(
                    out=whi[:], in_=wtmp[ln][:])),
                    reads=[bwt[ln]], writes=[bwhi])
                em.op("ge", (lambda h, ln=ln: h.tensor_tensor(
                    out=wlo[:], in0=wtmp[ln][:], in1=whi[:], op=OP.subtract)),
                    reads=[bwt[ln], bwhi], writes=[bwlo])
                self.em.dma("ge:st1", self.h_w2hi[:, oc, :], whi[:],
                            reads=[bwhi], writes=[hw2])
                self.em.dma("ge:st2", self.h_w2lo[:, oc, :], wlo[:],
                            reads=[bwlo], writes=[hw2])
            self.em.barrier()
            self.pes = old

        with ExitStack() as st:
            old = self.pes; self.pes = st
            wqh = self.sb(f"l{l}_wqh", [128, 8 * 1536], BF16)
            wql = self.sb(f"l{l}_wql", [128, 8 * 1536], BF16)
            wt1 = self.sb(f"l{l}_f1wt", [128, 1536])
            bwqh, bwql, bwt1 = B(), B(), B()
            for c in range(8):
                self.em.dma(f"sq:xld{c % 4}", wt1[:],
                            self.i_f_wqkv[l, :, c, :], writes=[bwt1])
                em.op("ve", (lambda h, c=c: h.tensor_copy(
                    out=wqh[:, c * 1536:(c + 1) * 1536], in_=wt1[:])),
                    reads=[bwt1], writes=[bwqh])
                em.op("ge", (lambda h, c=c: h.tensor_tensor(
                    out=wql[:, c * 1536:(c + 1) * 1536], in0=wt1[:],
                    in1=wqh[:, c * 1536:(c + 1) * 1536], op=OP.subtract)),
                    reads=[bwt1, bwqh], writes=[bwql])
            hft = [self.sb(f"l{l}_hf{c}", [128, TT]) for c in range(8)]
            hht = [self.sb(f"l{l}_hh{c}", [128, TT], BF16) for c in range(8)]
            hlt = [self.sb(f"l{l}_hl{c}", [128, TT], BF16) for c in range(8)]
            bhf = [B() for _ in range(8)]
            bhh = [B() for _ in range(8)]
            bhl = [B() for _ in range(8)]
            qevac = self.sb(f"l{l}_qev", [128, TT])
            bqev = B()
            psq = [self.ps(f"l{l}_psq{i}", [128, TT]) for i in range(2)]
            bpsq = [B(), B()]
            for tt in range(NTT):
                sl = slice(tt * TT, (tt + 1) * TT)
                for c in range(8):
                    self.load_shifted(f"sq:xld{c % 4}", hft[c], bhf[c],
                                      xin, bxin, c, tt, True)
                    em.op("ve", (lambda h, c=c: h.tensor_copy(
                        out=hht[c][:], in_=hft[c][:])),
                        reads=[bhf[c]], writes=[bhh[c]])
                    em.op("ge", (lambda h, c=c: h.tensor_tensor(
                        out=hlt[c][:], in0=hft[c][:], in1=hht[c][:],
                        op=OP.subtract)), reads=[bhf[c], bhh[c]],
                        writes=[bhl[c]])
                for ob in range(12):
                    pp = ob % 2
                    passes = ((wqh, bwqh, hht, bhh), (wqh, bwqh, hlt, bhl),
                              (wql, bwql, hht, bhh))
                    n = 0
                    for i, (ww, bww, hh_, bhh_) in enumerate(passes):
                        for c in range(8):
                            em.op("te", (lambda h, ww=ww, hh_=hh_, c=c, ob=ob,
                                         i=i, pp=pp, n=n: h.matmul(
                                out=psq[pp][:],
                                lhsT=ww[:, c * 1536 + ob * 128:
                                        c * 1536 + (ob + 1) * 128],
                                rhs=hh_[c][:],
                                start=(n == 0), stop=(n == 23))),
                                reads=[bww, bhh_[c]], writes=[bpsq[pp]],
                                notify=(n == 23))
                            n += 1
                    em.op("ve", (lambda h, pp=pp: h.tensor_copy(
                        out=qevac[:], in_=psq[pp][:])),
                        reads=[bpsq[pp]], writes=[bqev])
                    self.em.dma("ge:st1", hqkv[ob * 128:(ob + 1) * 128, sl],
                                qevac[:], reads=[bqev], writes=[bhqkv])
            self.em.barrier()
            self.pes = old

        with ExitStack() as st:
            old = self.pes; self.pes = st
            qraw = self.sb(f"l{l}_qraw", [128, S])
            rot = self.sb(f"l{l}_rot", [128, S])
            binv = self.sb(f"l{l}_binv", [128, S])
            tblb = self.sb(f"l{l}_tblb", [128, S])
            qkhi = [self.sb(f"l{l}_qkhi{i}", [128, S], BF16) for i in range(2)]
            qklo = [self.sb(f"l{l}_qklo{i}", [128, S], BF16) for i in range(2)]
            vth = [self.sb(f"l{l}_vth{i}", [128, NTC * 65], BF16)
                   for i in range(2)]
            vtl = [self.sb(f"l{l}_vtl{i}", [128, NTC * 65], BF16)
                   for i in range(2)]
            maskb = self.sb(f"l{l}_maskb", [128, 8 * 512], BF16)
            ssr = self.sb(f"l{l}_ssr", [2, 4 * TT])
            smb = [[self.sb(f"l{l}_smb{i}{j}", [128, TT]) for j in range(2)]
                   for i in range(2)]
            ef32 = [[self.sb(f"l{l}_ef{i}{j}", [128, TT]) for j in range(2)]
                    for i in range(2)]
            ehi = [[self.sb(f"l{l}_ehi{i}{j}", [128, TT], BF16)
                    for j in range(2)] for i in range(2)]
            elo = [[self.sb(f"l{l}_elo{i}{j}", [128, TT], BF16)
                    for j in range(2)] for i in range(2)]
            lrow = [self.sb(f"l{l}_lrow{i}", [1, TT]) for i in range(2)]
            blv = [self.sb(f"l{l}_blv{i}", [64, TT]) for i in range(2)]
            oslc = [self.sb(f"l{l}_oslc{i}", [64, TT]) for i in range(2)]
            ohh = [self.sb(f"l{l}_ohh{i}", [64, TT], BF16) for i in range(2)]
            oll = [self.sb(f"l{l}_oll{i}", [64, TT], BF16) for i in range(2)]
            psS = [[self.ps(f"l{l}_psS{i}{j}", [128, TT]) for j in range(2)]
                   for i in range(2)]
            psC = [[self.ps(f"l{l}_psC{i}{j}", [128, TT]) for j in range(2)]
                   for i in range(2)]
            psv = psC[0][0]
            psr = psC[1][0]
            bqraw, brot, bbinv, btbl = B(), B(), B(), B()
            bqkhi = [B(), B()]
            bqklo = [B(), B()]
            bvth = [B(), B()]
            bvtl = [B(), B()]
            bmaskb, bssr = B(), B()
            bsmb = [[B(), B()], [B(), B()]]
            bef = [[B(), B()], [B(), B()]]
            behi = [[B(), B()], [B(), B()]]
            belo = [[B(), B()], [B(), B()]]
            blrow = [B(), B()]
            bblv = [B(), B()]
            boslc = [B(), B()]
            bohh = [B(), B()]
            boll = [B(), B()]
            bpsS = [[B(), B()], [B(), B()]]
            bpsC = [[B(), B()], [B(), B()]]
            bpsv = bpsC[0][0]
            bpsr = bpsC[1][0]
            self.em.dma("sq:wsl", maskb[:], self.i_maskb, writes=[bmaskb])
            for p in range(FH // 2):
                for qk in range(2):
                    base = qk * INNER + p * 128
                    self.em.dma("sq:xld0", qraw[:],
                                hqkv[base:base + 128, :],
                                reads=[bhqkv], writes=[bqraw])
                    em.op("ve", lambda h: h.tensor_tensor(
                        out=rot[:], in0=qraw[:], in1=qraw[:], op=OP.mult),
                        reads=[bqraw], writes=[brot])
                    for tt in range(NTT):
                        sl = slice(tt * TT, (tt + 1) * TT)
                        em.op("te", (lambda h, sl=sl: h.matmul(
                            out=psr[0:2, :], lhsT=self.c_ones2[:],
                            rhs=rot[:, sl],
                            start=True, stop=True)),
                            reads=[brot, self.b_ones2], writes=[bpsr])
                        em.op("ve", lambda h: h.tensor_scalar(
                            out=ssr[:, TT:2 * TT], in0=psr[0:2, :],
                            scalar1=1e-24,
                            scalar2=None, op0=OP.max),
                            reads=[bpsr], writes=[bssr])
                        self.nr_rsqrt(ssr[:, 0:TT], ssr[:, TT:2 * TT],
                                      ssr[:, 2 * TT:3 * TT],
                                      ssr[:, 3 * TT:4 * TT], [], bssr)
                        hl2 = self.hb(f"l{l}l2")
                        self.em.dma("ge:rows",
                                    self.row_ap(self.R_L, tt * TT, TT),
                                    ssr[0:1, 0:TT], reads=[bssr], writes=[hl2])
                        self.em.dma("ge:rows",
                                    self.row_ap(self.R_L + 1, tt * TT, TT),
                                    ssr[1:2, 0:TT], reads=[bssr], writes=[hl2])
                    for hd in range(2):
                        self.em.dma("se:bc", binv[hd * 64:(hd + 1) * 64, :],
                                    self.bcast_in(self.R_L + hd, 0, S, 64),
                                    reads=[self.hb(f"l{l}l2")], writes=[bbinv])
                    em.op("ve", (lambda h, qk=qk: h.scalar_tensor_tensor(
                        out=qraw[:], in0=qraw[:],
                        scalar=cols[:, C_QS + qk:C_QS + qk + 1], in1=binv[:],
                        op0=OP.mult, op1=OP.mult)),
                        reads=[bbinv, bcols, bqraw], writes=[bqraw])
                    for a, bq in ((0, 32), (32, 0), (64, 96), (96, 64)):
                        self.em.dma("ge:tree", rot[a:a + 32, :],
                                    qraw[bq:bq + 32, :],
                                    reads=[bqraw], writes=[brot])
                    self.em.dma("sq:xld1", tblb[:], self.i_tbl[2 * qk],
                                writes=[btbl])
                    em.op("ve", lambda h: h.tensor_tensor(
                        out=qraw[:], in0=qraw[:], in1=tblb[:], op=OP.mult),
                        reads=[btbl, bqraw], writes=[bqraw])
                    self.em.dma("sq:xld1", tblb[:], self.i_tbl[2 * qk + 1],
                                writes=[btbl])
                    em.op("ve", lambda h: h.tensor_tensor(
                        out=rot[:], in0=rot[:], in1=tblb[:], op=OP.mult),
                        reads=[btbl, brot], writes=[brot])
                    em.op("ve", lambda h: h.tensor_tensor(
                        out=qraw[:], in0=qraw[:], in1=rot[:], op=OP.add),
                        reads=[brot, bqraw], writes=[bqraw])
                    em.op("ve", (lambda h, qk=qk: h.tensor_copy(
                        out=qkhi[qk][:], in_=qraw[:])),
                        reads=[bqraw], writes=[bqkhi[qk]])
                    em.op("ge", (lambda h, qk=qk: h.tensor_tensor(
                        out=qklo[qk][:], in0=qraw[:], in1=qkhi[qk][:],
                        op=OP.subtract)),
                        reads=[bqraw, bqkhi[qk]], writes=[bqklo[qk]])
                self.em.dma("sq:xld0", rot[:],
                            hqkv[2 * INNER + p * 128:2 * INNER + (p + 1) * 128, :],
                            reads=[bhqkv], writes=[brot])
                for tc in range(NTC):
                    em.op("te", (lambda h, tc=tc: h.transpose(
                        out=psv[:, 0:128], in_=rot[:, tc * 128:(tc + 1) * 128],
                        identity=self.c_ident[:])),
                        reads=[brot, self.b_ident], writes=[bpsv])
                    for hd in range(2):
                        em.op("ve", (lambda h, tc=tc, hd=hd: h.tensor_copy(
                            out=vth[hd][:, tc * 65:tc * 65 + 64],
                            in_=psv[:, hd * 64:hd * 64 + 64])),
                            reads=[bpsv], writes=[bvth[hd]])
                        em.op("ve", (lambda h, tc=tc, hd=hd: h.tensor_tensor(
                            out=vtl[hd][:, tc * 65:tc * 65 + 64],
                            in0=psv[:, hd * 64:hd * 64 + 64],
                            in1=vth[hd][:, tc * 65:tc * 65 + 64],
                            op=OP.subtract)),
                            reads=[bpsv, bvth[hd]], writes=[bvtl[hd]])
                for hd in range(2):
                    em.op("ve", (lambda h, hd=hd: h.memset(
                        vth[hd][:].rearrange("p (t j) -> p t j",
                                             j=65)[:, :, 64:65], 1.0)),
                        writes=[bvth[hd]])
                    em.op("ve", (lambda h, hd=hd: h.memset(
                        vtl[hd][:].rearrange("p (t j) -> p t j",
                                             j=65)[:, :, 64:65], 0.0)),
                        writes=[bvtl[hd]])
                for w in range(NW):
                    qsl = slice(w * FW, (w + 1) * FW)
                    kcs = [kc for kc in range(8) if w > 0 or kc >= 4]
                    for ik, kc in enumerate(kcs):
                        par = ik % 2
                        kt = (w - 1) * 4 + kc
                        ksl = slice(kt * 128, (kt + 1) * 128)
                        for hd in range(2):
                            hsl = slice(hd * 64, hd * 64 + 64)
                            tp = (hd * 64, 0)
                            passes = ((qkhi[1], bqkhi[1], qkhi[0], bqkhi[0]),
                                      (qkhi[1], bqkhi[1], qklo[0], bqklo[0]),
                                      (qklo[1], bqklo[1], qkhi[0], bqkhi[0]))
                            for i, (kk, bkk, qq, bqq) in enumerate(passes):
                                em.op("te", (lambda h, kk=kk, qq=qq, i=i,
                                             hd=hd, par=par, hsl=hsl,
                                             ksl=ksl, qsl=qsl, tp=tp:
                                             h.matmul(
                                    out=psS[hd][par][:], lhsT=kk[hsl, ksl],
                                    rhs=qq[hsl, qsl], start=(i == 0),
                                    stop=(i == 2), tile_position=tp)),
                                    reads=[bkk, bqq],
                                    writes=[bpsS[hd][par]], notify=(i == 2))
                            em.op("ve", (lambda h, hd=hd, par=par, kc=kc:
                                         h.scalar_tensor_tensor(
                                out=smb[hd][par][:], in0=psS[hd][par][:],
                                scalar=8.0,
                                in1=maskb[:, kc * 512:(kc + 1) * 512],
                                op0=OP.mult, op1=OP.add)),
                                reads=[bpsS[hd][par], bmaskb],
                                writes=[bsmb[hd][par]])
                            em.op("se", (lambda h, hd=hd, par=par: h.activation(
                                out=ef32[hd][par][:], in_=smb[hd][par][:],
                                func=AF.Exp,
                                bias=neglc[:, 0:1], scale=1.0)),
                                reads=[bsmb[hd][par], bneglc],
                                writes=[bef[hd][par]])
                            em.op("ve", (lambda h, hd=hd, par=par:
                                         h.tensor_copy(
                                out=ehi[hd][par][:], in_=ef32[hd][par][:])),
                                reads=[bef[hd][par]], writes=[behi[hd][par]])
                            em.op("ge", (lambda h, hd=hd, par=par:
                                         h.tensor_tensor(
                                out=elo[hd][par][:], in0=ef32[hd][par][:],
                                in1=ehi[hd][par][:], op=OP.subtract)),
                                reads=[bef[hd][par], behi[hd][par]],
                                writes=[belo[hd][par]])
                            vsl = slice(kt * 65, (kt + 1) * 65)
                            cpasses = ((vth[hd], bvth[hd], ehi[hd][par],
                                        behi[hd][par]),
                                       (vtl[hd], bvtl[hd], ehi[hd][par],
                                        behi[hd][par]),
                                       (vth[hd], bvth[hd], elo[hd][par],
                                        belo[hd][par]))
                            wp = w % 2
                            for i, (vv, bvv, ee, bee) in enumerate(cpasses):
                                em.op("te", (lambda h, vv=vv, ee=ee, i=i,
                                             hd=hd, vsl=vsl, ik=ik, wp=wp,
                                             last=len(kcs) - 1: h.matmul(
                                    out=psC[hd][wp][0:65, :], lhsT=vv[:, vsl],
                                    rhs=ee[:],
                                    start=(ik == 0 and i == 0),
                                    stop=(ik == last and i == 2))),
                                    reads=[bvv, bee], writes=[bpsC[hd][wp]],
                                    notify=(i == 2))
                    wp = w % 2
                    for hd in range(2):
                        orow = p * 128 + hd * 64
                        em.op("ve", (lambda h, hd=hd, wp=wp: h.reciprocal(
                            out=lrow[wp][:], in_=psC[hd][wp][64:65, :])),
                            reads=[bpsC[hd][wp]], writes=[blrow[wp]])
                        hl3 = self.hb(f"l{l}lv{wp}")
                        self.em.dma("ge:rows",
                                    self.row_ap(self.R_L + 2 + wp, w * FW, FW),
                                    lrow[wp][:], reads=[blrow[wp]],
                                    writes=[hl3])
                        self.em.dma(f"se:bc{wp}", blv[wp][:],
                                    self.bcast_in(self.R_L + 2 + wp,
                                                  w * FW, FW, 64),
                                    reads=[hl3], writes=[bblv[wp]])
                        em.op("ve", (lambda h, hd=hd, wp=wp: h.tensor_tensor(
                            out=oslc[wp][:], in0=psC[hd][wp][0:64, :],
                            in1=blv[wp][:],
                            op=OP.mult)),
                            reads=[bpsC[hd][wp], bblv[wp]], writes=[boslc[wp]])
                        em.op("ve", (lambda h, wp=wp: h.tensor_copy(
                            out=ohh[wp][:], in_=oslc[wp][:])),
                            reads=[boslc[wp]], writes=[bohh[wp]])
                        em.op("ge", (lambda h, wp=wp: h.tensor_tensor(
                            out=oll[wp][:], in0=oslc[wp][:], in1=ohh[wp][:],
                            op=OP.subtract)),
                            reads=[boslc[wp], bohh[wp]], writes=[boll[wp]])
                        self.em.dma("ge:st1",
                                    self.h_ohi[orow:orow + 64, qsl],
                                    ohh[wp][:],
                                    reads=[bohh[wp]], writes=[bho])
                        self.em.dma("ge:st2",
                                    self.h_olo[orow:orow + 64, qsl],
                                    oll[wp][:],
                                    reads=[boll[wp]], writes=[bho2])
            self.em.barrier()
            self.pes = old

        with ExitStack() as st:
            old = self.pes; self.pes = st
            woh = self.sb(f"l{l}_woh", [128, 4 * D], BF16)
            wol = self.sb(f"l{l}_wol", [128, 4 * D], BF16)
            wt3 = self.sb(f"l{l}_f3wt", [128, D])
            bwoh, bwol, bwt3 = B(), B(), B()
            for c in range(4):
                self.em.dma(f"sq:xld{c % 4}", wt3[:], self.i_f_wo[l, :, c, :],
                            writes=[bwt3])
                em.op("ve", (lambda h, c=c: h.tensor_copy(
                    out=woh[:, c * D:(c + 1) * D], in_=wt3[:])),
                    reads=[bwt3], writes=[bwoh])
                em.op("ge", (lambda h, c=c: h.tensor_tensor(
                    out=wol[:, c * D:(c + 1) * D], in0=wt3[:],
                    in1=woh[:, c * D:(c + 1) * D], op=OP.subtract)),
                    reads=[bwt3, bwoh], writes=[bwol])
            oht = [self.sb(f"l{l}_oht{c}", [128, TT], BF16) for c in range(4)]
            olt = [self.sb(f"l{l}_olt{c}", [128, TT], BF16) for c in range(4)]
            boht = [B() for _ in range(4)]
            bolt = [B() for _ in range(4)]
            xt = self.sb(f"l{l}_f3x", [128, TT])
            bxt = B()
            psm = [self.ps(f"l{l}_psm{i}", [128, TT]) for i in range(2)]
            bpsm = [B(), B()]
            for tt in range(NTT):
                sl = slice(tt * TT, (tt + 1) * TT)
                for c in range(4):
                    self.em.dma(f"sq:xld{c % 4}", oht[c][:],
                                self.h_ohi[c * 128:(c + 1) * 128, sl],
                                reads=[bho], writes=[boht[c]])
                    self.em.dma(f"sq:xld{c % 4}", olt[c][:],
                                self.h_olo[c * 128:(c + 1) * 128, sl],
                                reads=[bho2], writes=[bolt[c]])
                for oc in range(8):
                    pp = oc % 2
                    n = 0
                    passes = ((woh, bwoh, oht, boht), (woh, bwoh, olt, bolt),
                              (wol, bwol, oht, boht))
                    for i, (ww, bww, tt_, btt_) in enumerate(passes):
                        for c in range(4):
                            em.op("te", (lambda h, ww=ww, tt_=tt_, c=c,
                                         oc=oc, n=n, pp=pp: h.matmul(
                                out=psm[pp][:],
                                lhsT=ww[:, c * D + oc * 128:
                                        c * D + (oc + 1) * 128],
                                rhs=tt_[c][:],
                                start=(n == 0), stop=(n == 11))),
                                reads=[bww, btt_[c]], writes=[bpsm[pp]],
                                notify=(n == 11))
                            n += 1
                    self.em.dma(f"sq:xld{oc % 4}", xt[:],
                                xin[oc * 128:(oc + 1) * 128, sl],
                                reads=[bxin], writes=[bxt])
                    em.op("ve", (lambda h, pp=pp: h.tensor_tensor(
                        out=xt[:], in0=xt[:], in1=psm[pp][:], op=OP.add)),
                        reads=[bpsm[pp], bxt], writes=[bxt])
                    self.em.dma("ge:st1", xmid[oc * 128:(oc + 1) * 128, sl],
                                xt[:], reads=[bxt], writes=[bxmid])
            self.em.barrier()
            self.pes = old

        with ExitStack() as st:
            old = self.pes; self.pes = st
            ht = [self.sb(f"l{l}_s4h{i}", [128, TT]) for i in range(2)]
            sqt = [self.sb(f"l{l}_s4q{i}", [128, TT]) for i in range(2)]
            rows = self.sb(f"l{l}_s4r", [1, 5 * TT])
            RMU, RSG = slice(0, TT), slice(TT, 2 * TT)
            RS1, RS2, RS3 = (slice(2 * TT, 3 * TT), slice(3 * TT, 4 * TT),
                             slice(4 * TT, 5 * TT))
            psr1 = self.ps(f"l{l}_psr1", [1, TT])
            psr2 = self.ps(f"l{l}_psr2", [1, TT])
            bht = [B(), B()]
            bsqt = [B(), B()]
            brows = B()
            bpsr1, bpsr2 = B(), B()
            hmu, hsg = self.hb(f"l{l}mu"), self.hb(f"l{l}sg")
            for tt in range(NTT):
                for c in range(8):
                    ln = c % 2
                    self.load_shifted(f"sq:xld{ln}", ht[ln], bht[ln],
                                      xmid, bxmid, c, tt, True)
                    em.op("te", (lambda h, ln=ln, c=c: h.matmul(
                        out=psr1[:], lhsT=self.c_ones1[:], rhs=ht[ln][:],
                        start=(c == 0), stop=(c == 7))),
                        reads=[bht[ln], self.b_ones1], writes=[bpsr1],
                        notify=True)
                    em.op("ge", (lambda h, ln=ln: h.tensor_tensor(
                        out=sqt[ln][:], in0=ht[ln][:], in1=ht[ln][:],
                        op=OP.mult)), reads=[bht[ln]], writes=[bsqt[ln]])
                    em.op("te", (lambda h, ln=ln, c=c: h.matmul(
                        out=psr2[:], lhsT=self.c_ones1[:], rhs=sqt[ln][:],
                        start=(c == 0), stop=(c == 7))),
                        reads=[bsqt[ln], self.b_ones1], writes=[bpsr2],
                        notify=True)
                em.op("ve", lambda h: h.tensor_scalar(
                    out=rows[:, RMU], in0=psr1[:], scalar1=1.0 / D,
                    scalar2=None, op0=OP.mult), reads=[bpsr1], writes=[brows])
                em.op("ve", lambda h: h.tensor_tensor(
                    out=rows[:, RSG], in0=rows[:, RMU], in1=rows[:, RMU],
                    op=OP.mult), reads=[brows], writes=[brows])
                em.op("ve", lambda h: h.scalar_tensor_tensor(
                    out=rows[:, RSG], in0=psr2[:], scalar=1.0 / D,
                    in1=rows[:, RSG], op0=OP.mult, op1=OP.subtract),
                    reads=[bpsr2, brows], writes=[brows])
                em.op("ve", lambda h: h.tensor_scalar(
                    out=rows[:, RS1], in0=rows[:, RSG], scalar1=1e-5,
                    scalar2=None, op0=OP.add), reads=[brows], writes=[brows])
                self.nr_rsqrt(rows[:, RSG], rows[:, RS1], rows[:, RS2],
                              rows[:, RS3], [], brows)
                self.em.dma("ge:rows", self.row_ap(self.R_MU, tt * TT, TT),
                            rows[:, RMU], reads=[brows], writes=[hmu])
                self.em.dma("ge:rows", self.row_ap(self.R_SIG, tt * TT, TT),
                            rows[:, RSG], reads=[brows], writes=[hsg])
            self.em.barrier()
            self.pes = old

        with ExitStack() as st:
            old = self.pes; self.pes = st
            h2h = [self.sb(f"l{l}_h2h{c}", [128, TT], BF16) for c in range(8)]
            h2l = [self.sb(f"l{l}_h2l{c}", [128, TT], BF16) for c in range(8)]
            bh2h = [B() for _ in range(8)]
            bh2l = [B() for _ in range(8)]
            hf = self.sb(f"l{l}_h2f", [128, TT])
            bhf = B()
            bmu = self.sb(f"l{l}_bmu", [128, TT])
            bsg = self.sb(f"l{l}_bsg", [128, TT])
            bbmu, bbsg = B(), B()
            mhi = self.sb(f"l{l}_mhi", [128, MOB * TT], BF16)
            mlo = self.sb(f"l{l}_mlo", [128, MOB * TT], BF16)
            bmhi, bmlo = B(), B()
            w1b = [self.sb(f"l{l}_w1b{i}", [128, D], BF16) for i in range(8)]
            bw1b = [B() for _ in range(8)]
            w2b = [self.sb(f"l{l}_w2b{i}", [128, HIDP], BF16) for i in range(4)]
            bw2b = [B() for _ in range(4)]
            gel = self.sb(f"l{l}_gel", [128, TT])
            mf = self.sb(f"l{l}_mf", [128, TT])
            bgel, bmf = B(), B()
            x4t = self.sb(f"l{l}_x4t", [128, TT])
            bx4t = B()
            psA = self.ps(f"l{l}_psA", [128, TT])
            psG = self.ps(f"l{l}_psG", [128, TT])
            psY = [self.ps(f"l{l}_psY{i}", [128, TT]) for i in range(2)]
            bpsA, bpsG = B(), B()
            bpsY = [B(), B()]
            hmu, hsg = self.hb(f"l{l}mu"), self.hb(f"l{l}sg")
            for tt in range(NTT):
                sl = slice(tt * TT, (tt + 1) * TT)
                self.em.dma("se:bc", bmu[:],
                            self.bcast_in(self.R_MU, tt * TT, TT, 128),
                            reads=[hmu], writes=[bbmu])
                self.em.dma("se:bc", bsg[:],
                            self.bcast_in(self.R_SIG, tt * TT, TT, 128),
                            reads=[hsg], writes=[bbsg])
                for c in range(8):
                    self.load_shifted(f"sq:xld{c % 4}", hf, bhf,
                                      xmid, bxmid, c, tt, True)
                    em.op("ve", lambda h: h.tensor_tensor(
                        out=hf[:], in0=hf[:], in1=bmu[:], op=OP.subtract),
                        reads=[bbmu, bhf], writes=[bhf])
                    em.op("ve", lambda h: h.tensor_tensor(
                        out=hf[:], in0=hf[:], in1=bsg[:], op=OP.mult),
                        reads=[bbsg, bhf], writes=[bhf])
                    em.op("ve", (lambda h, c=c: h.tensor_scalar(
                        out=hf[:], in0=hf[:],
                        scalar1=cols[:, C_LNG + c:C_LNG + c + 1],
                        scalar2=cols[:, C_LNB + c:C_LNB + c + 1],
                        op0=OP.mult, op1=OP.add)),
                        reads=[bcols, bhf], writes=[bhf])
                    em.op("ve", (lambda h, c=c: h.tensor_copy(
                        out=h2h[c][:], in_=hf[:])), reads=[bhf],
                        writes=[bh2h[c]])
                    em.op("ge", (lambda h, c=c: h.tensor_tensor(
                        out=h2l[c][:], in0=hf[:], in1=h2h[c][:],
                        op=OP.subtract)), reads=[bhf, bh2h[c]],
                        writes=[bh2l[c]])
                for mob in range(MOB):
                    o4 = (mob % 2) * 4
                    self.em.dma("sq:xld0", w1b[o4 + 0][:],
                                self.h_w1hi[0, :, mob, :],
                                reads=[self.hb("w1s")], writes=[bw1b[o4 + 0]])
                    self.em.dma("sq:xld1", w1b[o4 + 1][:],
                                self.h_w1lo[0, :, mob, :],
                                reads=[self.hb("w1s")], writes=[bw1b[o4 + 1]])
                    self.em.dma("sq:xld2", w1b[o4 + 2][:],
                                self.h_w1hi[1, :, mob, :],
                                reads=[self.hb("w1s")], writes=[bw1b[o4 + 2]])
                    self.em.dma("sq:xld3", w1b[o4 + 3][:],
                                self.h_w1lo[1, :, mob, :],
                                reads=[self.hb("w1s")], writes=[bw1b[o4 + 3]])
                    for half, ps_, bps_ in ((0, psA, bpsA), (1, psG, bpsG)):
                        n = 0
                        passes = ((w1b[o4 + 2 * half], bw1b[o4 + 2 * half],
                                   h2h, bh2h),
                                  (w1b[o4 + 2 * half], bw1b[o4 + 2 * half],
                                   h2l, bh2l),
                                  (w1b[o4 + 2 * half + 1],
                                   bw1b[o4 + 2 * half + 1], h2h, bh2h))
                        for i, (ww, bww, hh_, bhh_) in enumerate(passes):
                            for c in range(8):
                                em.op("te", (lambda h, ww=ww, hh_=hh_, c=c,
                                             ps_=ps_, n=n: h.matmul(
                                    out=ps_[:],
                                    lhsT=ww[:, c * 128:(c + 1) * 128],
                                    rhs=hh_[c][:],
                                    start=(n == 0), stop=(n == 23))),
                                    reads=[bww, bhh_[c]], writes=[bps_],
                                    notify=(n == 23))
                                n += 1
                    em.op("se", (lambda h, mob=mob: h.activation(
                        out=gel[:], in_=psG[:], func=AF.Gelu,
                        bias=cols[:, C_B1G + mob:C_B1G + mob + 1],
                        scale=1.0)), reads=[bpsG, bcols], writes=[bgel])
                    em.op("ve", (lambda h, mob=mob: h.scalar_tensor_tensor(
                        out=mf[:], in0=psA[:],
                        scalar=cols[:, C_B1A + mob:C_B1A + mob + 1],
                        in1=gel[:], op0=OP.add, op1=OP.mult)),
                        reads=[bpsA, bgel, bcols], writes=[bmf])
                    em.op("ve", (lambda h, mob=mob: h.tensor_copy(
                        out=mhi[:, mob * TT:(mob + 1) * TT], in_=mf[:])),
                        reads=[bmf], writes=[bmhi])
                    em.op("ge", (lambda h, mob=mob: h.tensor_tensor(
                        out=mlo[:, mob * TT:(mob + 1) * TT], in0=mf[:],
                        in1=mhi[:, mob * TT:(mob + 1) * TT],
                        op=OP.subtract)), reads=[bmf, bmhi], writes=[bmlo])
                for oc in range(8):
                    pp = oc % 2
                    o2 = pp * 2
                    self.em.dma("sq:xld0", w2b[o2][:], self.h_w2hi[:, oc, :],
                                reads=[self.hb("w2s")], writes=[bw2b[o2]])
                    self.em.dma("sq:xld1", w2b[o2 + 1][:], self.h_w2lo[:, oc, :],
                                reads=[self.hb("w2s")], writes=[bw2b[o2 + 1]])
                    n = 0
                    passes = ((w2b[o2], bw2b[o2], mhi, bmhi),
                              (w2b[o2], bw2b[o2], mlo, bmlo),
                              (w2b[o2 + 1], bw2b[o2 + 1], mhi, bmhi))
                    for i, (ww, bww, mm_, bmm_) in enumerate(passes):
                        for mob in range(MOB):
                            em.op("te", (lambda h, ww=ww, mm_=mm_, mob=mob,
                                         n=n, pp=pp: h.matmul(
                                out=psY[pp][:],
                                lhsT=ww[:, mob * 128:(mob + 1) * 128],
                                rhs=mm_[:, mob * TT:(mob + 1) * TT],
                                start=(n == 0), stop=(n == 3 * MOB - 1))),
                                reads=[bww, bmm_], writes=[bpsY[pp]],
                                notify=(n == 3 * MOB - 1))
                            n += 1
                    self.em.dma(f"sq:xld{oc % 4}", x4t[:],
                                xmid[oc * 128:(oc + 1) * 128, sl],
                                reads=[bxmid], writes=[bx4t])
                    em.op("ve", (lambda h, oc=oc, pp=pp:
                                 h.scalar_tensor_tensor(
                        out=x4t[:], in0=psY[pp][:],
                        scalar=cols[:, C_B2 + oc:C_B2 + oc + 1], in1=x4t[:],
                        op0=OP.add, op1=OP.add)),
                        reads=[bpsY[pp], bcols, bx4t], writes=[bx4t])
                    self.em.dma("ge:st1", xout[oc * 128:(oc + 1) * 128, sl],
                                x4t[:], reads=[bx4t], writes=[bxout])
            self.em.barrier()
            self.pes = old

    def p7_final(self):
        em, S, NTT = self.em, self.S, self.NTT
        x6t = [self.sb(f"p7_x{c}", [128, TT]) for c in range(8)]
        sqb = self.sb("p7_sq", [128, TT])
        rows = self.sb("p7_rows", [1, 5 * TT])
        RMU, RSG = slice(0, TT), slice(TT, 2 * TT)
        RS1, RS2, RS3 = (slice(2 * TT, 3 * TT), slice(3 * TT, 4 * TT),
                         slice(4 * TT, 5 * TT))
        bmu = self.sb("p7_bmu", [128, TT])
        bsg = self.sb("p7_bsg", [128, TT])
        yt = self.sb("p7_yt", [128, TT])
        psr1 = self.ps("p7_psr1", [1, TT])
        psr2 = self.ps("p7_psr2", [1, TT])
        bx6 = [B() for _ in range(8)]
        bsqb, brows, bbmu, bbsg, byt = B(), B(), B(), B(), B()
        bpsr1, bpsr2 = B(), B()
        hy = self.hb("y")
        bx3h, bx6h = self.hb("x3"), self.hb("x6")
        for tt in range(NTT):
            sl = slice(tt * TT, (tt + 1) * TT)
            for c in range(8):
                self.em.dma(f"sq:xld{c % 4}", x6t[c][:],
                            self.h_x6[c * 128:(c + 1) * 128, sl],
                            reads=[bx6h], writes=[bx6[c]])
                self.em.dma(f"sq:xld{c % 4}", yt[:],
                            self.h_x3[c * 128:(c + 1) * 128, sl],
                            reads=[bx3h], writes=[byt])
                em.op("ve", (lambda h, c=c: h.tensor_tensor(
                    out=x6t[c][:], in0=x6t[c][:], in1=yt[:], op=OP.add)),
                    reads=[byt, bx6[c]], writes=[bx6[c]])
                em.op("te", (lambda h, c=c: h.matmul(
                    out=psr1[:], lhsT=self.c_ones1[:], rhs=x6t[c][:],
                    start=(c == 0), stop=(c == 7))),
                    reads=[bx6[c], self.b_ones1], writes=[bpsr1],
                    notify=(c == 7))
                em.op("ge", (lambda h, c=c: h.tensor_tensor(
                    out=sqb[:], in0=x6t[c][:], in1=x6t[c][:], op=OP.mult)),
                    reads=[bx6[c]], writes=[bsqb])
                em.op("te", (lambda h, c=c: h.matmul(
                    out=psr2[:], lhsT=self.c_ones1[:], rhs=sqb[:],
                    start=(c == 0), stop=(c == 7))),
                    reads=[bsqb, self.b_ones1], writes=[bpsr2],
                    notify=True)
            em.op("ve", lambda h: h.tensor_scalar(
                out=rows[:, RMU], in0=psr1[:], scalar1=1.0 / D, scalar2=None,
                op0=OP.mult), reads=[bpsr1], writes=[brows])
            em.op("ve", lambda h: h.tensor_tensor(
                out=rows[:, RSG], in0=rows[:, RMU], in1=rows[:, RMU],
                op=OP.mult), reads=[brows], writes=[brows])
            em.op("ve", lambda h: h.scalar_tensor_tensor(
                out=rows[:, RSG], in0=psr2[:], scalar=1.0 / D,
                in1=rows[:, RSG], op0=OP.mult, op1=OP.subtract),
                reads=[bpsr2, brows], writes=[brows])
            em.op("ve", lambda h: h.tensor_scalar(
                out=rows[:, RS1], in0=rows[:, RSG], scalar1=1e-5,
                scalar2=None, op0=OP.add), reads=[brows], writes=[brows])
            self.nr_rsqrt(rows[:, RSG], rows[:, RS1], rows[:, RS2],
                          rows[:, RS3], [], brows)
            hmu, hsg = self.hb("p7mu"), self.hb("p7sg")
            self.em.dma("ge:rows", self.row_ap(self.R_MU, tt * TT, TT),
                        rows[:, RMU], reads=[brows], writes=[hmu])
            self.em.dma("ge:rows", self.row_ap(self.R_SIG, tt * TT, TT),
                        rows[:, RSG], reads=[brows], writes=[hsg])
            self.em.dma("se:bc", bmu[:],
                        self.bcast_in(self.R_MU, tt * TT, TT, 128),
                        reads=[hmu], writes=[bbmu])
            self.em.dma("se:bc", bsg[:],
                        self.bcast_in(self.R_SIG, tt * TT, TT, 128),
                        reads=[hsg], writes=[bbsg])
            for c in range(8):
                em.op("ve" if c % 2 == 0 else "ge",
                      (lambda h, c=c: h.tensor_tensor(
                          out=x6t[c][:], in0=x6t[c][:], in1=bmu[:],
                          op=OP.subtract)),
                      reads=[bbmu, bx6[c]], writes=[bx6[c]])
                em.op("ve" if c % 2 == 0 else "ge",
                      (lambda h, c=c: h.tensor_tensor(
                          out=x6t[c][:], in0=x6t[c][:], in1=bsg[:],
                          op=OP.mult)),
                      reads=[bbsg, bx6[c]], writes=[bx6[c]])
                self.em.dma("ge:st1" if c % 2 else "sq:st0",
                            self.o_y[c * 128:(c + 1) * 128, sl], x6t[c][:],
                            reads=[bx6[c]], writes=[hy])
        self.em.barrier()

    def build(self):
        self.c_ones1 = self.sb("c_ones1", [128, 1])
        self.c_ones2 = self.sb("c_ones2", [128, 2])
        self.c_ident = self.sb("c_ident", [128, 128])
        self.b_ones1, self.b_ones2, self.b_ident = B(), B(), B()
        self.em.dma("sq:xld0", self.c_ones1[:], self.i_ones1, writes=[self.b_ones1])
        self.em.dma("sq:xld1", self.c_ones2[:], self.i_ones2, writes=[self.b_ones2])
        self.em.dma("sq:xld2", self.c_ident[:], self.i_ident, writes=[self.b_ident])
        def tapdump(name, region, rows):
            if name in self.debug_taps:
                ap = self.nc.dram_tensor("tap_" + name, [rows, self.S], F32,
                                         kind="ExternalOutput").ap()
                self.em.dma("ge:tap", ap, region[0:rows, :])
                self.em.barrier()
        phases = [self.p0_quant_x, self.p1_ternarize, ("stop_p1",),
                  self.p2_attention,
                  (lambda: tapdump("s_ctx", self.h_ctx, D)), ("stop_p2",),
                  self.p3_scramble_quant, self.p4_oproj_ln,
                  (lambda: tapdump("s_x3", self.h_x3, D)), ("stop_p4",),
                  (lambda: self.p6_layer(0)),
                  (lambda: tapdump("s_qkv0", self.h_qkv, 3 * INNER)),
                  (lambda: tapdump("s_x40", self.h_x4, D)),
                  (lambda: tapdump("s_x50", self.h_x5, D)), ("stop_f0",),
                  (lambda: self.p6_layer(1)),
                  (lambda: tapdump("s_x6", self.h_x6, D)),
                  self.p7_final]
        for ph in phases:
            if isinstance(ph, tuple):
                if ph[0] in self.debug_taps:
                    return
                continue
            self.pes = ExitStack()
            with self.pes:
                ph()
            self.pes = None


def build_program(S, debug_taps=()):
    b = Builder(S, debug_taps)
    with b.es:
        b.build()
        b.em.emit(b.nc, b.es)
    return b


def kernel(**inputs):
    x = np.asarray(inputs["L0_attn_x"])
    S = x.shape[1]
    prog = build_program(S)
    in_maps = _host_in_maps(inputs, prog)
    res = run_bass_kernel_spmd(prog.nc, in_maps, core_ids=list(range(8)))
    out = np.stack([np.ascontiguousarray(r["y_fm"]).T for r in res.results], axis=0)
    return out.astype(np.float32)


def _host_in_maps(inputs, prog):
    S = prog.S
    x = np.asarray(inputs["L0_attn_x"], dtype=np.float32)
    wq = np.asarray(inputs["L0_attn_wq"], dtype=np.float32)
    wk = np.asarray(inputs["L0_attn_wk"], dtype=np.float32)
    wv = np.asarray(inputs["L0_attn_wv"], dtype=np.float32)
    wo = np.asarray(inputs["L0_attn_wo"], dtype=np.float32)
    wqkv = np.asarray(inputs["L0_ffn_wqkv"], dtype=np.float32)
    qs = np.asarray(inputs["L0_ffn_q_scale"], dtype=np.float32)
    ks = np.asarray(inputs["L0_ffn_k_scale"], dtype=np.float32)
    fwo = np.asarray(inputs["L0_ffn_wo"], dtype=np.float32)
    lng = np.asarray(inputs["L0_ffn_ln_g"], dtype=np.float32)
    lnb = np.asarray(inputs["L0_ffn_ln_b"], dtype=np.float32)
    w1 = np.asarray(inputs["L0_ffn_w1"], dtype=np.float32)
    b1 = np.asarray(inputs["L0_ffn_b1"], dtype=np.float32)
    w2 = np.asarray(inputs["L0_ffn_w2"], dtype=np.float32)
    b2 = np.asarray(inputs["L0_ffn_b2"], dtype=np.float32)

    wq_t = np.ascontiguousarray(wq.T)
    wk_t = np.ascontiguousarray(wk.T)
    wv_t = np.ascontiguousarray(wv.T)
    wo_t = np.ascontiguousarray(wo.T)
    w1a = np.zeros((2, D, HIDP), np.float32)
    w1g = np.zeros((2, D, HIDP), np.float32)
    w1a[:, :, :HID] = w1[:, :, :HID]
    w1g[:, :, :HID] = w1[:, :, HID:]
    w2p = np.zeros((2, HIDP, D), np.float32)
    w2p[:, :HID, :] = w2
    wqkv_r = np.ascontiguousarray(
        wqkv.reshape(2, 8, 128, 3 * INNER).transpose(0, 2, 1, 3))
    fwo_r = np.ascontiguousarray(fwo.reshape(2, 4, 128, D).transpose(0, 2, 1, 3))
    w1a_r = np.ascontiguousarray(
        w1a.reshape(2, 8, 128, MOB, 128).transpose(0, 2, 3, 1, 4)
        .reshape(2, 128, MOB, D))
    w1g_r = np.ascontiguousarray(
        w1g.reshape(2, 8, 128, MOB, 128).transpose(0, 2, 3, 1, 4)
        .reshape(2, 128, MOB, D))
    w2_r = np.ascontiguousarray(
        w2p.reshape(2, MOB, 128, 8, 128).transpose(0, 2, 3, 1, 4)
        .reshape(2, 128, 8, HIDP))
    b1a = np.zeros((2, HIDP), np.float32)
    b1g = np.zeros((2, HIDP), np.float32)
    b1a[:, :HID] = b1[:, :HID]
    b1g[:, :HID] = b1[:, HID:]
    b1a_col = np.ascontiguousarray(b1a.reshape(2, MOB, 128).transpose(0, 2, 1))
    b1g_col = np.ascontiguousarray(b1g.reshape(2, MOB, 128).transpose(0, 2, 1))
    b2_col = np.ascontiguousarray(b2.reshape(2, 8, 128).transpose(0, 2, 1))
    lng_col = np.ascontiguousarray(lng.reshape(2, 8, 128).transpose(0, 2, 1))
    lnb_col = np.ascontiguousarray(lnb.reshape(2, 8, 128).transpose(0, 2, 1))
    qs_col = np.ascontiguousarray(np.tile(qs, (1, 2))[:, :, None])
    ks_col = np.ascontiguousarray(np.tile(ks, (1, 2))[:, :, None])
    tbl = np.stack(_rotary_tables(S), axis=0)
    mask = _local_mask()
    maskb = ((1.0 - np.asarray(mask, np.float32)) * -3.0e38).astype(
        ml_dtypes.bfloat16)
    ident = np.eye(128, dtype=np.float32)
    ones1 = np.ones((128, 1), np.float32)
    ones2 = np.zeros((128, 2), np.float32)
    ones2[:64, 0] = 1.0
    ones2[64:, 1] = 1.0

    shared = dict(
        wq_t=wq_t, wk_t=wk_t, wv_t=wv_t, wo_t=wo_t,
        f_wqkv=wqkv_r, f_wo=fwo_r, f_w1a=w1a_r, f_w1g=w1g_r, f_w2=w2_r,
        b1a_col=b1a_col, b1g_col=b1g_col, b2_col=b2_col,
        lng_col=lng_col, lnb_col=lnb_col, qs_col=qs_col, ks_col=ks_col,
        tbl=tbl, mask_loc=mask, maskb_loc=maskb, ident=ident, ones1=ones1,
        ones2=ones2,
    )
    maps = []
    for b_ in range(x.shape[0]):
        m = dict(shared)
        m["x_fm"] = np.ascontiguousarray(x[b_].T)
        maps.append(m)
    return maps


# version 12
# speedup vs baseline: 1.0856x; 1.0004x over previous
import math
import numpy as np
import ml_dtypes
import concourse.bass as bass
import concourse.mybir as mybir
from concourse.bass_utils import run_bass_kernel_spmd
from concourse.alu_op_type import AluOpType as OP
from contextlib import ExitStack

F32 = mybir.dt.float32
BF16 = mybir.dt.bfloat16
AF = mybir.ActivationFunctionType
AX = mybir.AxisListType

D = 1024
NH = 16
DH = 64
FH = 8
FW = 512
INNER = 512
HID = 2730
HIDP = 2816
MOB = HIDP // 128
TT = 512
EPS_Q = 1e-5
NEG_BIG = -3.4028234663852886e38



def _rotary_tables(S):
    d = np.arange(0, DH, 2, dtype=np.float64)
    inv_freq = 1.0 / 10000.0 ** (d / DH)
    t = np.arange(S, dtype=np.float64)
    freqs = t[:, None] * inv_freq[None, :]
    freqs = np.concatenate([freqs, freqs], axis=1)
    base = (d + 0.4 * DH) / (1.4 * DH)
    power = (t - S // 2) / (FW // 2)
    scale = base[None, :] ** power[:, None]
    scale = np.concatenate([scale, scale], axis=1)
    cos, sin = np.cos(freqs), np.sin(freqs)
    q1 = (cos * scale).T
    q2 = (sin * scale).T
    k1 = (cos / scale).T
    k2 = (sin / scale).T
    sgn = np.ones((DH, 1)); sgn[:32] = -1.0
    q2 = q2 * sgn
    k2 = k2 * sgn
    out = []
    for a in (q1, q2, k1, k2):
        out.append(np.tile(a, (2, 1)).astype(np.float32))
    return out


def _local_mask():
    kk = np.arange(1024)[:, None]
    qq = np.arange(512)[None, :]
    m = ((kk >= qq) & (kk <= qq + 512)).astype(np.float32)
    m = m.reshape(8, 128, 512).transpose(1, 0, 2).reshape(128, 8 * 512)
    return m.astype(ml_dtypes.bfloat16)



ENGS = ("te", "ve", "se", "ge", "sq")


class B:
    __slots__ = ("w", "r")

    def __init__(self):
        self.w = None
        self.r = []


class Em:

    def __init__(self):
        self.plan = {e: [] for e in ENGS}
        self.cnt = {e: 0 for e in ENGS}
        self.dcnt = {}
        self.dqueue = {}
        self.waited = {e: {} for e in ENGS}
        self.pend_r = {e: [] for e in ENGS}
        self.pend_w = {e: [] for e in ENGS}

    def _wait(self, eng, tok):
        s, v = tok
        if s == eng == "te":
            return
        if self.waited[eng].get(s, 0) >= v:
            return
        self.waited[eng][s] = v
        self.plan[eng].append(("w", s, v))

    def _deps(self, eng, reads, writes, extra):
        toks = []
        for b in reads:
            if b.w is not None:
                toks.append(b.w)
        for b in writes:
            toks.extend(b.r)
            if b.w is not None:
                toks.append(b.w)
        toks.extend(extra)
        for t in toks:
            self._wait(eng, t)

    def _post(self, tok, reads, writes):
        if tok is None:
            return
        for b in reads:
            b.r.append(tok)
        for b in writes:
            b.w = tok
            b.r = []

    def op(self, eng, fn, reads=(), writes=(), extra=(), notify=True):
        self._deps(eng, reads, writes, extra)
        tok = None
        if notify:
            self.cnt[eng] += 1
            tok = (eng, self.cnt[eng])
        self.plan[eng].append(("o", fn, tok))
        if notify:
            self._post(tok, list(reads) + self.pend_r[eng],
                       list(writes) + self.pend_w[eng])
            self.pend_r[eng] = []
            self.pend_w[eng] = []
        else:
            self.pend_r[eng].extend(reads)
            self.pend_w[eng].extend(writes)
        return tok

    def dma(self, stream, out, in_, reads=(), writes=(), extra=(), eng=None,
            transpose=False, **kw):
        if stream not in self.dqueue:
            self.dqueue[stream] = eng or {"sq": "sq", "se": "se", "ge": "ge",
                                          "ve": "ve"}[stream.split(":")[0]]
            self.dcnt[stream] = 0
        eng = self.dqueue[stream]
        if self.dcnt[stream] > 0:
            self._wait(eng, (stream, self.dcnt[stream]))
        self._deps(eng, reads, writes, extra)
        self.dcnt[stream] += 16
        tok = (stream, self.dcnt[stream])
        self.plan[eng].append(("d", out, in_, stream, transpose, kw, tok))
        self._post(tok, reads, writes)
        return tok

    def barrier(self):
        toks = [(e, self.cnt[e]) for e in ENGS if self.cnt[e] > 0]
        toks += [(s, c) for s, c in self.dcnt.items() if c > 0]
        for e in ENGS:
            for t in toks:
                if t[0] == e == "te":
                    continue
                s, v = t
                if self.waited[e].get(s, 0) >= v:
                    continue
                self.waited[e][s] = v
                self.plan[e].append(("w", s, v))

    def emit(self, nc, es):
        sems = {}
        for e in ENGS:
            sems[e] = es.enter_context(nc.semaphore(f"sem_{e}"))
        for s in self.dcnt:
            nm = "sem_" + s.replace(":", "_")
            sems[s] = es.enter_context(nc.semaphore(nm))
        n_sems = len(sems)
        if n_sems > 70:
            raise RuntimeError(f"too many semaphores: {n_sems}")
        block = es.enter_context(nc.Block())
        eng_attr = {"te": "tensor", "ve": "vector", "se": "scalar",
                    "ge": "gpsimd", "sq": "sync"}
        for e in ENGS:
            items = self.plan[e]

            def section(h, items=items, e=e):
                for it in items:
                    if it[0] == "w":
                        h.wait_ge(sems[it[1]], it[2])
                    elif it[0] == "o":
                        inst = it[1](h)
                        if it[2] is not None:
                            if inst is None:
                                raise RuntimeError("notify op returned no inst")
                            inst.then_inc(sems[e], 1)
                    else:
                        _, out, in_, s, tr, kw, tok = it
                        h.dma_start(out=out, in_=in_, transpose=tr,
                                    **kw).then_inc(sems[s], 16)
            getattr(block, eng_attr[e])(section)



def ceil_div(a, b):
    return (a + b - 1) // b


class Builder:

    def __init__(self, S, debug_taps=()):
        self.S = S
        self.NTT = S // TT
        self.NTC = S // 128
        self.NW = S // FW
        self.nc = bass.Bass("TRN2", target_bir_lowering=False)
        self.em = Em()
        self.es = ExitStack()
        self.debug_taps = set(debug_taps)
        self.taps = {}
        self._sbuf_bytes = 0
        self._declare_io()

    def _declare_io(self):
        nc, S = self.nc, self.S
        dt = nc.dram_tensor
        self.i_x = dt("x_fm", [D, S], F32, kind="ExternalInput").ap()
        for nm in ("wq_t", "wk_t", "wv_t", "wo_t"):
            setattr(self, "i_" + nm, dt(nm, [D, D], F32, kind="ExternalInput").ap())
        self.i_f_wqkv = dt("f_wqkv", [2, 128, 8, 3 * INNER], F32,
                           kind="ExternalInput").ap()
        self.i_f_wo = dt("f_wo", [2, 128, 4, D], F32, kind="ExternalInput").ap()
        self.i_f_w1a = dt("f_w1a", [2, 128, MOB, D], F32, kind="ExternalInput").ap()
        self.i_f_w1g = dt("f_w1g", [2, 128, MOB, D], F32, kind="ExternalInput").ap()
        self.i_f_w2 = dt("f_w2", [2, 128, 8, HIDP], F32, kind="ExternalInput").ap()
        self.i_b1a = dt("b1a_col", [2, 128, MOB], F32, kind="ExternalInput").ap()
        self.i_b1g = dt("b1g_col", [2, 128, MOB], F32, kind="ExternalInput").ap()
        self.i_b2 = dt("b2_col", [2, 128, 8], F32, kind="ExternalInput").ap()
        self.i_lng = dt("lng_col", [2, 128, 8], F32, kind="ExternalInput").ap()
        self.i_lnb = dt("lnb_col", [2, 128, 8], F32, kind="ExternalInput").ap()
        self.i_qs = dt("qs_col", [2, 128, 1], F32, kind="ExternalInput").ap()
        self.i_ks = dt("ks_col", [2, 128, 1], F32, kind="ExternalInput").ap()
        self.i_tbl = dt("tbl", [4, 128, S], F32, kind="ExternalInput").ap()
        self.i_mask = dt("mask_loc", [128, 8 * 512], BF16, kind="ExternalInput").ap()
        self.i_maskb = dt("maskb_loc", [128, 8 * 512], BF16, kind="ExternalInput").ap()
        self.i_ident = dt("ident", [128, 128], F32, kind="ExternalInput").ap()
        self.i_ones1 = dt("ones1", [128, 1], F32, kind="ExternalInput").ap()
        self.i_ones2 = dt("ones2", [128, 2], F32, kind="ExternalInput").ap()
        self.o_y = dt("y_fm", [D, S], F32, kind="ExternalOutput").ap()

        self.h_xq = dt("h_xq", [D, S], BF16, kind="Internal").ap()
        self.h_wt = dt("h_wt", [4, D, D], BF16, kind="Internal").ap()
        self.h_ctx = dt("h_ctx", [D, S], F32, kind="Internal").ap()
        self.h_ctxq = dt("h_ctxq", [D, S], BF16, kind="Internal").ap()
        self.h_x2 = dt("h_x2", [D, S], F32, kind="Internal").ap()
        self.h_x3 = dt("h_x3", [D, S], F32, kind="Internal").ap()
        self.h_x4 = dt("h_x4", [D, S], F32, kind="Internal").ap()
        self.h_x5 = dt("h_x5", [D, S], F32, kind="Internal").ap()
        self.h_x6 = dt("h_x6", [D, S], F32, kind="Internal").ap()
        self.h_qkv = dt("h_qkv", [3 * INNER, S], F32, kind="Internal").ap()
        self.h_ohi = dt("h_ohi", [INNER, S], BF16, kind="Internal").ap()
        self.h_olo = dt("h_olo", [INNER, S], BF16, kind="Internal").ap()
        self.h_w1hi = dt("h_w1hi", [2, 128, MOB, D], BF16, kind="Internal").ap()
        self.h_w1lo = dt("h_w1lo", [2, 128, MOB, D], BF16, kind="Internal").ap()
        self.h_w2hi = dt("h_w2hi", [128, 8, HIDP], BF16, kind="Internal").ap()
        self.h_w2lo = dt("h_w2lo", [128, 8, HIDP], BF16, kind="Internal").ap()
        self.h_rows = dt("h_rows", [64, S], F32, kind="Internal").ap()

    R_SS = 0
    R_MX = 1
    R_RHO = 2
    R_G0 = 3
    R_SWQ = 4
    R_L = 12
    R_MU = 6
    R_SIG = 7
    R_C = 8
    R_TMP = 9
    R_RHO2 = 10
    R_G02 = 11

    def tap(self, name, shape, dtype=F32):
        if name in self.debug_taps:
            ap = self.nc.dram_tensor("tap_" + name, shape, dtype,
                                     kind="ExternalOutput").ap()
            self.taps[name] = ap
            return ap
        return None

    def sb(self, name, shape, dtype=F32, scope=None):
        st = scope or getattr(self, "pes", None) or self.es
        return st.enter_context(self.nc.sbuf_tensor(name, shape, dtype))

    def ps(self, name, shape, dtype=F32):
        st = getattr(self, "pes", None) or self.es
        return st.enter_context(self.nc.psum_tensor(name, shape, dtype))


    def hb(self, name):
        if not hasattr(self, "_hb"):
            self._hb = {}
        if name not in self._hb:
            self._hb[name] = B()
        return self._hb[name]

    def row_ap(self, slot, lo, n):
        return self.h_rows[slot, lo:lo + n]

    def bcast_in(self, slot, lo, n, p):
        return self.h_rows[slot, lo:lo + n][None, :].to_broadcast([p, n])

    def st(self, key, k=4):
        if not hasattr(self, "_rr"):
            self._rr = {}
        i = self._rr.get(key, 0)
        self._rr[key] = i + 1
        return f"{key}{i % k}"

    def nr_rsqrt(self, dst, src, sA, sB, b_src, b_all):
        em = self.em
        em.op("se", lambda h: h.activation(out=sA, in_=src, func=AF.Sqrt),
              reads=[b_all] + b_src, writes=[b_all])
        em.op("ve", lambda h: h.reciprocal(out=dst, in_=sA),
              reads=[b_all], writes=[b_all])
        em.op("ve", lambda h: h.tensor_tensor(out=sB, in0=src, in1=dst,
                                              op=OP.mult),
              reads=[b_all], writes=[b_all])
        em.op("ve", lambda h: h.tensor_tensor(out=sB, in0=sB, in1=dst,
                                              op=OP.mult),
              reads=[b_all], writes=[b_all])
        em.op("ve", lambda h: h.tensor_scalar(out=sB, in0=sB, scalar1=-0.5,
                                              scalar2=1.5, op0=OP.mult,
                                              op1=OP.add),
              reads=[b_all], writes=[b_all])
        em.op("ve", lambda h: h.tensor_tensor(out=dst, in0=dst, in1=sB,
                                              op=OP.mult),
              reads=[b_all], writes=[b_all])

    def p0_quant_x(self):
        self.quant_pipeline(self.i_x, self.h_xq, self.R_RHO,
                            self.R_G0, 'p0', None)

    def quant_pipeline(self, src, dst, r_rho, r_g0, tag, src_hb):
        em, S = self.em, self.S
        xt = [self.sb(f"{tag}_x{c}", [128, TT]) for c in range(8)]
        sqt = [self.sb(f"{tag}_sq{i}", [128, TT]) for i in range(2)]
        acc = self.sb("{}_acc".format(tag), [128, TT])
        tmp = self.sb("{}_tmp".format(tag), [64, TT])
        tmp2 = self.sb("{}_tmp2".format(tag), [128, TT])
        rows = self.sb("{}_rows".format(tag), [1, 8 * TT])
        RS, RM, RR, RD_, RC, RP, RG = (slice(0, TT), slice(TT, 2 * TT),
                                       slice(2 * TT, 3 * TT), slice(3 * TT, 4 * TT),
                                       slice(4 * TT, 5 * TT), slice(5 * TT, 6 * TT),
                                       slice(6 * TT, 7 * TT))
        RSC = slice(7 * TT, 8 * TT)
        brho = self.sb("{}_brho".format(tag), [128, TT])
        t0 = [self.sb(f"{tag}_t0{i}", [128, TT]) for i in range(2)]
        t1 = [self.sb(f"{tag}_t1{i}", [128, TT]) for i in range(2)]
        ti = [self.sb(f"{tag}_ti{i}", [128, TT], mybir.dt.int32)
              for i in range(2)]
        xqt = [self.sb(f"{tag}_xq{i}", [128, TT], BF16) for i in range(2)]
        pss = self.ps("{}_pss".format(tag), [1, TT])
        bx = [B() for _ in range(8)]
        bsq = [B(), B()]
        bacc, btmp, brows, bbrho = B(), B(), B(), B()
        btmp2 = B()
        bps = B()
        bt = [[B(), B()] for _ in range(2)]
        bti = [B(), B()]
        bxq = [B(), B()]
        tap_rho = self.tap(tag + "_rho", [self.NTT, TT])
        tap_mx = self.tap(tag + "_mx", [self.NTT, TT])

        for tt in range(self.NTT):
            sl = slice(tt * TT, (tt + 1) * TT)
            for c in range(8):
                self.em.dma(f"sq:xld{c % 4}", xt[c][:],
                            src[c * 128:(c + 1) * 128, sl],
                            writes=[bx[c]], reads=[src_hb] if src_hb else [])
            for c in range(8):
                s = sqt[c % 2]
                em.op("ve", (lambda h, c=c, s=s: h.tensor_tensor(
                    out=s[:], in0=xt[c][:], in1=xt[c][:], op=OP.mult)),
                    reads=[bx[c]], writes=[bsq[c % 2]])
                em.op("te", (lambda h, c=c, s=s: h.matmul(
                    out=pss[:], lhsT=self.c_ones1[:], rhs=s[:],
                    start=(c == 0), stop=(c == 7))),
                    reads=[bsq[c % 2], self.b_ones1], writes=[bps],
                    notify=True)
            em.op("ve", lambda h: h.tensor_tensor(
                out=acc[:], in0=xt[0][:], in1=xt[0][:], op=OP.mult),
                reads=[bx[0]], writes=[bacc])
            for c in range(1, 8):
                em.op("ve", (lambda h, c=c: h.tensor_tensor(
                    out=tmp2[:], in0=xt[c][:], in1=xt[c][:], op=OP.mult)),
                    reads=[bx[c]], writes=[btmp2])
                em.op("ve", lambda h: h.tensor_tensor(
                    out=acc[:], in0=acc[:], in1=tmp2[:], op=OP.max),
                    reads=[btmp2, bacc], writes=[bacc])
            n = 64
            while n >= 1:
                self.em.dma(self.st("ge:tree", 3), tmp[0:n, :], acc[n:2 * n, :],
                            reads=[bacc], writes=[btmp])
                em.op("ve", (lambda h, n=n: h.tensor_tensor(
                    out=acc[0:n, :], in0=acc[0:n, :], in1=tmp[0:n, :],
                    op=OP.max)), reads=[btmp, bacc], writes=[bacc])
                n //= 2
            em.op("ve", lambda h: h.tensor_copy(out=rows[:, RS], in_=pss[:]),
                  reads=[bps], writes=[brows])
            em.op("ve", lambda h: h.tensor_scalar(
                out=rows[:, RM], in0=acc[0:1, :], scalar1=1e-30, scalar2=None,
                op0=OP.max), reads=[bacc], writes=[brows])
            self.nr_rsqrt(rows[:, RC], rows[:, RM], rows[:, RSC], rows[:, RD_],
                          [], brows)
            em.op("ve", lambda h: h.tensor_tensor(
                out=rows[:, RM], in0=rows[:, RM], in1=rows[:, RC], op=OP.mult),
                reads=[brows], writes=[brows])
            em.op("ve", lambda h: h.tensor_scalar(
                out=rows[:, RSC], in0=rows[:, RS], scalar1=1.0 / D, scalar2=1e-6,
                op0=OP.mult, op1=OP.add), reads=[brows], writes=[brows])
            self.nr_rsqrt(rows[:, RR], rows[:, RSC], rows[:, RD_], rows[:, RC],
                          [], brows)
            em.op("ve", lambda h: h.tensor_tensor(
                out=rows[:, RD_], in0=rows[:, RR], in1=rows[:, RM], op=OP.mult),
                reads=[brows], writes=[brows])
            em.op("ve", lambda h: h.tensor_scalar(
                out=rows[:, RD_], in0=rows[:, RD_], scalar1=EPS_Q, scalar2=None,
                op0=OP.max), reads=[brows], writes=[brows])
            em.op("ve", lambda h: h.reciprocal(out=rows[:, RC], in_=rows[:, RD_]),
                  reads=[brows], writes=[brows])
            em.op("ve", lambda h: h.scalar_tensor_tensor(
                out=rows[:, RP], in0=rows[:, RR], scalar=127.0, in1=rows[:, RC],
                op0=OP.mult, op1=OP.mult), reads=[brows], writes=[brows])
            em.op("ve", lambda h: h.tensor_scalar(
                out=rows[:, RG], in0=rows[:, RD_], scalar1=1.0 / 127.0,
                scalar2=None, op0=OP.mult), reads=[brows], writes=[brows])
            hrho, hg0 = self.hb(tag + "rho"), self.hb(tag + "g0")
            self.em.dma(self.st("ge:rows", 3), self.row_ap(r_rho, tt * TT, TT),
                        rows[:, RP], reads=[brows], writes=[hrho])
            self.em.dma(self.st("ge:rows", 3), self.row_ap(r_g0, tt * TT, TT),
                        rows[:, RG], reads=[brows], writes=[hg0])
            if tap_rho is not None:
                self.em.dma(self.st("ge:rows", 3), tap_rho[tt, :][None, :],
                            rows[:, RP], reads=[brows])
                self.em.dma(self.st("ge:rows", 3), tap_mx[tt, :][None, :],
                            rows[:, RM], reads=[brows])
            self.em.dma(self.st("se:bc"), brho[:], self.bcast_in(r_rho, tt * TT, TT, 128),
                        reads=[hrho], writes=[bbrho])
            for c in range(8):
                ln = c % 2
                eng = "ve" if ln == 0 else "ge"
                em.op(eng, (lambda h, c=c, ln=ln: h.tensor_tensor(
                    out=t0[ln][:], in0=xt[c][:], in1=brho[:], op=OP.mult)),
                    reads=[bx[c], bbrho], writes=[bt[ln][0]])
                em.op("ve", (lambda h, ln=ln: h.tensor_scalar(
                    out=ti[ln][:], in0=t0[ln][:], scalar1=0.5, scalar2=None,
                    op0=OP.add)), reads=[bt[ln][0]], writes=[bti[ln]])
                em.op("ve", (lambda h, ln=ln: h.scalar_tensor_tensor(
                    out=t1[ln][:], in0=t0[ln][:], scalar=0.5, in1=ti[ln][:],
                    op0=OP.add, op1=OP.subtract)),
                    reads=[bti[ln], bt[ln][0]], writes=[bt[ln][1]])
                em.op(eng, (lambda h, ln=ln: h.tensor_scalar(
                    out=t1[ln][:], in0=t1[ln][:], scalar1=0.0, scalar2=None,
                    op0=OP.is_lt)), reads=[bt[ln][1]], writes=[bt[ln][1]])
                em.op("ve", (lambda h, ln=ln: h.tensor_tensor(
                    out=t0[ln][:], in0=ti[ln][:], in1=t1[ln][:],
                    op=OP.subtract)), reads=[bti[ln], bt[ln][1]],
                    writes=[bt[ln][0]])
                em.op(eng, (lambda h, ln=ln: h.tensor_scalar(
                    out=xqt[ln][:], in0=t0[ln][:], scalar1=-128.0, scalar2=127.0,
                    op0=OP.max, op1=OP.min)), reads=[bt[ln][0]], writes=[bxq[ln]])
                self.em.dma(self.st("ge:st") if ln else self.st("ge:st"),
                            dst[c * 128:(c + 1) * 128, sl], xqt[ln][:],
                            reads=[bxq[ln]], writes=[self.hb(tag + "dst")])
        self.em.barrier()

    def p1_ternarize(self):
        em = self.em
        wch = [self.sb(f"p1_w{i}", [128, D]) for i in range(2)]
        wsum = self.sb("p1_wsum", [128, 8])
        wtot = self.sb("p1_wtot", [128, 1])
        srow = self.sb("p1_srow", [1, 8])
        sbc = self.sb("p1_sbc", [128, 1])
        ya = self.sb("p1_ya", [128, D])
        ab = self.sb("p1_ab", [128, D])
        mk = self.sb("p1_mk", [128, D])
        sg = self.sb("p1_sg", [128, D])
        tq = self.sb("p1_tq", [128, D], BF16)
        pmm = self.ps("p1_pmm", [1, 1])
        mats = [self.i_wq_t, self.i_wk_t, self.i_wv_t, self.i_wo_t]
        bw = [B(), B()]
        bws, bwt, bsr, bsb = B(), B(), B(), B()
        bya, bab, bmk, bsg, btq = B(), B(), B(), B(), B()
        bp = B()
        for M in range(4):
            for c in range(8):
                ln = c % 2
                self.em.dma(f"sq:xld{ln}", wch[ln][:],
                            mats[M][c * 128:(c + 1) * 128, :], writes=[bw[ln]])
                em.op("ve", (lambda h, c=c, ln=ln: h.reduce_sum(
                    out=wsum[:, c:c + 1], in_=wch[ln][:], axis=AX.X,
                    apply_absolute_value=True)), reads=[bw[ln]], writes=[bws])
            em.op("ve", lambda h: h.reduce_sum(out=wtot[:], in_=wsum[:], axis=AX.X),
                  reads=[bws], writes=[bwt])
            em.op("te", lambda h: h.matmul(out=pmm[:], lhsT=self.c_ones1[:],
                                           rhs=wtot[:], start=True, stop=True),
                  reads=[bwt, self.b_ones1], writes=[bp])
            em.op("ve", lambda h: h.tensor_scalar(
                out=srow[:, 0:1], in0=pmm[:], scalar1=1.0 / (D * D), scalar2=EPS_Q,
                op0=OP.mult, op1=OP.max), reads=[bp], writes=[bsr])
            em.op("ve", lambda h: h.reciprocal(out=srow[:, 1:2], in_=srow[:, 0:1]),
                  reads=[bsr], writes=[bsr])
            hsw = self.hb("sw")
            self.em.dma(self.st("ge:rows", 3), self.row_ap(self.R_SWQ, M, 1),
                        srow[:, 0:1], reads=[bsr], writes=[hsw])
            self.em.dma(self.st("ge:rows", 3), self.row_ap(self.R_SWQ, 4 + M, 1),
                        srow[:, 1:2], reads=[bsr], writes=[hsw])
            self.em.dma(self.st("se:bc"), sbc[:], self.bcast_in(self.R_SWQ, 4 + M, 1, 128),
                        reads=[hsw], writes=[bsb])
            for c in range(8):
                ln = c % 2
                self.em.dma(f"sq:xld{ln}", wch[ln][:],
                            mats[M][c * 128:(c + 1) * 128, :], writes=[bw[ln]])
                em.op("ve", (lambda h, ln=ln: h.tensor_scalar(
                    out=ya[:], in0=wch[ln][:], scalar1=sbc[:, 0:1], scalar2=None,
                    op0=OP.mult)), reads=[bw[ln], bsb], writes=[bya])
                em.op("ge", lambda h: h.tensor_tensor(
                    out=ab[:], in0=ya[:], in1=ya[:], op=OP.mult),
                    reads=[bya], writes=[bab])
                em.op("ge", lambda h: h.tensor_scalar(
                    out=mk[:], in0=ab[:], scalar1=0.25, scalar2=None,
                    op0=OP.is_gt), reads=[bab], writes=[bmk])
                em.op("se", lambda h: h.activation(out=sg[:], in_=ya[:],
                                                   func=AF.Sign),
                      reads=[bya], writes=[bsg])
                em.op("ve", lambda h: h.tensor_tensor(
                    out=tq[:], in0=sg[:], in1=mk[:], op=OP.mult),
                    reads=[bsg, bmk], writes=[btq])
                self.em.dma(self.st("ge:st"), self.h_wt[M, c * 128:(c + 1) * 128, :],
                            tq[:], reads=[btq], writes=[self.hb("wt")])
        self.em.barrier()

    def p2_attention(self):
        em, S, NTC = self.em, self.S, self.NTC
        xq = self.sb("p2_xq", [128, 8 * S], BF16)
        wt3 = [self.sb(f"p2_wt{m}", [128, 8 * D], BF16) for m in range(3)]
        bg0 = self.sb("p2_bg0", [128, S])
        g0col = self.sb("p2_g0col", [128, NTC])
        swc = self.sb("p2_swc", [128, 4])
        qt2 = [self.sb(f"p2_qt{i}", [128, D]) for i in range(2)]
        kt2 = [self.sb(f"p2_kt{i}", [128, D]) for i in range(2)]
        vt2 = [self.sb(f"p2_vt{i}", [128, D]) for i in range(2)]
        prod = [self.sb(f"p2_prod{i}", [128, D]) for i in range(4)]
        prod2 = [self.sb(f"p2_prod2{i}", [128, D]) for i in range(4)]
        sc2 = [self.sb(f"p2_sc{i}", [128, 256]) for i in range(2)]
        ee2 = [self.sb(f"p2_ee{i}", [128, 256]) for i in range(2)]
        linv2 = [self.sb(f"p2_linv{i}", [128, 16]) for i in range(2)]
        redq2 = [self.sb(f"p2_redq{i}", [128, 16]) for i in range(2)]
        ctx2 = [self.sb(f"p2_ctx{i}", [128, D]) for i in range(2)]
        ctile2 = [self.sb(f"p2_ctile{i}", [128, 128]) for i in range(2)]
        psq = [self.ps(f"p2_psq{i}", [128, TT]) for i in range(4)]
        pst = self.ps("p2_pst", [128, 128])
        bxq, bwt = B(), [B(), B(), B()]
        bbg0, bg0c, bswc = B(), B(), B()
        bqt2, bkt2, bvt2 = [B(), B()], [B(), B()], [B(), B()]
        bprod = [B(), B(), B(), B()]
        bprod2 = [B(), B(), B(), B()]
        bsc2, bee2, blinv2 = [B(), B()], [B(), B()], [B(), B()]
        bredq2 = [B(), B()]
        bctx2, bctile2 = [B(), B()], [B(), B()]
        bpsq = [B(), B(), B(), B()]
        bpst = B()
        hctx = self.hb("ctx")

        for c in range(8):
            self.em.dma(f"sq:xld{c % 4}", xq[:, c * S:(c + 1) * S],
                        self.h_xq[c * 128:(c + 1) * 128, :],
                        reads=[self.hb("p0dst")], writes=[bxq])
            for m in range(3):
                self.em.dma(self.st("sq:wsl", 3), wt3[m][:, c * D:(c + 1) * D],
                            self.h_wt[m, c * 128:(c + 1) * 128, :],
                            reads=[self.hb("wt")], writes=[bwt[m]])
        self.em.dma(self.st("se:bc"), bg0[:], self.bcast_in(self.R_G0, 0, S, 128),
                    reads=[self.hb("p0g0")], writes=[bbg0])
        for M in range(4):
            self.em.dma(self.st("se:bc"), swc[:, M:M + 1],
                        self.bcast_in(self.R_SWQ, M, 1, 128),
                        reads=[self.hb("sw")], writes=[bswc])
        for tc in range(NTC):
            em.op("te", (lambda h, tc=tc: h.transpose(
                out=pst[:], in_=bg0[:, tc * 128:(tc + 1) * 128],
                identity=self.c_ident[:])),
                reads=[bbg0, self.b_ident], writes=[bpst])
            em.op("ve", (lambda h, tc=tc: h.tensor_copy(
                out=g0col[:, tc:tc + 1], in_=pst[:, 0:1])),
                reads=[bpst], writes=[bg0c])

        r3 = lambda ap, d: ap.rearrange("p (a b) -> p a b", b=d)

        def chunk_body(scnk, pc2, qt, kt, vt, bqt, bkt, bvt, sc, ee, linv,
                       bsc, bee, blinv, redq, bredq, ctx, bctx, ctile,
                       bctile):
            ssl = slice(scnk * 128, (scnk + 1) * 128)
            for m, (dst, bdst) in enumerate(((qt, bqt), (kt, bkt), (vt, bvt))):
                for et in range(2):
                    pp = pc2 * 2 + et
                    for c in range(8):
                        em.op("te", (lambda h, m=m, et=et, c=c, pp=pp,
                                     ssl=ssl: h.matmul(
                            out=psq[pp][:],
                            lhsT=xq[:, c * S + ssl.start:c * S + ssl.stop],
                            rhs=wt3[m][:, c * D + et * TT:c * D + (et + 1) * TT],
                            start=(c == 0), stop=(c == 7))),
                            reads=[bxq, bwt[m]], writes=[bpsq[pp]],
                            notify=(c == 7))
                    em.op("ve", (lambda h, m=m, et=et, pp=pp, scnk=scnk,
                                 dst=dst: h.tensor_scalar(
                        out=dst[:, et * TT:(et + 1) * TT], in0=psq[pp][:],
                        scalar1=g0col[:, scnk:scnk + 1],
                        scalar2=swc[:, m:m + 1], op0=OP.mult, op1=OP.mult)),
                        reads=[bpsq[pp], bg0c, bswc], writes=[bdst])
            for t in range(16):
                pi = pc2 * 2 + t % 2
                em.op("ge", (lambda h, t=t, pi=pi: h.tensor_tensor(
                    out=r3(prod[pi][:], 64), in0=r3(qt[:], 64),
                    in1=kt[:, t * 64:(t + 1) * 64].rearrange(
                        "p (a d) -> p a d", a=1).to_broadcast([128, 16, 64]),
                    op=OP.mult)), reads=[bqt, bkt], writes=[bprod[pi]])
                em.op("ve", (lambda h, t=t, pi=pi: h.reduce_sum(
                    out=sc[:, t::16].rearrange("p (a b) -> p a b", b=1),
                    in_=r3(prod[pi][:], 64), axis=AX.X)),
                    reads=[bprod[pi]], writes=[bsc])
            em.op("ve", lambda h: h.reduce_max(
                out=redq[:, 0:16].rearrange("p (a b) -> p a b", b=1),
                in_=r3(sc[:], 16), axis=AX.X), reads=[bsc], writes=[bredq])
            em.op("ve", lambda h: h.tensor_tensor(
                out=r3(sc[:], 16), in0=r3(sc[:], 16),
                in1=redq[:, 0:16].rearrange("p (h a) -> p h a",
                                            a=1).to_broadcast([128, 16, 16]),
                op=OP.subtract), reads=[bredq, bsc], writes=[bsc])
            em.op("se", lambda h: h.activation(
                out=ee[:], in_=sc[:], func=AF.Exp,
                scale=0.125), reads=[bsc], writes=[bee])
            em.op("ve", lambda h: h.reduce_sum(
                out=linv[:].rearrange("p (a b) -> p a b", b=1),
                in_=r3(ee[:], 16), axis=AX.X), reads=[bee], writes=[blinv])
            em.op("ve", lambda h: h.reciprocal(out=linv[:], in_=linv[:]),
                  reads=[blinv], writes=[blinv])
            for hh in range(16):
                pi = pc2 * 2 + hh % 2
                em.op("ge", (lambda h, hh=hh, pi=pi: h.tensor_tensor(
                    out=r3(prod2[pi][:], 64),
                    in0=ee[:, hh * 16:(hh + 1) * 16].rearrange(
                        "p (t a) -> p t a", a=1).to_broadcast([128, 16, 64]),
                    in1=r3(vt[:], 64), op=OP.mult)),
                    reads=[bee, bvt], writes=[bprod2[pi]])
                em.op("ve", (lambda h, hh=hh, pi=pi: h.reduce_sum(
                    out=ctx[:, hh * 64:(hh + 1) * 64].rearrange(
                        "p (d a) -> p d a", a=1),
                    in_=prod2[pi][:].rearrange("p (t d) -> p d t", d=64),
                    axis=AX.X)), reads=[bprod2[pi]], writes=[bctx])
            em.op("ge", lambda h: h.tensor_tensor(
                out=r3(ctx[:], 64), in0=r3(ctx[:], 64),
                in1=linv[:].rearrange("p (h a) -> p h a", a=1).to_broadcast(
                    [128, 16, 64]), op=OP.mult),
                reads=[blinv, bctx], writes=[bctx])
            for ec in range(8):
                em.op("te", (lambda h, ec=ec: h.transpose(
                    out=pst[:], in_=ctx[:, ec * 128:(ec + 1) * 128],
                    identity=self.c_ident[:])),
                    reads=[bctx, self.b_ident], writes=[bpst])
                em.op("ve", (lambda h, ec=ec: h.tensor_copy(
                    out=ctile[:], in_=pst[:])),
                    reads=[bpst], writes=[bctile])
                self.em.dma(self.st("ge:st"), self.h_ctx[ec * 128:(ec + 1) * 128, ssl],
                            ctile[:], reads=[bctile], writes=[hctx])

        for scnk in range(NTC):
            pc2 = scnk % 2
            chunk_body(scnk, pc2, qt2[pc2], kt2[pc2], vt2[pc2], bqt2[pc2],
                       bkt2[pc2], bvt2[pc2], sc2[pc2], ee2[pc2], linv2[pc2],
                       bsc2[pc2], bee2[pc2], blinv2[pc2], redq2[pc2],
                       bredq2[pc2], ctx2[pc2], bctx2[pc2], ctile2[pc2],
                       bctile2[pc2])
        self.em.barrier()

    def p3_scramble_quant(self):
        em, S = self.em, self.S
        NB = S // 16
        cnk = self.sb("p3_cnk", [128, S])
        cnk2 = self.sb("p3_cnk2", [128, S])
        yb = [self.sb(f"p3_yb{a}", [128, NB]) for a in range(8)]
        sqt = self.sb("p3_sqt", [128, NB])
        sq2 = self.sb("p3_sq2", [128, NB])
        acc = self.sb("p3_acc", [128, NB])
        tmp = self.sb("p3_tmp", [64, NB])
        rows = self.sb("p3_rows", [1, 8 * NB])
        RS, RM, RR, RD_, RP, RG = (slice(0, NB), slice(NB, 2 * NB),
                                   slice(2 * NB, 3 * NB), slice(3 * NB, 4 * NB),
                                   slice(4 * NB, 5 * NB), slice(5 * NB, 6 * NB))
        RC = slice(6 * NB, 7 * NB)
        RSC = slice(7 * NB, 8 * NB)
        brho = self.sb("p3_brho", [128, NB])
        t0 = self.sb("p3_t0", [128, NB])
        t1 = self.sb("p3_t1", [128, NB])
        ti = self.sb("p3_ti", [128, NB], mybir.dt.int32)
        yq = self.sb("p3_yq", [128, NB], BF16)
        pss = self.ps("p3_pss", [1, NB])
        bcnk, bcnk2 = B(), B()
        byb = [B() for _ in range(8)]
        bsqt, bsq2, bacc, btmp, brows, bbrho = B(), B(), B(), B(), B(), B()
        bt0, bt1, bti, byq, bps = B(), B(), B(), B(), B()
        hctx = self.hb("ctx")
        hrho, hg0 = self.hb("p3rho"), self.hb("p3g0")
        hdst = self.hb("p3dst")

        for hh in range(16):
            base = hh * NB
            if hh % 2 == 0:
                cc = hh // 2
                self.em.dma("sq:xld0", cnk[:],
                            self.h_ctx[cc * 128:(cc + 1) * 128, :],
                            reads=[hctx], writes=[bcnk])
                self.em.dma(self.st("ge:tree", 3), cnk2[0:64, :], cnk[64:128, :],
                            reads=[bcnk], writes=[bcnk2])
                self.em.dma(self.st("ge:tree", 3), cnk2[64:128, :], cnk[0:64, :],
                            reads=[bcnk], writes=[bcnk2])
            for a in range(8):
                for j in (2 * a, 2 * a + 1):
                    same = (j % 2) == (hh % 2)
                    src = cnk if same else cnk2
                    bsrc = bcnk if same else bcnk2
                    ob = (j % 2) * 64
                    em.op("ve" if a % 2 == 0 else "ge",
                          (lambda h, a=a, j=j, src=src, ob=ob: h.tensor_copy(
                              out=yb[a][ob:ob + 64, :],
                              in_=src[ob:ob + 64, j::16])),
                          reads=[bsrc], writes=[byb[a]])
            for a in range(8):
                em.op("ve", (lambda h, a=a: h.tensor_tensor(
                    out=sqt[:], in0=yb[a][:], in1=yb[a][:], op=OP.mult)),
                    reads=[byb[a]], writes=[bsqt])
                em.op("te", (lambda h, a=a: h.matmul(
                    out=pss[:], lhsT=self.c_ones1[:], rhs=sqt[:],
                    start=(a == 0), stop=(a == 7))),
                    reads=[bsqt, self.b_ones1], writes=[bps], notify=True)
                if a == 0:
                    em.op("ge", lambda h: h.tensor_tensor(
                        out=acc[:], in0=yb[0][:], in1=yb[0][:], op=OP.mult),
                        reads=[byb[0]], writes=[bacc])
                else:
                    em.op("ge", (lambda h, a=a: h.tensor_tensor(
                        out=sq2[:], in0=yb[a][:], in1=yb[a][:], op=OP.mult)),
                        reads=[byb[a]], writes=[bsq2])
                    em.op("ve", lambda h: h.tensor_tensor(
                        out=acc[:], in0=acc[:], in1=sq2[:], op=OP.max),
                        reads=[bsq2, bacc], writes=[bacc])
            n = 64
            while n >= 1:
                self.em.dma(self.st("ge:tree", 3), tmp[0:n, :], acc[n:2 * n, :],
                            reads=[bacc], writes=[btmp])
                em.op("ve", (lambda h, n=n: h.tensor_tensor(
                    out=acc[0:n, :], in0=acc[0:n, :], in1=tmp[0:n, :],
                    op=OP.max)), reads=[btmp, bacc], writes=[bacc])
                n //= 2
            em.op("ve", lambda h: h.tensor_copy(out=rows[:, RS], in_=pss[:]),
                  reads=[bps], writes=[brows])
            em.op("ve", lambda h: h.tensor_scalar(
                out=rows[:, RM], in0=acc[0:1, :], scalar1=1e-30, scalar2=None,
                op0=OP.max), reads=[bacc], writes=[brows])
            self.nr_rsqrt(rows[:, RC], rows[:, RM], rows[:, RSC], rows[:, RD_],
                          [], brows)
            em.op("ve", lambda h: h.tensor_tensor(
                out=rows[:, RM], in0=rows[:, RM], in1=rows[:, RC], op=OP.mult),
                reads=[brows], writes=[brows])
            em.op("ve", lambda h: h.tensor_scalar(
                out=rows[:, RSC], in0=rows[:, RS], scalar1=1.0 / D, scalar2=1e-6,
                op0=OP.mult, op1=OP.add), reads=[brows], writes=[brows])
            self.nr_rsqrt(rows[:, RR], rows[:, RSC], rows[:, RD_], rows[:, RC],
                          [], brows)
            em.op("ve", lambda h: h.tensor_tensor(
                out=rows[:, RD_], in0=rows[:, RR], in1=rows[:, RM], op=OP.mult),
                reads=[brows], writes=[brows])
            em.op("ve", lambda h: h.tensor_scalar(
                out=rows[:, RD_], in0=rows[:, RD_], scalar1=EPS_Q, scalar2=None,
                op0=OP.max), reads=[brows], writes=[brows])
            em.op("ve", lambda h: h.reciprocal(out=rows[:, RP], in_=rows[:, RD_]),
                  reads=[brows], writes=[brows])
            em.op("ve", lambda h: h.scalar_tensor_tensor(
                out=rows[:, RP], in0=rows[:, RR], scalar=127.0, in1=rows[:, RP],
                op0=OP.mult, op1=OP.mult), reads=[brows], writes=[brows])
            em.op("ve", lambda h: h.tensor_scalar(
                out=rows[:, RG], in0=rows[:, RD_], scalar1=1.0 / 127.0,
                scalar2=None, op0=OP.mult), reads=[brows], writes=[brows])
            self.em.dma(self.st("ge:rows", 3), self.row_ap(self.R_RHO2, base, NB),
                        rows[:, RP], reads=[brows], writes=[hrho])
            self.em.dma(self.st("ge:rows", 3), self.row_ap(self.R_G02, base, NB),
                        rows[:, RG], reads=[brows], writes=[hg0])
            self.em.dma(self.st("se:bc"), brho[:], self.bcast_in(self.R_RHO2, base, NB, 128),
                        reads=[hrho], writes=[bbrho])
            for a in range(8):
                em.op("ve", (lambda h, a=a: h.tensor_tensor(
                    out=t0[:], in0=yb[a][:], in1=brho[:], op=OP.mult)),
                    reads=[byb[a], bbrho], writes=[bt0])
                em.op("ve", lambda h: h.tensor_scalar(
                    out=ti[:], in0=t0[:], scalar1=0.5, scalar2=None,
                    op0=OP.add), reads=[bt0], writes=[bti])
                em.op("ve", lambda h: h.scalar_tensor_tensor(
                    out=t1[:], in0=t0[:], scalar=0.5, in1=ti[:],
                    op0=OP.add, op1=OP.subtract),
                    reads=[bti, bt0], writes=[bt1])
                em.op("ge", lambda h: h.tensor_scalar(
                    out=t1[:], in0=t1[:], scalar1=0.0, scalar2=None,
                    op0=OP.is_lt), reads=[bt1], writes=[bt1])
                em.op("ve", lambda h: h.tensor_tensor(
                    out=t0[:], in0=ti[:], in1=t1[:], op=OP.subtract),
                    reads=[bti, bt1], writes=[bt0])
                em.op("ve", lambda h: h.tensor_scalar(
                    out=yq[:], in0=t0[:], scalar1=-128.0, scalar2=127.0,
                    op0=OP.max, op1=OP.min), reads=[bt0], writes=[byq])
                self.em.dma(self.st("ge:st"),
                            self.h_ctxq[a * 128:(a + 1) * 128, base:base + NB],
                            yq[:], reads=[byq], writes=[hdst])
        self.em.barrier()

    def p4_oproj_ln(self):
        em, S, NTT = self.em, self.S, self.NTT
        cq = self.sb("p4_cq", [128, 8 * S], BF16)
        wot = self.sb("p4_wot", [128, 8 * D], BF16)
        bgo = self.sb("p4_bgo", [128, S])
        swo = self.sb("p4_swo", [128, 1])
        x2t = [self.sb(f"p4_x2t{c}", [128, TT]) for c in range(8)]
        yt = self.sb("p4_yt", [128, TT])
        sqb = self.sb("p4_sqb", [128, TT])
        rows = self.sb("p4_rows", [1, 7 * TT])
        RSU, RSS, RMU, RSG = (slice(0, TT), slice(TT, 2 * TT),
                              slice(2 * TT, 3 * TT), slice(3 * TT, 4 * TT))
        RS1, RS2, RS3 = (slice(4 * TT, 5 * TT), slice(5 * TT, 6 * TT),
                         slice(6 * TT, 7 * TT))
        bmu = self.sb("p4_bmu", [128, TT])
        bsg = self.sb("p4_bsg", [128, TT])
        x3t = self.sb("p4_x3t", [128, TT])
        psy = self.ps("p4_psy", [128, TT])
        psr = self.ps("p4_psr", [1, TT])
        psr2 = self.ps("p4_psr2", [1, TT])
        bcq, bwot, bbgo, bswo = B(), B(), B(), B()
        bx2 = [B() for _ in range(8)]
        byt, bsqb, brows, bbmu, bbsg, bx3 = B(), B(), B(), B(), B(), B()
        bpsy, bpsr, bpsr2 = B(), B(), B()
        hx3 = self.hb("x3")

        for c in range(8):
            self.em.dma(f"sq:xld{c % 4}", cq[:, c * S:(c + 1) * S],
                        self.h_ctxq[c * 128:(c + 1) * 128, :],
                        reads=[self.hb("p3dst")], writes=[bcq])
            self.em.dma(self.st("sq:wsl", 3), wot[:, c * D:(c + 1) * D],
                        self.h_wt[3, c * 128:(c + 1) * 128, :],
                        reads=[self.hb("wt")], writes=[bwot])
        self.em.dma(self.st("se:bc"), bgo[:], self.bcast_in(self.R_G02, 0, S, 128),
                    reads=[self.hb("p3g0")], writes=[bbgo])
        self.em.dma(self.st("se:bc"), swo[:], self.bcast_in(self.R_SWQ, 3, 1, 128),
                    reads=[self.hb("sw")], writes=[bswo])

        for tt in range(NTT):
            sl = slice(tt * TT, (tt + 1) * TT)
            for oc in range(8):
                for c in range(8):
                    em.op("te", (lambda h, oc=oc, c=c, sl=sl: h.matmul(
                        out=psy[:],
                        lhsT=wot[:, c * D + oc * 128:c * D + (oc + 1) * 128],
                        rhs=cq[:, c * S + sl.start:c * S + sl.stop],
                        start=(c == 0), stop=(c == 7))),
                        reads=[bwot, bcq], writes=[bpsy], notify=(c == 7))
                em.op("ve", (lambda h, sl=sl: h.scalar_tensor_tensor(
                    out=yt[:], in0=psy[:], scalar=swo[:, 0:1], in1=bgo[:, sl],
                    op0=OP.mult, op1=OP.mult)),
                    reads=[bpsy, bswo, bbgo], writes=[byt])
                self.em.dma(f"sq:xld{oc % 4}", x2t[oc][:],
                            self.i_x[oc * 128:(oc + 1) * 128, sl],
                            writes=[bx2[oc]])
                em.op("ve", (lambda h, oc=oc: h.tensor_tensor(
                    out=x2t[oc][:], in0=x2t[oc][:], in1=yt[:], op=OP.add)),
                    reads=[byt, bx2[oc]], writes=[bx2[oc]])
                em.op("te", (lambda h, oc=oc: h.matmul(
                    out=psr[:], lhsT=self.c_ones1[:], rhs=x2t[oc][:],
                    start=(oc == 0), stop=(oc == 7))),
                    reads=[bx2[oc], self.b_ones1], writes=[bpsr],
                    notify=(oc == 7))
                em.op("ge", (lambda h, oc=oc: h.tensor_tensor(
                    out=sqb[:], in0=x2t[oc][:], in1=x2t[oc][:], op=OP.mult)),
                    reads=[bx2[oc]], writes=[bsqb])
                em.op("te", (lambda h, oc=oc: h.matmul(
                    out=psr2[:], lhsT=self.c_ones1[:], rhs=sqb[:],
                    start=(oc == 0), stop=(oc == 7))),
                    reads=[bsqb, self.b_ones1], writes=[bpsr2],
                    notify=True)
            em.op("ve", lambda h: h.tensor_scalar(
                out=rows[:, RMU], in0=psr[:], scalar1=1.0 / D, scalar2=None,
                op0=OP.mult), reads=[bpsr], writes=[brows])
            em.op("ve", lambda h: h.tensor_tensor(
                out=rows[:, RSU], in0=rows[:, RMU], in1=rows[:, RMU],
                op=OP.mult), reads=[brows], writes=[brows])
            em.op("ve", lambda h: h.scalar_tensor_tensor(
                out=rows[:, RSG], in0=psr2[:], scalar=1.0 / D, in1=rows[:, RSU],
                op0=OP.mult, op1=OP.subtract),
                reads=[bpsr2, brows], writes=[brows])
            em.op("ve", lambda h: h.tensor_scalar(
                out=rows[:, RS1], in0=rows[:, RSG], scalar1=1e-5, scalar2=None,
                op0=OP.add), reads=[brows], writes=[brows])
            self.nr_rsqrt(rows[:, RSG], rows[:, RS1], rows[:, RS2],
                          rows[:, RS3], [], brows)
            hmu, hsg = self.hb("mu"), self.hb("sig")
            self.em.dma(self.st("ge:rows", 3), self.row_ap(self.R_MU, tt * TT, TT),
                        rows[:, RMU], reads=[brows], writes=[hmu])
            self.em.dma(self.st("ge:rows", 3), self.row_ap(self.R_SIG, tt * TT, TT),
                        rows[:, RSG], reads=[brows], writes=[hsg])
            self.em.dma(self.st("se:bc"), bmu[:], self.bcast_in(self.R_MU, tt * TT, TT, 128),
                        reads=[hmu], writes=[bbmu])
            self.em.dma(self.st("se:bc"), bsg[:], self.bcast_in(self.R_SIG, tt * TT, TT, 128),
                        reads=[hsg], writes=[bbsg])
            for c in range(8):
                em.op("ve" if c % 2 == 0 else "ge",
                      (lambda h, c=c: h.tensor_tensor(
                          out=x3t[:], in0=x2t[c][:], in1=bmu[:],
                          op=OP.subtract)),
                      reads=[bx2[c], bbmu], writes=[bx3])
                em.op("ve" if c % 2 == 0 else "ge",
                      (lambda h: h.tensor_tensor(
                          out=x3t[:], in0=x3t[:], in1=bsg[:], op=OP.mult)),
                      reads=[bbsg, bx3], writes=[bx3])
                self.em.dma(self.st("ge:st") if c % 2 else self.st("ge:st"),
                            self.h_x3[c * 128:(c + 1) * 128, sl], x3t[:],
                            reads=[bx3], writes=[hx3])
        self.em.barrier()

    def load_shifted(self, stream, dst, dst_b, src, src_b, c, tt, shifted):
        sl = slice(tt * TT, (tt + 1) * TT)
        if not shifted or c < 4:
            self.em.dma(stream, dst[:], src[c * 128:(c + 1) * 128, sl],
                        reads=[src_b], writes=[dst_b])
        elif tt > 0:
            self.em.dma(stream, dst[:],
                        src[c * 128:(c + 1) * 128, sl.start - 1:sl.stop - 1],
                        reads=[src_b], writes=[dst_b])
        else:
            self.em.op("ve", (lambda h: h.memset(dst[:, 0:1], 0.0)),
                       writes=[dst_b])
            self.em.dma(stream, dst[:, 1:TT],
                        src[c * 128:(c + 1) * 128, 0:TT - 1],
                        reads=[src_b], writes=[dst_b])

    def p6_layer(self, l):
        em, S, NTT, NTC, NW = self.em, self.S, self.NTT, self.NTC, self.NW
        xin = self.h_x3 if l == 0 else self.h_x5
        bxin = self.hb("x3" if l == 0 else "x5")
        xout = self.h_x5 if l == 0 else self.h_x6
        bxout = self.hb("x5" if l == 0 else "x6")
        xmid, bxmid = self.h_x4, self.hb("x4")
        hqkv, bhqkv = self.h_qkv, self.hb("qkv")
        bho, bho2 = self.hb("ohi"), self.hb("olo")
        cols = self.sb(f"l{l}_cols", [128, 2 * MOB + 3 * 8 + 2])
        C_B1A, C_B1G = 0, MOB
        C_B2, C_LNG, C_LNB = 2 * MOB, 2 * MOB + 8, 2 * MOB + 16
        C_QS, C_KS = 2 * MOB + 24, 2 * MOB + 25
        bcols = B()
        for ap_, off, n in ((self.i_b1a[l], C_B1A, MOB),
                            (self.i_b1g[l], C_B1G, MOB),
                            (self.i_b2[l], C_B2, 8),
                            (self.i_lng[l], C_LNG, 8),
                            (self.i_lnb[l], C_LNB, 8),
                            (self.i_qs[l], C_QS, 1),
                            (self.i_ks[l], C_KS, 1)):
            self.em.dma(self.st("sq:wsl", 3), cols[:, off:off + n], ap_, writes=[bcols])
        neglc = self.sb(f"l{l}_neglc", [128, 1])
        srw = self.sb(f"l{l}_srw", [1, 260])
        bneglc, bsrw = B(), B()
        _pst_stack = ExitStack()
        pst = _pst_stack.enter_context(
            self.nc.psum_tensor(f"l{l}_pst", [128, 128], F32))
        bpst = B()
        red2 = self.sb(f"l{l}_red2", [2, 1])
        bred2 = B()
        em.op("te", lambda h: h.transpose(
            out=pst[0:2, :], in_=cols[:, C_QS:C_KS + 1],
            identity=self.c_ident[:]),
            reads=[bcols, self.b_ident], writes=[bpst])
        em.op("ve", lambda h: h.reduce_max(
            out=red2[:], in_=pst[0:2, :], axis=AX.X,
            apply_absolute_value=True), reads=[bpst], writes=[bred2])
        em.op("te", lambda h: h.transpose(
            out=pst[0:1, 0:2], in_=red2[:], identity=self.c_ident[0:2, 0:2]),
            reads=[bred2, self.b_ident], writes=[bpst])
        em.op("ve", lambda h: h.tensor_copy(out=srw[:, 0:2],
                                            in_=pst[0:1, 0:2]),
              reads=[bpst], writes=[bsrw])
        em.op("ve", lambda h: h.tensor_tensor(
            out=srw[:, 258:259], in0=srw[:, 0:1], in1=srw[:, 1:2],
            op=OP.mult), reads=[bsrw], writes=[bsrw])
        em.op("ve", lambda h: h.tensor_scalar(
            out=srw[:, 258:259], in0=srw[:, 258:259], scalar1=-8.0,
            scalar2=None, op0=OP.mult), reads=[bsrw], writes=[bsrw])
        hcl = self.hb(f"l{l}c")
        self.em.dma(self.st("ge:rows", 3), self.row_ap(self.R_C, 16 + l, 1),
                    srw[:, 258:259], reads=[bsrw], writes=[hcl])
        self.em.dma(self.st("se:bc"), neglc[:], self.bcast_in(self.R_C, 16 + l, 1, 128),
                    reads=[hcl], writes=[bneglc])
        self.em.barrier()
        _pst_stack.close()

        with ExitStack() as st:
            old = self.pes; self.pes = st
            wtmp = [self.sb(f"l{l}_wt{i}", [128, HIDP]) for i in range(2)]
            whi = self.sb(f"l{l}_whi", [128, HIDP], BF16)
            wlo = self.sb(f"l{l}_wlo", [128, HIDP], BF16)
            bwt = [B(), B()]
            bwhi, bwlo = B(), B()
            hw1, hw2 = self.hb("w1s"), self.hb("w2s")
            for half, isrc in ((0, self.i_f_w1a), (1, self.i_f_w1g)):
                for mob in range(MOB):
                    ln = mob % 2
                    self.em.dma(f"sq:xld{ln}", wtmp[ln][:, 0:D],
                                isrc[l, :, mob, :], writes=[bwt[ln]])
                    em.op("ve", (lambda h, ln=ln: h.tensor_copy(
                        out=whi[:, 0:D], in_=wtmp[ln][:, 0:D])),
                        reads=[bwt[ln]], writes=[bwhi])
                    em.op("ge", (lambda h, ln=ln: h.tensor_tensor(
                        out=wlo[:, 0:D], in0=wtmp[ln][:, 0:D],
                        in1=whi[:, 0:D], op=OP.subtract)),
                        reads=[bwt[ln], bwhi], writes=[bwlo])
                    self.em.dma(self.st("ge:st"), self.h_w1hi[half, :, mob, :],
                                whi[:, 0:D], reads=[bwhi], writes=[hw1])
                    self.em.dma(self.st("ge:st"), self.h_w1lo[half, :, mob, :],
                                wlo[:, 0:D], reads=[bwlo], writes=[hw1])
            for oc in range(8):
                ln = oc % 2
                self.em.dma(f"sq:xld{ln}", wtmp[ln][:], self.i_f_w2[l, :, oc, :],
                            writes=[bwt[ln]])
                em.op("ve", (lambda h, ln=ln: h.tensor_copy(
                    out=whi[:], in_=wtmp[ln][:])),
                    reads=[bwt[ln]], writes=[bwhi])
                em.op("ge", (lambda h, ln=ln: h.tensor_tensor(
                    out=wlo[:], in0=wtmp[ln][:], in1=whi[:], op=OP.subtract)),
                    reads=[bwt[ln], bwhi], writes=[bwlo])
                self.em.dma(self.st("ge:st"), self.h_w2hi[:, oc, :], whi[:],
                            reads=[bwhi], writes=[hw2])
                self.em.dma(self.st("ge:st"), self.h_w2lo[:, oc, :], wlo[:],
                            reads=[bwlo], writes=[hw2])
            self.em.barrier()
            self.pes = old

        with ExitStack() as st:
            old = self.pes; self.pes = st
            wqh = self.sb(f"l{l}_wqh", [128, 8 * 1536], BF16)
            wql = self.sb(f"l{l}_wql", [128, 8 * 1536], BF16)
            wt1 = self.sb(f"l{l}_f1wt", [128, 1536])
            bwqh, bwql, bwt1 = B(), B(), B()
            for c in range(8):
                self.em.dma(f"sq:xld{c % 4}", wt1[:],
                            self.i_f_wqkv[l, :, c, :], writes=[bwt1])
                em.op("ve", (lambda h, c=c: h.tensor_copy(
                    out=wqh[:, c * 1536:(c + 1) * 1536], in_=wt1[:])),
                    reads=[bwt1], writes=[bwqh])
                em.op("ge", (lambda h, c=c: h.tensor_tensor(
                    out=wql[:, c * 1536:(c + 1) * 1536], in0=wt1[:],
                    in1=wqh[:, c * 1536:(c + 1) * 1536], op=OP.subtract)),
                    reads=[bwt1, bwqh], writes=[bwql])
            hft = [self.sb(f"l{l}_hf{c}", [128, TT]) for c in range(8)]
            hht = [self.sb(f"l{l}_hh{c}", [128, TT], BF16) for c in range(8)]
            hlt = [self.sb(f"l{l}_hl{c}", [128, TT], BF16) for c in range(8)]
            bhf = [B() for _ in range(8)]
            bhh = [B() for _ in range(8)]
            bhl = [B() for _ in range(8)]
            qevac = self.sb(f"l{l}_qev", [128, TT])
            bqev = B()
            psq = [self.ps(f"l{l}_psq{i}", [128, TT]) for i in range(2)]
            bpsq = [B(), B()]
            for tt in range(NTT):
                sl = slice(tt * TT, (tt + 1) * TT)
                for c in range(8):
                    self.load_shifted(f"sq:xld{c % 4}", hft[c], bhf[c],
                                      xin, bxin, c, tt, True)
                    em.op("ve", (lambda h, c=c: h.tensor_copy(
                        out=hht[c][:], in_=hft[c][:])),
                        reads=[bhf[c]], writes=[bhh[c]])
                    em.op("ge", (lambda h, c=c: h.tensor_tensor(
                        out=hlt[c][:], in0=hft[c][:], in1=hht[c][:],
                        op=OP.subtract)), reads=[bhf[c], bhh[c]],
                        writes=[bhl[c]])
                for ob in range(12):
                    pp = ob % 2
                    passes = ((wqh, bwqh, hht, bhh), (wqh, bwqh, hlt, bhl),
                              (wql, bwql, hht, bhh))
                    n = 0
                    for i, (ww, bww, hh_, bhh_) in enumerate(passes):
                        for c in range(8):
                            em.op("te", (lambda h, ww=ww, hh_=hh_, c=c, ob=ob,
                                         i=i, pp=pp, n=n: h.matmul(
                                out=psq[pp][:],
                                lhsT=ww[:, c * 1536 + ob * 128:
                                        c * 1536 + (ob + 1) * 128],
                                rhs=hh_[c][:],
                                start=(n == 0), stop=(n == 23))),
                                reads=[bww, bhh_[c]], writes=[bpsq[pp]],
                                notify=(n == 23))
                            n += 1
                    em.op("ve", (lambda h, pp=pp: h.tensor_copy(
                        out=qevac[:], in_=psq[pp][:])),
                        reads=[bpsq[pp]], writes=[bqev])
                    self.em.dma(self.st("ge:st"), hqkv[ob * 128:(ob + 1) * 128, sl],
                                qevac[:], reads=[bqev], writes=[bhqkv])
            self.em.barrier()
            self.pes = old

        with ExitStack() as st:
            old = self.pes; self.pes = st
            qraw = self.sb(f"l{l}_qraw", [128, S])
            rot = self.sb(f"l{l}_rot", [128, S])
            binv = self.sb(f"l{l}_binv", [128, S])
            tblb = self.sb(f"l{l}_tblb", [128, S])
            qkhi = [self.sb(f"l{l}_qkhi{i}", [128, S], BF16) for i in range(2)]
            qklo = [self.sb(f"l{l}_qklo{i}", [128, S], BF16) for i in range(2)]
            vth = [self.sb(f"l{l}_vth{i}", [128, NTC * 65], BF16)
                   for i in range(2)]
            vtl = [self.sb(f"l{l}_vtl{i}", [128, NTC * 65], BF16)
                   for i in range(2)]
            maskb = self.sb(f"l{l}_maskb", [128, 8 * 512], BF16)
            ssr = self.sb(f"l{l}_ssr", [2, 4 * TT])
            smb = [[self.sb(f"l{l}_smb{i}{j}", [128, TT]) for j in range(2)]
                   for i in range(2)]
            ef32 = [[self.sb(f"l{l}_ef{i}{j}", [128, TT]) for j in range(2)]
                    for i in range(2)]
            ehi = [[self.sb(f"l{l}_ehi{i}{j}", [128, TT], BF16)
                    for j in range(2)] for i in range(2)]
            elo = [[self.sb(f"l{l}_elo{i}{j}", [128, TT], BF16)
                    for j in range(2)] for i in range(2)]
            lrow = [self.sb(f"l{l}_lrow{i}", [1, TT]) for i in range(2)]
            blv = [self.sb(f"l{l}_blv{i}", [64, TT]) for i in range(2)]
            oslc = [self.sb(f"l{l}_oslc{i}", [64, TT]) for i in range(2)]
            ohh = [self.sb(f"l{l}_ohh{i}", [64, TT], BF16) for i in range(2)]
            oll = [self.sb(f"l{l}_oll{i}", [64, TT], BF16) for i in range(2)]
            psS = [[self.ps(f"l{l}_psS{i}{j}", [128, TT]) for j in range(2)]
                   for i in range(2)]
            psC = [[self.ps(f"l{l}_psC{i}{j}", [128, TT]) for j in range(2)]
                   for i in range(2)]
            psv = psC[0][0]
            psr = psC[1][0]
            bqraw, brot, bbinv, btbl = B(), B(), B(), B()
            bqkhi = [B(), B()]
            bqklo = [B(), B()]
            bvth = [B(), B()]
            bvtl = [B(), B()]
            bmaskb, bssr = B(), B()
            bsmb = [[B(), B()], [B(), B()]]
            bef = [[B(), B()], [B(), B()]]
            behi = [[B(), B()], [B(), B()]]
            belo = [[B(), B()], [B(), B()]]
            blrow = [B(), B()]
            bblv = [B(), B()]
            boslc = [B(), B()]
            bohh = [B(), B()]
            boll = [B(), B()]
            bpsS = [[B(), B()], [B(), B()]]
            bpsC = [[B(), B()], [B(), B()]]
            bpsv = bpsC[0][0]
            bpsr = bpsC[1][0]
            self.em.dma(self.st("sq:wsl", 3), maskb[:], self.i_maskb, writes=[bmaskb])
            for p in range(FH // 2):
                for qk in range(2):
                    base = qk * INNER + p * 128
                    self.em.dma("sq:xld0", qraw[:],
                                hqkv[base:base + 128, :],
                                reads=[bhqkv], writes=[bqraw])
                    em.op("ve", lambda h: h.tensor_tensor(
                        out=rot[:], in0=qraw[:], in1=qraw[:], op=OP.mult),
                        reads=[bqraw], writes=[brot])
                    for tt in range(NTT):
                        sl = slice(tt * TT, (tt + 1) * TT)
                        em.op("te", (lambda h, sl=sl: h.matmul(
                            out=psr[0:2, :], lhsT=self.c_ones2[:],
                            rhs=rot[:, sl],
                            start=True, stop=True)),
                            reads=[brot, self.b_ones2], writes=[bpsr])
                        em.op("ve", lambda h: h.tensor_scalar(
                            out=ssr[:, TT:2 * TT], in0=psr[0:2, :],
                            scalar1=1e-24,
                            scalar2=None, op0=OP.max),
                            reads=[bpsr], writes=[bssr])
                        self.nr_rsqrt(ssr[:, 0:TT], ssr[:, TT:2 * TT],
                                      ssr[:, 2 * TT:3 * TT],
                                      ssr[:, 3 * TT:4 * TT], [], bssr)
                        hl2 = self.hb(f"l{l}l2")
                        self.em.dma(self.st("ge:rows", 3),
                                    self.row_ap(self.R_L, tt * TT, TT),
                                    ssr[0:1, 0:TT], reads=[bssr], writes=[hl2])
                        self.em.dma(self.st("ge:rows", 3),
                                    self.row_ap(self.R_L + 1, tt * TT, TT),
                                    ssr[1:2, 0:TT], reads=[bssr], writes=[hl2])
                    for hd in range(2):
                        self.em.dma(self.st("se:bc"), binv[hd * 64:(hd + 1) * 64, :],
                                    self.bcast_in(self.R_L + hd, 0, S, 64),
                                    reads=[self.hb(f"l{l}l2")], writes=[bbinv])
                    em.op("ve", (lambda h, qk=qk: h.scalar_tensor_tensor(
                        out=qraw[:], in0=qraw[:],
                        scalar=cols[:, C_QS + qk:C_QS + qk + 1], in1=binv[:],
                        op0=OP.mult, op1=OP.mult)),
                        reads=[bbinv, bcols, bqraw], writes=[bqraw])
                    for a, bq in ((0, 32), (32, 0), (64, 96), (96, 64)):
                        self.em.dma(self.st("ge:tree", 3), rot[a:a + 32, :],
                                    qraw[bq:bq + 32, :],
                                    reads=[bqraw], writes=[brot])
                    self.em.dma("sq:xld1", tblb[:], self.i_tbl[2 * qk],
                                writes=[btbl])
                    em.op("ve", lambda h: h.tensor_tensor(
                        out=qraw[:], in0=qraw[:], in1=tblb[:], op=OP.mult),
                        reads=[btbl, bqraw], writes=[bqraw])
                    self.em.dma("sq:xld1", tblb[:], self.i_tbl[2 * qk + 1],
                                writes=[btbl])
                    em.op("ve", lambda h: h.tensor_tensor(
                        out=rot[:], in0=rot[:], in1=tblb[:], op=OP.mult),
                        reads=[btbl, brot], writes=[brot])
                    em.op("ve", lambda h: h.tensor_tensor(
                        out=qraw[:], in0=qraw[:], in1=rot[:], op=OP.add),
                        reads=[brot, bqraw], writes=[bqraw])
                    em.op("ve", (lambda h, qk=qk: h.tensor_copy(
                        out=qkhi[qk][:], in_=qraw[:])),
                        reads=[bqraw], writes=[bqkhi[qk]])
                    em.op("ge", (lambda h, qk=qk: h.tensor_tensor(
                        out=qklo[qk][:], in0=qraw[:], in1=qkhi[qk][:],
                        op=OP.subtract)),
                        reads=[bqraw, bqkhi[qk]], writes=[bqklo[qk]])
                self.em.dma("sq:xld0", rot[:],
                            hqkv[2 * INNER + p * 128:2 * INNER + (p + 1) * 128, :],
                            reads=[bhqkv], writes=[brot])
                for tc in range(NTC):
                    em.op("te", (lambda h, tc=tc: h.transpose(
                        out=psv[:, 0:128], in_=rot[:, tc * 128:(tc + 1) * 128],
                        identity=self.c_ident[:])),
                        reads=[brot, self.b_ident], writes=[bpsv])
                    for hd in range(2):
                        em.op("ve", (lambda h, tc=tc, hd=hd: h.tensor_copy(
                            out=vth[hd][:, tc * 65:tc * 65 + 64],
                            in_=psv[:, hd * 64:hd * 64 + 64])),
                            reads=[bpsv], writes=[bvth[hd]])
                        em.op("ve", (lambda h, tc=tc, hd=hd: h.tensor_tensor(
                            out=vtl[hd][:, tc * 65:tc * 65 + 64],
                            in0=psv[:, hd * 64:hd * 64 + 64],
                            in1=vth[hd][:, tc * 65:tc * 65 + 64],
                            op=OP.subtract)),
                            reads=[bpsv, bvth[hd]], writes=[bvtl[hd]])
                for hd in range(2):
                    em.op("ve", (lambda h, hd=hd: h.memset(
                        vth[hd][:].rearrange("p (t j) -> p t j",
                                             j=65)[:, :, 64:65], 1.0)),
                        writes=[bvth[hd]])
                    em.op("ve", (lambda h, hd=hd: h.memset(
                        vtl[hd][:].rearrange("p (t j) -> p t j",
                                             j=65)[:, :, 64:65], 0.0)),
                        writes=[bvtl[hd]])
                for w in range(NW):
                    qsl = slice(w * FW, (w + 1) * FW)
                    kcs = [kc for kc in range(8) if w > 0 or kc >= 4]
                    for ik, kc in enumerate(kcs):
                        par = ik % 2
                        kt = (w - 1) * 4 + kc
                        ksl = slice(kt * 128, (kt + 1) * 128)
                        for hd in range(2):
                            hsl = slice(hd * 64, hd * 64 + 64)
                            tp = (hd * 64, 0)
                            passes = ((qkhi[1], bqkhi[1], qkhi[0], bqkhi[0]),
                                      (qkhi[1], bqkhi[1], qklo[0], bqklo[0]),
                                      (qklo[1], bqklo[1], qkhi[0], bqkhi[0]))
                            for i, (kk, bkk, qq, bqq) in enumerate(passes):
                                em.op("te", (lambda h, kk=kk, qq=qq, i=i,
                                             hd=hd, par=par, hsl=hsl,
                                             ksl=ksl, qsl=qsl, tp=tp:
                                             h.matmul(
                                    out=psS[hd][par][:], lhsT=kk[hsl, ksl],
                                    rhs=qq[hsl, qsl], start=(i == 0),
                                    stop=(i == 2), tile_position=tp)),
                                    reads=[bkk, bqq],
                                    writes=[bpsS[hd][par]], notify=(i == 2))
                            em.op("ve", (lambda h, hd=hd, par=par, kc=kc:
                                         h.scalar_tensor_tensor(
                                out=smb[hd][par][:], in0=psS[hd][par][:],
                                scalar=8.0,
                                in1=maskb[:, kc * 512:(kc + 1) * 512],
                                op0=OP.mult, op1=OP.add)),
                                reads=[bpsS[hd][par], bmaskb],
                                writes=[bsmb[hd][par]])
                            em.op("se", (lambda h, hd=hd, par=par: h.activation(
                                out=ef32[hd][par][:], in_=smb[hd][par][:],
                                func=AF.Exp,
                                bias=neglc[:, 0:1], scale=1.0)),
                                reads=[bsmb[hd][par], bneglc],
                                writes=[bef[hd][par]])
                            em.op("ve", (lambda h, hd=hd, par=par:
                                         h.tensor_copy(
                                out=ehi[hd][par][:], in_=ef32[hd][par][:])),
                                reads=[bef[hd][par]], writes=[behi[hd][par]])
                            em.op("ge", (lambda h, hd=hd, par=par:
                                         h.tensor_tensor(
                                out=elo[hd][par][:], in0=ef32[hd][par][:],
                                in1=ehi[hd][par][:], op=OP.subtract)),
                                reads=[bef[hd][par], behi[hd][par]],
                                writes=[belo[hd][par]])
                            vsl = slice(kt * 65, (kt + 1) * 65)
                            cpasses = ((vth[hd], bvth[hd], ehi[hd][par],
                                        behi[hd][par]),
                                       (vtl[hd], bvtl[hd], ehi[hd][par],
                                        behi[hd][par]),
                                       (vth[hd], bvth[hd], elo[hd][par],
                                        belo[hd][par]))
                            wp = w % 2
                            for i, (vv, bvv, ee, bee) in enumerate(cpasses):
                                em.op("te", (lambda h, vv=vv, ee=ee, i=i,
                                             hd=hd, vsl=vsl, ik=ik, wp=wp,
                                             last=len(kcs) - 1: h.matmul(
                                    out=psC[hd][wp][0:65, :], lhsT=vv[:, vsl],
                                    rhs=ee[:],
                                    start=(ik == 0 and i == 0),
                                    stop=(ik == last and i == 2))),
                                    reads=[bvv, bee], writes=[bpsC[hd][wp]],
                                    notify=(i == 2))
                    wp = w % 2
                    for hd in range(2):
                        orow = p * 128 + hd * 64
                        em.op("ve", (lambda h, hd=hd, wp=wp: h.reciprocal(
                            out=lrow[wp][:], in_=psC[hd][wp][64:65, :])),
                            reads=[bpsC[hd][wp]], writes=[blrow[wp]])
                        hl3 = self.hb(f"l{l}lv{wp}")
                        self.em.dma(self.st("ge:rows", 3),
                                    self.row_ap(self.R_L + 2 + wp, w * FW, FW),
                                    lrow[wp][:], reads=[blrow[wp]],
                                    writes=[hl3])
                        self.em.dma(self.st("se:bc"), blv[wp][:],
                                    self.bcast_in(self.R_L + 2 + wp,
                                                  w * FW, FW, 64),
                                    reads=[hl3], writes=[bblv[wp]])
                        em.op("ve", (lambda h, hd=hd, wp=wp: h.tensor_tensor(
                            out=oslc[wp][:], in0=psC[hd][wp][0:64, :],
                            in1=blv[wp][:],
                            op=OP.mult)),
                            reads=[bpsC[hd][wp], bblv[wp]], writes=[boslc[wp]])
                        em.op("ve", (lambda h, wp=wp: h.tensor_copy(
                            out=ohh[wp][:], in_=oslc[wp][:])),
                            reads=[boslc[wp]], writes=[bohh[wp]])
                        em.op("ge", (lambda h, wp=wp: h.tensor_tensor(
                            out=oll[wp][:], in0=oslc[wp][:], in1=ohh[wp][:],
                            op=OP.subtract)),
                            reads=[boslc[wp], bohh[wp]], writes=[boll[wp]])
                        self.em.dma(self.st("ge:st"),
                                    self.h_ohi[orow:orow + 64, qsl],
                                    ohh[wp][:],
                                    reads=[bohh[wp]], writes=[bho])
                        self.em.dma(self.st("ge:st"),
                                    self.h_olo[orow:orow + 64, qsl],
                                    oll[wp][:],
                                    reads=[boll[wp]], writes=[bho2])
            self.em.barrier()
            self.pes = old

        with ExitStack() as st:
            old = self.pes; self.pes = st
            woh = self.sb(f"l{l}_woh", [128, 4 * D], BF16)
            wol = self.sb(f"l{l}_wol", [128, 4 * D], BF16)
            wt3 = self.sb(f"l{l}_f3wt", [128, D])
            bwoh, bwol, bwt3 = B(), B(), B()
            for c in range(4):
                self.em.dma(f"sq:xld{c % 4}", wt3[:], self.i_f_wo[l, :, c, :],
                            writes=[bwt3])
                em.op("ve", (lambda h, c=c: h.tensor_copy(
                    out=woh[:, c * D:(c + 1) * D], in_=wt3[:])),
                    reads=[bwt3], writes=[bwoh])
                em.op("ge", (lambda h, c=c: h.tensor_tensor(
                    out=wol[:, c * D:(c + 1) * D], in0=wt3[:],
                    in1=woh[:, c * D:(c + 1) * D], op=OP.subtract)),
                    reads=[bwt3, bwoh], writes=[bwol])
            oht = [self.sb(f"l{l}_oht{c}", [128, TT], BF16) for c in range(4)]
            olt = [self.sb(f"l{l}_olt{c}", [128, TT], BF16) for c in range(4)]
            boht = [B() for _ in range(4)]
            bolt = [B() for _ in range(4)]
            xt = self.sb(f"l{l}_f3x", [128, TT])
            bxt = B()
            psm = [self.ps(f"l{l}_psm{i}", [128, TT]) for i in range(2)]
            bpsm = [B(), B()]
            for tt in range(NTT):
                sl = slice(tt * TT, (tt + 1) * TT)
                for c in range(4):
                    self.em.dma(f"sq:xld{c % 4}", oht[c][:],
                                self.h_ohi[c * 128:(c + 1) * 128, sl],
                                reads=[bho], writes=[boht[c]])
                    self.em.dma(f"sq:xld{c % 4}", olt[c][:],
                                self.h_olo[c * 128:(c + 1) * 128, sl],
                                reads=[bho2], writes=[bolt[c]])
                for oc in range(8):
                    pp = oc % 2
                    n = 0
                    passes = ((woh, bwoh, oht, boht), (woh, bwoh, olt, bolt),
                              (wol, bwol, oht, boht))
                    for i, (ww, bww, tt_, btt_) in enumerate(passes):
                        for c in range(4):
                            em.op("te", (lambda h, ww=ww, tt_=tt_, c=c,
                                         oc=oc, n=n, pp=pp: h.matmul(
                                out=psm[pp][:],
                                lhsT=ww[:, c * D + oc * 128:
                                        c * D + (oc + 1) * 128],
                                rhs=tt_[c][:],
                                start=(n == 0), stop=(n == 11))),
                                reads=[bww, btt_[c]], writes=[bpsm[pp]],
                                notify=(n == 11))
                            n += 1
                    self.em.dma(f"sq:xld{oc % 4}", xt[:],
                                xin[oc * 128:(oc + 1) * 128, sl],
                                reads=[bxin], writes=[bxt])
                    em.op("ve", (lambda h, pp=pp: h.tensor_tensor(
                        out=xt[:], in0=xt[:], in1=psm[pp][:], op=OP.add)),
                        reads=[bpsm[pp], bxt], writes=[bxt])
                    self.em.dma(self.st("ge:st"), xmid[oc * 128:(oc + 1) * 128, sl],
                                xt[:], reads=[bxt], writes=[bxmid])
            self.em.barrier()
            self.pes = old

        with ExitStack() as st:
            old = self.pes; self.pes = st
            ht = [self.sb(f"l{l}_s4h{i}", [128, TT]) for i in range(2)]
            sqt = [self.sb(f"l{l}_s4q{i}", [128, TT]) for i in range(2)]
            rows = self.sb(f"l{l}_s4r", [1, 5 * TT])
            RMU, RSG = slice(0, TT), slice(TT, 2 * TT)
            RS1, RS2, RS3 = (slice(2 * TT, 3 * TT), slice(3 * TT, 4 * TT),
                             slice(4 * TT, 5 * TT))
            psr1 = self.ps(f"l{l}_psr1", [1, TT])
            psr2 = self.ps(f"l{l}_psr2", [1, TT])
            bht = [B(), B()]
            bsqt = [B(), B()]
            brows = B()
            bpsr1, bpsr2 = B(), B()
            hmu, hsg = self.hb(f"l{l}mu"), self.hb(f"l{l}sg")
            for tt in range(NTT):
                for c in range(8):
                    ln = c % 2
                    self.load_shifted(f"sq:xld{ln}", ht[ln], bht[ln],
                                      xmid, bxmid, c, tt, True)
                    em.op("te", (lambda h, ln=ln, c=c: h.matmul(
                        out=psr1[:], lhsT=self.c_ones1[:], rhs=ht[ln][:],
                        start=(c == 0), stop=(c == 7))),
                        reads=[bht[ln], self.b_ones1], writes=[bpsr1],
                        notify=True)
                    em.op("ge", (lambda h, ln=ln: h.tensor_tensor(
                        out=sqt[ln][:], in0=ht[ln][:], in1=ht[ln][:],
                        op=OP.mult)), reads=[bht[ln]], writes=[bsqt[ln]])
                    em.op("te", (lambda h, ln=ln, c=c: h.matmul(
                        out=psr2[:], lhsT=self.c_ones1[:], rhs=sqt[ln][:],
                        start=(c == 0), stop=(c == 7))),
                        reads=[bsqt[ln], self.b_ones1], writes=[bpsr2],
                        notify=True)
                em.op("ve", lambda h: h.tensor_scalar(
                    out=rows[:, RMU], in0=psr1[:], scalar1=1.0 / D,
                    scalar2=None, op0=OP.mult), reads=[bpsr1], writes=[brows])
                em.op("ve", lambda h: h.tensor_tensor(
                    out=rows[:, RSG], in0=rows[:, RMU], in1=rows[:, RMU],
                    op=OP.mult), reads=[brows], writes=[brows])
                em.op("ve", lambda h: h.scalar_tensor_tensor(
                    out=rows[:, RSG], in0=psr2[:], scalar=1.0 / D,
                    in1=rows[:, RSG], op0=OP.mult, op1=OP.subtract),
                    reads=[bpsr2, brows], writes=[brows])
                em.op("ve", lambda h: h.tensor_scalar(
                    out=rows[:, RS1], in0=rows[:, RSG], scalar1=1e-5,
                    scalar2=None, op0=OP.add), reads=[brows], writes=[brows])
                self.nr_rsqrt(rows[:, RSG], rows[:, RS1], rows[:, RS2],
                              rows[:, RS3], [], brows)
                self.em.dma(self.st("ge:rows", 3), self.row_ap(self.R_MU, tt * TT, TT),
                            rows[:, RMU], reads=[brows], writes=[hmu])
                self.em.dma(self.st("ge:rows", 3), self.row_ap(self.R_SIG, tt * TT, TT),
                            rows[:, RSG], reads=[brows], writes=[hsg])
            self.em.barrier()
            self.pes = old

        with ExitStack() as st:
            old = self.pes; self.pes = st
            h2h = [self.sb(f"l{l}_h2h{c}", [128, TT], BF16) for c in range(8)]
            h2l = [self.sb(f"l{l}_h2l{c}", [128, TT], BF16) for c in range(8)]
            bh2h = [B() for _ in range(8)]
            bh2l = [B() for _ in range(8)]
            hf = self.sb(f"l{l}_h2f", [128, TT])
            bhf = B()
            bmu = self.sb(f"l{l}_bmu", [128, TT])
            bsg = self.sb(f"l{l}_bsg", [128, TT])
            bbmu, bbsg = B(), B()
            mhi = self.sb(f"l{l}_mhi", [128, MOB * TT], BF16)
            mlo = self.sb(f"l{l}_mlo", [128, MOB * TT], BF16)
            bmhi, bmlo = B(), B()
            w1b = [self.sb(f"l{l}_w1b{i}", [128, D], BF16) for i in range(8)]
            bw1b = [B() for _ in range(8)]
            w2b = [self.sb(f"l{l}_w2b{i}", [128, HIDP], BF16) for i in range(4)]
            bw2b = [B() for _ in range(4)]
            gel = self.sb(f"l{l}_gel", [128, TT])
            mf = self.sb(f"l{l}_mf", [128, TT])
            bgel, bmf = B(), B()
            x4t = self.sb(f"l{l}_x4t", [128, TT])
            bx4t = B()
            psA = self.ps(f"l{l}_psA", [128, TT])
            psG = self.ps(f"l{l}_psG", [128, TT])
            psY = [self.ps(f"l{l}_psY{i}", [128, TT]) for i in range(2)]
            bpsA, bpsG = B(), B()
            bpsY = [B(), B()]
            hmu, hsg = self.hb(f"l{l}mu"), self.hb(f"l{l}sg")
            for tt in range(NTT):
                sl = slice(tt * TT, (tt + 1) * TT)
                self.em.dma(self.st("se:bc"), bmu[:],
                            self.bcast_in(self.R_MU, tt * TT, TT, 128),
                            reads=[hmu], writes=[bbmu])
                self.em.dma(self.st("se:bc"), bsg[:],
                            self.bcast_in(self.R_SIG, tt * TT, TT, 128),
                            reads=[hsg], writes=[bbsg])
                for c in range(8):
                    self.load_shifted(f"sq:xld{c % 4}", hf, bhf,
                                      xmid, bxmid, c, tt, True)
                    em.op("ve", lambda h: h.tensor_tensor(
                        out=hf[:], in0=hf[:], in1=bmu[:], op=OP.subtract),
                        reads=[bbmu, bhf], writes=[bhf])
                    em.op("ve", lambda h: h.tensor_tensor(
                        out=hf[:], in0=hf[:], in1=bsg[:], op=OP.mult),
                        reads=[bbsg, bhf], writes=[bhf])
                    em.op("ve", (lambda h, c=c: h.tensor_scalar(
                        out=hf[:], in0=hf[:],
                        scalar1=cols[:, C_LNG + c:C_LNG + c + 1],
                        scalar2=cols[:, C_LNB + c:C_LNB + c + 1],
                        op0=OP.mult, op1=OP.add)),
                        reads=[bcols, bhf], writes=[bhf])
                    em.op("ve", (lambda h, c=c: h.tensor_copy(
                        out=h2h[c][:], in_=hf[:])), reads=[bhf],
                        writes=[bh2h[c]])
                    em.op("ge", (lambda h, c=c: h.tensor_tensor(
                        out=h2l[c][:], in0=hf[:], in1=h2h[c][:],
                        op=OP.subtract)), reads=[bhf, bh2h[c]],
                        writes=[bh2l[c]])
                for mob in range(MOB):
                    o4 = (mob % 2) * 4
                    self.em.dma("sq:xld0", w1b[o4 + 0][:],
                                self.h_w1hi[0, :, mob, :],
                                reads=[self.hb("w1s")], writes=[bw1b[o4 + 0]])
                    self.em.dma("sq:xld1", w1b[o4 + 1][:],
                                self.h_w1lo[0, :, mob, :],
                                reads=[self.hb("w1s")], writes=[bw1b[o4 + 1]])
                    self.em.dma("sq:xld2", w1b[o4 + 2][:],
                                self.h_w1hi[1, :, mob, :],
                                reads=[self.hb("w1s")], writes=[bw1b[o4 + 2]])
                    self.em.dma("sq:xld3", w1b[o4 + 3][:],
                                self.h_w1lo[1, :, mob, :],
                                reads=[self.hb("w1s")], writes=[bw1b[o4 + 3]])
                    for half, ps_, bps_ in ((0, psA, bpsA), (1, psG, bpsG)):
                        n = 0
                        passes = ((w1b[o4 + 2 * half], bw1b[o4 + 2 * half],
                                   h2h, bh2h),
                                  (w1b[o4 + 2 * half], bw1b[o4 + 2 * half],
                                   h2l, bh2l),
                                  (w1b[o4 + 2 * half + 1],
                                   bw1b[o4 + 2 * half + 1], h2h, bh2h))
                        for i, (ww, bww, hh_, bhh_) in enumerate(passes):
                            for c in range(8):
                                em.op("te", (lambda h, ww=ww, hh_=hh_, c=c,
                                             ps_=ps_, n=n: h.matmul(
                                    out=ps_[:],
                                    lhsT=ww[:, c * 128:(c + 1) * 128],
                                    rhs=hh_[c][:],
                                    start=(n == 0), stop=(n == 23))),
                                    reads=[bww, bhh_[c]], writes=[bps_],
                                    notify=(n == 23))
                                n += 1
                    em.op("se", (lambda h, mob=mob: h.activation(
                        out=gel[:], in_=psG[:], func=AF.Gelu,
                        bias=cols[:, C_B1G + mob:C_B1G + mob + 1],
                        scale=1.0)), reads=[bpsG, bcols], writes=[bgel])
                    em.op("ve", (lambda h, mob=mob: h.scalar_tensor_tensor(
                        out=mf[:], in0=psA[:],
                        scalar=cols[:, C_B1A + mob:C_B1A + mob + 1],
                        in1=gel[:], op0=OP.add, op1=OP.mult)),
                        reads=[bpsA, bgel, bcols], writes=[bmf])
                    em.op("ve", (lambda h, mob=mob: h.tensor_copy(
                        out=mhi[:, mob * TT:(mob + 1) * TT], in_=mf[:])),
                        reads=[bmf], writes=[bmhi])
                    em.op("ge", (lambda h, mob=mob: h.tensor_tensor(
                        out=mlo[:, mob * TT:(mob + 1) * TT], in0=mf[:],
                        in1=mhi[:, mob * TT:(mob + 1) * TT],
                        op=OP.subtract)), reads=[bmf, bmhi], writes=[bmlo])
                for oc in range(8):
                    pp = oc % 2
                    o2 = pp * 2
                    self.em.dma("sq:xld0", w2b[o2][:], self.h_w2hi[:, oc, :],
                                reads=[self.hb("w2s")], writes=[bw2b[o2]])
                    self.em.dma("sq:xld1", w2b[o2 + 1][:], self.h_w2lo[:, oc, :],
                                reads=[self.hb("w2s")], writes=[bw2b[o2 + 1]])
                    n = 0
                    passes = ((w2b[o2], bw2b[o2], mhi, bmhi),
                              (w2b[o2], bw2b[o2], mlo, bmlo),
                              (w2b[o2 + 1], bw2b[o2 + 1], mhi, bmhi))
                    for i, (ww, bww, mm_, bmm_) in enumerate(passes):
                        for mob in range(MOB):
                            em.op("te", (lambda h, ww=ww, mm_=mm_, mob=mob,
                                         n=n, pp=pp: h.matmul(
                                out=psY[pp][:],
                                lhsT=ww[:, mob * 128:(mob + 1) * 128],
                                rhs=mm_[:, mob * TT:(mob + 1) * TT],
                                start=(n == 0), stop=(n == 3 * MOB - 1))),
                                reads=[bww, bmm_], writes=[bpsY[pp]],
                                notify=(n == 3 * MOB - 1))
                            n += 1
                    self.em.dma(f"sq:xld{oc % 4}", x4t[:],
                                xmid[oc * 128:(oc + 1) * 128, sl],
                                reads=[bxmid], writes=[bx4t])
                    em.op("ve", (lambda h, oc=oc, pp=pp:
                                 h.scalar_tensor_tensor(
                        out=x4t[:], in0=psY[pp][:],
                        scalar=cols[:, C_B2 + oc:C_B2 + oc + 1], in1=x4t[:],
                        op0=OP.add, op1=OP.add)),
                        reads=[bpsY[pp], bcols, bx4t], writes=[bx4t])
                    self.em.dma(self.st("ge:st"), xout[oc * 128:(oc + 1) * 128, sl],
                                x4t[:], reads=[bx4t], writes=[bxout])
            self.em.barrier()
            self.pes = old

    def p7_final(self):
        em, S, NTT = self.em, self.S, self.NTT
        x6t = [self.sb(f"p7_x{c}", [128, TT]) for c in range(8)]
        sqb = self.sb("p7_sq", [128, TT])
        rows = self.sb("p7_rows", [1, 5 * TT])
        RMU, RSG = slice(0, TT), slice(TT, 2 * TT)
        RS1, RS2, RS3 = (slice(2 * TT, 3 * TT), slice(3 * TT, 4 * TT),
                         slice(4 * TT, 5 * TT))
        bmu = self.sb("p7_bmu", [128, TT])
        bsg = self.sb("p7_bsg", [128, TT])
        yt = self.sb("p7_yt", [128, TT])
        psr1 = self.ps("p7_psr1", [1, TT])
        psr2 = self.ps("p7_psr2", [1, TT])
        bx6 = [B() for _ in range(8)]
        bsqb, brows, bbmu, bbsg, byt = B(), B(), B(), B(), B()
        bpsr1, bpsr2 = B(), B()
        hy = self.hb("y")
        bx3h, bx6h = self.hb("x3"), self.hb("x6")
        for tt in range(NTT):
            sl = slice(tt * TT, (tt + 1) * TT)
            for c in range(8):
                self.em.dma(f"sq:xld{c % 4}", x6t[c][:],
                            self.h_x6[c * 128:(c + 1) * 128, sl],
                            reads=[bx6h], writes=[bx6[c]])
                self.em.dma(f"sq:xld{c % 4}", yt[:],
                            self.h_x3[c * 128:(c + 1) * 128, sl],
                            reads=[bx3h], writes=[byt])
                em.op("ve", (lambda h, c=c: h.tensor_tensor(
                    out=x6t[c][:], in0=x6t[c][:], in1=yt[:], op=OP.add)),
                    reads=[byt, bx6[c]], writes=[bx6[c]])
                em.op("te", (lambda h, c=c: h.matmul(
                    out=psr1[:], lhsT=self.c_ones1[:], rhs=x6t[c][:],
                    start=(c == 0), stop=(c == 7))),
                    reads=[bx6[c], self.b_ones1], writes=[bpsr1],
                    notify=(c == 7))
                em.op("ge", (lambda h, c=c: h.tensor_tensor(
                    out=sqb[:], in0=x6t[c][:], in1=x6t[c][:], op=OP.mult)),
                    reads=[bx6[c]], writes=[bsqb])
                em.op("te", (lambda h, c=c: h.matmul(
                    out=psr2[:], lhsT=self.c_ones1[:], rhs=sqb[:],
                    start=(c == 0), stop=(c == 7))),
                    reads=[bsqb, self.b_ones1], writes=[bpsr2],
                    notify=True)
            em.op("ve", lambda h: h.tensor_scalar(
                out=rows[:, RMU], in0=psr1[:], scalar1=1.0 / D, scalar2=None,
                op0=OP.mult), reads=[bpsr1], writes=[brows])
            em.op("ve", lambda h: h.tensor_tensor(
                out=rows[:, RSG], in0=rows[:, RMU], in1=rows[:, RMU],
                op=OP.mult), reads=[brows], writes=[brows])
            em.op("ve", lambda h: h.scalar_tensor_tensor(
                out=rows[:, RSG], in0=psr2[:], scalar=1.0 / D,
                in1=rows[:, RSG], op0=OP.mult, op1=OP.subtract),
                reads=[bpsr2, brows], writes=[brows])
            em.op("ve", lambda h: h.tensor_scalar(
                out=rows[:, RS1], in0=rows[:, RSG], scalar1=1e-5,
                scalar2=None, op0=OP.add), reads=[brows], writes=[brows])
            self.nr_rsqrt(rows[:, RSG], rows[:, RS1], rows[:, RS2],
                          rows[:, RS3], [], brows)
            hmu, hsg = self.hb("p7mu"), self.hb("p7sg")
            self.em.dma(self.st("ge:rows", 3), self.row_ap(self.R_MU, tt * TT, TT),
                        rows[:, RMU], reads=[brows], writes=[hmu])
            self.em.dma(self.st("ge:rows", 3), self.row_ap(self.R_SIG, tt * TT, TT),
                        rows[:, RSG], reads=[brows], writes=[hsg])
            self.em.dma(self.st("se:bc"), bmu[:],
                        self.bcast_in(self.R_MU, tt * TT, TT, 128),
                        reads=[hmu], writes=[bbmu])
            self.em.dma(self.st("se:bc"), bsg[:],
                        self.bcast_in(self.R_SIG, tt * TT, TT, 128),
                        reads=[hsg], writes=[bbsg])
            for c in range(8):
                em.op("ve" if c % 2 == 0 else "ge",
                      (lambda h, c=c: h.tensor_tensor(
                          out=x6t[c][:], in0=x6t[c][:], in1=bmu[:],
                          op=OP.subtract)),
                      reads=[bbmu, bx6[c]], writes=[bx6[c]])
                em.op("ve" if c % 2 == 0 else "ge",
                      (lambda h, c=c: h.tensor_tensor(
                          out=x6t[c][:], in0=x6t[c][:], in1=bsg[:],
                          op=OP.mult)),
                      reads=[bbsg, bx6[c]], writes=[bx6[c]])
                self.em.dma(self.st("ge:st") if c % 2 else self.st("ge:st"),
                            self.o_y[c * 128:(c + 1) * 128, sl], x6t[c][:],
                            reads=[bx6[c]], writes=[hy])
        self.em.barrier()

    def build(self):
        self.c_ones1 = self.sb("c_ones1", [128, 1])
        self.c_ones2 = self.sb("c_ones2", [128, 2])
        self.c_ident = self.sb("c_ident", [128, 128])
        self.b_ones1, self.b_ones2, self.b_ident = B(), B(), B()
        self.em.dma("sq:xld0", self.c_ones1[:], self.i_ones1, writes=[self.b_ones1])
        self.em.dma("sq:xld1", self.c_ones2[:], self.i_ones2, writes=[self.b_ones2])
        self.em.dma("sq:xld2", self.c_ident[:], self.i_ident, writes=[self.b_ident])
        def tapdump(name, region, rows):
            if name in self.debug_taps:
                ap = self.nc.dram_tensor("tap_" + name, [rows, self.S], F32,
                                         kind="ExternalOutput").ap()
                self.em.dma("ge:tap", ap, region[0:rows, :])
                self.em.barrier()
        phases = [self.p0_quant_x, self.p1_ternarize, ("stop_p1",),
                  self.p2_attention,
                  (lambda: tapdump("s_ctx", self.h_ctx, D)), ("stop_p2",),
                  self.p3_scramble_quant, self.p4_oproj_ln,
                  (lambda: tapdump("s_x3", self.h_x3, D)), ("stop_p4",),
                  (lambda: self.p6_layer(0)),
                  (lambda: tapdump("s_qkv0", self.h_qkv, 3 * INNER)),
                  (lambda: tapdump("s_x40", self.h_x4, D)),
                  (lambda: tapdump("s_x50", self.h_x5, D)), ("stop_f0",),
                  (lambda: self.p6_layer(1)),
                  (lambda: tapdump("s_x6", self.h_x6, D)),
                  self.p7_final]
        for ph in phases:
            if isinstance(ph, tuple):
                if ph[0] in self.debug_taps:
                    return
                continue
            self.pes = ExitStack()
            with self.pes:
                ph()
            self.pes = None


def build_program(S, debug_taps=()):
    b = Builder(S, debug_taps)
    with b.es:
        b.build()
        b.em.emit(b.nc, b.es)
    return b


def kernel(**inputs):
    x = np.asarray(inputs["L0_attn_x"])
    S = x.shape[1]
    prog = build_program(S)
    in_maps = _host_in_maps(inputs, prog)
    res = run_bass_kernel_spmd(prog.nc, in_maps, core_ids=list(range(8)))
    out = np.stack([np.ascontiguousarray(r["y_fm"]).T for r in res.results], axis=0)
    return out.astype(np.float32)


def _host_in_maps(inputs, prog):
    S = prog.S
    x = np.asarray(inputs["L0_attn_x"], dtype=np.float32)
    wq = np.asarray(inputs["L0_attn_wq"], dtype=np.float32)
    wk = np.asarray(inputs["L0_attn_wk"], dtype=np.float32)
    wv = np.asarray(inputs["L0_attn_wv"], dtype=np.float32)
    wo = np.asarray(inputs["L0_attn_wo"], dtype=np.float32)
    wqkv = np.asarray(inputs["L0_ffn_wqkv"], dtype=np.float32)
    qs = np.asarray(inputs["L0_ffn_q_scale"], dtype=np.float32)
    ks = np.asarray(inputs["L0_ffn_k_scale"], dtype=np.float32)
    fwo = np.asarray(inputs["L0_ffn_wo"], dtype=np.float32)
    lng = np.asarray(inputs["L0_ffn_ln_g"], dtype=np.float32)
    lnb = np.asarray(inputs["L0_ffn_ln_b"], dtype=np.float32)
    w1 = np.asarray(inputs["L0_ffn_w1"], dtype=np.float32)
    b1 = np.asarray(inputs["L0_ffn_b1"], dtype=np.float32)
    w2 = np.asarray(inputs["L0_ffn_w2"], dtype=np.float32)
    b2 = np.asarray(inputs["L0_ffn_b2"], dtype=np.float32)

    wq_t = np.ascontiguousarray(wq.T)
    wk_t = np.ascontiguousarray(wk.T)
    wv_t = np.ascontiguousarray(wv.T)
    wo_t = np.ascontiguousarray(wo.T)
    w1a = np.zeros((2, D, HIDP), np.float32)
    w1g = np.zeros((2, D, HIDP), np.float32)
    w1a[:, :, :HID] = w1[:, :, :HID]
    w1g[:, :, :HID] = w1[:, :, HID:]
    w2p = np.zeros((2, HIDP, D), np.float32)
    w2p[:, :HID, :] = w2
    wqkv_r = np.ascontiguousarray(
        wqkv.reshape(2, 8, 128, 3 * INNER).transpose(0, 2, 1, 3))
    fwo_r = np.ascontiguousarray(fwo.reshape(2, 4, 128, D).transpose(0, 2, 1, 3))
    w1a_r = np.ascontiguousarray(
        w1a.reshape(2, 8, 128, MOB, 128).transpose(0, 2, 3, 1, 4)
        .reshape(2, 128, MOB, D))
    w1g_r = np.ascontiguousarray(
        w1g.reshape(2, 8, 128, MOB, 128).transpose(0, 2, 3, 1, 4)
        .reshape(2, 128, MOB, D))
    w2_r = np.ascontiguousarray(
        w2p.reshape(2, MOB, 128, 8, 128).transpose(0, 2, 3, 1, 4)
        .reshape(2, 128, 8, HIDP))
    b1a = np.zeros((2, HIDP), np.float32)
    b1g = np.zeros((2, HIDP), np.float32)
    b1a[:, :HID] = b1[:, :HID]
    b1g[:, :HID] = b1[:, HID:]
    b1a_col = np.ascontiguousarray(b1a.reshape(2, MOB, 128).transpose(0, 2, 1))
    b1g_col = np.ascontiguousarray(b1g.reshape(2, MOB, 128).transpose(0, 2, 1))
    b2_col = np.ascontiguousarray(b2.reshape(2, 8, 128).transpose(0, 2, 1))
    lng_col = np.ascontiguousarray(lng.reshape(2, 8, 128).transpose(0, 2, 1))
    lnb_col = np.ascontiguousarray(lnb.reshape(2, 8, 128).transpose(0, 2, 1))
    qs_col = np.ascontiguousarray(np.tile(qs, (1, 2))[:, :, None])
    ks_col = np.ascontiguousarray(np.tile(ks, (1, 2))[:, :, None])
    tbl = np.stack(_rotary_tables(S), axis=0)
    mask = _local_mask()
    maskb = ((1.0 - np.asarray(mask, np.float32)) * -3.0e38).astype(
        ml_dtypes.bfloat16)
    ident = np.eye(128, dtype=np.float32)
    ones1 = np.ones((128, 1), np.float32)
    ones2 = np.zeros((128, 2), np.float32)
    ones2[:64, 0] = 1.0
    ones2[64:, 1] = 1.0

    shared = dict(
        wq_t=wq_t, wk_t=wk_t, wv_t=wv_t, wo_t=wo_t,
        f_wqkv=wqkv_r, f_wo=fwo_r, f_w1a=w1a_r, f_w1g=w1g_r, f_w2=w2_r,
        b1a_col=b1a_col, b1g_col=b1g_col, b2_col=b2_col,
        lng_col=lng_col, lnb_col=lnb_col, qs_col=qs_col, ks_col=ks_col,
        tbl=tbl, mask_loc=mask, maskb_loc=maskb, ident=ident, ones1=ones1,
        ones2=ones2,
    )
    maps = []
    for b_ in range(x.shape[0]):
        m = dict(shared)
        m["x_fm"] = np.ascontiguousarray(x[b_].T)
        maps.append(m)
    return maps
